# version 02
# baseline (speedup 1.0000x reference)
import math
import numpy as np
import ml_dtypes
from contextlib import ExitStack

import concourse.bass as bass
import concourse.mybir as mybir
from concourse.bass_utils import run_bass_kernel_spmd

F32 = mybir.dt.float32
BF16 = mybir.dt.bfloat16
AF = mybir.ActivationFunctionType
ALU = mybir.AluOpType
AX = mybir.AxisListType

ENGINES = ("tensor", "vector", "scalar", "gpsimd", "sync")
DMA_QUEUES = ("sync", "scalar", "gpsimd")

T_SEQ = 2048
D_MODEL = 1024
NT = T_SEQ // 128
ND = D_MODEL // 128
N_CORES = 8



class Buf:
    __slots__ = ("name", "writer", "readers")

    def __init__(self, name):
        self.name = name
        self.writer = None
        self.readers = []


class Op:
    __slots__ = ("fn", "deps", "inc", "seq", "name", "is_dma", "dma_queue_idx")

    def __init__(self, fn, deps, seq, name, is_dma=False):
        self.fn = fn
        self.deps = deps
        self.inc = False
        self.seq = seq
        self.name = name
        self.is_dma = is_dma
        self.dma_queue_idx = -1


class Prog:
    def __init__(self, nc):
        self.nc = nc
        self.streams = {e: [] for e in ENGINES}
        self.dep_targets = set()
        self.dma_idx = {q: 0 for q in DMA_QUEUES}
        self._ctx = ExitStack()
        self.bufmap = {}
        self._handles = []

    def sb(self, name, shape, dtype):
        h = self._ctx.enter_context(self.nc.sbuf_tensor(name, list(shape), dtype))
        self.bufmap[id(h)] = Buf(name)
        self._handles.append(h)
        return h

    def ps(self, name, shape, dtype):
        h = self._ctx.enter_context(self.nc.psum_tensor(name, list(shape), dtype))
        self.bufmap[id(h)] = Buf(name)
        self._handles.append(h)
        return h

    def _b(self, t):
        if isinstance(t, Buf):
            return t
        return self.bufmap[id(t)]

    def _mkdeps(self, eng, seq, rbufs, wbufs, is_dma=False):
        deps = []
        for b in rbufs:
            if b.writer is not None:
                deps.append(b.writer)
        for b in wbufs:
            if b.writer is not None:
                deps.append(b.writer)
            deps.extend(b.readers)
        dd = {}
        for d in deps:
            if d[0] == "e" and d[1] == eng and not is_dma:
                if eng == "tensor":
                    continue
                if d[2] >= seq:
                    continue
            key = (d[0], d[1])
            if key not in dd or dd[key][2] < d[2]:
                dd[key] = d
        return list(dd.values())

    def op(self, eng, fn, reads=(), writes=(), name=""):
        seq = len(self.streams[eng])
        rbufs = [self._b(t) for t in reads]
        wbufs = [self._b(t) for t in writes]
        deps = self._mkdeps(eng, seq, rbufs, wbufs)
        o = Op(fn, deps, seq, name)
        self.streams[eng].append(o)
        me = ("e", eng, seq)
        for b in rbufs:
            b.readers.append(me)
        for b in wbufs:
            b.writer = me
            b.readers = []
        for d in deps:
            if d[0] == "e":
                self.dep_targets.add((d[1], d[2]))
        return o

    def dma(self, queue, out, in_, reads=(), writes=(), name="dma", **kw):
        seq = len(self.streams[queue])
        rbufs = [self._b(t) for t in reads]
        wbufs = [self._b(t) for t in writes]
        deps = self._mkdeps(queue, seq, rbufs, wbufs, is_dma=True)
        idx = self.dma_idx[queue]
        self.dma_idx[queue] = idx + 1

        def fn(eng, out=out, in_=in_, kw=kw):
            return eng.dma_start(out=out, in_=in_, **kw)

        o = Op(fn, deps, seq, name, is_dma=True)
        o.dma_queue_idx = idx
        self.streams[queue].append(o)
        me = ("d", queue, idx)
        for b in rbufs:
            b.readers.append(me)
        for b in wbufs:
            b.writer = me
            b.readers = []
        for d in deps:
            if d[0] == "e":
                self.dep_targets.add((d[1], d[2]))
        return o

    def finalize(self, block):
        for (eng, seq) in self.dep_targets:
            self.streams[eng][seq].inc = True
        prefix = {}
        for eng in ENGINES:
            c = 0
            pc = []
            for o in self.streams[eng]:
                if o.inc and not o.is_dma:
                    c += 1
                pc.append(c)
            prefix[eng] = pc
        self.sem = {}
        self.dsem = {}
        for eng in ENGINES:
            if any((o.inc and not o.is_dma) for o in self.streams[eng]):
                self.sem[eng] = self._ctx.enter_context(
                    self.nc.semaphore("sem_" + eng))
        for q in DMA_QUEUES:
            if any(o.is_dma for o in self.streams[q]):
                self.dsem[q] = self._ctx.enter_context(
                    self.nc.semaphore("dsem_" + q))

        def emit(proxy, eng_name):
            waited = {}
            for o in self.streams[eng_name]:
                for d in o.deps:
                    if d[0] == "e":
                        sem = self.sem[d[1]]
                        val = prefix[d[1]][d[2]]
                    else:
                        sem = self.dsem[d[1]]
                        val = 16 * (d[2] + 1)
                    key = id(sem)
                    if waited.get(key, -1) >= val:
                        continue
                    waited[key] = val
                    proxy.wait_ge(sem, val)
                if o.fn is None:
                    continue
                inst = o.fn(proxy)
                if o.is_dma:
                    inst.then_inc(self.dsem[eng_name], 16)
                elif o.inc:
                    inst.then_inc(self.sem[eng_name], 1)

        for eng_name in ENGINES:
            getattr(block, eng_name)(lambda proxy, e=eng_name: emit(proxy, e))


class Env:
    def __init__(self):
        self.pool = {}
        self.flags = set()

    def get(self, P, name, shape, dtype):
        if name not in self.pool:
            self.pool[name] = P.sb(name, list(shape), dtype)
        return self.pool[name]

    def once(self, flag):
        if flag in self.flags:
            return False
        self.flags.add(flag)
        return True



def build_identity(P, env):
    env.ident = P.sb("ident", [128, 128], BF16)
    P.op("gpsimd", lambda e: e.memset(env.ident[:], 0.0), writes=[env.ident])
    P.op("gpsimd", lambda e: e.affine_select(
        out=env.ident[:], in_=env.ident[:], compare_op=ALU.not_equal, fill=1.0,
        base=0, pattern=[[-1, 128]], channel_multiplier=1),
        reads=[env.ident], writes=[env.ident])


def alloc_common(P, env):
    env.xnT = P.sb("xnT", [128, ND * T_SEQ], BF16)
    env.stats = [P.sb(f"st{i}", [128, 8], F32) for i in range(NT)]
    env.xn_tm = [P.sb(f"xntm{i}", [128, D_MODEL], BF16) for i in range(2)]
    env.psB = [P.ps(f"psB{i}", [128, 512], F32) for i in range(8)]
    env.psA = env.psB[:6]
    env.ps_tp = [env.psB[6], env.psB[7]]
    env.SA0 = P.sb("SA0", [128, 8192], BF16)
    env.SA1a = P.sb("SA1a", [128, 4096], BF16)
    env.SA1b = P.sb("SA1b", [128, 4096], BF16)
    env.SB = P.sb("SB", [128, 16384], BF16)
    env.SF = P.sb("SF", [128, 4096], F32)
    env.gbB = P.sb("gbB", [128, 2048], BF16)
    env.scF1 = P.sb("scF1", [128, 1024], F32)
    env.scF2 = P.sb("scF2", [128, 1024], F32)
    env.ones_f = P.sb("ones_f", [1, 128], F32)
    P.op("gpsimd", lambda e: e.memset(env.ones_f[:], 1.0), writes=[env.ones_f])
    env.wslot = [P.sb(f"wslot{j}", [128, 1024], BF16) for j in range(2)]
    env.lam_sb = P.sb("lam_sb", [1, 16], F32)
    env.lamrep = P.sb("lamrep", [128, 8], F32)
    env.row_sc = P.sb("row_sc", [1, 2048], F32)
    env.srow = env.row_sc[0:1, 0:512]
    P.bufmap[id(env.srow)] = P._b(env.row_sc)
    env.cst = P.sb("cst", [128, 4], F32)
    P.op("gpsimd", lambda e: e.memset(env.cst[:, 0:1], 1e-5), writes=[env.cst])
    P.op("gpsimd", lambda e: e.memset(env.cst[:, 1:2], 1.0), writes=[env.cst])
    P.op("gpsimd", lambda e: e.memset(env.cst[:, 2:3], 1e-6), writes=[env.cst])
    env.identf = P.sb("identf", [128, 128], F32)
    P.op("gpsimd", lambda e: e.memset(env.identf[:], 0.0), writes=[env.identf])
    P.op("gpsimd", lambda e: e.affine_select(
        out=env.identf[:], in_=env.identf[:], compare_op=ALU.not_equal, fill=1.0,
        base=0, pattern=[[-1, 128]], channel_multiplier=1),
        reads=[env.identf], writes=[env.identf])
    env.selH = P.sb("selH", [128, 2], BF16)
    P.op("gpsimd", lambda e: e.memset(env.selH[:], 1.0), writes=[env.selH])
    P.op("gpsimd", lambda e: e.affine_select(
        out=env.selH[:], in_=env.selH[:], compare_op=ALU.is_ge, fill=0.0,
        base=0, pattern=[[-64, 2]], channel_multiplier=1),
        reads=[env.selH], writes=[env.selH])
    P.op("gpsimd", lambda e: e.affine_select(
        out=env.selH[:], in_=env.selH[:], compare_op=ALU.is_ge, fill=0.0,
        base=63, pattern=[[64, 2]], channel_multiplier=-1),
        reads=[env.selH], writes=[env.selH])
    env.rts = P.sb("rts", [128, 64], F32)


def load_x(P, env, x_dram):
    env.x = [P.sb(f"x{i}", [128, D_MODEL], F32) for i in range(NT)]
    for i in range(NT):
        P.dma("sync", env.x[i][:], x_dram[i * 128:(i + 1) * 128, :],
              writes=[env.x[i]], name=f"ldx{i}")


def store_x(P, env, x_out_dram):
    for i in range(NT):
        P.dma("sync", x_out_dram[i * 128:(i + 1) * 128, :], env.x[i][:],
              reads=[env.x[i]], name=f"stx{i}")
    o = Op(None, [("d", "sync", P.dma_idx["sync"] - 1)],
           len(P.streams["sync"]), "final")
    P.streams["sync"].append(o)


def layernorm_to_xnT(P, env):
    for i in range(NT):
        st = env.stats[i]
        xi = env.x[i]
        P.op("vector", lambda e, xi=xi, st=st: e.reduce_sum(
            st[:, 0:1], xi[:], axis=AX.X), reads=[xi], writes=[st])
        P.op("vector", lambda e, st=st: e.tensor_scalar(
            out=st[:, 1:2], in0=st[:, 0:1], scalar1=-1.0 / D_MODEL, scalar2=None,
            op0=ALU.mult), reads=[st], writes=[st])
        sqs = env.xn_tm[i % 2]
        P.op("scalar", lambda e, xi=xi, st=st, sqs=sqs: e.activation(
            out=sqs[:], in_=xi[:], func=AF.Square, bias=st[:, 1:2],
            accum_out=st[:, 2:3]), reads=[xi, st], writes=[sqs, st])
        P.op("scalar", lambda e, st=st: e.activation(
            out=st[:, 3:4], in_=st[:, 2:3], func=AF.Ln, scale=1.0 / D_MODEL,
            bias=env.cst[:, 0:1]), reads=[st, env.cst], writes=[st])
        P.op("scalar", lambda e, st=st: e.activation(
            out=st[:, 4:5], in_=st[:, 3:4], func=AF.Exp, scale=-0.5),
            reads=[st], writes=[st])
        P.op("vector", lambda e, st=st: e.tensor_tensor(
            out=st[:, 5:6], in0=st[:, 1:2], in1=st[:, 4:5], op=ALU.mult),
            reads=[st], writes=[st])
        tm = env.xn_tm[i % 2]
        P.op("vector", lambda e, xi=xi, st=st, tm=tm: e.tensor_scalar(
            out=tm[:], in0=xi[:], scalar1=st[:, 4:5], scalar2=st[:, 5:6],
            op0=ALU.mult, op1=ALU.add), reads=[xi, st], writes=[tm])
        pstp = env.ps_tp[i % 2]
        pstp_bf = pstp[:].bitcast(BF16)
        for k in range(ND):
            P.op("tensor", lambda e, tm=tm, pstp_bf=pstp_bf, k=k: e.transpose(
                pstp_bf[:, k * 128:(k + 1) * 128], tm[:, k * 128:(k + 1) * 128],
                env.ident[:]), reads=[tm, env.ident], writes=[pstp])
        dst = env.xnT[:].rearrange("p (k t) -> p k t", k=ND)[:, :, i * 128:(i + 1) * 128]
        src_ap = pstp_bf.rearrange("p (k t) -> p k t", k=ND)
        P.op("vector", lambda e, dst=dst, src_ap=src_ap: e.tensor_copy(
            out=dst, in_=src_ap), reads=[pstp], writes=[env.xnT])


def xnt(env, k, t0, t1):
    return env.xnT[:, k * T_SEQ + t0:k * T_SEQ + t1]


def residual_add_tile(P, env, i, cols, src_ap, reads):
    xi = env.x[i]
    P.op("gpsimd", lambda e, xi=xi, src_ap=src_ap, cols=cols: e.tensor_tensor(
        out=xi[:, cols], in0=xi[:, cols], in1=src_ap, op=ALU.add),
        reads=[xi] + list(reads), writes=[xi])



def spec_l1ffn():
    def t(i, key):
        return np.ascontiguousarray(np.asarray(i[key]).T)

    def mu(i):
        return np.ascontiguousarray(np.concatenate([
            np.asarray(i["L1_ffn_mu_r"]).reshape(8, 128).T,
            np.asarray(i["L1_ffn_mu_k"]).reshape(8, 128).T], axis=1))

    return {
        "f1_WrT": ((1024, 1024), F32, lambda i: t(i, "L1_ffn_W_r")),
        "f1_WkT": ((1024, 1024), F32, lambda i: t(i, "L1_ffn_W_k")),
        "f1_WvT": ((1024, 1024), F32, lambda i: t(i, "L1_ffn_W_v")),
        "f1_mu": ((128, 16), F32, mu),
    }


def build_l1ffn(P, env, stage=99):
    layernorm_to_xnT(P, env)
    d = env.dram
    mu = env.get(P, "f1mu_sb", (128, 16), F32)
    if env.once("f1mu"):
        P.dma("sync", mu[:], d["f1_mu"], writes=[mu], name="f1mu")
    WrT = env.SB[:, 0:8192]
    WkT = env.SB[:, 8192:16384]
    WvT = env.SF[:].bitcast(BF16)
    for (t, tb_, w, off) in ((WrT, env.SB, "f1_WrT", 0), (WkT, env.SB, "f1_WkT", 0),
                             (WvT, env.SF, "f1_WvT", 0)):
        for k in range(ND):
            P.dma("gpsimd", t[:, k * 1024:(k + 1) * 1024],
                  d[w][k * 128:(k + 1) * 128, :], writes=[tb_], name=w + str(k))
    bSA0 = env.SA0
    xr = [env.SA0[:, k * 512:(k + 1) * 512] for k in range(ND)]
    xk = [env.SA0[:, 4096 + k * 512:4096 + (k + 1) * 512] for k in range(ND)]
    ka = [env.SA1a[:, k * 512:(k + 1) * 512] for k in range(ND)]
    rsb = [env.SA1b[:, j * 1024:(j + 1) * 1024] for j in range(4)]
    tmp = env.scF2[:].bitcast(BF16)[:, 0:512]
    krf = env.scF1[:, 0:512]
    osc = env.scF1[:, 512:1024]

    if stage < 1:
        return
    for c in range(4):
        t0 = c * 512
        for k in range(ND):
            for (dst, mcol) in ((xr[k], k), (xk[k], 8 + k)):
                lo = 1 if t0 == 0 else 0
                if t0 == 0:
                    P.op("vector", lambda e, dst=dst, k=k, mcol=mcol: e.tensor_scalar(
                        out=dst[:, 0:1], in0=xnt(env, k, 0, 1),
                        scalar1=mu[:, mcol:mcol + 1], scalar2=None, op0=ALU.mult),
                        reads=[env.xnT, mu], writes=[bSA0])
                xs = xnt(env, k, t0 + lo, t0 + 512)
                xp = xnt(env, k, t0 + lo - 1, t0 + 511)
                P.op("vector", lambda e, xs=xs, xp=xp, lo=lo: e.tensor_tensor(
                    out=tmp[:, lo:512], in0=xs, in1=xp, op=ALU.subtract),
                    reads=[env.xnT], writes=[env.scF2])
                P.op("vector", lambda e, dst=dst, xp=xp, lo=lo, mcol=mcol:
                     e.scalar_tensor_tensor(
                         out=dst[:, lo:512], in0=tmp[:, lo:512],
                         scalar=mu[:, mcol:mcol + 1], in1=xp,
                         op0=ALU.mult, op1=ALU.add),
                     reads=[env.scF2, env.xnT, mu], writes=[bSA0])
        if stage < 2:
            continue
        for f in range(ND):
            pk = env.psA[2 + (f % 2)]
            for k in range(ND):
                lhsT = WkT[:, k * 1024 + f * 128:k * 1024 + (f + 1) * 128]
                P.op("tensor", lambda e, pk=pk, lhsT=lhsT, xkk=xk[k], k=k: e.matmul(
                    pk[:], lhsT, xkk, start=(k == 0), stop=(k == ND - 1)),
                    reads=[env.SB, bSA0], writes=[pk])
            P.op("vector", lambda e, pk=pk: e.tensor_scalar(
                out=krf, in0=pk[:], scalar1=0.0, scalar2=None, op0=ALU.max),
                reads=[pk], writes=[env.scF1])
            P.op("scalar", lambda e, f=f: e.activation(
                out=ka[f], in_=krf, func=AF.Square),
                reads=[env.scF1], writes=[env.SA1a])
        if stage < 3:
            continue
        for tt in range(4):
            for h in range(2):
                pr = env.psA[h]
                for k in range(ND):
                    lhsT = xr[k][:, tt * 128:(tt + 1) * 128]
                    rhs = WrT[:, k * 1024 + h * 512:k * 1024 + h * 512 + 512]
                    P.op("tensor", lambda e, pr=pr, lhsT=lhsT, rhs=rhs, k=k: e.matmul(
                        pr[:], lhsT, rhs, start=(k == 0), stop=(k == ND - 1)),
                        reads=[env.SB, bSA0], writes=[pr])
                P.op("scalar", lambda e, pr=pr, tt=tt, h=h: e.activation(
                    out=rsb[tt][:, h * 512:(h + 1) * 512], in_=pr[:], func=AF.Sigmoid),
                    reads=[pr], writes=[env.SA1b])
        if stage < 4:
            continue
        for tt in range(4):
            i_tile = c * 4 + tt
            for h in range(2):
                po = env.psA[4 + h]
                for f in range(ND):
                    lhsT = ka[f][:, tt * 128:(tt + 1) * 128]
                    rhs = WvT[:, f * 1024 + h * 512:f * 1024 + h * 512 + 512]
                    P.op("tensor", lambda e, po=po, lhsT=lhsT, rhs=rhs, f=f: e.matmul(
                        po[:], lhsT, rhs, start=(f == 0), stop=(f == ND - 1)),
                        reads=[env.SF, env.SA1a], writes=[po])
                if stage < 5:
                    P.op("vector", lambda e, po=po: e.tensor_copy(
                        out=osc, in_=po[:]), reads=[po], writes=[env.scF1])
                else:
                    P.op("vector", lambda e, po=po, tt=tt, h=h: e.tensor_tensor(
                        out=osc, in0=po[:], in1=rsb[tt][:, h * 512:(h + 1) * 512],
                        op=ALU.mult), reads=[po, env.SA1b], writes=[env.scF1])
                if stage >= 6:
                    residual_add_tile(P, env, i_tile, slice(h * 512, (h + 1) * 512),
                                      osc, [env.scF1])




def spec_l0ffn():
    def f(key, fn):
        return lambda i, key=key, fn=fn: fn(np.asarray(i[key]))

    return {
        "fr_WrT": ((1024, 8), F32, f("L0_ffn_Wr", lambda a: np.ascontiguousarray(a.T))),
        "fr_br": ((1, 8), F32, f("L0_ffn_br", lambda a: a.reshape(1, 8))),
        "f0_W1T": ((8192, 4096), F32, f("L0_ffn_W1", lambda a: np.ascontiguousarray(
            a.transpose(0, 2, 1).reshape(8192, 4096)))),
        "f0_b1": ((128, 256), F32, f("L0_ffn_b1", lambda a: np.ascontiguousarray(
            a.reshape(8, 32, 128).transpose(2, 0, 1).reshape(128, 256)))),
        "f0_W2T": ((32768, 1024), F32, f("L0_ffn_W2", lambda a: np.ascontiguousarray(
            a.transpose(0, 2, 1).reshape(32768, 1024)))),
        "f0_b2": ((1, 8192), F32, f("L0_ffn_b2", lambda a: a.reshape(1, 8192))),
        "f0_gb": ((1, 2048), F32, lambda i: np.concatenate(
            [np.asarray(i["L0_ffn_gamma"]), np.asarray(i["L0_ffn_beta"])]).reshape(1, 2048)),
    }


def build_l0ffn(P, env, stage=99):
    layernorm_to_xnT(P, env)
    d = env.dram
    if env.once("f0small"):
        env.wr_sb = P.sb("wr_sb", [128, 64], BF16)
        for k in range(ND):
            P.dma("gpsimd", env.wr_sb[:, 8 * k:8 * (k + 1)],
                  d["fr_WrT"][k * 128:(k + 1) * 128, :], writes=[env.wr_sb])
        env.br_row = P.sb("br_row", [1, 8], F32)
        P.dma("sync", env.br_row[:], d["fr_br"], writes=[env.br_row])
        env.b1_sb = P.sb("b1_sb", [128, 256], F32)
        P.dma("sync", env.b1_sb[:], d["f0_b1"], writes=[env.b1_sb])

        P.dma("sync", env.row_sc[:], d["f0_gb"], writes=[env.row_sc])
        env.brb = P.sb("brb", [128, 8], F32)
        env.coef = P.sb("coef", [128, 128], F32)
        pb = env.psB[0]
        P.op("tensor", lambda e, pb=pb: e.matmul(
            pb[:, 0:8], env.ones_f[0:1, :], env.br_row[0:1, :],
            start=True, stop=True), reads=[env.ones_f, env.br_row], writes=[pb])
        P.op("vector", lambda e, pb=pb: e.tensor_copy(out=env.brb[:], in_=pb[:, 0:8]),
             reads=[pb], writes=[env.brb])
        for j in range(4):
            pb = env.psB[j % 2]
            P.op("tensor", lambda e, pb=pb, j=j: e.matmul(
                pb[:], env.ones_f[0:1, :], env.row_sc[0:1, j * 512:(j + 1) * 512],
                start=True, stop=True), reads=[env.ones_f, env.row_sc], writes=[pb])
            P.op("vector", lambda e, pb=pb, j=j: e.tensor_copy(
                out=env.gbB[:, j * 512:(j + 1) * 512], in_=pb[:]),
                reads=[pb], writes=[env.gbB])

    rts = env.rts
    for i in range(NT):
        pr = env.psB[i % 2]
        for k in range(ND):
            P.op("tensor", lambda e, pr=pr, k=k, i=i: e.matmul(
                pr[:, 0:8], xnt(env, k, i * 128, (i + 1) * 128),
                env.wr_sb[:, 8 * k:8 * (k + 1)], start=(k == 0), stop=(k == ND - 1)),
                reads=[env.xnT, env.wr_sb], writes=[pr])
        lg, mx = rts[:, 0:8], rts[:, 8:16]
        e1, e2, t8, e2b = rts[:, 16:24], rts[:, 24:32], rts[:, 40:48], rts[:, 48:56]
        c1, c2 = rts[:, 0:8], rts[:, 8:16]
        P.op("vector", lambda e, pr=pr, lg=lg: e.tensor_tensor(
            out=lg, in0=pr[:, 0:8], in1=env.brb[:], op=ALU.add),
            reads=[pr, env.brb], writes=[rts])
        P.op("vector", lambda e, lg=lg, mx=mx: e.max(out=mx, in_=lg),
             reads=[rts], writes=[rts])
        P.op("vector", lambda e, mx=mx: e.tensor_tensor(
            out=rts[:, 32:33], in0=mx[:, 0:1], in1=mx[:, 1:2], op=ALU.subtract),
            reads=[rts], writes=[rts])
        P.op("scalar", lambda e: e.activation(
            out=rts[:, 33:34], in_=rts[:, 32:33], func=AF.Sigmoid),
            reads=[rts], writes=[rts])
        P.op("vector", lambda e: e.tensor_scalar(
            out=rts[:, 34:35], in0=rts[:, 33:34], scalar1=-1.0, scalar2=1.0,
            op0=ALU.mult, op1=ALU.add), reads=[rts], writes=[rts])
        P.op("vector", lambda e, lg=lg, e1=e1, mx=mx: e.tensor_scalar(
            out=e1, in0=lg, scalar1=mx[:, 0:1], scalar2=None, op0=ALU.is_equal),
            reads=[rts], writes=[rts])
        P.op("vector", lambda e, lg=lg, e2=e2, mx=mx: e.tensor_scalar(
            out=e2, in0=lg, scalar1=mx[:, 1:2], scalar2=None, op0=ALU.is_equal),
            reads=[rts], writes=[rts])
        P.op("vector", lambda e, t8=t8, e1=e1, e2=e2: e.tensor_tensor(
            out=t8, in0=e2, in1=e1, op=ALU.mult), reads=[rts], writes=[rts])
        P.op("vector", lambda e, e2b=e2b, e2=e2, t8=t8: e.tensor_tensor(
            out=e2b, in0=e2, in1=t8, op=ALU.subtract), reads=[rts], writes=[rts])
        P.op("vector", lambda e, c1=c1, e1=e1: e.tensor_scalar(
            out=c1, in0=e1, scalar1=rts[:, 33:34], scalar2=None, op0=ALU.mult),
            reads=[rts], writes=[rts])
        P.op("vector", lambda e, c2=c2, e2b=e2b: e.tensor_scalar(
            out=c2, in0=e2b, scalar1=rts[:, 34:35], scalar2=None, op0=ALU.mult),
            reads=[rts], writes=[rts])
        P.op("vector", lambda e, c1=c1, c2=c2, i=i: e.tensor_tensor(
            out=env.coef[:, 8 * i:8 * (i + 1)], in0=c1, in1=c2, op=ALU.add),
            reads=[rts], writes=[env.coef])
    if stage < 1:
        return

    w1v = d["f0_W1T"].rearrange("(k p) f -> p k f", p=128)
    for tb in range(4):
        for e_x in range(8):
            P.dma("sync", env.row_sc[0:1, 0:1024],
                  d["f0_b2"][0:1, e_x * 1024:(e_x + 1) * 1024],
                  writes=[env.row_sc], name="b2ld")
            for fc in range(8):
                half = env.SA0[:, (fc % 2) * 4096:((fc % 2) + 1) * 4096]
                src_ap = w1v[:, e_x * 8:(e_x + 1) * 8, fc * 512:(fc + 1) * 512]
                dst_ap = half.rearrange("p (k f) -> p k f", k=8)
                P.dma("gpsimd", dst_ap, src_ap, writes=[env.SA0], name="w1ld")
                for ft in range(4):
                    fg = fc * 4 + ft
                    ph = env.psB[fg % 2]
                    for k in range(ND):
                        lhsT = half[:, k * 512 + ft * 128:k * 512 + (ft + 1) * 128]
                        P.op("tensor", lambda e, ph=ph, lhsT=lhsT, k=k, tb=tb: e.matmul(
                            ph[:], lhsT, xnt(env, k, tb * 512, (tb + 1) * 512),
                            start=(k == 0), stop=(k == ND - 1)),
                            reads=[env.SA0, env.xnT], writes=[ph])
                    P.op("scalar", lambda e, ph=ph, fg=fg, e_x=e_x: e.activation(
                        out=env.SB[:, fg * 512:(fg + 1) * 512], in_=ph[:], func=AF.Relu,
                        bias=env.b1_sb[:, e_x * 32 + fg:e_x * 32 + fg + 1]),
                        reads=[ph, env.b1_sb], writes=[env.SB])
            w2v = d["f0_W2T"]
            for fgp in range(8):
                buf = env.SA1a if fgp % 2 == 0 else env.SA1b
                src_ap = w2v[e_x * 4096 + fgp * 512:e_x * 4096 + (fgp + 1) * 512, :]
                dst_ap = buf[:].rearrange("p (j f) -> p j f", j=4)
                P.dma("gpsimd", dst_ap,
                      src_ap.rearrange("(j p) f -> p j f", p=128),
                      writes=[buf], name="w2ld")
                for j in range(4):
                    fg = fgp * 4 + j
                    for tt in range(4):
                        for h in range(2):
                            po = env.psB[tt * 2 + h]
                            lhsT = env.SB[:, fg * 512 + tt * 128:fg * 512 + (tt + 1) * 128]
                            rhs = buf[:, j * 1024 + h * 512:j * 1024 + h * 512 + 512]
                            P.op("tensor", lambda e, po=po, lhsT=lhsT, rhs=rhs, fg=fg:
                                 e.matmul(po[:], lhsT, rhs, start=(fg == 0), stop=False),
                                 reads=[env.SB, buf], writes=[po])
            for tt in range(4):
                i_tile = tb * 4 + tt
                for h in range(2):
                    po = env.psB[tt * 2 + h]
                    P.op("tensor", lambda e, po=po, h=h: e.matmul(
                        po[:], env.ones_f[0:1, :],
                        env.row_sc[0:1, h * 512:h * 512 + 512],
                        start=False, stop=True),
                        reads=[env.ones_f, env.row_sc], writes=[po])
                    cc = env.coef[:, 8 * i_tile + e_x:8 * i_tile + e_x + 1]
                    if e_x == 0:
                        P.op("scalar", lambda e, po=po, cc=cc, tt=tt, h=h: e.activation(
                            out=env.SF[:, tt * 1024 + h * 512:tt * 1024 + (h + 1) * 512
                                       if False else tt * 1024 + h * 512 + 512],
                            in_=po[:], func=AF.Copy, scale=cc),
                            reads=[po, env.coef], writes=[env.SF])
                    else:
                        P.op("scalar", lambda e, po=po, cc=cc: e.activation(
                            out=env.scF1[:, 0:512], in_=po[:], func=AF.Copy, scale=cc),
                            reads=[po, env.coef], writes=[env.scF1])
                        P.op("gpsimd", lambda e, tt=tt, h=h: e.tensor_tensor(
                            out=env.SF[:, tt * 1024 + h * 512:tt * 1024 + h * 512 + 512],
                            in0=env.SF[:, tt * 1024 + h * 512:tt * 1024 + h * 512 + 512],
                            in1=env.scF1[:, 0:512], op=ALU.add),
                            reads=[env.SF, env.scF1], writes=[env.SF])
        if stage < 2:
            continue
        for tt in range(4):
            i = tb * 4 + tt
            st = env.stats[i]
            xi = env.x[i]
            P.op("vector", lambda e, xi=xi, st=st: e.tensor_scalar(
                out=env.scF1[:], in0=xi[:], scalar1=st[:, 4:5], scalar2=st[:, 5:6],
                op0=ALU.mult, op1=ALU.add), reads=[xi, st], writes=[env.scF1])
            P.op("vector", lambda e, tt=tt: e.tensor_tensor(
                out=env.scF2[:], in0=env.scF1[:],
                in1=env.SF[:, tt * 1024:(tt + 1) * 1024], op=ALU.add),
                reads=[env.scF1, env.SF], writes=[env.scF2])
            P.op("vector", lambda e, st=st: e.reduce_sum(
                st[:, 0:1], env.scF2[:], axis=AX.X), reads=[env.scF2, st], writes=[st])
            P.op("vector", lambda e, st=st: e.tensor_scalar(
                out=st[:, 1:2], in0=st[:, 0:1], scalar1=-1.0 / D_MODEL, scalar2=None,
                op0=ALU.mult), reads=[st], writes=[st])
            P.op("scalar", lambda e, st=st: e.activation(
                out=env.xn_tm[0][:], in_=env.scF2[:], func=AF.Square, bias=st[:, 1:2],
                accum_out=st[:, 2:3]), reads=[env.scF2, st],
                writes=[env.xn_tm[0], st])
            P.op("scalar", lambda e, st=st: e.activation(
                out=st[:, 3:4], in_=st[:, 2:3], func=AF.Ln, scale=1.0 / D_MODEL,
                bias=env.cst[:, 0:1]), reads=[st, env.cst], writes=[st])
            P.op("scalar", lambda e, st=st: e.activation(
                out=st[:, 4:5], in_=st[:, 3:4], func=AF.Exp, scale=-0.5),
                reads=[st], writes=[st])
            P.op("vector", lambda e, st=st: e.tensor_tensor(
                out=st[:, 5:6], in0=st[:, 1:2], in1=st[:, 4:5], op=ALU.mult),
                reads=[st], writes=[st])
            P.op("vector", lambda e, st=st: e.tensor_scalar(
                out=env.scF1[:], in0=env.scF2[:], scalar1=st[:, 4:5], scalar2=st[:, 5:6],
                op0=ALU.mult, op1=ALU.add), reads=[env.scF2, st], writes=[env.scF1])
            P.op("vector", lambda e: e.tensor_tensor(
                out=env.scF2[:], in0=env.scF1[:], in1=env.gbB[:, 0:1024], op=ALU.mult),
                reads=[env.scF1, env.gbB], writes=[env.scF2])
            P.op("vector", lambda e: e.tensor_tensor(
                out=env.scF1[:], in0=env.scF2[:], in1=env.gbB[:, 1024:2048], op=ALU.add),
                reads=[env.scF2, env.gbB], writes=[env.scF1])
            P.op("gpsimd", lambda e, xi=xi: e.tensor_tensor(
                out=xi[:], in0=xi[:], in1=env.scF1[:], op=ALU.add),
                reads=[xi, env.scF1], writes=[xi])



def spec_l0attnb():
    def raw(key):
        return lambda i, key=key: np.asarray(i[key])

    return {
        "ab_Wq": ((1024, 1024), F32, raw("L0_attn_b_Wq")),
        "ab_Wk": ((1024, 1024), F32, raw("L0_attn_b_Wk")),
        "ab_Wv": ((1024, 1024), F32, raw("L0_attn_b_Wv")),
        "ab_Wqp": ((1024, 1024), F32, raw("L0_attn_b_Wqp")),
        "ab_Wkp": ((1024, 1024), F32, raw("L0_attn_b_Wkp")),
        "ab_Wo": ((1024, 1024), F32, raw("L0_attn_b_Wo")),
        "ab_lam": ((1, 16), F32, lambda i: np.ascontiguousarray(
            np.asarray(i["L0_attn_b_lambda_raw"]).reshape(1, 16))),
    }


def _proj_fm(P, env, wdram_col, wslot, evac):
    P.dma("gpsimd", wslot[:].rearrange("p (k m) -> p k m", k=8),
          wdram_col.rearrange("(k p) m -> p k m", p=128), writes=[wslot], name="wld")
    for tc in range(4):
        ph = env.psB[4 + tc % 2]
        for k in range(ND):
            lhsT = wslot[:, k * 128:(k + 1) * 128]
            P.op("tensor", lambda e, ph=ph, lhsT=lhsT, k=k, tc=tc: e.matmul(
                ph[:], lhsT, xnt(env, k, tc * 512, (tc + 1) * 512),
                start=(k == 0), stop=(k == ND - 1)),
                reads=[wslot, env.xnT], writes=[ph])
        evac(tc, ph)


def build_l0attnb(P, env, stage=99):
    layernorm_to_xnT(P, env)
    d = env.dram
    if env.once("absmall"):
        pass
        env.selHT_f = P.sb("selHT_f", [2, 128], F32)
        P.op("gpsimd", lambda e: e.memset(env.selHT_f[:], 1.0), writes=[env.selHT_f])
        P.op("gpsimd", lambda e: e.affine_select(
            out=env.selHT_f[:], in_=env.selHT_f[:], compare_op=ALU.is_ge, fill=0.0,
            base=0, pattern=[[1, 128]], channel_multiplier=-64),
            reads=[env.selHT_f], writes=[env.selHT_f])
        P.op("gpsimd", lambda e: e.affine_select(
            out=env.selHT_f[:], in_=env.selHT_f[:], compare_op=ALU.is_ge, fill=0.0,
            base=63, pattern=[[-1, 128]], channel_multiplier=64),
            reads=[env.selHT_f], writes=[env.selHT_f])
    P.dma("sync", env.lam_sb[:], d["ab_lam"], writes=[env.lam_sb], name="lamld")
    P.op("scalar", lambda e: e.activation(
        out=env.lam_sb[:], in_=env.lam_sb[:], func=AF.Tanh),
        reads=[env.lam_sb], writes=[env.lam_sb])
    for g in range(8):
        pb = env.psB[0]
        for hl in range(2):
            P.op("tensor", lambda e, pb=pb, g=g, hl=hl: e.matmul(
                pb[64 * hl:64 * (hl + 1), 0:1], env.ones_f[0:1, 0:64],
                env.lam_sb[0:1, 2 * g + hl:2 * g + hl + 1],
                start=(hl == 0), stop=(hl == 1)),
                reads=[env.ones_f, env.lam_sb], writes=[pb])
        P.op("vector", lambda e, pb=pb, g=g: e.tensor_scalar(
            out=env.lamrep[:, g:g + 1], in0=pb[:, 0:1], scalar1=0.5 / 8.0,
            scalar2=None, op0=ALU.mult), reads=[pb], writes=[env.lamrep])

    SFbf = env.SF[:].bitcast(BF16)
    qT, qcT, qsT, kT = (env.SA0[:, j * 2048:(j + 1) * 2048] for j in range(4))
    kcT = env.SA1a[:, 0:2048]
    ksT = env.SA1a[:, 2048:4096]
    vtm = env.SA1b[:, 0:2048]
    spq = SFbf[:, 0:2048]
    sq2 = SFbf[:, 2048:4096]
    ampq = SFbf[:, 4096:6144]
    ampk = SFbf[:, 6144:8192]
    atnA = SFbf[:, 0:1024]
    atnB = SFbf[:, 1024:2048]
    asc = env.rts

    for g in range(8):
        def ev_sc(dst, dstbuf, scale):
            def f(tc, ph):
                P.op("vector", lambda e, ph=ph, tc=tc: e.tensor_scalar(
                    out=dst[:, tc * 512:(tc + 1) * 512], in0=ph[:],
                    scalar1=scale, scalar2=None, op0=ALU.mult),
                    reads=[ph], writes=[dstbuf])
            return f

        _proj_fm(P, env, d["ab_Wq"][:, g * 128:(g + 1) * 128], env.wslot[0],
                 ev_sc(qT, env.SA0, 0.125))
        _proj_fm(P, env, d["ab_Wk"][:, g * 128:(g + 1) * 128], env.wslot[1],
                 ev_sc(kT, env.SA0, 1.0))
        P.dma("gpsimd", env.wslot[0][:].rearrange("p (k m) -> p k m", k=8),
              d["ab_Wv"][:, g * 128:(g + 1) * 128].rearrange("(k p) m -> p k m", p=128),
              writes=[env.wslot[0]], name="wvld")
        for tt in range(NT):
            pv = env.psB[4 + tt % 2]
            for k in range(ND):
                P.op("tensor", lambda e, pv=pv, k=k, tt=tt: e.matmul(
                    pv[:, 0:128], xnt(env, k, tt * 128, (tt + 1) * 128),
                    env.wslot[0][:, k * 128:(k + 1) * 128],
                    start=(k == 0), stop=(k == ND - 1)),
                    reads=[env.wslot[0], env.xnT], writes=[pv])
            P.op("scalar", lambda e, pv=pv, tt=tt: e.activation(
                out=vtm[:, tt * 128:(tt + 1) * 128], in_=pv[:, 0:128], func=AF.Copy),
                reads=[pv], writes=[env.SA1b])
        for (srcT, srcbuf, ampdst, sc8) in ((qT, env.SA0, ampq, 8.0),
                                            (kT, env.SA0, ampk, 1.0)):
            for hh in range(2):
                cl = slice(hh * 1024, (hh + 1) * 1024)
                P.op("scalar", lambda e, srcT=srcT, cl=cl, sc8=sc8: e.activation(
                    out=env.scF1[:], in_=srcT[:, cl], func=AF.Abs, scale=sc8),
                    reads=[srcbuf], writes=[env.scF1])
                P.op("scalar", lambda e: e.activation(
                    out=env.scF2[:], in_=env.scF1[:], func=AF.Exp, scale=-1.0),
                    reads=[env.scF1], writes=[env.scF2])
                P.op("scalar", lambda e: e.activation(
                    out=env.scF1[:], in_=env.scF2[:], func=AF.Ln,
                    bias=env.cst[:, 1:2]), reads=[env.scF2, env.cst],
                    writes=[env.scF1])
                P.op("vector", lambda e, srcT=srcT, cl=cl, sc8=sc8: e.tensor_scalar(
                    out=env.scF2[:], in0=srcT[:, cl], scalar1=sc8, scalar2=0.0,
                    op0=ALU.mult, op1=ALU.max), reads=[srcbuf], writes=[env.scF2])
                P.op("vector", lambda e, cl=cl: e.tensor_tensor(
                    out=spq[:, cl], in0=env.scF1[:], in1=env.scF2[:], op=ALU.add),
                    reads=[env.scF1, env.scF2], writes=[env.SF])
            P.op("scalar", lambda e: e.activation(
                out=sq2[:], in_=spq[:], func=AF.Square), reads=[env.SF],
                writes=[env.SF])
            for tc in range(4):
                pa = env.psB[tc % 2]
                P.op("tensor", lambda e, pa=pa, tc=tc: e.matmul(
                    pa[0:2, :], env.selH[:], sq2[:, tc * 512:(tc + 1) * 512],
                    start=True, stop=True), reads=[env.selH, env.SF], writes=[pa])
                P.op("scalar", lambda e, pa=pa: e.activation(
                    out=env.scF1[0:2, 0:512], in_=pa[0:2, :], func=AF.Ln,
                    scale=1.0 / 64.0, bias=env.cst[0:2, 2:3]),
                    reads=[pa, env.cst], writes=[env.scF1])
                P.op("scalar", lambda e: e.activation(
                    out=env.scF1[0:2, 512:1024], in_=env.scF1[0:2, 0:512],
                    func=AF.Exp, scale=-0.5), reads=[env.scF1], writes=[env.scF1])
                pr = env.psB[2 + tc % 2]
                P.op("tensor", lambda e, pr=pr: e.matmul(
                    pr[:], env.selHT_f[:], env.scF1[0:2, 512:1024],
                    start=True, stop=True),
                    reads=[env.selHT_f, env.scF1], writes=[pr])
                P.op("vector", lambda e, pr=pr, tc=tc, ampdst=ampdst: e.tensor_tensor(
                    out=ampdst[:, tc * 512:(tc + 1) * 512],
                    in0=spq[:, tc * 512:(tc + 1) * 512], in1=pr[:], op=ALU.mult),
                    reads=[env.SF, pr], writes=[env.SF])
        for (wnm, srcamp, cdst, sdst, dstbuf, uselam) in (
                ("ab_Wqp", ampq, qcT, qsT, env.SA0, True),
                ("ab_Wkp", ampk, kcT, ksT, env.SA1a, False)):
            def ev_ph(tc, ph, srcamp=srcamp, cdst=cdst, sdst=sdst,
                      dstbuf=dstbuf, uselam=uselam, g=g):
                cs = slice(tc * 512, (tc + 1) * 512)
                P.op("scalar", lambda e, ph=ph: e.activation(
                    out=env.scF1[:, 0:512], in_=ph[:], func=AF.Tanh),
                    reads=[ph], writes=[env.scF1])
                P.op("scalar", lambda e: e.activation(
                    out=env.scF2[:, 0:512], in_=env.scF1[:, 0:512], func=AF.Sin,
                    scale=math.pi), reads=[env.scF1], writes=[env.scF2])
                P.op("scalar", lambda e: e.activation(
                    out=env.scF1[:, 512:1024], in_=env.scF1[:, 0:512], func=AF.Sin,
                    scale=math.pi / 2), reads=[env.scF1], writes=[env.scF1])
                P.op("scalar", lambda e: e.activation(
                    out=env.scF2[:, 512:1024], in_=env.scF1[:, 512:1024],
                    func=AF.Square), reads=[env.scF1], writes=[env.scF2])
                P.op("vector", lambda e: e.tensor_scalar(
                    out=env.scF1[:, 512:1024], in0=env.scF2[:, 512:1024],
                    scalar1=-2.0, scalar2=1.0, op0=ALU.mult, op1=ALU.add),
                    reads=[env.scF2], writes=[env.scF1])
                if uselam:
                    P.op("vector", lambda e, cs=cs, g=g: e.scalar_tensor_tensor(
                        out=cdst[:, cs], in0=env.scF1[:, 512:1024],
                        scalar=env.lamrep[:, g:g + 1], in1=srcamp[:, cs],
                        op0=ALU.mult, op1=ALU.mult),
                        reads=[env.scF1, env.lamrep, env.SF], writes=[dstbuf])
                    P.op("vector", lambda e, cs=cs, g=g: e.scalar_tensor_tensor(
                        out=sdst[:, cs], in0=env.scF2[:, 0:512],
                        scalar=env.lamrep[:, g:g + 1], in1=srcamp[:, cs],
                        op0=ALU.mult, op1=ALU.mult),
                        reads=[env.scF2, env.lamrep, env.SF], writes=[dstbuf])
                else:
                    P.op("vector", lambda e, cs=cs: e.tensor_tensor(
                        out=cdst[:, cs], in0=env.scF1[:, 512:1024],
                        in1=srcamp[:, cs], op=ALU.mult),
                        reads=[env.scF1, env.SF], writes=[dstbuf])
                    P.op("vector", lambda e, cs=cs: e.tensor_tensor(
                        out=sdst[:, cs], in0=env.scF2[:, 0:512],
                        in1=srcamp[:, cs], op=ALU.mult),
                        reads=[env.scF2, env.SF], writes=[dstbuf])
            _proj_fm(P, env, d[wnm][:, g * 128:(g + 1) * 128],
                     env.wslot[0] if uselam else env.wslot[1], ev_ph)
        if stage < 2:
            continue
        parts = ((qT, kT), (qcT, kcT), (qsT, ksT))
        for ic in range(4):
            for hl in range(2):
                rows = slice(64 * hl, 64 * (hl + 1))
                for iw in range(4):
                    ii = ic * 4 + iw
                    jw = (ii + 1) * 128
                    nb = (jw + 511) // 512
                    for jc in range(nb):
                        n = min(512, jw - jc * 512)
                        pA = env.psB[jc]
                        for pi, (qa, kb) in enumerate(parts):
                            P.op("tensor", lambda e, pA=pA, qa=qa, kb=kb, pi=pi,
                                 ii=ii, jc=jc, n=n, rows=rows: e.matmul(
                                     pA[:, 0:n], qa[rows, ii * 128:(ii + 1) * 128],
                                     kb[rows, jc * 512:jc * 512 + n],
                                     start=(pi == 0), stop=(pi == 2)),
                                 reads=[env.SA0, env.SA1a], writes=[pA])
                        P.op("vector", lambda e, pA=pA, jc=jc, n=n: e.reduce_max(
                            asc[:, jc:jc + 1], pA[:, 0:n], axis=AX.X),
                            reads=[pA], writes=[asc])
                    for jc in range(1, nb):
                        P.op("vector", lambda e, jc=jc: e.tensor_tensor(
                            out=asc[:, 0:1], in0=asc[:, 0:1], in1=asc[:, jc:jc + 1],
                            op=ALU.max), reads=[asc], writes=[asc], name="mxcomb")
                    P.op("vector", lambda e: e.tensor_scalar(
                        out=asc[:, 5:6], in0=asc[:, 0:1], scalar1=-1.0, scalar2=None,
                        op0=ALU.mult), reads=[asc], writes=[asc])
                    for jc in range(nb):
                        n = min(512, jw - jc * 512)
                        pA = env.psB[jc]
                        full = n if jc < nb - 1 else n - 128
                        if full > 0:
                            P.op("scalar", lambda e, pA=pA, full=full, jc=jc:
                                 e.activation(
                                     out=atnA[:, 0:full], in_=pA[:, 0:full],
                                     func=AF.Exp, bias=asc[:, 5:6],
                                     accum_out=asc[:, 8 + jc:9 + jc]),
                                 reads=[pA, asc], writes=[env.SF, asc])
                        else:
                            P.op("gpsimd", lambda e, jc=jc: e.memset(
                                asc[:, 8 + jc:9 + jc], 0.0), writes=[asc])
                    pA = env.psB[nb - 1]
                    n = jw - (nb - 1) * 512
                    P.op("scalar", lambda e, pA=pA, n=n: e.activation(
                        out=atnB[:, 0:128], in_=pA[:, n - 128:n], func=AF.Exp,
                        bias=asc[:, 5:6]), reads=[pA, asc], writes=[env.SF])
                    P.op("gpsimd", lambda e: e.affine_select(
                        out=atnB[:, 0:128], in_=atnB[:, 0:128],
                        compare_op=ALU.is_ge, fill=0.0, base=0,
                        pattern=[[-1, 128]], channel_multiplier=1),
                        reads=[env.SF], writes=[env.SF])
                    P.op("vector", lambda e: e.reduce_sum(
                        asc[:, 12:13], atnB[:, 0:128], axis=AX.X),
                        reads=[env.SF], writes=[asc])
                    for jc in range(nb):
                        P.op("vector", lambda e, jc=jc: e.tensor_tensor(
                            out=asc[:, 12:13], in0=asc[:, 12:13],
                            in1=asc[:, 8 + jc:9 + jc], op=ALU.add),
                            reads=[asc], writes=[asc], name="sumcomb")
                    P.op("scalar", lambda e: e.activation(
                        out=asc[:, 13:14], in_=asc[:, 12:13], func=AF.Ln),
                        reads=[asc], writes=[asc])
                    P.op("vector", lambda e: e.tensor_tensor(
                        out=asc[:, 14:15], in0=asc[:, 13:14], in1=asc[:, 0:1],
                        op=ALU.add), reads=[asc], writes=[asc])
                    P.op("vector", lambda e, iw=iw: e.tensor_scalar(
                        out=asc[:, 16 + iw:17 + iw], in0=asc[:, 14:15],
                        scalar1=-1.0, scalar2=None, op0=ALU.mult),
                        reads=[asc], writes=[asc])
                ps6 = env.psB[6]
                for iw in range(4):
                    P.op("tensor", lambda e, ps6=ps6, iw=iw: e.transpose(
                        ps6[0:1, iw * 128:(iw + 1) * 128],
                        asc[:, 16 + iw:17 + iw], env.identf[:]),
                        reads=[asc, env.identf], writes=[ps6])
                P.op("vector", lambda e, ps6=ps6: e.tensor_copy(
                    out=env.srow[0:1, :], in_=ps6[0:1, 0:512]),
                    reads=[ps6], writes=[env.srow])
                pav = env.psB[4 + hl]
                njt = 4 * ic + 4
                for jt in range(njt):
                    di = max(0, (jt - 4 * ic)) * 128
                    i0 = ic * 512 + di
                    nn = 512 - di
                    pB = env.psB[jt % 4]
                    for pi, (qa, kb) in enumerate(parts):
                        P.op("tensor", lambda e, pB=pB, qa=qa, kb=kb, pi=pi, jt=jt,
                             i0=i0, nn=nn, rows=rows: e.matmul(
                                 pB[:, 0:nn], kb[rows, jt * 128:(jt + 1) * 128],
                                 qa[rows, i0:i0 + nn], start=(pi == 0), stop=False),
                             reads=[env.SA0, env.SA1a], writes=[pB])
                    P.op("tensor", lambda e, pB=pB, i0=i0, nn=nn, ic=ic: e.matmul(
                        pB[:, 0:nn], env.ones_f[0:1, :],
                        env.srow[0:1, i0 - ic * 512:i0 - ic * 512 + nn],
                        start=False, stop=True),
                        reads=[env.ones_f, env.srow], writes=[pB])
                    at = atnA if jt % 2 == 0 else atnB
                    P.op("scalar", lambda e, pB=pB, at=at, nn=nn: e.activation(
                        out=at[:, 0:nn], in_=pB[:, 0:nn], func=AF.Exp),
                        reads=[pB], writes=[env.SF])
                    if jt >= 4 * ic:
                        P.op("gpsimd", lambda e, at=at: e.affine_select(
                            out=at[:, 0:128], in_=at[:, 0:128],
                            compare_op=ALU.is_ge, fill=0.0, base=0,
                            pattern=[[1, 128]], channel_multiplier=-1),
                            reads=[env.SF], writes=[env.SF])
                    P.op("tensor", lambda e, pav=pav, at=at, jt=jt, di=di, nn=nn,
                         hl=hl: e.matmul(
                             pav[64 * hl:64 * hl + 64, di:di + nn],
                             vtm[:, jt * 128 + 64 * hl:jt * 128 + 64 * hl + 64],
                             at[:, 0:nn], start=(jt == 0 and di == 0),
                             stop=(jt == njt - 1)),
                         reads=[env.SA1b, env.SF], writes=[pav])
                if hl == 1:
                    for hx in range(2):
                        pvx = env.psB[4 + hx]
                        P.op("scalar", lambda e, pvx=pvx, ic=ic, g=g, hx=hx:
                             e.activation(
                                 out=env.SB[64 * hx:64 * (hx + 1),
                                            g * 2048 + ic * 512:g * 2048 + (ic + 1) * 512],
                                 in_=pvx[64 * hx:64 * (hx + 1), :], func=AF.Copy),
                             reads=[pvx], writes=[env.SB])
    if stage < 3:
        return
    for itc in range(4):
        for j in range(ND):
            ws = env.wslot[j % 2]
            P.dma("gpsimd", ws[:], d["ab_Wo"][j * 128:(j + 1) * 128, :],
                  writes=[ws], name="wold")
            for iw in range(4):
                it = itc * 4 + iw
                for h in range(2):
                    po = env.psB[iw * 2 + h]
                    P.op("tensor", lambda e, po=po, j=j, it=it, h=h, ws=ws: e.matmul(
                        po[:], env.SB[:, j * 2048 + it * 128:j * 2048 + (it + 1) * 128],
                        ws[:, h * 512:(h + 1) * 512],
                        start=(j == 0), stop=(j == ND - 1)),
                        reads=[env.SB, ws], writes=[po])
        for iw in range(4):
            it = itc * 4 + iw
            for h in range(2):
                po = env.psB[iw * 2 + h]
                P.op("vector", lambda e, po=po, h=h: e.tensor_copy(
                    out=env.scF1[:, h * 512:(h + 1) * 512], in_=po[:]),
                    reads=[po], writes=[env.scF1])
            xi = env.x[it]
            P.op("gpsimd", lambda e, xi=xi: e.tensor_tensor(
                out=xi[:], in0=xi[:], in1=env.scF1[:], op=ALU.add),
                reads=[xi, env.scF1], writes=[xi])




def spec_l1attnb():
    def t(key):
        return lambda i, key=key: np.ascontiguousarray(np.asarray(i[key]).T)

    return {
        "bb_W1qT": ((1024, 64), F32, t("L1_attn_b_W1_q")),
        "bb_W2qT": ((1024, 64), F32, t("L1_attn_b_W2_q")),
        "bb_W1kT": ((1024, 64), F32, t("L1_attn_b_W1_k")),
        "bb_W2kT": ((1024, 64), F32, t("L1_attn_b_W2_k")),
        "bb_WvT": ((1024, 1024), F32, t("L1_attn_b_W_v")),
        "bb_WoT": ((1024, 1024), F32, t("L1_attn_b_W_out")),
        "bb_bv": ((1, 1024), F32, lambda i: np.ascontiguousarray(
            np.asarray(i["L1_attn_b_b_v"]).reshape(1, 1024))),
        "bb_bout": ((1, 1024), F32, lambda i: np.ascontiguousarray(
            np.asarray(i["L1_attn_b_b_out"]).reshape(1, 1024))),
    }


_PAIRS = [(0, 1), (0, 2), (0, 3), (1, 2), (1, 3), (2, 3)]
_JMAP = [(5, 1.0), (4, -1.0), (3, 1.0), (2, 1.0), (1, -1.0), (0, 1.0)]


def build_l1attnb(P, env, stage=99):
    layernorm_to_xnT(P, env)
    d = env.dram
    SFbf = env.SF[:].bitcast(BF16)
    P.dma("sync", env.row_sc[0:1, 1024:2048], d["bb_bout"], writes=[env.row_sc])
    stg = [(env.SA1a[0:8, 0:2048], env.SA1a[0:8, 2048:4096], env.SA1a),
           (env.SA1b[0:8, 0:2048], env.SA1b[0:8, 2048:4096], env.SA1b)]
    for (sl, wa, wb) in ((env.wslot[0], "bb_W1qT", "bb_W2qT"),
                         (env.wslot[1], "bb_W1kT", "bb_W2kT")):
        for (off, w) in ((0, wa), (512, wb)):
            P.dma("gpsimd", sl[:, off:off + 512].rearrange("p (k m) -> p k m", k=8),
                  d[w][:, :].rearrange("(k p) m -> p k m", p=128), writes=[sl],
                  name="bbw")
    LqT = env.SA0[:, 0:2048]
    JkT = env.SA0[:, 2048:4096]
    for tt in range(NT):
        for (pi, (sl, off)) in enumerate(((env.wslot[0], 0), (env.wslot[0], 512),
                                          (env.wslot[1], 0), (env.wslot[1], 512))):
            pq = env.psB[pi]
            for k in range(ND):
                P.op("tensor", lambda e, pq=pq, sl=sl, off=off, k=k, tt=tt: e.matmul(
                    pq[:, 0:64], xnt(env, k, tt * 128, (tt + 1) * 128),
                    sl[:, off + k * 64:off + (k + 1) * 64],
                    start=(k == 0), stop=(k == ND - 1)),
                    reads=[sl, env.xnT], writes=[pq])
        for pi in range(4):
            P.op("vector", lambda e, pi=pi: e.tensor_copy(
                out=env.scF2[:, 512 + pi * 64:512 + (pi + 1) * 64],
                in_=env.psB[pi][:, 0:64]), reads=[env.psB[pi]], writes=[env.scF2])
        for (pa, pb_, dst0, side) in ((env.scF2[:, 512:576], env.scF2[:, 576:640], 0, "q"),
                                      (env.scF2[:, 640:704], env.scF2[:, 704:768], 128, "k")):
            for c, (i, j) in enumerate(_PAIRS):
                if side == "q":
                    dc, sign = c, 1.0
                else:
                    dc = next(m for m, (srcc, _s) in enumerate(_JMAP) if srcc == c)
                    sign = _JMAP[dc][1]
                m1 = env.scF2[:, 256:272]
                m2 = env.scF2[:, 272:288]
                P.op("vector", lambda e, pa=pa, pb_=pb_, i=i, j=j, m1=m1:
                     e.tensor_tensor(out=m1, in0=pa[:, i:64:4], in1=pb_[:, j:64:4],
                                     op=ALU.mult), reads=[env.scF2], writes=[env.scF2])
                P.op("vector", lambda e, pa=pa, pb_=pb_, i=i, j=j, m2=m2:
                     e.tensor_tensor(out=m2, in0=pa[:, j:64:4], in1=pb_[:, i:64:4],
                                     op=ALU.mult), reads=[env.scF2], writes=[env.scF2])
                a_, b_ = (m1, m2) if sign > 0 else (m2, m1)
                P.op("vector", lambda e, a_=a_, b_=b_, dst0=dst0, dc=dc:
                     e.tensor_tensor(out=env.scF1[:, dst0 + dc:dst0 + 128:8],
                                     in0=a_, in1=b_, op=ALU.subtract),
                     reads=[env.scF2], writes=[env.scF1])
            P.op("scalar", lambda e, dst0=dst0: e.activation(
                out=env.scF2[:, 0:128], in_=env.scF1[:, dst0:dst0 + 128],
                func=AF.Square), reads=[env.scF1], writes=[env.scF2])
            sv = env.scF2[:, 0:128].rearrange("p (h c) -> p h c", h=16)[:, :, 0:6]
            P.op("vector", lambda e, sv=sv: e.reduce_sum(
                env.rts[:, 0:16], sv, axis=AX.X), reads=[env.scF2], writes=[env.rts])
            P.op("vector", lambda e: e.tensor_scalar(
                out=env.rts[:, 16:32], in0=env.rts[:, 0:16], scalar1=1e-24,
                scalar2=None, op0=ALU.max), reads=[env.rts], writes=[env.rts])
            P.op("scalar", lambda e: e.activation(
                out=env.rts[:, 0:16], in_=env.rts[:, 16:32], func=AF.Ln),
                reads=[env.rts], writes=[env.rts])
            P.op("scalar", lambda e: e.activation(
                out=env.rts[:, 16:32], in_=env.rts[:, 0:16], func=AF.Exp,
                scale=-0.5), reads=[env.rts], writes=[env.rts])
            if side == "q":
                P.op("vector", lambda e: e.tensor_scalar(
                    out=env.rts[:, 16:32], in0=env.rts[:, 16:32],
                    scalar1=6.0 ** -0.5, scalar2=None, op0=ALU.mult),
                    reads=[env.rts], writes=[env.rts])
            for c in range(6):
                P.op("vector", lambda e, dst0=dst0, c=c: e.tensor_tensor(
                    out=SFbf[:, dst0 + c:dst0 + 128:8],
                    in0=env.scF1[:, dst0 + c:dst0 + 128:8], in1=env.rts[:, 16:32],
                    op=ALU.mult), reads=[env.scF1, env.rts], writes=[env.SF])
            for pc in (6, 7):
                P.op("vector", lambda e, dst0=dst0, pc=pc: e.memset(
                    SFbf[:, dst0 + pc:dst0 + 128:8], 0.0), writes=[env.SF])
        for (src0, dstT) in ((0, LqT), (128, JkT)):
            ptp = env.ps_tp[tt % 2]
            ptpb = ptp[:].bitcast(BF16)
            P.op("tensor", lambda e, src0=src0, ptpb=ptpb: e.transpose(
                ptpb[:, 0:128], SFbf[:, src0:src0 + 128], env.ident[:]),
                reads=[env.SF, env.ident], writes=[ptp])
            P.op("vector", lambda e, dstT=dstT, tt=tt, ptpb=ptpb: e.tensor_copy(
                out=dstT[:, tt * 128:(tt + 1) * 128], in_=ptpb[:, 0:128]),
                reads=[ptp], writes=[env.SA0])
    if stage < 2:
        return
    atnA = SFbf[:, 4096:5120]
    atnB = SFbf[:, 5120:6144]
    asc = env.rts
    for half in range(2):
        P.dma("sync", env.row_sc[0:1, 0:1024], d["bb_bv"], writes=[env.row_sc])
        P.dma("gpsimd", env.SA1a[:].rearrange("p (k m) -> p k m", k=8),
              d["bb_WvT"][:, half * 512:(half + 1) * 512].rearrange(
                  "(k p) m -> p k m", p=128), writes=[env.SA1a], name="wvld")
        for tt in range(NT):
            pv = env.psB[4 + tt % 2]
            for k in range(ND):
                P.op("tensor", lambda e, pv=pv, k=k, tt=tt: e.matmul(
                    pv[:], xnt(env, k, tt * 128, (tt + 1) * 128),
                    env.SA1a[:, k * 512:(k + 1) * 512],
                    start=(k == 0), stop=False), reads=[env.SA1a, env.xnT],
                    writes=[pv])
            P.op("tensor", lambda e, pv=pv, half=half: e.matmul(
                pv[:], env.ones_f[0:1, :],
                env.row_sc[0:1, half * 512:half * 512 + 512],
                start=False, stop=True), reads=[env.ones_f, env.row_sc], writes=[pv])
            P.op("scalar", lambda e, pv=pv, tt=tt: e.activation(
                out=env.SB[:, tt * 512:(tt + 1) * 512], in_=pv[:], func=AF.Copy),
                reads=[pv], writes=[env.SB])
        for gp in range(4):
            for hl0 in range(2):
                habs0 = 8 * half + 2 * gp + hl0
                P.dma("sync", stg[hl0][0], LqT[8 * habs0:8 * habs0 + 8, :],
                      reads=[env.SA0], writes=[stg[hl0][2]], name="stgq")
                P.dma("sync", stg[hl0][1], JkT[8 * habs0:8 * habs0 + 8, :],
                      reads=[env.SA0], writes=[stg[hl0][2]], name="stgk")
            for ic in range(4):
                for hl in range(2):
                    habs = 8 * half + 2 * gp + hl
                    stgQ, stgK, stgB = stg[hl]
                    vcol = (habs % 8) * 64
                    for iw in range(4):
                        ii = ic * 4 + iw
                        jw = (ii + 1) * 128
                        nb = (jw + 511) // 512
                        for jc in range(nb):
                            n = min(512, jw - jc * 512)
                            pA = env.psB[jc]
                            P.op("tensor", lambda e, pA=pA, ii=ii, jc=jc, n=n,
                                 stgQ=stgQ, stgK=stgK:
                                 e.matmul(pA[:, 0:n],
                                          stgQ[:, ii * 128:(ii + 1) * 128],
                                          stgK[:, jc * 512:jc * 512 + n],
                                          start=True, stop=True),
                                 reads=[stgB], writes=[pA])
                            P.op("vector", lambda e, pA=pA, jc=jc, n=n: e.reduce_max(
                                asc[:, jc:jc + 1], pA[:, 0:n], axis=AX.X),
                                reads=[pA], writes=[asc])
                        for jc in range(1, nb):
                            P.op("vector", lambda e, jc=jc: e.tensor_tensor(
                                out=asc[:, 0:1], in0=asc[:, 0:1],
                                in1=asc[:, jc:jc + 1], op=ALU.max),
                                reads=[asc], writes=[asc])
                        P.op("vector", lambda e: e.tensor_scalar(
                            out=asc[:, 5:6], in0=asc[:, 0:1], scalar1=-1.0,
                            scalar2=None, op0=ALU.mult), reads=[asc], writes=[asc])
                        for jc in range(nb):
                            n = min(512, jw - jc * 512)
                            pA = env.psB[jc]
                            full = n if jc < nb - 1 else n - 128
                            if full > 0:
                                P.op("scalar", lambda e, pA=pA, full=full, jc=jc:
                                     e.activation(
                                         out=atnA[:, 0:full], in_=pA[:, 0:full],
                                         func=AF.Exp, bias=asc[:, 5:6],
                                         accum_out=asc[:, 8 + jc:9 + jc]),
                                     reads=[pA, asc], writes=[env.SF, asc])
                            else:
                                P.op("gpsimd", lambda e, jc=jc: e.memset(
                                    asc[:, 8 + jc:9 + jc], 0.0), writes=[asc])
                        pA = env.psB[nb - 1]
                        n = jw - (nb - 1) * 512
                        P.op("scalar", lambda e, pA=pA, n=n: e.activation(
                            out=atnB[:, 0:128], in_=pA[:, n - 128:n], func=AF.Exp,
                            bias=asc[:, 5:6]), reads=[pA, asc], writes=[env.SF])
                        P.op("gpsimd", lambda e: e.affine_select(
                            out=atnB[:, 0:128], in_=atnB[:, 0:128],
                            compare_op=ALU.is_ge, fill=0.0, base=0,
                            pattern=[[-1, 128]], channel_multiplier=1),
                            reads=[env.SF], writes=[env.SF])
                        P.op("vector", lambda e: e.reduce_sum(
                            asc[:, 12:13], atnB[:, 0:128], axis=AX.X),
                            reads=[env.SF], writes=[asc])
                        for jc in range(nb):
                            P.op("vector", lambda e, jc=jc: e.tensor_tensor(
                                out=asc[:, 12:13], in0=asc[:, 12:13],
                                in1=asc[:, 8 + jc:9 + jc], op=ALU.add),
                                reads=[asc], writes=[asc])
                        P.op("scalar", lambda e: e.activation(
                            out=asc[:, 13:14], in_=asc[:, 12:13], func=AF.Ln),
                            reads=[asc], writes=[asc])
                        P.op("vector", lambda e: e.tensor_tensor(
                            out=asc[:, 14:15], in0=asc[:, 13:14], in1=asc[:, 0:1],
                            op=ALU.add), reads=[asc], writes=[asc])
                        P.op("vector", lambda e, iw=iw: e.tensor_scalar(
                            out=asc[:, 16 + iw:17 + iw], in0=asc[:, 14:15],
                            scalar1=-1.0, scalar2=None, op0=ALU.mult),
                            reads=[asc], writes=[asc])
                    ps6 = env.psB[6]
                    for iw in range(4):
                        P.op("tensor", lambda e, ps6=ps6, iw=iw: e.transpose(
                            ps6[0:1, iw * 128:(iw + 1) * 128],
                            asc[:, 16 + iw:17 + iw], env.identf[:]),
                            reads=[asc, env.identf], writes=[ps6])
                    P.op("vector", lambda e, ps6=ps6: e.tensor_copy(
                        out=env.srow[0:1, :], in_=ps6[0:1, 0:512]),
                        reads=[ps6], writes=[env.srow])
                    pav = env.psB[5 - hl]
                    njt = 4 * ic + 4
                    for jt in range(njt):
                        di = max(0, (jt - 4 * ic)) * 128
                        i0 = ic * 512 + di
                        nn = 512 - di
                        pB = env.psB[jt % 4]
                        P.op("tensor", lambda e, pB=pB, jt=jt, i0=i0, nn=nn,
                             stgQ=stgQ, stgK=stgK:
                             e.matmul(pB[:, 0:nn],
                                      stgK[:, jt * 128:(jt + 1) * 128],
                                      stgQ[:, i0:i0 + nn],
                                      start=True, stop=False),
                             reads=[stgB], writes=[pB])
                        P.op("tensor", lambda e, pB=pB, i0=i0, nn=nn, ic=ic:
                             e.matmul(pB[:, 0:nn], env.ones_f[0:1, :],
                                      env.srow[0:1, i0 - ic * 512:i0 - ic * 512 + nn],
                                      start=False, stop=True),
                             reads=[env.ones_f, env.srow], writes=[pB])
                        at = atnA if jt % 2 == 0 else atnB
                        P.op("scalar", lambda e, pB=pB, at=at, nn=nn: e.activation(
                            out=at[:, 0:nn], in_=pB[:, 0:nn], func=AF.Exp),
                            reads=[pB], writes=[env.SF])
                        if jt >= 4 * ic:
                            P.op("gpsimd", lambda e, at=at: e.affine_select(
                                out=at[:, 0:128], in_=at[:, 0:128],
                                compare_op=ALU.is_ge, fill=0.0, base=0,
                                pattern=[[1, 128]], channel_multiplier=-1),
                                reads=[env.SF], writes=[env.SF])
                        P.op("tensor", lambda e, pav=pav, at=at, jt=jt, di=di,
                             nn=nn, hl=hl, vcol=vcol: e.matmul(
                                 pav[64 * hl:64 * hl + 64, di:di + nn],
                                 env.SB[:, jt * 512 + vcol:jt * 512 + vcol + 64],
                                 at[:, 0:nn], start=(jt == 0 and di == 0),
                                 stop=(jt == njt - 1)),
                             reads=[env.SB, env.SF], writes=[pav])
                    if hl == 1:
                        for hx in range(2):
                            pvx = env.psB[5 - hx]
                            P.op("scalar", lambda e, pvx=pvx, ic=ic, gp=gp, hx=hx:
                                 e.activation(
                                     out=env.SB[64 * hx:64 * (hx + 1),
                                                8192 + gp * 2048 + ic * 512:
                                                8192 + gp * 2048 + (ic + 1) * 512],
                                     in_=pvx[64 * hx:64 * (hx + 1), :],
                                     func=AF.Copy),
                                 reads=[pvx], writes=[env.SB])
        if stage < 3:
            continue
        for itc in range(4):
            for gp in range(4):
                ws = env.wslot[gp % 2]
                P.dma("gpsimd", ws[:],
                      d["bb_WoT"][half * 512 + gp * 128:half * 512 + (gp + 1) * 128, :],
                      writes=[ws], name="wood")
                for iw in range(4):
                    it = itc * 4 + iw
                    for h2 in range(2):
                        po = env.psB[iw * 2 + h2]
                        P.op("tensor", lambda e, po=po, gp=gp, it=it, h2=h2, ws=ws,
                             half=half: e.matmul(
                                 po[:],
                                 env.SB[:, 8192 + gp * 2048 + it * 128:
                                        8192 + gp * 2048 + (it + 1) * 128],
                                 ws[:, h2 * 512:(h2 + 1) * 512],
                                 start=(gp == 0), stop=(half == 1 and gp == 3)),
                             reads=[env.SB, ws], writes=[po])
            for iw in range(4):
                it = itc * 4 + iw
                for h2 in range(2):
                    po = env.psB[iw * 2 + h2]
                    if half == 0:
                        P.op("tensor", lambda e, po=po, h2=h2: e.matmul(
                            po[:], env.ones_f[0:1, :],
                            env.row_sc[0:1, 1024 + h2 * 512:1024 + h2 * 512 + 512],
                            start=False, stop=True),
                            reads=[env.ones_f, env.row_sc], writes=[po])
                    P.op("vector", lambda e, po=po, h2=h2: e.tensor_copy(
                        out=env.scF1[:, h2 * 512:(h2 + 1) * 512], in_=po[:]),
                        reads=[po], writes=[env.scF1])
                xi = env.x[it]
                P.op("gpsimd", lambda e, xi=xi: e.tensor_tensor(
                    out=xi[:], in0=xi[:], in1=env.scF1[:], op=ALU.add),
                    reads=[xi, env.scF1], writes=[xi])



def spec_l1attna():
    def raw(key):
        return lambda i, key=key: np.ascontiguousarray(np.asarray(i[key]))

    return {
        "aa_Wqv": ((1024, 2048), F32, raw("L1_attn_a_W_qv")),
        "aa_W20": ((1024, 84), F32, raw("L1_attn_a_W2_0")),
        "aa_W21": ((1024, 164), F32, raw("L1_attn_a_W2_1")),
        "aa_Wout": ((1024, 1024), F32, raw("L1_attn_a_W_out")),
        "aa_sw": ((1, 2), F32, lambda i: np.ascontiguousarray(
            np.asarray(i["L1_attn_a_scale_weights"]).reshape(1, 2))),
    }


def build_l1attna(P, env, stage=99):
    layernorm_to_xnT(P, env)
    d = env.dram
    SFbf = env.SF[:].bitcast(BF16)
    scr = env.aa_scr
    scrbuf = env.aa_scrbuf
    P.dma("sync", env.lam_sb[0:1, 0:2], d["aa_sw"], writes=[env.lam_sb])
    P.op("vector", lambda e: e.tensor_tensor(
        out=env.lam_sb[0:1, 2:3], in0=env.lam_sb[0:1, 0:1],
        in1=env.lam_sb[0:1, 1:2], op=ALU.subtract),
        reads=[env.lam_sb], writes=[env.lam_sb])
    P.op("scalar", lambda e: e.activation(
        out=env.lam_sb[0:1, 3:4], in_=env.lam_sb[0:1, 2:3], func=AF.Sigmoid),
        reads=[env.lam_sb], writes=[env.lam_sb])
    P.op("vector", lambda e: e.tensor_scalar(
        out=env.lam_sb[0:1, 4:5], in0=env.lam_sb[0:1, 3:4], scalar1=-1.0,
        scalar2=1.0, op0=ALU.mult, op1=ALU.add),
        reads=[env.lam_sb], writes=[env.lam_sb])
    pb = env.psB[0]
    for j in range(2):
        P.op("tensor", lambda e, pb=pb, j=j: e.matmul(
            pb[:, j:j + 1], env.ones_f[0:1, :], env.lam_sb[0:1, 3 + j:4 + j],
            start=(j == 0), stop=(j == 1)), reads=[env.ones_f, env.lam_sb],
            writes=[pb])
    P.op("vector", lambda e, pb=pb: e.tensor_copy(
        out=env.lamrep[:, 6:8], in_=pb[:, 0:2]), reads=[pb], writes=[env.lamrep])
    P.op("gpsimd", lambda e: e.memset(SFbf[:, 0:512], 0.0), writes=[env.SF])
    for zz in range(2):
        P.dma("scalar", scr[zz * 136:zz * 136 + 128, :], SFbf[:, 0:512],
              reads=[env.SF], writes=[scrbuf], name="scrz0")
        P.dma("scalar", scr[zz * 136 + 128:zz * 136 + 136, :], SFbf[0:8, 0:512],
              reads=[env.SF], writes=[scrbuf], name="scrz1")

    for hh in range(2):
        buf = env.SA1a if hh == 0 else env.SA1b
        P.dma("gpsimd", buf[:].rearrange("p (k m) -> p k m", k=4),
              d["aa_Wqv"][hh * 512:(hh + 1) * 512, 1024:2048].rearrange(
                  "(k p) m -> p k m", p=128), writes=[buf], name="wvld")
    for tt in range(NT):
        for h2 in range(2):
            pv = env.psB[4 + h2]
            for k in range(ND):
                buf = env.SA1a if k < 4 else env.SA1b
                P.op("tensor", lambda e, pv=pv, k=k, tt=tt, h2=h2, buf=buf: e.matmul(
                    pv[:], xnt(env, k, tt * 128, (tt + 1) * 128),
                    buf[:, (k % 4) * 1024 + h2 * 512:(k % 4) * 1024 + h2 * 512 + 512],
                    start=(k == 0), stop=(k == ND - 1)),
                    reads=[buf, env.xnT], writes=[pv])
            P.op("scalar", lambda e, pv=pv, tt=tt, h2=h2: e.activation(
                out=env.SB[:, tt * 1024 + h2 * 512:tt * 1024 + h2 * 512 + 512],
                in_=pv[:], func=AF.Copy), reads=[pv], writes=[env.SB])
    qr = ([env.SA0[:, m * 2048:(m + 1) * 2048] for m in range(4)]
          + [env.SA1a[:, m * 2048:(m + 1) * 2048] for m in range(2)]
          + [env.SA1b[:, m * 2048:(m + 1) * 2048] for m in range(2)])
    qrb = [env.SA0] * 4 + [env.SA1a] * 2 + [env.SA1b] * 2
    for m in range(8):
        ws = env.wslot[m % 2]
        P.dma("gpsimd", ws[:].rearrange("p (k f) -> p k f", k=8),
              d["aa_Wqv"][:, m * 128:(m + 1) * 128].rearrange(
                  "(k p) f -> p k f", p=128), writes=[ws], name="wqld")
        for tc in range(4):
            ph = env.psB[tc % 2]
            for k in range(ND):
                P.op("tensor", lambda e, ph=ph, ws=ws, k=k, tc=tc: e.matmul(
                    ph[:], ws[:, k * 128:(k + 1) * 128],
                    xnt(env, k, tc * 512, (tc + 1) * 512),
                    start=(k == 0), stop=(k == ND - 1)),
                    reads=[ws, env.xnT], writes=[ph])
            P.op("scalar", lambda e, ph=ph, m=m, tc=tc: e.activation(
                out=qr[m][:, tc * 512:(tc + 1) * 512], in_=ph[:], func=AF.Relu),
                reads=[ph], writes=[qrb[m]])
    if stage < 2:
        return
    W2sb = {0: SFbf[:, 1040:1712], 1: SFbf[:, 1712:3024]}
    for (ci, wnm, wc) in ((0, "aa_W20", 84), (1, "aa_W21", 164)):
        for m in range(8):
            P.dma("gpsimd", W2sb[ci][:, m * wc:(m + 1) * wc],
                  d[wnm][m * 128:(m + 1) * 128, :], writes=[env.SF], name="w2ld")
    attnw = SFbf[:, 0:656]
    band = SFbf[:, 656:1040]
    bandT = SFbf[:, 7120:7504]
    CSZ = ((0, 21, 10), (1, 41, 20))
    for itc in range(4):
        for iw in range(4):
            tt = itc * 4 + iw
            first = {4: True, 5: True}
            last_of = {}
            for (ci, c, half) in CSZ:
                for h in range(4):
                    for jls in range(3):
                        jt = tt - 1 + jls
                        if 0 <= jt < NT:
                            for m2 in range(2):
                                dti = 2 * h + m2
                                last_of[4 + dti // 4] = (ci, h, jls, m2)
            for (ci, c, half) in CSZ:
                pl = env.psB[2 + ci]
                wc = 84 if ci == 0 else 164
                for m in range(8):
                    P.op("tensor", lambda e, pl=pl, m=m, tt=tt, ci=ci, wc=wc:
                         e.matmul(pl[:, 0:wc],
                                  qr[m][:, tt * 128:(tt + 1) * 128],
                                  W2sb[ci][:, m * wc:(m + 1) * wc],
                                  start=(m == 0), stop=(m == ND - 1)),
                         reads=[qrb[m], env.SF], writes=[pl])
                P.op("vector", lambda e, pl=pl, wc=wc: e.tensor_copy(
                    out=env.scF1[:, 0:wc], in_=pl[:, 0:wc]), reads=[pl],
                    writes=[env.scF1])
                for h in range(4):
                    cs = slice(h * c, (h + 1) * c)
                    P.op("vector", lambda e, cs=cs: e.reduce_max(
                        env.rts[:, 0:1], env.scF1[:, cs], axis=AX.X),
                        reads=[env.scF1], writes=[env.rts])
                    P.op("vector", lambda e: e.tensor_scalar(
                        out=env.rts[:, 1:2], in0=env.rts[:, 0:1], scalar1=-1.0,
                        scalar2=None, op0=ALU.mult), reads=[env.rts],
                        writes=[env.rts])
                    P.op("scalar", lambda e, cs=cs, c=c: e.activation(
                        out=env.scF2[:, 0:c], in_=env.scF1[:, cs], func=AF.Exp,
                        bias=env.rts[:, 1:2], accum_out=env.rts[:, 2:3]),
                        reads=[env.scF1, env.rts], writes=[env.scF2, env.rts])
                    P.op("vector", lambda e: e.reciprocal(
                        out=env.rts[:, 3:4], in_=env.rts[:, 2:3]),
                        reads=[env.rts], writes=[env.rts])
                    P.op("vector", lambda e, cs=cs, c=c, ci=ci: e.tensor_scalar(
                        out=attnw[:, cs], in0=env.scF2[:, 0:c],
                        scalar1=env.rts[:, 3:4], scalar2=env.lamrep[:, 6 + ci:7 + ci],
                        op0=ALU.mult, op1=ALU.mult),
                        reads=[env.scF2, env.rts, env.lamrep], writes=[env.SF])
                for h in range(4):
                    P.dma("scalar", bass.AP(
                        tensor=scr.tensor, offset=(ci * 136 + 1) * 512,
                        ap=[[512, 128], [1, c]]),
                        attnw[:, h * c:(h + 1) * c],
                        reads=[env.SF], writes=[scrbuf], name="bandout")
                    P.dma("scalar", band,
                          bass.AP(tensor=scr.tensor,
                                  offset=ci * 136 * 512 + 384 + half,
                                  ap=[[511, 128], [1, 384]]),
                          reads=[scrbuf], writes=[env.SF], name="bandin")
                    for jls in range(3):
                        ptp = env.psB[2 + jls % 2]
                        ptpb = ptp[:].bitcast(BF16)
                        P.op("tensor", lambda e, ptpb=ptpb, jls=jls: e.transpose(
                            ptpb[:, 0:128], band[:, jls * 128:(jls + 1) * 128],
                            env.ident[:]), reads=[env.SF, env.ident],
                            writes=[env.psB[2 + jls % 2]])
                        P.op("vector", lambda e, ptpb=ptpb, jls=jls: e.tensor_copy(
                            out=bandT[:, jls * 128:(jls + 1) * 128],
                            in_=ptpb[:, 0:128]),
                            reads=[env.psB[2 + jls % 2]], writes=[env.SF])
                        jt = tt - 1 + jls
                        if not (0 <= jt < NT):
                            continue
                        for m2 in range(2):
                            dti = 2 * h + m2
                            bank = env.psB[4 + dti // 4]
                            col = (dti % 4) * 128
                            st_ = first[4 + dti // 4]
                            first[4 + dti // 4] = False
                            sp_ = (last_of[4 + dti // 4] == (ci, h, jls, m2))
                            P.op("tensor", lambda e, bank=bank, col=col, jt=jt,
                                 h=h, m2=m2, jls=jls, st_=st_, sp_=sp_: e.matmul(
                                     bank[:, col:col + 128],
                                     env.SB[:, jt * 1024 + h * 256 + m2 * 128:
                                            jt * 1024 + h * 256 + m2 * 128 + 128],
                                     bandT[:, jls * 128:(jls + 1) * 128],
                                     start=st_, stop=sp_),
                                 reads=[env.SB, env.SF], writes=[bank])
            for b2 in range(2):
                P.op("vector", lambda e, b2=b2, iw=iw: e.tensor_copy(
                    out=SFbf[:, 3024 + iw * 1024 + b2 * 512:
                             3024 + iw * 1024 + b2 * 512 + 512],
                    in_=env.psB[4 + b2][:]), reads=[env.psB[4 + b2]],
                    writes=[env.SF])
        for k in range(ND):
            ws = env.wslot[k % 2]
            P.dma("gpsimd", ws[:], d["aa_Wout"][k * 128:(k + 1) * 128, :],
                  writes=[ws], name="wold")
            for iw in range(4):
                for h2 in range(2):
                    po = env.psB[iw * 2 + h2]
                    P.op("tensor", lambda e, po=po, k=k, iw=iw, h2=h2, ws=ws:
                         e.matmul(po[:],
                                  SFbf[:, 3024 + iw * 1024 + k * 128:
                                       3024 + iw * 1024 + (k + 1) * 128],
                                  ws[:, h2 * 512:(h2 + 1) * 512],
                                  start=(k == 0), stop=(k == ND - 1)),
                         reads=[env.SF, ws], writes=[po])
        for iw in range(4):
            it = itc * 4 + iw
            for h2 in range(2):
                po = env.psB[iw * 2 + h2]
                P.op("vector", lambda e, po=po, h2=h2: e.tensor_copy(
                    out=env.scF1[:, h2 * 512:(h2 + 1) * 512], in_=po[:]),
                    reads=[po], writes=[env.scF1])
            xi = env.x[it]
            P.op("gpsimd", lambda e, xi=xi: e.tensor_tensor(
                out=xi[:], in0=xi[:], in1=env.scF1[:], op=ALU.add),
                reads=[xi, env.scF1], writes=[xi])



MTC = 256
MNC = T_SEQ // MTC


def spec_l0attna():
    def f(key, fn):
        return lambda i, key=key, fn=fn: np.ascontiguousarray(fn(np.asarray(i[key])))

    def tile16(a):
        return a.reshape(16, 128).T

    return {
        "ma_inpT": ((1024, 4096), F32, f("L0_attn_a_h_in_proj_w", lambda a: a.T)),
        "ma_convw": ((128, 64), F32, f("L0_attn_a_h_conv_w", lambda a:
            a.reshape(16, 128, 4).transpose(1, 0, 2).reshape(128, 64))),
        "ma_convb": ((128, 16), F32, f("L0_attn_a_h_conv_b", tile16)),
        "ma_dtb": ((128, 16), F32, f("L0_attn_a_h_dt_proj_b", tile16)),
        "ma_D": ((128, 16), F32, f("L0_attn_a_h_D_param", tile16)),
        "ma_xpwT": ((2048, 96), F32, f("L0_attn_a_h_x_proj_w", lambda a: a.T)),
        "ma_dtwT": ((64, 2048), F32, f("L0_attn_a_h_dt_proj_w", lambda a: a.T)),
        "ma_AlogT": ((128, 256), F32, f("L0_attn_a_h_A_log", lambda a:
            a.reshape(256, 8, 16).transpose(1, 2, 0).reshape(128, 256))),
        "ma_opwT": ((2048, 1024), F32, f("L0_attn_a_h_out_proj_w", lambda a: a.T)),
    }


def build_l0attna(P, env, stage=99):
    layernorm_to_xnT(P, env)
    d = env.dram
    SFbf = env.SF[:].bitcast(BF16)
    S1bF = env.SA1b[:].bitcast(F32)
    A_sb = S1bF[:, 0:256]
    hcar = S1bF[:, 256:512]
    halo = env.SA1b[:, 1024:1072]
    selD = env.SA0[:, 4096:6144]
    selR = env.SA0[:, 6144:8192]
    if env.once("masmall"):
        env.ma_cw = P.sb("s_ma_cw", [128, 64], F32)
        env.ma_cb = P.sb("s_ma_cb", [128, 16], F32)
        env.ma_dtb = P.sb("s_ma_dtb", [128, 16], F32)
        env.ma_Dp = P.sb("s_ma_Dp", [128, 16], F32)
        env.ma_selB = P.sb("s_ma_selB", [32, 256], BF16)
    P.dma("sync", env.ma_cw[:], d["ma_convw"], writes=[env.ma_cw])
    P.dma("sync", env.ma_cb[:], d["ma_convb"], writes=[env.ma_cb])
    P.dma("sync", env.ma_dtb[:], d["ma_dtb"], writes=[env.ma_dtb])
    P.dma("sync", env.ma_Dp[:], d["ma_D"], writes=[env.ma_Dp])
    P.dma("sync", A_sb, d["ma_AlogT"], writes=[env.SA1b])
    P.op("scalar", lambda e: e.activation(out=A_sb, in_=A_sb, func=AF.Exp),
         reads=[env.SA1b], writes=[env.SA1b])
    P.op("vector", lambda e: e.tensor_scalar(
        out=A_sb, in0=A_sb, scalar1=-1.0, scalar2=None, op0=ALU.mult),
        reads=[env.SA1b], writes=[env.SA1b])
    P.op("gpsimd", lambda e: e.memset(hcar, 0.0), writes=[env.SA1b])
    P.op("gpsimd", lambda e: e.memset(halo, 0.0), writes=[env.SA1b])
    P.op("gpsimd", lambda e: e.memset(env.ma_selB[:], 0.0), writes=[env.ma_selB])
    for which in range(2):
        P.op("gpsimd", lambda e, which=which: e.affine_select(
            out=env.ma_selB[:, which * 128:(which + 1) * 128].rearrange(
                "s (dlo n) -> s dlo n", n=16),
            in_=env.ma_selB[:, which * 128:(which + 1) * 128].rearrange(
                "s (dlo n) -> s dlo n", n=16),
            compare_op=ALU.not_equal, fill=1.0, base=-16 * which,
            pattern=[[0, 8], [-1, 16]], channel_multiplier=1),
            reads=[env.ma_selB], writes=[env.ma_selB])
    for r8 in range(16):
        sl = selD[:, r8 * 128:(r8 + 1) * 128]
        P.op("gpsimd", lambda e, sl=sl: e.memset(sl, 0.0), writes=[env.SA0])
        P.op("gpsimd", lambda e, sl=sl, r8=r8: e.affine_select(
            out=sl.rearrange("s (dlo n) -> s dlo n", n=16),
            in_=sl.rearrange("s (dlo n) -> s dlo n", n=16),
            compare_op=ALU.not_equal, fill=1.0, base=-8 * r8,
            pattern=[[-1, 8], [0, 16]], channel_multiplier=1),
            reads=[env.SA0], writes=[env.SA0])
        sr = selR[:, r8 * 128:(r8 + 1) * 128]
        P.op("gpsimd", lambda e, sr=sr: e.memset(sr, 1.0), writes=[env.SA0])
        P.op("gpsimd", lambda e, sr=sr, r8=r8: e.affine_select(
            out=sr, in_=sr, compare_op=ALU.is_ge, fill=0.0, base=16 * 8 * r8,
            pattern=[[-16, 128]], channel_multiplier=1),
            reads=[env.SA0], writes=[env.SA0])
        P.op("gpsimd", lambda e, sr=sr, r8=r8: e.affine_select(
            out=sr, in_=sr, compare_op=ALU.is_ge, fill=0.0,
            base=15 - 16 * 8 * r8, pattern=[[16, 128]], channel_multiplier=-1),
            reads=[env.SA0], writes=[env.SA0])
    if stage < 1:
        return

    xin = [env.SA0[:, dt * MTC:(dt + 1) * MTC] for dt in range(16)]
    zt = [env.SA1a[:, dt * MTC:(dt + 1) * MTC] for dt in range(16)]
    ut = [env.SB[:, dt * MTC:(dt + 1) * MTC] for dt in range(16)]
    yt = [env.SB[:, 4096 + dt * MTC:4096 + (dt + 1) * MTC] for dt in range(16)]
    dl = [SFbf[:, dt * MTC:(dt + 1) * MTC] for dt in range(16)]
    xdbl = env.wslot[1][0:96, 0:MTC]
    brep = env.wslot[1][:, 256:512]
    crep = env.wslot[0][:, 512:768]
    dtw_t = [env.wslot[1][0:64, 512 + j * 128:512 + (j + 1) * 128] for j in range(2)]
    xpw_t = [env.wslot[0][:, 768 + j * 128:768 + (j + 1) * 128] for j in range(2)]
    du = env.wslot[0][:, 0:256]
    slz = env.wslot[0][:, 256:512]
    uacc = env.scF1[:, 0:256]
    a_f = env.scF1[:, 256:512]
    b_f = env.scF1[:, 512:768]
    h_f = env.scF2[:, 0:256]
    hC = env.scF2[:].bitcast(BF16)[:, 512:768]
    S2b = env.scF2

    for c in range(MNC):
        t0 = c * MTC
        for et in range(32):
            ws = env.wslot[et % 2]
            P.dma("gpsimd", ws[:].rearrange("p (k m) -> p k m", k=8),
                  d["ma_inpT"][:, et * 128:(et + 1) * 128].rearrange(
                      "(k p) m -> p k m", p=128), writes=[ws], name="mainp")
            ph = env.psB[et % 2]
            for k in range(ND):
                P.op("tensor", lambda e, ph=ph, ws=ws, k=k, t0=t0: e.matmul(
                    ph[:, 0:MTC], ws[:, k * 128:(k + 1) * 128],
                    xnt(env, k, t0, t0 + MTC),
                    start=(k == 0), stop=(k == ND - 1)),
                    reads=[ws, env.xnT], writes=[ph])
            dst = xin[et] if et < 16 else zt[et - 16]
            dstb = env.SA0 if et < 16 else env.SA1a
            P.op("scalar", lambda e, ph=ph, dst=dst: e.activation(
                out=dst, in_=ph[:, 0:MTC], func=AF.Copy),
                reads=[ph], writes=[dstb])
        if stage < 2:
            continue
        uA = env.scF1[:, 0:256]
        uB = env.scF1[:, 256:512]
        for dt in range(16):
            P.op("vector", lambda e, dt=dt, uA=uA: e.tensor_scalar(
                out=uA, in0=xin[dt], scalar1=env.ma_cw[:, dt * 4 + 3:dt * 4 + 4],
                scalar2=None, op0=ALU.mult),
                reads=[env.SA0, env.ma_cw], writes=[env.scF1])
            cur, nxt = uA, uB
            for j in (2, 1, 0):
                s = 3 - j
                P.op("vector", lambda e, dt=dt, j=j, s=s, cur=cur, nxt=nxt:
                     e.scalar_tensor_tensor(
                         out=nxt[:, s:MTC], in0=xin[dt][:, 0:MTC - s],
                         scalar=env.ma_cw[:, dt * 4 + j:dt * 4 + j + 1],
                         in1=cur[:, s:MTC], op0=ALU.mult, op1=ALU.add),
                     reads=[env.SA0, env.ma_cw, env.scF1], writes=[env.scF1])
                P.op("vector", lambda e, dt=dt, j=j, s=s, cur=cur, nxt=nxt:
                     e.scalar_tensor_tensor(
                         out=nxt[:, 0:s], in0=halo[:, dt * 3 + 3 - s:dt * 3 + 3],
                         scalar=env.ma_cw[:, dt * 4 + j:dt * 4 + j + 1],
                         in1=cur[:, 0:s], op0=ALU.mult, op1=ALU.add),
                     reads=[env.SA1b, env.ma_cw, env.scF1], writes=[env.scF1])
                cur, nxt = nxt, cur
            P.op("scalar", lambda e, dt=dt, cur=cur: e.activation(
                out=ut[dt], in_=cur, func=AF.Silu,
                bias=env.ma_cb[:, dt:dt + 1]),
                reads=[env.scF1, env.ma_cb], writes=[env.SB])
        for dt in range(16):
            P.op("gpsimd", lambda e, dt=dt: e.tensor_copy(
                out=halo[:, dt * 3:dt * 3 + 3], in_=xin[dt][:, MTC - 3:MTC]),
                reads=[env.SA0], writes=[env.SA1b])
        if stage < 3:
            continue
        px = env.psB[2]
        for dt in range(16):
            xp = xpw_t[dt % 2]
            P.dma("gpsimd", xp[:, 0:96], d["ma_xpwT"][dt * 128:(dt + 1) * 128, :],
                  writes=[env.wslot[0]], name="maxpw")
            P.op("tensor", lambda e, px=px, xp=xp, dt=dt: e.matmul(
                px[0:96, 0:MTC], xp[:, 0:96], ut[dt],
                start=(dt == 0), stop=(dt == 15)),
                reads=[env.wslot[0], env.SB], writes=[px])
        P.op("scalar", lambda e, px=px: e.activation(
            out=env.wslot[1][0:64, 0:MTC], in_=px[0:64, 0:MTC], func=AF.Copy),
            reads=[px], writes=[env.wslot[1]])
        BC_sb = env.wslot[1][0:32, 768:1024]
        P.op("scalar", lambda e, px=px, BC_sb=BC_sb: e.activation(
            out=BC_sb, in_=px[64:96, 0:MTC], func=AF.Copy),
            reads=[px], writes=[env.wslot[1]])
        for (rep, which) in ((brep, 0), (crep, 1)):
            pr = env.psB[3]
            P.op("tensor", lambda e, pr=pr, which=which, BC_sb=BC_sb: e.matmul(
                pr[:, 0:MTC], env.ma_selB[:, which * 128:(which + 1) * 128],
                BC_sb, start=True, stop=True),
                reads=[env.ma_selB, env.wslot[1]], writes=[pr])
            P.op("vector", lambda e, pr=pr, rep=rep: e.tensor_copy(
                out=rep, in_=pr[:, 0:MTC]), reads=[pr],
                writes=[env.wslot[1] if rep is brep else env.wslot[0]])
        for dt in range(16):
            dwt = dtw_t[dt % 2]
            P.dma("gpsimd", dwt, d["ma_dtwT"][:, dt * 128:(dt + 1) * 128],
                  writes=[env.wslot[1]], name="madtw")
            pd = env.psB[3]
            P.op("tensor", lambda e, pd=pd, dwt=dwt: e.matmul(
                pd[:, 0:MTC], dwt, env.wslot[1][0:64, 0:MTC],
                start=True, stop=True),
                reads=[env.wslot[1]], writes=[pd])
            P.op("scalar", lambda e, pd=pd, dt=dt: e.activation(
                out=S2b[:, 256:512], in_=pd[:, 0:MTC], func=AF.Abs,
                bias=env.ma_dtb[:, dt:dt + 1]),
                reads=[pd, env.ma_dtb], writes=[env.scF2])
            P.op("scalar", lambda e: e.activation(
                out=S2b[:, 512:768], in_=S2b[:, 256:512], func=AF.Exp, scale=-1.0),
                reads=[env.scF2], writes=[env.scF2])
            P.op("scalar", lambda e: e.activation(
                out=S2b[:, 256:512], in_=S2b[:, 512:768], func=AF.Ln,
                bias=env.cst[:, 1:2]), reads=[env.scF2, env.cst], writes=[env.scF2])
            P.op("vector", lambda e, pd=pd, dt=dt: e.tensor_scalar(
                out=S2b[:, 512:768], in0=pd[:, 0:MTC],
                scalar1=env.ma_dtb[:, dt:dt + 1], scalar2=0.0,
                op0=ALU.add, op1=ALU.max),
                reads=[pd, env.ma_dtb], writes=[env.scF2])
            P.op("vector", lambda e, dt=dt: e.tensor_tensor(
                out=dl[dt], in0=S2b[:, 256:512], in1=S2b[:, 512:768], op=ALU.add),
                reads=[env.scF2], writes=[env.SF])
        if stage < 4:
            continue
        for dt in range(16):
            P.op("vector", lambda e, dt=dt: e.tensor_tensor(
                out=du, in0=dl[dt], in1=ut[dt], op=ALU.mult),
                reads=[env.SF, env.SB], writes=[env.wslot[0]])
            yps = env.psB[2]
            for gg in range(16):
                g = dt * 16 + gg
                pr1 = env.psB[4 + 2 * (gg % 2)]
                pr2 = env.psB[5 + 2 * (gg % 2)]
                sl = selD[:, (g % 16) * 128:(g % 16 + 1) * 128]
                P.op("tensor", lambda e, pr1=pr1, sl=sl, dt=dt: e.matmul(
                    pr1[:, 0:MTC], sl, dl[dt], start=True, stop=True),
                    reads=[env.SA0, env.SF], writes=[pr1])
                P.op("tensor", lambda e, pr2=pr2, sl=sl: e.matmul(
                    pr2[:, 0:MTC], sl, du, start=True, stop=True),
                    reads=[env.SA0, env.wslot[0]], writes=[pr2])
                P.op("scalar", lambda e, pr1=pr1, g=g: e.activation(
                    out=a_f, in_=pr1[:, 0:MTC], func=AF.Exp,
                    scale=A_sb[:, g:g + 1]),
                    reads=[pr1, env.SA1b], writes=[env.scF1])
                P.op("vector", lambda e, pr2=pr2: e.tensor_tensor(
                    out=b_f, in0=pr2[:, 0:MTC], in1=brep, op=ALU.mult),
                    reads=[pr2, env.wslot[1]], writes=[env.scF1])
                P.op("vector", lambda e, g=g: e.tensor_tensor_scan(
                    out=h_f, data0=a_f, data1=b_f, initial=hcar[:, g:g + 1],
                    op0=ALU.mult, op1=ALU.add),
                    reads=[env.scF1, env.SA1b], writes=[env.scF2])
                P.op("vector", lambda e, g=g: e.tensor_copy(
                    out=hcar[:, g:g + 1], in_=h_f[:, MTC - 1:MTC]),
                    reads=[env.scF2], writes=[env.SA1b])
                P.op("gpsimd", lambda e: e.tensor_tensor(
                    out=hC, in0=h_f, in1=crep, op=ALU.mult),
                    reads=[env.scF2, env.wslot[0]], writes=[env.scF2])
                P.op("tensor", lambda e, yps=yps, gg=gg: e.matmul(
                    yps[:, 0:MTC], selR[:, gg * 128:(gg + 1) * 128], hC,
                    start=(gg == 0), stop=(gg == 15)),
                    reads=[env.SA0, env.scF2], writes=[yps])
            P.op("scalar", lambda e, dt=dt: e.activation(
                out=slz, in_=zt[dt], func=AF.Silu),
                reads=[env.SA1a], writes=[env.wslot[0]])
            P.op("vector", lambda e, yps=yps, dt=dt: e.scalar_tensor_tensor(
                out=a_f, in0=ut[dt], scalar=env.ma_Dp[:, dt:dt + 1], in1=yps[:, 0:MTC],
                op0=ALU.mult, op1=ALU.add),
                reads=[env.SB, env.ma_Dp, yps], writes=[env.scF1])
            P.op("vector", lambda e, dt=dt: e.tensor_tensor(
                out=yt[dt], in0=a_f, in1=slz, op=ALU.mult),
                reads=[env.scF1, env.wslot[0]], writes=[env.SB])
        if stage < 5:
            continue
        for k in range(16):
            ws = env.wslot[k % 2]
            P.dma("gpsimd", ws[:], d["ma_opwT"][k * 128:(k + 1) * 128, :],
                  writes=[ws], name="maopw")
            for tt2 in range(2):
                for h2 in range(2):
                    po = env.psB[tt2 * 2 + h2]
                    P.op("tensor", lambda e, po=po, ws=ws, k=k, tt2=tt2, h2=h2:
                         e.matmul(po[:], yt[k][:, tt2 * 128:(tt2 + 1) * 128],
                                  ws[:, h2 * 512:(h2 + 1) * 512],
                                  start=(k == 0), stop=(k == 15)),
                         reads=[env.SB, ws], writes=[po])
        for tt2 in range(2):
            it = c * 2 + tt2
            for h2 in range(2):
                po = env.psB[tt2 * 2 + h2]
                P.op("vector", lambda e, po=po, h2=h2: e.tensor_copy(
                    out=env.scF1[:, h2 * 512:(h2 + 1) * 512], in_=po[:]),
                    reads=[po], writes=[env.scF1])
            xi = env.x[it]
            P.op("gpsimd", lambda e, xi=xi: e.tensor_tensor(
                out=xi[:], in0=xi[:], in1=env.scF1[:], op=ALU.add),
                reads=[xi, env.scF1], writes=[xi])



def spec_none():
    return {}


def build_lnonly(P, env):
    layernorm_to_xnT(P, env)
    for i in range(NT):
        tm = env.xn_tm[0]
        pstp = env.ps_tp[0]
        pstp_bf = pstp[:].bitcast(BF16)
        for k in range(ND):
            src_ap = env.xnT[:, k * T_SEQ + i * 128:k * T_SEQ + (i + 1) * 128]
            P.op("tensor", lambda e, src_ap=src_ap, pstp_bf=pstp_bf, k=k: e.transpose(
                pstp_bf[:, k * 128:(k + 1) * 128], src_ap, env.ident[:]),
                reads=[env.xnT, env.ident], writes=[pstp])
        xi = env.x[i]
        P.op("vector", lambda e, xi=xi, pstp_bf=pstp_bf: e.tensor_copy(
            out=xi[:], in_=pstp_bf), reads=[pstp], writes=[xi])


SUBLAYERS = {
    0: ("l0attna", spec_l0attna, build_l0attna),
    21: ("maa_s1", spec_l0attna, lambda P, env: build_l0attna(P, env, 1)),
    22: ("maa_s2", spec_l0attna, lambda P, env: build_l0attna(P, env, 2)),
    23: ("maa_s3", spec_l0attna, lambda P, env: build_l0attna(P, env, 3)),
    24: ("maa_s4", spec_l0attna, lambda P, env: build_l0attna(P, env, 4)),
    1: ("l0ffn_a", spec_l0ffn, build_l0ffn),
    3: ("l0ffn_b", spec_l0ffn, build_l0ffn),
    2: ("l0attnb", spec_l0attnb, build_l0attnb),
    31: ("ab_s1", spec_l0attnb, lambda P, env: build_l0attnb(P, env, 1)),
    34: ("ab_dbg", spec_l0attnb, lambda P, env: (build_l0attnb(P, env, 2), [
        P.op("vector", lambda e, i=i: e.tensor_copy(
            out=env.x[i][:], in_=env.SB[:, i * 2048:(i + 1) * 2048].bitcast(F32)),
            reads=[env.SB], writes=[env.x[i]]) for i in range(8)])[0]),
    32: ("ab_s2", spec_l0attnb, lambda P, env: build_l0attnb(P, env, 2)),
    41: ("l0f_s0", spec_l0ffn, lambda P, env: build_l0ffn(P, env, 0)),
    42: ("l0f_s1", spec_l0ffn, lambda P, env: build_l0ffn(P, env, 1)),
    5: ("l1ffn_a", spec_l1ffn, build_l1ffn),
    4: ("l1attna", spec_l1attna, build_l1attna),
    51: ("aa_s1", spec_l1attna, lambda P, env: build_l1attna(P, env, 1)),
    6: ("l1attnb", spec_l1attnb, build_l1attnb),
    61: ("bb_s1", spec_l1attnb, lambda P, env: build_l1attnb(P, env, 1)),
    62: ("bb_dbg", spec_l1attnb, lambda P, env: (build_l1attnb(P, env, 2.5), [
        P.op("vector", lambda e, i=i: e.tensor_copy(
            out=env.x[i][:], in_=env.SB[:, 8192 + i * 2048:8192 + (i + 1) * 2048]
            .bitcast(F32)), reads=[env.SB], writes=[env.x[i]]) for i in range(4)])[0]),
    7: ("l1ffn_b", spec_l1ffn, build_l1ffn),
    98: ("pass", spec_none, lambda P, env: None),
    50: ("l1f_s0", spec_l1ffn, lambda P, env: build_l1ffn(P, env, 0)),
    51: ("l1f_s1", spec_l1ffn, lambda P, env: build_l1ffn(P, env, 1)),
    52: ("l1f_s2", spec_l1ffn, lambda P, env: build_l1ffn(P, env, 2)),
    53: ("l1f_s3", spec_l1ffn, lambda P, env: build_l1ffn(P, env, 3)),
    54: ("l1f_s4", spec_l1ffn, lambda P, env: build_l1ffn(P, env, 4)),
    55: ("l1f_s5", spec_l1ffn, lambda P, env: build_l1ffn(P, env, 5)),
    99: ("lnonly", spec_none, build_lnonly),
}

ALL_IDX = [0, 1, 2, 3, 4, 5, 6, 7]
LAUNCH_PLAN = [[0], [1], [2], [3], [4], [5], [6], [7]]


def build_program(idxs):
    nc = bass.Bass("TRN2", target_bir_lowering=False)
    x_in = nc.dram_tensor("x_in", [T_SEQ, D_MODEL], F32, kind="ExternalInput").ap()
    x_out = nc.dram_tensor("x_out", [T_SEQ, D_MODEL], F32, kind="ExternalOutput").ap()
    specs = {}
    for i in idxs:
        specs.update(SUBLAYERS[i][1]())
    dram = {}
    for name, (shape, dtype, _fn) in specs.items():
        dram[name] = nc.dram_tensor(name, list(shape), dtype, kind="ExternalInput").ap()
    P = Prog(nc)
    env = Env()
    env.dram = dram
    env.aa_scr = nc.dram_tensor("aa_scr", [272, 512], BF16, kind="Internal").ap()
    env.aa_scrbuf = Buf("aa_scr")
    with P._ctx:
        alloc_common(P, env)
        build_identity(P, env)
        load_x(P, env, x_in)
        for i in idxs:
            SUBLAYERS[i][2](P, env)
        store_x(P, env, x_out)
        with nc.Block() as block:
            P.finalize(block)
    return nc, specs


def kernel(**inputs):
    x = np.ascontiguousarray(np.asarray(inputs["L0_attn_a_hidden"], dtype=np.float32))
    for idxs in LAUNCH_PLAN:
        nc, specs = build_program(idxs)
        wmap = {name: np.ascontiguousarray(np.asarray(fn(inputs), dtype=np.float32))
                for name, (_s, _d, fn) in specs.items()}
        in_maps = [dict(wmap, x_in=x[b]) for b in range(N_CORES)]
        res = run_bass_kernel_spmd(nc, in_maps, core_ids=list(range(N_CORES)))
        x = np.stack([np.asarray(res.results[b]["x_out"]) for b in range(N_CORES)])
    return x


# version 04
# speedup vs baseline: 1.0016x; 1.0016x over previous
import math
import numpy as np
import ml_dtypes
from contextlib import ExitStack

import concourse.bass as bass
import concourse.mybir as mybir
from concourse.bass_utils import run_bass_kernel_spmd

F32 = mybir.dt.float32
BF16 = mybir.dt.bfloat16
AF = mybir.ActivationFunctionType
ALU = mybir.AluOpType
AX = mybir.AxisListType

ENGINES = ("tensor", "vector", "scalar", "gpsimd", "sync")
DMA_QUEUES = ("sync", "scalar", "gpsimd")

T_SEQ = 2048
D_MODEL = 1024
NT = T_SEQ // 128
ND = D_MODEL // 128
N_CORES = 8



class Buf:
    __slots__ = ("name", "writer", "readers")

    def __init__(self, name):
        self.name = name
        self.writer = None
        self.readers = []


class Op:
    __slots__ = ("fn", "deps", "inc", "seq", "name", "is_dma", "dma_queue_idx")

    def __init__(self, fn, deps, seq, name, is_dma=False):
        self.fn = fn
        self.deps = deps
        self.inc = False
        self.seq = seq
        self.name = name
        self.is_dma = is_dma
        self.dma_queue_idx = -1


class Prog:
    def __init__(self, nc):
        self.nc = nc
        self.streams = {e: [] for e in ENGINES}
        self.dep_targets = set()
        self.dma_idx = {q: 0 for q in DMA_QUEUES}
        self._ctx = ExitStack()
        self.bufmap = {}
        self._handles = []

    def sb(self, name, shape, dtype):
        h = self._ctx.enter_context(self.nc.sbuf_tensor(name, list(shape), dtype))
        self.bufmap[id(h)] = Buf(name)
        self._handles.append(h)
        return h

    def ps(self, name, shape, dtype):
        h = self._ctx.enter_context(self.nc.psum_tensor(name, list(shape), dtype))
        self.bufmap[id(h)] = Buf(name)
        self._handles.append(h)
        return h

    def _b(self, t):
        if isinstance(t, Buf):
            return t
        return self.bufmap[id(t)]

    def _mkdeps(self, eng, seq, rbufs, wbufs, is_dma=False):
        deps = []
        for b in rbufs:
            if b.writer is not None:
                deps.append(b.writer)
        for b in wbufs:
            if b.writer is not None:
                deps.append(b.writer)
            deps.extend(b.readers)
        dd = {}
        for d in deps:
            if d[0] == "e" and d[1] == eng and not is_dma:
                if eng == "tensor":
                    continue
                if d[2] >= seq:
                    continue
            key = (d[0], d[1])
            if key not in dd or dd[key][2] < d[2]:
                dd[key] = d
        return list(dd.values())

    def op(self, eng, fn, reads=(), writes=(), name=""):
        seq = len(self.streams[eng])
        rbufs = [self._b(t) for t in reads]
        wbufs = [self._b(t) for t in writes]
        deps = self._mkdeps(eng, seq, rbufs, wbufs)
        o = Op(fn, deps, seq, name)
        self.streams[eng].append(o)
        me = ("e", eng, seq)
        for b in rbufs:
            b.readers.append(me)
        for b in wbufs:
            b.writer = me
            b.readers = []
        for d in deps:
            if d[0] == "e":
                self.dep_targets.add((d[1], d[2]))
        return o

    def dma(self, queue, out, in_, reads=(), writes=(), name="dma", **kw):
        seq = len(self.streams[queue])
        rbufs = [self._b(t) for t in reads]
        wbufs = [self._b(t) for t in writes]
        deps = self._mkdeps(queue, seq, rbufs, wbufs, is_dma=True)
        idx = self.dma_idx[queue]
        self.dma_idx[queue] = idx + 1

        def fn(eng, out=out, in_=in_, kw=kw):
            return eng.dma_start(out=out, in_=in_, **kw)

        o = Op(fn, deps, seq, name, is_dma=True)
        o.dma_queue_idx = idx
        self.streams[queue].append(o)
        me = ("d", queue, idx)
        for b in rbufs:
            b.readers.append(me)
        for b in wbufs:
            b.writer = me
            b.readers = []
        for d in deps:
            if d[0] == "e":
                self.dep_targets.add((d[1], d[2]))
        return o

    def finalize(self, block):
        for (eng, seq) in self.dep_targets:
            self.streams[eng][seq].inc = True
        prefix = {}
        for eng in ENGINES:
            c = 0
            pc = []
            for o in self.streams[eng]:
                if o.inc and not o.is_dma:
                    c += 1
                pc.append(c)
            prefix[eng] = pc
        self.sem = {}
        self.dsem = {}
        for eng in ENGINES:
            if any((o.inc and not o.is_dma) for o in self.streams[eng]):
                self.sem[eng] = self._ctx.enter_context(
                    self.nc.semaphore("sem_" + eng))
        for q in DMA_QUEUES:
            if any(o.is_dma for o in self.streams[q]):
                self.dsem[q] = self._ctx.enter_context(
                    self.nc.semaphore("dsem_" + q))

        def emit(proxy, eng_name):
            waited = {}
            for o in self.streams[eng_name]:
                for d in o.deps:
                    if d[0] == "e":
                        sem = self.sem[d[1]]
                        val = prefix[d[1]][d[2]]
                    else:
                        sem = self.dsem[d[1]]
                        val = 16 * (d[2] + 1)
                    key = id(sem)
                    if waited.get(key, -1) >= val:
                        continue
                    waited[key] = val
                    proxy.wait_ge(sem, val)
                if o.fn is None:
                    continue
                inst = o.fn(proxy)
                if o.is_dma:
                    inst.then_inc(self.dsem[eng_name], 16)
                elif o.inc:
                    inst.then_inc(self.sem[eng_name], 1)

        for eng_name in ENGINES:
            getattr(block, eng_name)(lambda proxy, e=eng_name: emit(proxy, e))


class Env:
    def __init__(self):
        self.pool = {}
        self.flags = set()

    def get(self, P, name, shape, dtype):
        if name not in self.pool:
            self.pool[name] = P.sb(name, list(shape), dtype)
        return self.pool[name]

    def once(self, flag):
        if flag in self.flags:
            return False
        self.flags.add(flag)
        return True



def build_identity(P, env):
    env.ident = P.sb("ident", [128, 128], BF16)
    P.op("gpsimd", lambda e: e.memset(env.ident[:], 0.0), writes=[env.ident])
    P.op("gpsimd", lambda e: e.affine_select(
        out=env.ident[:], in_=env.ident[:], compare_op=ALU.not_equal, fill=1.0,
        base=0, pattern=[[-1, 128]], channel_multiplier=1),
        reads=[env.ident], writes=[env.ident])


def alloc_common(P, env):
    env.xnT = P.sb("xnT", [128, ND * T_SEQ], BF16)
    env.stats = [P.sb(f"st{i}", [128, 8], F32) for i in range(NT)]
    env.xn_tm = [P.sb(f"xntm{i}", [128, D_MODEL], BF16) for i in range(2)]
    env.psB = [P.ps(f"psB{i}", [128, 512], F32) for i in range(8)]
    env.psA = env.psB[:6]
    env.ps_tp = [env.psB[6], env.psB[7]]
    env.SA0 = P.sb("SA0", [128, 8192], BF16)
    env.SA1a = P.sb("SA1a", [128, 4096], BF16)
    env.SA1b = P.sb("SA1b", [128, 4096], BF16)
    env.SB = P.sb("SB", [128, 16384], BF16)
    env.SF = P.sb("SF", [128, 4096], F32)
    env.gbB = P.sb("gbB", [128, 2048], BF16)
    env.scF1 = P.sb("scF1", [128, 1024], F32)
    env.scF2 = P.sb("scF2", [128, 1024], F32)
    env.ones_f = P.sb("ones_f", [1, 128], F32)
    P.op("gpsimd", lambda e: e.memset(env.ones_f[:], 1.0), writes=[env.ones_f])
    env.wslot = [P.sb(f"wslot{j}", [128, 1024], BF16) for j in range(2)]
    env.lam_sb = P.sb("lam_sb", [1, 16], F32)
    env.lamrep = P.sb("lamrep", [128, 8], F32)
    env.row_sc = P.sb("row_sc", [1, 1024], F32)
    env.srow = env.row_sc[0:1, 0:512]
    P.bufmap[id(env.srow)] = P._b(env.row_sc)
    env.cst = P.sb("cst", [128, 4], F32)
    P.op("gpsimd", lambda e: e.memset(env.cst[:, 0:1], 1e-5), writes=[env.cst])
    P.op("gpsimd", lambda e: e.memset(env.cst[:, 1:2], 1.0), writes=[env.cst])
    P.op("gpsimd", lambda e: e.memset(env.cst[:, 2:3], 1e-6), writes=[env.cst])
    env.identf = P.sb("identf", [128, 128], F32)
    P.op("gpsimd", lambda e: e.memset(env.identf[:], 0.0), writes=[env.identf])
    P.op("gpsimd", lambda e: e.affine_select(
        out=env.identf[:], in_=env.identf[:], compare_op=ALU.not_equal, fill=1.0,
        base=0, pattern=[[-1, 128]], channel_multiplier=1),
        reads=[env.identf], writes=[env.identf])
    env.selH = P.sb("selH", [128, 2], BF16)
    P.op("gpsimd", lambda e: e.memset(env.selH[:], 1.0), writes=[env.selH])
    P.op("gpsimd", lambda e: e.affine_select(
        out=env.selH[:], in_=env.selH[:], compare_op=ALU.is_ge, fill=0.0,
        base=0, pattern=[[-64, 2]], channel_multiplier=1),
        reads=[env.selH], writes=[env.selH])
    P.op("gpsimd", lambda e: e.affine_select(
        out=env.selH[:], in_=env.selH[:], compare_op=ALU.is_ge, fill=0.0,
        base=63, pattern=[[64, 2]], channel_multiplier=-1),
        reads=[env.selH], writes=[env.selH])
    env.rts = P.sb("rts", [128, 64], F32)


def load_x(P, env, x_dram):
    env.x = [P.sb(f"x{i}", [128, D_MODEL], F32) for i in range(NT)]
    for i in range(NT):
        P.dma("sync", env.x[i][:], x_dram[i * 128:(i + 1) * 128, :],
              writes=[env.x[i]], name=f"ldx{i}")


def store_x(P, env, x_out_dram):
    for i in range(NT):
        P.dma("sync", x_out_dram[i * 128:(i + 1) * 128, :], env.x[i][:],
              reads=[env.x[i]], name=f"stx{i}")
    o = Op(None, [("d", "sync", P.dma_idx["sync"] - 1)],
           len(P.streams["sync"]), "final")
    P.streams["sync"].append(o)


def layernorm_to_xnT(P, env):
    for i in range(NT):
        st = env.stats[i]
        xi = env.x[i]
        P.op("vector", lambda e, xi=xi, st=st: e.reduce_sum(
            st[:, 0:1], xi[:], axis=AX.X), reads=[xi], writes=[st])
        P.op("vector", lambda e, st=st: e.tensor_scalar(
            out=st[:, 1:2], in0=st[:, 0:1], scalar1=-1.0 / D_MODEL, scalar2=None,
            op0=ALU.mult), reads=[st], writes=[st])
        sqs = env.xn_tm[i % 2]
        P.op("scalar", lambda e, xi=xi, st=st, sqs=sqs: e.activation(
            out=sqs[:], in_=xi[:], func=AF.Square, bias=st[:, 1:2],
            accum_out=st[:, 2:3]), reads=[xi, st], writes=[sqs, st])
        P.op("scalar", lambda e, st=st: e.activation(
            out=st[:, 3:4], in_=st[:, 2:3], func=AF.Ln, scale=1.0 / D_MODEL,
            bias=env.cst[:, 0:1]), reads=[st, env.cst], writes=[st])
        P.op("scalar", lambda e, st=st: e.activation(
            out=st[:, 4:5], in_=st[:, 3:4], func=AF.Exp, scale=-0.5),
            reads=[st], writes=[st])
        P.op("vector", lambda e, st=st: e.tensor_tensor(
            out=st[:, 5:6], in0=st[:, 1:2], in1=st[:, 4:5], op=ALU.mult),
            reads=[st], writes=[st])
        tm = env.xn_tm[i % 2]
        P.op("vector", lambda e, xi=xi, st=st, tm=tm: e.tensor_scalar(
            out=tm[:], in0=xi[:], scalar1=st[:, 4:5], scalar2=st[:, 5:6],
            op0=ALU.mult, op1=ALU.add), reads=[xi, st], writes=[tm])
        pstp = env.ps_tp[i % 2]
        pstp_bf = pstp[:].bitcast(BF16)
        for k in range(ND):
            P.op("tensor", lambda e, tm=tm, pstp_bf=pstp_bf, k=k: e.transpose(
                pstp_bf[:, k * 128:(k + 1) * 128], tm[:, k * 128:(k + 1) * 128],
                env.ident[:]), reads=[tm, env.ident], writes=[pstp])
        dst = env.xnT[:].rearrange("p (k t) -> p k t", k=ND)[:, :, i * 128:(i + 1) * 128]
        src_ap = pstp_bf.rearrange("p (k t) -> p k t", k=ND)
        P.op("vector", lambda e, dst=dst, src_ap=src_ap: e.tensor_copy(
            out=dst, in_=src_ap), reads=[pstp], writes=[env.xnT])


def xnt(env, k, t0, t1):
    return env.xnT[:, k * T_SEQ + t0:k * T_SEQ + t1]


def residual_add_tile(P, env, i, cols, src_ap, reads):
    xi = env.x[i]
    P.op("gpsimd", lambda e, xi=xi, src_ap=src_ap, cols=cols: e.tensor_tensor(
        out=xi[:, cols], in0=xi[:, cols], in1=src_ap, op=ALU.add),
        reads=[xi] + list(reads), writes=[xi])



def spec_l1ffn():
    def t(i, key):
        return np.ascontiguousarray(np.asarray(i[key]).T)

    def mu(i):
        return np.ascontiguousarray(np.concatenate([
            np.asarray(i["L1_ffn_mu_r"]).reshape(8, 128).T,
            np.asarray(i["L1_ffn_mu_k"]).reshape(8, 128).T], axis=1))

    return {
        "f1_WrT": ((1024, 1024), F32, lambda i: t(i, "L1_ffn_W_r")),
        "f1_WkT": ((1024, 1024), F32, lambda i: t(i, "L1_ffn_W_k")),
        "f1_WvT": ((1024, 1024), F32, lambda i: t(i, "L1_ffn_W_v")),
        "f1_mu": ((128, 16), F32, mu),
    }


def build_l1ffn(P, env, stage=99):
    layernorm_to_xnT(P, env)
    d = env.dram
    mu = env.get(P, "f1mu_sb", (128, 16), F32)
    if env.once("f1mu"):
        P.dma("sync", mu[:], d["f1_mu"], writes=[mu], name="f1mu")
    WrT = env.SB[:, 0:8192]
    WkT = env.SB[:, 8192:16384]
    WvT = env.SF[:].bitcast(BF16)
    for (t, tb_, w, off) in ((WrT, env.SB, "f1_WrT", 0), (WkT, env.SB, "f1_WkT", 0),
                             (WvT, env.SF, "f1_WvT", 0)):
        for k in range(ND):
            P.dma("gpsimd", t[:, k * 1024:(k + 1) * 1024],
                  d[w][k * 128:(k + 1) * 128, :], writes=[tb_], name=w + str(k))
    bSA0 = env.SA0
    xr = [env.SA0[:, k * 512:(k + 1) * 512] for k in range(ND)]
    xk = [env.SA0[:, 4096 + k * 512:4096 + (k + 1) * 512] for k in range(ND)]
    ka = [env.SA1a[:, k * 512:(k + 1) * 512] for k in range(ND)]
    rsb = [env.SA1b[:, j * 1024:(j + 1) * 1024] for j in range(4)]
    tmp = env.scF2[:].bitcast(BF16)[:, 0:512]
    krf = env.scF1[:, 0:512]
    osc = env.scF1[:, 512:1024]

    if stage < 1:
        return
    for c in range(4):
        t0 = c * 512
        for k in range(ND):
            for (dst, mcol) in ((xr[k], k), (xk[k], 8 + k)):
                lo = 1 if t0 == 0 else 0
                if t0 == 0:
                    P.op("vector", lambda e, dst=dst, k=k, mcol=mcol: e.tensor_scalar(
                        out=dst[:, 0:1], in0=xnt(env, k, 0, 1),
                        scalar1=mu[:, mcol:mcol + 1], scalar2=None, op0=ALU.mult),
                        reads=[env.xnT, mu], writes=[bSA0])
                xs = xnt(env, k, t0 + lo, t0 + 512)
                xp = xnt(env, k, t0 + lo - 1, t0 + 511)
                P.op("vector", lambda e, xs=xs, xp=xp, lo=lo: e.tensor_tensor(
                    out=tmp[:, lo:512], in0=xs, in1=xp, op=ALU.subtract),
                    reads=[env.xnT], writes=[env.scF2])
                P.op("vector", lambda e, dst=dst, xp=xp, lo=lo, mcol=mcol:
                     e.scalar_tensor_tensor(
                         out=dst[:, lo:512], in0=tmp[:, lo:512],
                         scalar=mu[:, mcol:mcol + 1], in1=xp,
                         op0=ALU.mult, op1=ALU.add),
                     reads=[env.scF2, env.xnT, mu], writes=[bSA0])
        if stage < 2:
            continue
        for f in range(ND):
            pk = env.psA[2 + (f % 2)]
            for k in range(ND):
                lhsT = WkT[:, k * 1024 + f * 128:k * 1024 + (f + 1) * 128]
                P.op("tensor", lambda e, pk=pk, lhsT=lhsT, xkk=xk[k], k=k: e.matmul(
                    pk[:], lhsT, xkk, start=(k == 0), stop=(k == ND - 1)),
                    reads=[env.SB, bSA0], writes=[pk])
            P.op("vector", lambda e, pk=pk: e.tensor_scalar(
                out=krf, in0=pk[:], scalar1=0.0, scalar2=None, op0=ALU.max),
                reads=[pk], writes=[env.scF1])
            P.op("scalar", lambda e, f=f: e.activation(
                out=ka[f], in_=krf, func=AF.Square),
                reads=[env.scF1], writes=[env.SA1a])
        if stage < 3:
            continue
        for tt in range(4):
            for h in range(2):
                pr = env.psA[h]
                for k in range(ND):
                    lhsT = xr[k][:, tt * 128:(tt + 1) * 128]
                    rhs = WrT[:, k * 1024 + h * 512:k * 1024 + h * 512 + 512]
                    P.op("tensor", lambda e, pr=pr, lhsT=lhsT, rhs=rhs, k=k: e.matmul(
                        pr[:], lhsT, rhs, start=(k == 0), stop=(k == ND - 1)),
                        reads=[env.SB, bSA0], writes=[pr])
                P.op("scalar", lambda e, pr=pr, tt=tt, h=h: e.activation(
                    out=rsb[tt][:, h * 512:(h + 1) * 512], in_=pr[:], func=AF.Sigmoid),
                    reads=[pr], writes=[env.SA1b])
        if stage < 4:
            continue
        for tt in range(4):
            i_tile = c * 4 + tt
            for h in range(2):
                po = env.psA[4 + h]
                for f in range(ND):
                    lhsT = ka[f][:, tt * 128:(tt + 1) * 128]
                    rhs = WvT[:, f * 1024 + h * 512:f * 1024 + h * 512 + 512]
                    P.op("tensor", lambda e, po=po, lhsT=lhsT, rhs=rhs, f=f: e.matmul(
                        po[:], lhsT, rhs, start=(f == 0), stop=(f == ND - 1)),
                        reads=[env.SF, env.SA1a], writes=[po])
                if stage < 5:
                    P.op("vector", lambda e, po=po: e.tensor_copy(
                        out=osc, in_=po[:]), reads=[po], writes=[env.scF1])
                else:
                    P.op("vector", lambda e, po=po, tt=tt, h=h: e.tensor_tensor(
                        out=osc, in0=po[:], in1=rsb[tt][:, h * 512:(h + 1) * 512],
                        op=ALU.mult), reads=[po, env.SA1b], writes=[env.scF1])
                if stage >= 6:
                    residual_add_tile(P, env, i_tile, slice(h * 512, (h + 1) * 512),
                                      osc, [env.scF1])




def spec_l0ffn():
    def f(key, fn):
        return lambda i, key=key, fn=fn: fn(np.asarray(i[key]))

    return {
        "fr_WrT": ((1024, 8), F32, f("L0_ffn_Wr", lambda a: np.ascontiguousarray(a.T))),
        "fr_br": ((1, 8), F32, f("L0_ffn_br", lambda a: a.reshape(1, 8))),
        "f0_W1T": ((8192, 4096), F32, f("L0_ffn_W1", lambda a: np.ascontiguousarray(
            a.transpose(0, 2, 1).reshape(8192, 4096)))),
        "f0_b1": ((128, 256), F32, f("L0_ffn_b1", lambda a: np.ascontiguousarray(
            a.reshape(8, 32, 128).transpose(2, 0, 1).reshape(128, 256)))),
        "f0_W2T": ((32768, 1024), F32, f("L0_ffn_W2", lambda a: np.ascontiguousarray(
            a.transpose(0, 2, 1).reshape(32768, 1024)))),
        "f0_b2": ((1, 8192), F32, f("L0_ffn_b2", lambda a: a.reshape(1, 8192))),
        "f0_gb": ((1, 2048), F32, lambda i: np.concatenate(
            [np.asarray(i["L0_ffn_gamma"]), np.asarray(i["L0_ffn_beta"])]).reshape(1, 2048)),
    }


def build_l0ffn(P, env, stage=99):
    layernorm_to_xnT(P, env)
    d = env.dram
    if env.once("f0small"):
        env.wr_sb = P.sb("wr_sb", [128, 64], BF16)
        for k in range(ND):
            P.dma("gpsimd", env.wr_sb[:, 8 * k:8 * (k + 1)],
                  d["fr_WrT"][k * 128:(k + 1) * 128, :], writes=[env.wr_sb])
        env.br_row = P.sb("br_row", [1, 8], F32)
        P.dma("sync", env.br_row[:], d["fr_br"], writes=[env.br_row])
        env.b1_sb = P.sb("b1_sb", [128, 256], F32)
        P.dma("sync", env.b1_sb[:], d["f0_b1"], writes=[env.b1_sb])

        pass
        env.brb = P.sb("brb", [128, 8], F32)
        env.coef = P.sb("coef", [128, 128], F32)
        pb = env.psB[0]
        P.op("tensor", lambda e, pb=pb: e.matmul(
            pb[:, 0:8], env.ones_f[0:1, :], env.br_row[0:1, :],
            start=True, stop=True), reads=[env.ones_f, env.br_row], writes=[pb])
        P.op("vector", lambda e, pb=pb: e.tensor_copy(out=env.brb[:], in_=pb[:, 0:8]),
             reads=[pb], writes=[env.brb])
        for gb2 in range(2):
            P.dma("sync", env.row_sc[0:1, :],
                  d["f0_gb"][0:1, gb2 * 1024:(gb2 + 1) * 1024],
                  writes=[env.row_sc], name="gbld")
            for j in range(2):
                pb = env.psB[j % 2]
                P.op("tensor", lambda e, pb=pb, j=j: e.matmul(
                    pb[:], env.ones_f[0:1, :], env.row_sc[0:1, j * 512:(j + 1) * 512],
                    start=True, stop=True), reads=[env.ones_f, env.row_sc],
                    writes=[pb])
                P.op("vector", lambda e, pb=pb, j=j, gb2=gb2: e.tensor_copy(
                    out=env.gbB[:, gb2 * 1024 + j * 512:gb2 * 1024 + (j + 1) * 512],
                    in_=pb[:]), reads=[pb], writes=[env.gbB])

    rts = env.rts
    for i in range(NT):
        pr = env.psB[i % 2]
        for k in range(ND):
            P.op("tensor", lambda e, pr=pr, k=k, i=i: e.matmul(
                pr[:, 0:8], xnt(env, k, i * 128, (i + 1) * 128),
                env.wr_sb[:, 8 * k:8 * (k + 1)], start=(k == 0), stop=(k == ND - 1)),
                reads=[env.xnT, env.wr_sb], writes=[pr])
        lg, mx = rts[:, 0:8], rts[:, 8:16]
        e1, e2, t8, e2b = rts[:, 16:24], rts[:, 24:32], rts[:, 40:48], rts[:, 48:56]
        c1, c2 = rts[:, 0:8], rts[:, 8:16]
        P.op("vector", lambda e, pr=pr, lg=lg: e.tensor_tensor(
            out=lg, in0=pr[:, 0:8], in1=env.brb[:], op=ALU.add),
            reads=[pr, env.brb], writes=[rts])
        P.op("vector", lambda e, lg=lg, mx=mx: e.max(out=mx, in_=lg),
             reads=[rts], writes=[rts])
        P.op("vector", lambda e, mx=mx: e.tensor_tensor(
            out=rts[:, 32:33], in0=mx[:, 0:1], in1=mx[:, 1:2], op=ALU.subtract),
            reads=[rts], writes=[rts])
        P.op("scalar", lambda e: e.activation(
            out=rts[:, 33:34], in_=rts[:, 32:33], func=AF.Sigmoid),
            reads=[rts], writes=[rts])
        P.op("vector", lambda e: e.tensor_scalar(
            out=rts[:, 34:35], in0=rts[:, 33:34], scalar1=-1.0, scalar2=1.0,
            op0=ALU.mult, op1=ALU.add), reads=[rts], writes=[rts])
        P.op("vector", lambda e, lg=lg, e1=e1, mx=mx: e.tensor_scalar(
            out=e1, in0=lg, scalar1=mx[:, 0:1], scalar2=None, op0=ALU.is_equal),
            reads=[rts], writes=[rts])
        P.op("vector", lambda e, lg=lg, e2=e2, mx=mx: e.tensor_scalar(
            out=e2, in0=lg, scalar1=mx[:, 1:2], scalar2=None, op0=ALU.is_equal),
            reads=[rts], writes=[rts])
        P.op("vector", lambda e, t8=t8, e1=e1, e2=e2: e.tensor_tensor(
            out=t8, in0=e2, in1=e1, op=ALU.mult), reads=[rts], writes=[rts])
        P.op("vector", lambda e, e2b=e2b, e2=e2, t8=t8: e.tensor_tensor(
            out=e2b, in0=e2, in1=t8, op=ALU.subtract), reads=[rts], writes=[rts])
        P.op("vector", lambda e, c1=c1, e1=e1: e.tensor_scalar(
            out=c1, in0=e1, scalar1=rts[:, 33:34], scalar2=None, op0=ALU.mult),
            reads=[rts], writes=[rts])
        P.op("vector", lambda e, c2=c2, e2b=e2b: e.tensor_scalar(
            out=c2, in0=e2b, scalar1=rts[:, 34:35], scalar2=None, op0=ALU.mult),
            reads=[rts], writes=[rts])
        P.op("vector", lambda e, c1=c1, c2=c2, i=i: e.tensor_tensor(
            out=env.coef[:, 8 * i:8 * (i + 1)], in0=c1, in1=c2, op=ALU.add),
            reads=[rts], writes=[env.coef])
    if stage < 1:
        return

    w1v = d["f0_W1T"].rearrange("(k p) f -> p k f", p=128)
    for tb in range(4):
        for e_x in range(8):
            P.dma("sync", env.row_sc[0:1, 0:1024],
                  d["f0_b2"][0:1, e_x * 1024:(e_x + 1) * 1024],
                  writes=[env.row_sc], name="b2ld")
            for fc in range(8):
                half = env.SA0[:, (fc % 2) * 4096:((fc % 2) + 1) * 4096]
                src_ap = w1v[:, e_x * 8:(e_x + 1) * 8, fc * 512:(fc + 1) * 512]
                dst_ap = half.rearrange("p (k f) -> p k f", k=8)
                P.dma("gpsimd", dst_ap, src_ap, writes=[env.SA0], name="w1ld")
                for ft in range(4):
                    fg = fc * 4 + ft
                    ph = env.psB[fg % 2]
                    for k in range(ND):
                        lhsT = half[:, k * 512 + ft * 128:k * 512 + (ft + 1) * 128]
                        P.op("tensor", lambda e, ph=ph, lhsT=lhsT, k=k, tb=tb: e.matmul(
                            ph[:], lhsT, xnt(env, k, tb * 512, (tb + 1) * 512),
                            start=(k == 0), stop=(k == ND - 1)),
                            reads=[env.SA0, env.xnT], writes=[ph])
                    P.op("scalar", lambda e, ph=ph, fg=fg, e_x=e_x: e.activation(
                        out=env.SB[:, fg * 512:(fg + 1) * 512], in_=ph[:], func=AF.Relu,
                        bias=env.b1_sb[:, e_x * 32 + fg:e_x * 32 + fg + 1]),
                        reads=[ph, env.b1_sb], writes=[env.SB])
            w2v = d["f0_W2T"]
            for fgp in range(8):
                buf = env.SA1a if fgp % 2 == 0 else env.SA1b
                src_ap = w2v[e_x * 4096 + fgp * 512:e_x * 4096 + (fgp + 1) * 512, :]
                dst_ap = buf[:].rearrange("p (j f) -> p j f", j=4)
                P.dma("gpsimd", dst_ap,
                      src_ap.rearrange("(j p) f -> p j f", p=128),
                      writes=[buf], name="w2ld")
                for j in range(4):
                    fg = fgp * 4 + j
                    for tt in range(4):
                        for h in range(2):
                            po = env.psB[tt * 2 + h]
                            lhsT = env.SB[:, fg * 512 + tt * 128:fg * 512 + (tt + 1) * 128]
                            rhs = buf[:, j * 1024 + h * 512:j * 1024 + h * 512 + 512]
                            P.op("tensor", lambda e, po=po, lhsT=lhsT, rhs=rhs, fg=fg:
                                 e.matmul(po[:], lhsT, rhs, start=(fg == 0), stop=False),
                                 reads=[env.SB, buf], writes=[po])
            for tt in range(4):
                i_tile = tb * 4 + tt
                for h in range(2):
                    po = env.psB[tt * 2 + h]
                    P.op("tensor", lambda e, po=po, h=h: e.matmul(
                        po[:], env.ones_f[0:1, :],
                        env.row_sc[0:1, h * 512:h * 512 + 512],
                        start=False, stop=True),
                        reads=[env.ones_f, env.row_sc], writes=[po])
                    cc = env.coef[:, 8 * i_tile + e_x:8 * i_tile + e_x + 1]
                    if e_x == 0:
                        P.op("scalar", lambda e, po=po, cc=cc, tt=tt, h=h: e.activation(
                            out=env.SF[:, tt * 1024 + h * 512:tt * 1024 + (h + 1) * 512
                                       if False else tt * 1024 + h * 512 + 512],
                            in_=po[:], func=AF.Copy, scale=cc),
                            reads=[po, env.coef], writes=[env.SF])
                    else:
                        P.op("scalar", lambda e, po=po, cc=cc: e.activation(
                            out=env.scF1[:, 0:512], in_=po[:], func=AF.Copy, scale=cc),
                            reads=[po, env.coef], writes=[env.scF1])
                        P.op("gpsimd", lambda e, tt=tt, h=h: e.tensor_tensor(
                            out=env.SF[:, tt * 1024 + h * 512:tt * 1024 + h * 512 + 512],
                            in0=env.SF[:, tt * 1024 + h * 512:tt * 1024 + h * 512 + 512],
                            in1=env.scF1[:, 0:512], op=ALU.add),
                            reads=[env.SF, env.scF1], writes=[env.SF])
        if stage < 2:
            continue
        for tt in range(4):
            i = tb * 4 + tt
            st = env.stats[i]
            xi = env.x[i]
            P.op("vector", lambda e, xi=xi, st=st: e.tensor_scalar(
                out=env.scF1[:], in0=xi[:], scalar1=st[:, 4:5], scalar2=st[:, 5:6],
                op0=ALU.mult, op1=ALU.add), reads=[xi, st], writes=[env.scF1])
            P.op("vector", lambda e, tt=tt: e.tensor_tensor(
                out=env.scF2[:], in0=env.scF1[:],
                in1=env.SF[:, tt * 1024:(tt + 1) * 1024], op=ALU.add),
                reads=[env.scF1, env.SF], writes=[env.scF2])
            P.op("vector", lambda e, st=st: e.reduce_sum(
                st[:, 0:1], env.scF2[:], axis=AX.X), reads=[env.scF2, st], writes=[st])
            P.op("vector", lambda e, st=st: e.tensor_scalar(
                out=st[:, 1:2], in0=st[:, 0:1], scalar1=-1.0 / D_MODEL, scalar2=None,
                op0=ALU.mult), reads=[st], writes=[st])
            P.op("scalar", lambda e, st=st: e.activation(
                out=env.xn_tm[0][:], in_=env.scF2[:], func=AF.Square, bias=st[:, 1:2],
                accum_out=st[:, 2:3]), reads=[env.scF2, st],
                writes=[env.xn_tm[0], st])
            P.op("scalar", lambda e, st=st: e.activation(
                out=st[:, 3:4], in_=st[:, 2:3], func=AF.Ln, scale=1.0 / D_MODEL,
                bias=env.cst[:, 0:1]), reads=[st, env.cst], writes=[st])
            P.op("scalar", lambda e, st=st: e.activation(
                out=st[:, 4:5], in_=st[:, 3:4], func=AF.Exp, scale=-0.5),
                reads=[st], writes=[st])
            P.op("vector", lambda e, st=st: e.tensor_tensor(
                out=st[:, 5:6], in0=st[:, 1:2], in1=st[:, 4:5], op=ALU.mult),
                reads=[st], writes=[st])
            P.op("vector", lambda e, st=st: e.tensor_scalar(
                out=env.scF1[:], in0=env.scF2[:], scalar1=st[:, 4:5], scalar2=st[:, 5:6],
                op0=ALU.mult, op1=ALU.add), reads=[env.scF2, st], writes=[env.scF1])
            P.op("vector", lambda e: e.tensor_tensor(
                out=env.scF2[:], in0=env.scF1[:], in1=env.gbB[:, 0:1024], op=ALU.mult),
                reads=[env.scF1, env.gbB], writes=[env.scF2])
            P.op("vector", lambda e: e.tensor_tensor(
                out=env.scF1[:], in0=env.scF2[:], in1=env.gbB[:, 1024:2048], op=ALU.add),
                reads=[env.scF2, env.gbB], writes=[env.scF1])
            P.op("gpsimd", lambda e, xi=xi: e.tensor_tensor(
                out=xi[:], in0=xi[:], in1=env.scF1[:], op=ALU.add),
                reads=[xi, env.scF1], writes=[xi])



def spec_l0attnb():
    def raw(key):
        return lambda i, key=key: np.asarray(i[key])

    return {
        "ab_Wq": ((1024, 1024), F32, raw("L0_attn_b_Wq")),
        "ab_Wk": ((1024, 1024), F32, raw("L0_attn_b_Wk")),
        "ab_Wv": ((1024, 1024), F32, raw("L0_attn_b_Wv")),
        "ab_Wqp": ((1024, 1024), F32, raw("L0_attn_b_Wqp")),
        "ab_Wkp": ((1024, 1024), F32, raw("L0_attn_b_Wkp")),
        "ab_Wo": ((1024, 1024), F32, raw("L0_attn_b_Wo")),
        "ab_lam": ((1, 16), F32, lambda i: np.ascontiguousarray(
            np.asarray(i["L0_attn_b_lambda_raw"]).reshape(1, 16))),
    }


def _proj_fm(P, env, wdram_col, wslot, evac):
    P.dma("gpsimd", wslot[:].rearrange("p (k m) -> p k m", k=8),
          wdram_col.rearrange("(k p) m -> p k m", p=128), writes=[wslot], name="wld")
    for tc in range(4):
        ph = env.psB[4 + tc % 2]
        for k in range(ND):
            lhsT = wslot[:, k * 128:(k + 1) * 128]
            P.op("tensor", lambda e, ph=ph, lhsT=lhsT, k=k, tc=tc: e.matmul(
                ph[:], lhsT, xnt(env, k, tc * 512, (tc + 1) * 512),
                start=(k == 0), stop=(k == ND - 1)),
                reads=[wslot, env.xnT], writes=[ph])
        evac(tc, ph)


def build_l0attnb(P, env, stage=99):
    layernorm_to_xnT(P, env)
    d = env.dram
    if env.once("absmall"):
        pass
        env.selHT_f = P.sb("selHT_f", [2, 128], F32)
        P.op("gpsimd", lambda e: e.memset(env.selHT_f[:], 1.0), writes=[env.selHT_f])
        P.op("gpsimd", lambda e: e.affine_select(
            out=env.selHT_f[:], in_=env.selHT_f[:], compare_op=ALU.is_ge, fill=0.0,
            base=0, pattern=[[1, 128]], channel_multiplier=-64),
            reads=[env.selHT_f], writes=[env.selHT_f])
        P.op("gpsimd", lambda e: e.affine_select(
            out=env.selHT_f[:], in_=env.selHT_f[:], compare_op=ALU.is_ge, fill=0.0,
            base=63, pattern=[[-1, 128]], channel_multiplier=64),
            reads=[env.selHT_f], writes=[env.selHT_f])
    P.dma("sync", env.lam_sb[:], d["ab_lam"], writes=[env.lam_sb], name="lamld")
    P.op("scalar", lambda e: e.activation(
        out=env.lam_sb[:], in_=env.lam_sb[:], func=AF.Tanh),
        reads=[env.lam_sb], writes=[env.lam_sb])
    for g in range(8):
        pb = env.psB[0]
        for hl in range(2):
            P.op("tensor", lambda e, pb=pb, g=g, hl=hl: e.matmul(
                pb[64 * hl:64 * (hl + 1), 0:1], env.ones_f[0:1, 0:64],
                env.lam_sb[0:1, 2 * g + hl:2 * g + hl + 1],
                start=(hl == 0), stop=(hl == 1)),
                reads=[env.ones_f, env.lam_sb], writes=[pb])
        P.op("vector", lambda e, pb=pb, g=g: e.tensor_scalar(
            out=env.lamrep[:, g:g + 1], in0=pb[:, 0:1], scalar1=0.5 / 8.0,
            scalar2=None, op0=ALU.mult), reads=[pb], writes=[env.lamrep])

    SFbf = env.SF[:].bitcast(BF16)
    qT, qcT, qsT, kT = (env.SA0[:, j * 2048:(j + 1) * 2048] for j in range(4))
    kcT = env.SA1a[:, 0:2048]
    ksT = env.SA1a[:, 2048:4096]
    vtm = env.SA1b[:, 0:2048]
    spq = SFbf[:, 0:2048]
    sq2 = SFbf[:, 2048:4096]
    ampq = SFbf[:, 4096:6144]
    ampk = SFbf[:, 6144:8192]
    atnA = SFbf[:, 0:1024]
    atnB = SFbf[:, 1024:2048]
    asc = env.rts

    for g in range(8):
        def ev_sc(dst, dstbuf, scale):
            def f(tc, ph):
                P.op("vector", lambda e, ph=ph, tc=tc: e.tensor_scalar(
                    out=dst[:, tc * 512:(tc + 1) * 512], in0=ph[:],
                    scalar1=scale, scalar2=None, op0=ALU.mult),
                    reads=[ph], writes=[dstbuf])
            return f

        _proj_fm(P, env, d["ab_Wq"][:, g * 128:(g + 1) * 128], env.wslot[0],
                 ev_sc(qT, env.SA0, 0.125))
        _proj_fm(P, env, d["ab_Wk"][:, g * 128:(g + 1) * 128], env.wslot[1],
                 ev_sc(kT, env.SA0, 1.0))
        P.dma("gpsimd", env.wslot[0][:].rearrange("p (k m) -> p k m", k=8),
              d["ab_Wv"][:, g * 128:(g + 1) * 128].rearrange("(k p) m -> p k m", p=128),
              writes=[env.wslot[0]], name="wvld")
        for tt in range(NT):
            pv = env.psB[4 + tt % 2]
            for k in range(ND):
                P.op("tensor", lambda e, pv=pv, k=k, tt=tt: e.matmul(
                    pv[:, 0:128], xnt(env, k, tt * 128, (tt + 1) * 128),
                    env.wslot[0][:, k * 128:(k + 1) * 128],
                    start=(k == 0), stop=(k == ND - 1)),
                    reads=[env.wslot[0], env.xnT], writes=[pv])
            P.op("scalar", lambda e, pv=pv, tt=tt: e.activation(
                out=vtm[:, tt * 128:(tt + 1) * 128], in_=pv[:, 0:128], func=AF.Copy),
                reads=[pv], writes=[env.SA1b])
        for (srcT, srcbuf, ampdst, sc8) in ((qT, env.SA0, ampq, 8.0),
                                            (kT, env.SA0, ampk, 1.0)):
            for hh in range(2):
                cl = slice(hh * 1024, (hh + 1) * 1024)
                P.op("scalar", lambda e, srcT=srcT, cl=cl, sc8=sc8: e.activation(
                    out=env.scF1[:], in_=srcT[:, cl], func=AF.Abs, scale=sc8),
                    reads=[srcbuf], writes=[env.scF1])
                P.op("scalar", lambda e: e.activation(
                    out=env.scF2[:], in_=env.scF1[:], func=AF.Exp, scale=-1.0),
                    reads=[env.scF1], writes=[env.scF2])
                P.op("scalar", lambda e: e.activation(
                    out=env.scF1[:], in_=env.scF2[:], func=AF.Ln,
                    bias=env.cst[:, 1:2]), reads=[env.scF2, env.cst],
                    writes=[env.scF1])
                P.op("vector", lambda e, srcT=srcT, cl=cl, sc8=sc8: e.tensor_scalar(
                    out=env.scF2[:], in0=srcT[:, cl], scalar1=sc8, scalar2=0.0,
                    op0=ALU.mult, op1=ALU.max), reads=[srcbuf], writes=[env.scF2])
                P.op("vector", lambda e, cl=cl: e.tensor_tensor(
                    out=spq[:, cl], in0=env.scF1[:], in1=env.scF2[:], op=ALU.add),
                    reads=[env.scF1, env.scF2], writes=[env.SF])
            P.op("scalar", lambda e: e.activation(
                out=sq2[:], in_=spq[:], func=AF.Square), reads=[env.SF],
                writes=[env.SF])
            for tc in range(4):
                pa = env.psB[tc % 2]
                P.op("tensor", lambda e, pa=pa, tc=tc: e.matmul(
                    pa[0:2, :], env.selH[:], sq2[:, tc * 512:(tc + 1) * 512],
                    start=True, stop=True), reads=[env.selH, env.SF], writes=[pa])
                P.op("scalar", lambda e, pa=pa: e.activation(
                    out=env.scF1[0:2, 0:512], in_=pa[0:2, :], func=AF.Ln,
                    scale=1.0 / 64.0, bias=env.cst[0:2, 2:3]),
                    reads=[pa, env.cst], writes=[env.scF1])
                P.op("scalar", lambda e: e.activation(
                    out=env.scF1[0:2, 512:1024], in_=env.scF1[0:2, 0:512],
                    func=AF.Exp, scale=-0.5), reads=[env.scF1], writes=[env.scF1])
                pr = env.psB[2 + tc % 2]
                P.op("tensor", lambda e, pr=pr: e.matmul(
                    pr[:], env.selHT_f[:], env.scF1[0:2, 512:1024],
                    start=True, stop=True),
                    reads=[env.selHT_f, env.scF1], writes=[pr])
                P.op("vector", lambda e, pr=pr, tc=tc, ampdst=ampdst: e.tensor_tensor(
                    out=ampdst[:, tc * 512:(tc + 1) * 512],
                    in0=spq[:, tc * 512:(tc + 1) * 512], in1=pr[:], op=ALU.mult),
                    reads=[env.SF, pr], writes=[env.SF])
        for (wnm, srcamp, cdst, sdst, dstbuf, uselam) in (
                ("ab_Wqp", ampq, qcT, qsT, env.SA0, True),
                ("ab_Wkp", ampk, kcT, ksT, env.SA1a, False)):
            def ev_ph(tc, ph, srcamp=srcamp, cdst=cdst, sdst=sdst,
                      dstbuf=dstbuf, uselam=uselam, g=g):
                cs = slice(tc * 512, (tc + 1) * 512)
                P.op("scalar", lambda e, ph=ph: e.activation(
                    out=env.scF1[:, 0:512], in_=ph[:], func=AF.Tanh),
                    reads=[ph], writes=[env.scF1])
                P.op("scalar", lambda e: e.activation(
                    out=env.scF2[:, 0:512], in_=env.scF1[:, 0:512], func=AF.Sin,
                    scale=math.pi), reads=[env.scF1], writes=[env.scF2])
                P.op("scalar", lambda e: e.activation(
                    out=env.scF1[:, 512:1024], in_=env.scF1[:, 0:512], func=AF.Sin,
                    scale=math.pi / 2), reads=[env.scF1], writes=[env.scF1])
                P.op("scalar", lambda e: e.activation(
                    out=env.scF2[:, 512:1024], in_=env.scF1[:, 512:1024],
                    func=AF.Square), reads=[env.scF1], writes=[env.scF2])
                P.op("vector", lambda e: e.tensor_scalar(
                    out=env.scF1[:, 512:1024], in0=env.scF2[:, 512:1024],
                    scalar1=-2.0, scalar2=1.0, op0=ALU.mult, op1=ALU.add),
                    reads=[env.scF2], writes=[env.scF1])
                if uselam:
                    P.op("vector", lambda e, cs=cs, g=g: e.scalar_tensor_tensor(
                        out=cdst[:, cs], in0=env.scF1[:, 512:1024],
                        scalar=env.lamrep[:, g:g + 1], in1=srcamp[:, cs],
                        op0=ALU.mult, op1=ALU.mult),
                        reads=[env.scF1, env.lamrep, env.SF], writes=[dstbuf])
                    P.op("vector", lambda e, cs=cs, g=g: e.scalar_tensor_tensor(
                        out=sdst[:, cs], in0=env.scF2[:, 0:512],
                        scalar=env.lamrep[:, g:g + 1], in1=srcamp[:, cs],
                        op0=ALU.mult, op1=ALU.mult),
                        reads=[env.scF2, env.lamrep, env.SF], writes=[dstbuf])
                else:
                    P.op("vector", lambda e, cs=cs: e.tensor_tensor(
                        out=cdst[:, cs], in0=env.scF1[:, 512:1024],
                        in1=srcamp[:, cs], op=ALU.mult),
                        reads=[env.scF1, env.SF], writes=[dstbuf])
                    P.op("vector", lambda e, cs=cs: e.tensor_tensor(
                        out=sdst[:, cs], in0=env.scF2[:, 0:512],
                        in1=srcamp[:, cs], op=ALU.mult),
                        reads=[env.scF2, env.SF], writes=[dstbuf])
            _proj_fm(P, env, d[wnm][:, g * 128:(g + 1) * 128],
                     env.wslot[0] if uselam else env.wslot[1], ev_ph)
        if stage < 2:
            continue
        parts = ((qT, kT), (qcT, kcT), (qsT, ksT))
        for ic in range(4):
            for hl in range(2):
                rows = slice(64 * hl, 64 * (hl + 1))
                for iw in range(4):
                    ii = ic * 4 + iw
                    jw = (ii + 1) * 128
                    nb = (jw + 511) // 512
                    for jc in range(nb):
                        n = min(512, jw - jc * 512)
                        pA = env.psB[jc]
                        for pi, (qa, kb) in enumerate(parts):
                            P.op("tensor", lambda e, pA=pA, qa=qa, kb=kb, pi=pi,
                                 ii=ii, jc=jc, n=n, rows=rows: e.matmul(
                                     pA[:, 0:n], qa[rows, ii * 128:(ii + 1) * 128],
                                     kb[rows, jc * 512:jc * 512 + n],
                                     start=(pi == 0), stop=(pi == 2)),
                                 reads=[env.SA0, env.SA1a], writes=[pA])
                        P.op("vector", lambda e, pA=pA, jc=jc, n=n: e.reduce_max(
                            asc[:, jc:jc + 1], pA[:, 0:n], axis=AX.X),
                            reads=[pA], writes=[asc])
                    for jc in range(1, nb):
                        P.op("vector", lambda e, jc=jc: e.tensor_tensor(
                            out=asc[:, 0:1], in0=asc[:, 0:1], in1=asc[:, jc:jc + 1],
                            op=ALU.max), reads=[asc], writes=[asc], name="mxcomb")
                    P.op("vector", lambda e: e.tensor_scalar(
                        out=asc[:, 5:6], in0=asc[:, 0:1], scalar1=-1.0, scalar2=None,
                        op0=ALU.mult), reads=[asc], writes=[asc])
                    for jc in range(nb):
                        n = min(512, jw - jc * 512)
                        pA = env.psB[jc]
                        full = n if jc < nb - 1 else n - 128
                        if full > 0:
                            P.op("scalar", lambda e, pA=pA, full=full, jc=jc:
                                 e.activation(
                                     out=atnA[:, 0:full], in_=pA[:, 0:full],
                                     func=AF.Exp, bias=asc[:, 5:6],
                                     accum_out=asc[:, 8 + jc:9 + jc]),
                                 reads=[pA, asc], writes=[env.SF, asc])
                        else:
                            P.op("gpsimd", lambda e, jc=jc: e.memset(
                                asc[:, 8 + jc:9 + jc], 0.0), writes=[asc])
                    pA = env.psB[nb - 1]
                    n = jw - (nb - 1) * 512
                    P.op("scalar", lambda e, pA=pA, n=n: e.activation(
                        out=atnB[:, 0:128], in_=pA[:, n - 128:n], func=AF.Exp,
                        bias=asc[:, 5:6]), reads=[pA, asc], writes=[env.SF])
                    P.op("gpsimd", lambda e: e.affine_select(
                        out=atnB[:, 0:128], in_=atnB[:, 0:128],
                        compare_op=ALU.is_ge, fill=0.0, base=0,
                        pattern=[[-1, 128]], channel_multiplier=1),
                        reads=[env.SF], writes=[env.SF])
                    P.op("vector", lambda e: e.reduce_sum(
                        asc[:, 12:13], atnB[:, 0:128], axis=AX.X),
                        reads=[env.SF], writes=[asc])
                    for jc in range(nb):
                        P.op("vector", lambda e, jc=jc: e.tensor_tensor(
                            out=asc[:, 12:13], in0=asc[:, 12:13],
                            in1=asc[:, 8 + jc:9 + jc], op=ALU.add),
                            reads=[asc], writes=[asc], name="sumcomb")
                    P.op("scalar", lambda e: e.activation(
                        out=asc[:, 13:14], in_=asc[:, 12:13], func=AF.Ln),
                        reads=[asc], writes=[asc])
                    P.op("vector", lambda e: e.tensor_tensor(
                        out=asc[:, 14:15], in0=asc[:, 13:14], in1=asc[:, 0:1],
                        op=ALU.add), reads=[asc], writes=[asc])
                    P.op("vector", lambda e, iw=iw: e.tensor_scalar(
                        out=asc[:, 16 + iw:17 + iw], in0=asc[:, 14:15],
                        scalar1=-1.0, scalar2=None, op0=ALU.mult),
                        reads=[asc], writes=[asc])
                ps6 = env.psB[6]
                for iw in range(4):
                    P.op("tensor", lambda e, ps6=ps6, iw=iw: e.transpose(
                        ps6[0:1, iw * 128:(iw + 1) * 128],
                        asc[:, 16 + iw:17 + iw], env.identf[:]),
                        reads=[asc, env.identf], writes=[ps6])
                P.op("vector", lambda e, ps6=ps6: e.tensor_copy(
                    out=env.srow[0:1, :], in_=ps6[0:1, 0:512]),
                    reads=[ps6], writes=[env.srow])
                pav = env.psB[4 + hl]
                njt = 4 * ic + 4
                for jt in range(njt):
                    di = max(0, (jt - 4 * ic)) * 128
                    i0 = ic * 512 + di
                    nn = 512 - di
                    pB = env.psB[jt % 4]
                    for pi, (qa, kb) in enumerate(parts):
                        P.op("tensor", lambda e, pB=pB, qa=qa, kb=kb, pi=pi, jt=jt,
                             i0=i0, nn=nn, rows=rows: e.matmul(
                                 pB[:, 0:nn], kb[rows, jt * 128:(jt + 1) * 128],
                                 qa[rows, i0:i0 + nn], start=(pi == 0), stop=False),
                             reads=[env.SA0, env.SA1a], writes=[pB])
                    P.op("tensor", lambda e, pB=pB, i0=i0, nn=nn, ic=ic: e.matmul(
                        pB[:, 0:nn], env.ones_f[0:1, :],
                        env.srow[0:1, i0 - ic * 512:i0 - ic * 512 + nn],
                        start=False, stop=True),
                        reads=[env.ones_f, env.srow], writes=[pB])
                    at = atnA if jt % 2 == 0 else atnB
                    P.op("scalar", lambda e, pB=pB, at=at, nn=nn: e.activation(
                        out=at[:, 0:nn], in_=pB[:, 0:nn], func=AF.Exp),
                        reads=[pB], writes=[env.SF])
                    if jt >= 4 * ic:
                        P.op("gpsimd", lambda e, at=at: e.affine_select(
                            out=at[:, 0:128], in_=at[:, 0:128],
                            compare_op=ALU.is_ge, fill=0.0, base=0,
                            pattern=[[1, 128]], channel_multiplier=-1),
                            reads=[env.SF], writes=[env.SF])
                    P.op("tensor", lambda e, pav=pav, at=at, jt=jt, di=di, nn=nn,
                         hl=hl: e.matmul(
                             pav[64 * hl:64 * hl + 64, di:di + nn],
                             vtm[:, jt * 128 + 64 * hl:jt * 128 + 64 * hl + 64],
                             at[:, 0:nn], start=(jt == 0 and di == 0),
                             stop=(jt == njt - 1)),
                         reads=[env.SA1b, env.SF], writes=[pav])
                if hl == 1:
                    for hx in range(2):
                        pvx = env.psB[4 + hx]
                        P.op("scalar", lambda e, pvx=pvx, ic=ic, g=g, hx=hx:
                             e.activation(
                                 out=env.SB[64 * hx:64 * (hx + 1),
                                            g * 2048 + ic * 512:g * 2048 + (ic + 1) * 512],
                                 in_=pvx[64 * hx:64 * (hx + 1), :], func=AF.Copy),
                             reads=[pvx], writes=[env.SB])
    if stage < 3:
        return
    for itc in range(4):
        for j in range(ND):
            ws = env.wslot[j % 2]
            P.dma("gpsimd", ws[:], d["ab_Wo"][j * 128:(j + 1) * 128, :],
                  writes=[ws], name="wold")
            for iw in range(4):
                it = itc * 4 + iw
                for h in range(2):
                    po = env.psB[iw * 2 + h]
                    P.op("tensor", lambda e, po=po, j=j, it=it, h=h, ws=ws: e.matmul(
                        po[:], env.SB[:, j * 2048 + it * 128:j * 2048 + (it + 1) * 128],
                        ws[:, h * 512:(h + 1) * 512],
                        start=(j == 0), stop=(j == ND - 1)),
                        reads=[env.SB, ws], writes=[po])
        for iw in range(4):
            it = itc * 4 + iw
            for h in range(2):
                po = env.psB[iw * 2 + h]
                P.op("vector", lambda e, po=po, h=h: e.tensor_copy(
                    out=env.scF1[:, h * 512:(h + 1) * 512], in_=po[:]),
                    reads=[po], writes=[env.scF1])
            xi = env.x[it]
            P.op("gpsimd", lambda e, xi=xi: e.tensor_tensor(
                out=xi[:], in0=xi[:], in1=env.scF1[:], op=ALU.add),
                reads=[xi, env.scF1], writes=[xi])




def spec_l1attnb():
    def t(key):
        return lambda i, key=key: np.ascontiguousarray(np.asarray(i[key]).T)

    return {
        "bb_W1qT": ((1024, 64), F32, t("L1_attn_b_W1_q")),
        "bb_W2qT": ((1024, 64), F32, t("L1_attn_b_W2_q")),
        "bb_W1kT": ((1024, 64), F32, t("L1_attn_b_W1_k")),
        "bb_W2kT": ((1024, 64), F32, t("L1_attn_b_W2_k")),
        "bb_WvT": ((1024, 1024), F32, t("L1_attn_b_W_v")),
        "bb_WoT": ((1024, 1024), F32, t("L1_attn_b_W_out")),
        "bb_bv": ((1, 1024), F32, lambda i: np.ascontiguousarray(
            np.asarray(i["L1_attn_b_b_v"]).reshape(1, 1024))),
        "bb_bout": ((1, 1024), F32, lambda i: np.ascontiguousarray(
            np.asarray(i["L1_attn_b_b_out"]).reshape(1, 1024))),
    }


_PAIRS = [(0, 1), (0, 2), (0, 3), (1, 2), (1, 3), (2, 3)]
_JMAP = [(5, 1.0), (4, -1.0), (3, 1.0), (2, 1.0), (1, -1.0), (0, 1.0)]


def build_l1attnb(P, env, stage=99):
    layernorm_to_xnT(P, env)
    d = env.dram
    SFbf = env.SF[:].bitcast(BF16)

    stg = [(env.SA1a[0:8, 0:2048], env.SA1a[0:8, 2048:4096], env.SA1a),
           (env.SA1b[0:8, 0:2048], env.SA1b[0:8, 2048:4096], env.SA1b)]
    for (sl, wa, wb) in ((env.wslot[0], "bb_W1qT", "bb_W2qT"),
                         (env.wslot[1], "bb_W1kT", "bb_W2kT")):
        for (off, w) in ((0, wa), (512, wb)):
            P.dma("gpsimd", sl[:, off:off + 512].rearrange("p (k m) -> p k m", k=8),
                  d[w][:, :].rearrange("(k p) m -> p k m", p=128), writes=[sl],
                  name="bbw")
    LqT = env.SA0[:, 0:2048]
    JkT = env.SA0[:, 2048:4096]
    for tt in range(NT):
        for (pi, (sl, off)) in enumerate(((env.wslot[0], 0), (env.wslot[0], 512),
                                          (env.wslot[1], 0), (env.wslot[1], 512))):
            pq = env.psB[pi]
            for k in range(ND):
                P.op("tensor", lambda e, pq=pq, sl=sl, off=off, k=k, tt=tt: e.matmul(
                    pq[:, 0:64], xnt(env, k, tt * 128, (tt + 1) * 128),
                    sl[:, off + k * 64:off + (k + 1) * 64],
                    start=(k == 0), stop=(k == ND - 1)),
                    reads=[sl, env.xnT], writes=[pq])
        for pi in range(4):
            P.op("vector", lambda e, pi=pi: e.tensor_copy(
                out=env.scF2[:, 512 + pi * 64:512 + (pi + 1) * 64],
                in_=env.psB[pi][:, 0:64]), reads=[env.psB[pi]], writes=[env.scF2])
        for (pa, pb_, dst0, side) in ((env.scF2[:, 512:576], env.scF2[:, 576:640], 0, "q"),
                                      (env.scF2[:, 640:704], env.scF2[:, 704:768], 128, "k")):
            for c, (i, j) in enumerate(_PAIRS):
                if side == "q":
                    dc, sign = c, 1.0
                else:
                    dc = next(m for m, (srcc, _s) in enumerate(_JMAP) if srcc == c)
                    sign = _JMAP[dc][1]
                m1 = env.scF2[:, 256:272]
                m2 = env.scF2[:, 272:288]
                P.op("vector", lambda e, pa=pa, pb_=pb_, i=i, j=j, m1=m1:
                     e.tensor_tensor(out=m1, in0=pa[:, i:64:4], in1=pb_[:, j:64:4],
                                     op=ALU.mult), reads=[env.scF2], writes=[env.scF2])
                P.op("vector", lambda e, pa=pa, pb_=pb_, i=i, j=j, m2=m2:
                     e.tensor_tensor(out=m2, in0=pa[:, j:64:4], in1=pb_[:, i:64:4],
                                     op=ALU.mult), reads=[env.scF2], writes=[env.scF2])
                a_, b_ = (m1, m2) if sign > 0 else (m2, m1)
                P.op("vector", lambda e, a_=a_, b_=b_, dst0=dst0, dc=dc:
                     e.tensor_tensor(out=env.scF1[:, dst0 + dc:dst0 + 128:8],
                                     in0=a_, in1=b_, op=ALU.subtract),
                     reads=[env.scF2], writes=[env.scF1])
            P.op("scalar", lambda e, dst0=dst0: e.activation(
                out=env.scF2[:, 0:128], in_=env.scF1[:, dst0:dst0 + 128],
                func=AF.Square), reads=[env.scF1], writes=[env.scF2])
            sv = env.scF2[:, 0:128].rearrange("p (h c) -> p h c", h=16)[:, :, 0:6]
            P.op("vector", lambda e, sv=sv: e.reduce_sum(
                env.rts[:, 0:16], sv, axis=AX.X), reads=[env.scF2], writes=[env.rts])
            P.op("vector", lambda e: e.tensor_scalar(
                out=env.rts[:, 16:32], in0=env.rts[:, 0:16], scalar1=1e-24,
                scalar2=None, op0=ALU.max), reads=[env.rts], writes=[env.rts])
            P.op("scalar", lambda e: e.activation(
                out=env.rts[:, 0:16], in_=env.rts[:, 16:32], func=AF.Ln),
                reads=[env.rts], writes=[env.rts])
            P.op("scalar", lambda e: e.activation(
                out=env.rts[:, 16:32], in_=env.rts[:, 0:16], func=AF.Exp,
                scale=-0.5), reads=[env.rts], writes=[env.rts])
            if side == "q":
                P.op("vector", lambda e: e.tensor_scalar(
                    out=env.rts[:, 16:32], in0=env.rts[:, 16:32],
                    scalar1=6.0 ** -0.5, scalar2=None, op0=ALU.mult),
                    reads=[env.rts], writes=[env.rts])
            for c in range(6):
                P.op("vector", lambda e, dst0=dst0, c=c: e.tensor_tensor(
                    out=SFbf[:, dst0 + c:dst0 + 128:8],
                    in0=env.scF1[:, dst0 + c:dst0 + 128:8], in1=env.rts[:, 16:32],
                    op=ALU.mult), reads=[env.scF1, env.rts], writes=[env.SF])
            for pc in (6, 7):
                P.op("vector", lambda e, dst0=dst0, pc=pc: e.memset(
                    SFbf[:, dst0 + pc:dst0 + 128:8], 0.0), writes=[env.SF])
        for (src0, dstT) in ((0, LqT), (128, JkT)):
            ptp = env.ps_tp[tt % 2]
            ptpb = ptp[:].bitcast(BF16)
            P.op("tensor", lambda e, src0=src0, ptpb=ptpb: e.transpose(
                ptpb[:, 0:128], SFbf[:, src0:src0 + 128], env.ident[:]),
                reads=[env.SF, env.ident], writes=[ptp])
            P.op("vector", lambda e, dstT=dstT, tt=tt, ptpb=ptpb: e.tensor_copy(
                out=dstT[:, tt * 128:(tt + 1) * 128], in_=ptpb[:, 0:128]),
                reads=[ptp], writes=[env.SA0])
    if stage < 2:
        return
    atnA = SFbf[:, 4096:5120]
    atnB = SFbf[:, 5120:6144]
    asc = env.rts
    for half in range(2):
        P.dma("sync", env.row_sc[0:1, 0:1024], d["bb_bv"], writes=[env.row_sc])
        P.dma("gpsimd", env.SA1a[:].rearrange("p (k m) -> p k m", k=8),
              d["bb_WvT"][:, half * 512:(half + 1) * 512].rearrange(
                  "(k p) m -> p k m", p=128), writes=[env.SA1a], name="wvld")
        for tt in range(NT):
            pv = env.psB[4 + tt % 2]
            for k in range(ND):
                P.op("tensor", lambda e, pv=pv, k=k, tt=tt: e.matmul(
                    pv[:], xnt(env, k, tt * 128, (tt + 1) * 128),
                    env.SA1a[:, k * 512:(k + 1) * 512],
                    start=(k == 0), stop=False), reads=[env.SA1a, env.xnT],
                    writes=[pv])
            P.op("tensor", lambda e, pv=pv, half=half: e.matmul(
                pv[:], env.ones_f[0:1, :],
                env.row_sc[0:1, half * 512:half * 512 + 512],
                start=False, stop=True), reads=[env.ones_f, env.row_sc], writes=[pv])
            P.op("scalar", lambda e, pv=pv, tt=tt: e.activation(
                out=env.SB[:, tt * 512:(tt + 1) * 512], in_=pv[:], func=AF.Copy),
                reads=[pv], writes=[env.SB])
        for gp in range(4):
            for hl0 in range(2):
                habs0 = 8 * half + 2 * gp + hl0
                P.dma("sync", stg[hl0][0], LqT[8 * habs0:8 * habs0 + 8, :],
                      reads=[env.SA0], writes=[stg[hl0][2]], name="stgq")
                P.dma("sync", stg[hl0][1], JkT[8 * habs0:8 * habs0 + 8, :],
                      reads=[env.SA0], writes=[stg[hl0][2]], name="stgk")
            for ic in range(4):
                for hl in range(2):
                    habs = 8 * half + 2 * gp + hl
                    stgQ, stgK, stgB = stg[hl]
                    vcol = (habs % 8) * 64
                    for iw in range(4):
                        ii = ic * 4 + iw
                        jw = (ii + 1) * 128
                        nb = (jw + 511) // 512
                        for jc in range(nb):
                            n = min(512, jw - jc * 512)
                            pA = env.psB[jc]
                            P.op("tensor", lambda e, pA=pA, ii=ii, jc=jc, n=n,
                                 stgQ=stgQ, stgK=stgK:
                                 e.matmul(pA[:, 0:n],
                                          stgQ[:, ii * 128:(ii + 1) * 128],
                                          stgK[:, jc * 512:jc * 512 + n],
                                          start=True, stop=True),
                                 reads=[stgB], writes=[pA])
                            P.op("vector", lambda e, pA=pA, jc=jc, n=n: e.reduce_max(
                                asc[:, jc:jc + 1], pA[:, 0:n], axis=AX.X),
                                reads=[pA], writes=[asc])
                        for jc in range(1, nb):
                            P.op("vector", lambda e, jc=jc: e.tensor_tensor(
                                out=asc[:, 0:1], in0=asc[:, 0:1],
                                in1=asc[:, jc:jc + 1], op=ALU.max),
                                reads=[asc], writes=[asc])
                        P.op("vector", lambda e: e.tensor_scalar(
                            out=asc[:, 5:6], in0=asc[:, 0:1], scalar1=-1.0,
                            scalar2=None, op0=ALU.mult), reads=[asc], writes=[asc])
                        for jc in range(nb):
                            n = min(512, jw - jc * 512)
                            pA = env.psB[jc]
                            full = n if jc < nb - 1 else n - 128
                            if full > 0:
                                P.op("scalar", lambda e, pA=pA, full=full, jc=jc:
                                     e.activation(
                                         out=atnA[:, 0:full], in_=pA[:, 0:full],
                                         func=AF.Exp, bias=asc[:, 5:6],
                                         accum_out=asc[:, 8 + jc:9 + jc]),
                                     reads=[pA, asc], writes=[env.SF, asc])
                            else:
                                P.op("gpsimd", lambda e, jc=jc: e.memset(
                                    asc[:, 8 + jc:9 + jc], 0.0), writes=[asc])
                        pA = env.psB[nb - 1]
                        n = jw - (nb - 1) * 512
                        P.op("scalar", lambda e, pA=pA, n=n: e.activation(
                            out=atnB[:, 0:128], in_=pA[:, n - 128:n], func=AF.Exp,
                            bias=asc[:, 5:6]), reads=[pA, asc], writes=[env.SF])
                        P.op("gpsimd", lambda e: e.affine_select(
                            out=atnB[:, 0:128], in_=atnB[:, 0:128],
                            compare_op=ALU.is_ge, fill=0.0, base=0,
                            pattern=[[-1, 128]], channel_multiplier=1),
                            reads=[env.SF], writes=[env.SF])
                        P.op("vector", lambda e: e.reduce_sum(
                            asc[:, 12:13], atnB[:, 0:128], axis=AX.X),
                            reads=[env.SF], writes=[asc])
                        for jc in range(nb):
                            P.op("vector", lambda e, jc=jc: e.tensor_tensor(
                                out=asc[:, 12:13], in0=asc[:, 12:13],
                                in1=asc[:, 8 + jc:9 + jc], op=ALU.add),
                                reads=[asc], writes=[asc])
                        P.op("scalar", lambda e: e.activation(
                            out=asc[:, 13:14], in_=asc[:, 12:13], func=AF.Ln),
                            reads=[asc], writes=[asc])
                        P.op("vector", lambda e: e.tensor_tensor(
                            out=asc[:, 14:15], in0=asc[:, 13:14], in1=asc[:, 0:1],
                            op=ALU.add), reads=[asc], writes=[asc])
                        P.op("vector", lambda e, iw=iw: e.tensor_scalar(
                            out=asc[:, 16 + iw:17 + iw], in0=asc[:, 14:15],
                            scalar1=-1.0, scalar2=None, op0=ALU.mult),
                            reads=[asc], writes=[asc])
                    ps6 = env.psB[6]
                    for iw in range(4):
                        P.op("tensor", lambda e, ps6=ps6, iw=iw: e.transpose(
                            ps6[0:1, iw * 128:(iw + 1) * 128],
                            asc[:, 16 + iw:17 + iw], env.identf[:]),
                            reads=[asc, env.identf], writes=[ps6])
                    P.op("vector", lambda e, ps6=ps6: e.tensor_copy(
                        out=env.srow[0:1, :], in_=ps6[0:1, 0:512]),
                        reads=[ps6], writes=[env.srow])
                    pav = env.psB[5 - hl]
                    njt = 4 * ic + 4
                    for jt in range(njt):
                        di = max(0, (jt - 4 * ic)) * 128
                        i0 = ic * 512 + di
                        nn = 512 - di
                        pB = env.psB[jt % 4]
                        P.op("tensor", lambda e, pB=pB, jt=jt, i0=i0, nn=nn,
                             stgQ=stgQ, stgK=stgK:
                             e.matmul(pB[:, 0:nn],
                                      stgK[:, jt * 128:(jt + 1) * 128],
                                      stgQ[:, i0:i0 + nn],
                                      start=True, stop=False),
                             reads=[stgB], writes=[pB])
                        P.op("tensor", lambda e, pB=pB, i0=i0, nn=nn, ic=ic:
                             e.matmul(pB[:, 0:nn], env.ones_f[0:1, :],
                                      env.srow[0:1, i0 - ic * 512:i0 - ic * 512 + nn],
                                      start=False, stop=True),
                             reads=[env.ones_f, env.srow], writes=[pB])
                        at = atnA if jt % 2 == 0 else atnB
                        P.op("scalar", lambda e, pB=pB, at=at, nn=nn: e.activation(
                            out=at[:, 0:nn], in_=pB[:, 0:nn], func=AF.Exp),
                            reads=[pB], writes=[env.SF])
                        if jt >= 4 * ic:
                            P.op("gpsimd", lambda e, at=at: e.affine_select(
                                out=at[:, 0:128], in_=at[:, 0:128],
                                compare_op=ALU.is_ge, fill=0.0, base=0,
                                pattern=[[1, 128]], channel_multiplier=-1),
                                reads=[env.SF], writes=[env.SF])
                        P.op("tensor", lambda e, pav=pav, at=at, jt=jt, di=di,
                             nn=nn, hl=hl, vcol=vcol: e.matmul(
                                 pav[64 * hl:64 * hl + 64, di:di + nn],
                                 env.SB[:, jt * 512 + vcol:jt * 512 + vcol + 64],
                                 at[:, 0:nn], start=(jt == 0 and di == 0),
                                 stop=(jt == njt - 1)),
                             reads=[env.SB, env.SF], writes=[pav])
                    if hl == 1:
                        for hx in range(2):
                            pvx = env.psB[5 - hx]
                            P.op("scalar", lambda e, pvx=pvx, ic=ic, gp=gp, hx=hx:
                                 e.activation(
                                     out=env.SB[64 * hx:64 * (hx + 1),
                                                8192 + gp * 2048 + ic * 512:
                                                8192 + gp * 2048 + (ic + 1) * 512],
                                     in_=pvx[64 * hx:64 * (hx + 1), :],
                                     func=AF.Copy),
                                 reads=[pvx], writes=[env.SB])
        if stage < 3:
            continue
        P.dma("sync", env.row_sc[0:1, 0:1024], d["bb_bout"], writes=[env.row_sc],
              name="boutld")
        for itc in range(4):
            for gp in range(4):
                ws = env.wslot[gp % 2]
                P.dma("gpsimd", ws[:],
                      d["bb_WoT"][half * 512 + gp * 128:half * 512 + (gp + 1) * 128, :],
                      writes=[ws], name="wood")
                for iw in range(4):
                    it = itc * 4 + iw
                    for h2 in range(2):
                        po = env.psB[iw * 2 + h2]
                        P.op("tensor", lambda e, po=po, gp=gp, it=it, h2=h2, ws=ws,
                             half=half: e.matmul(
                                 po[:],
                                 env.SB[:, 8192 + gp * 2048 + it * 128:
                                        8192 + gp * 2048 + (it + 1) * 128],
                                 ws[:, h2 * 512:(h2 + 1) * 512],
                                 start=(gp == 0), stop=(half == 1 and gp == 3)),
                             reads=[env.SB, ws], writes=[po])
            for iw in range(4):
                it = itc * 4 + iw
                for h2 in range(2):
                    po = env.psB[iw * 2 + h2]
                    if half == 0:
                        P.op("tensor", lambda e, po=po, h2=h2: e.matmul(
                            po[:], env.ones_f[0:1, :],
                            env.row_sc[0:1, h2 * 512:h2 * 512 + 512],
                            start=False, stop=True),
                            reads=[env.ones_f, env.row_sc], writes=[po])
                    P.op("vector", lambda e, po=po, h2=h2: e.tensor_copy(
                        out=env.scF1[:, h2 * 512:(h2 + 1) * 512], in_=po[:]),
                        reads=[po], writes=[env.scF1])
                xi = env.x[it]
                P.op("gpsimd", lambda e, xi=xi: e.tensor_tensor(
                    out=xi[:], in0=xi[:], in1=env.scF1[:], op=ALU.add),
                    reads=[xi, env.scF1], writes=[xi])



def spec_l1attna():
    def raw(key):
        return lambda i, key=key: np.ascontiguousarray(np.asarray(i[key]))

    return {
        "aa_Wqv": ((1024, 2048), F32, raw("L1_attn_a_W_qv")),
        "aa_W20": ((1024, 84), F32, raw("L1_attn_a_W2_0")),
        "aa_W21": ((1024, 164), F32, raw("L1_attn_a_W2_1")),
        "aa_Wout": ((1024, 1024), F32, raw("L1_attn_a_W_out")),
        "aa_sw": ((1, 2), F32, lambda i: np.ascontiguousarray(
            np.asarray(i["L1_attn_a_scale_weights"]).reshape(1, 2))),
    }


def build_l1attna(P, env, stage=99):
    layernorm_to_xnT(P, env)
    d = env.dram
    SFbf = env.SF[:].bitcast(BF16)
    scr = env.aa_scr
    scrbuf = env.aa_scrbuf
    P.dma("sync", env.lam_sb[0:1, 0:2], d["aa_sw"], writes=[env.lam_sb])
    P.op("vector", lambda e: e.tensor_tensor(
        out=env.lam_sb[0:1, 2:3], in0=env.lam_sb[0:1, 0:1],
        in1=env.lam_sb[0:1, 1:2], op=ALU.subtract),
        reads=[env.lam_sb], writes=[env.lam_sb])
    P.op("scalar", lambda e: e.activation(
        out=env.lam_sb[0:1, 3:4], in_=env.lam_sb[0:1, 2:3], func=AF.Sigmoid),
        reads=[env.lam_sb], writes=[env.lam_sb])
    P.op("vector", lambda e: e.tensor_scalar(
        out=env.lam_sb[0:1, 4:5], in0=env.lam_sb[0:1, 3:4], scalar1=-1.0,
        scalar2=1.0, op0=ALU.mult, op1=ALU.add),
        reads=[env.lam_sb], writes=[env.lam_sb])
    pb = env.psB[0]
    for j in range(2):
        P.op("tensor", lambda e, pb=pb, j=j: e.matmul(
            pb[:, j:j + 1], env.ones_f[0:1, :], env.lam_sb[0:1, 3 + j:4 + j],
            start=(j == 0), stop=(j == 1)), reads=[env.ones_f, env.lam_sb],
            writes=[pb])
    P.op("vector", lambda e, pb=pb: e.tensor_copy(
        out=env.lamrep[:, 6:8], in_=pb[:, 0:2]), reads=[pb], writes=[env.lamrep])
    P.op("gpsimd", lambda e: e.memset(SFbf[:, 0:512], 0.0), writes=[env.SF])
    for zz in range(2):
        P.dma("scalar", scr[zz * 136:zz * 136 + 128, :], SFbf[:, 0:512],
              reads=[env.SF], writes=[scrbuf], name="scrz0")
        P.dma("scalar", scr[zz * 136 + 128:zz * 136 + 136, :], SFbf[0:8, 0:512],
              reads=[env.SF], writes=[scrbuf], name="scrz1")

    for hh in range(2):
        buf = env.SA1a if hh == 0 else env.SA1b
        P.dma("gpsimd", buf[:].rearrange("p (k m) -> p k m", k=4),
              d["aa_Wqv"][hh * 512:(hh + 1) * 512, 1024:2048].rearrange(
                  "(k p) m -> p k m", p=128), writes=[buf], name="wvld")
    for tt in range(NT):
        for h2 in range(2):
            pv = env.psB[4 + h2]
            for k in range(ND):
                buf = env.SA1a if k < 4 else env.SA1b
                P.op("tensor", lambda e, pv=pv, k=k, tt=tt, h2=h2, buf=buf: e.matmul(
                    pv[:], xnt(env, k, tt * 128, (tt + 1) * 128),
                    buf[:, (k % 4) * 1024 + h2 * 512:(k % 4) * 1024 + h2 * 512 + 512],
                    start=(k == 0), stop=(k == ND - 1)),
                    reads=[buf, env.xnT], writes=[pv])
            P.op("scalar", lambda e, pv=pv, tt=tt, h2=h2: e.activation(
                out=env.SB[:, tt * 1024 + h2 * 512:tt * 1024 + h2 * 512 + 512],
                in_=pv[:], func=AF.Copy), reads=[pv], writes=[env.SB])
    qr = ([env.SA0[:, m * 2048:(m + 1) * 2048] for m in range(4)]
          + [env.SA1a[:, m * 2048:(m + 1) * 2048] for m in range(2)]
          + [env.SA1b[:, m * 2048:(m + 1) * 2048] for m in range(2)])
    qrb = [env.SA0] * 4 + [env.SA1a] * 2 + [env.SA1b] * 2
    for m in range(8):
        ws = env.wslot[m % 2]
        P.dma("gpsimd", ws[:].rearrange("p (k f) -> p k f", k=8),
              d["aa_Wqv"][:, m * 128:(m + 1) * 128].rearrange(
                  "(k p) f -> p k f", p=128), writes=[ws], name="wqld")
        for tc in range(4):
            ph = env.psB[tc % 2]
            for k in range(ND):
                P.op("tensor", lambda e, ph=ph, ws=ws, k=k, tc=tc: e.matmul(
                    ph[:], ws[:, k * 128:(k + 1) * 128],
                    xnt(env, k, tc * 512, (tc + 1) * 512),
                    start=(k == 0), stop=(k == ND - 1)),
                    reads=[ws, env.xnT], writes=[ph])
            P.op("scalar", lambda e, ph=ph, m=m, tc=tc: e.activation(
                out=qr[m][:, tc * 512:(tc + 1) * 512], in_=ph[:], func=AF.Relu),
                reads=[ph], writes=[qrb[m]])
    if stage < 2:
        return
    W2sb = {0: SFbf[:, 1040:1712], 1: SFbf[:, 1712:3024]}
    for (ci, wnm, wc) in ((0, "aa_W20", 84), (1, "aa_W21", 164)):
        for m in range(8):
            P.dma("gpsimd", W2sb[ci][:, m * wc:(m + 1) * wc],
                  d[wnm][m * 128:(m + 1) * 128, :], writes=[env.SF], name="w2ld")
    attnw = SFbf[:, 0:656]
    band = SFbf[:, 656:1040]
    bandT = SFbf[:, 7120:7504]
    CSZ = ((0, 21, 10), (1, 41, 20))
    for itc in range(4):
        for iw in range(4):
            tt = itc * 4 + iw
            first = {4: True, 5: True}
            last_of = {}
            for (ci, c, half) in CSZ:
                for h in range(4):
                    for jls in range(3):
                        jt = tt - 1 + jls
                        if 0 <= jt < NT:
                            for m2 in range(2):
                                dti = 2 * h + m2
                                last_of[4 + dti // 4] = (ci, h, jls, m2)
            for (ci, c, half) in CSZ:
                pl = env.psB[2 + ci]
                wc = 84 if ci == 0 else 164
                for m in range(8):
                    P.op("tensor", lambda e, pl=pl, m=m, tt=tt, ci=ci, wc=wc:
                         e.matmul(pl[:, 0:wc],
                                  qr[m][:, tt * 128:(tt + 1) * 128],
                                  W2sb[ci][:, m * wc:(m + 1) * wc],
                                  start=(m == 0), stop=(m == ND - 1)),
                         reads=[qrb[m], env.SF], writes=[pl])
                P.op("vector", lambda e, pl=pl, wc=wc: e.tensor_copy(
                    out=env.scF1[:, 0:wc], in_=pl[:, 0:wc]), reads=[pl],
                    writes=[env.scF1])
                for h in range(4):
                    cs = slice(h * c, (h + 1) * c)
                    P.op("vector", lambda e, cs=cs: e.reduce_max(
                        env.rts[:, 0:1], env.scF1[:, cs], axis=AX.X),
                        reads=[env.scF1], writes=[env.rts])
                    P.op("vector", lambda e: e.tensor_scalar(
                        out=env.rts[:, 1:2], in0=env.rts[:, 0:1], scalar1=-1.0,
                        scalar2=None, op0=ALU.mult), reads=[env.rts],
                        writes=[env.rts])
                    P.op("scalar", lambda e, cs=cs, c=c: e.activation(
                        out=env.scF2[:, 0:c], in_=env.scF1[:, cs], func=AF.Exp,
                        bias=env.rts[:, 1:2], accum_out=env.rts[:, 2:3]),
                        reads=[env.scF1, env.rts], writes=[env.scF2, env.rts])
                    P.op("vector", lambda e: e.reciprocal(
                        out=env.rts[:, 3:4], in_=env.rts[:, 2:3]),
                        reads=[env.rts], writes=[env.rts])
                    P.op("vector", lambda e, cs=cs, c=c, ci=ci: e.tensor_scalar(
                        out=attnw[:, cs], in0=env.scF2[:, 0:c],
                        scalar1=env.rts[:, 3:4], scalar2=env.lamrep[:, 6 + ci:7 + ci],
                        op0=ALU.mult, op1=ALU.mult),
                        reads=[env.scF2, env.rts, env.lamrep], writes=[env.SF])
                for h in range(4):
                    P.dma("scalar", bass.AP(
                        tensor=scr.tensor, offset=(ci * 136 + 1) * 512,
                        ap=[[512, 128], [1, c]]),
                        attnw[:, h * c:(h + 1) * c],
                        reads=[env.SF], writes=[scrbuf], name="bandout")
                    P.dma("scalar", band,
                          bass.AP(tensor=scr.tensor,
                                  offset=ci * 136 * 512 + 384 + half,
                                  ap=[[511, 128], [1, 384]]),
                          reads=[scrbuf], writes=[env.SF], name="bandin")
                    for jls in range(3):
                        ptp = env.psB[2 + jls % 2]
                        ptpb = ptp[:].bitcast(BF16)
                        P.op("tensor", lambda e, ptpb=ptpb, jls=jls: e.transpose(
                            ptpb[:, 0:128], band[:, jls * 128:(jls + 1) * 128],
                            env.ident[:]), reads=[env.SF, env.ident],
                            writes=[env.psB[2 + jls % 2]])
                        P.op("vector", lambda e, ptpb=ptpb, jls=jls: e.tensor_copy(
                            out=bandT[:, jls * 128:(jls + 1) * 128],
                            in_=ptpb[:, 0:128]),
                            reads=[env.psB[2 + jls % 2]], writes=[env.SF])
                        jt = tt - 1 + jls
                        if not (0 <= jt < NT):
                            continue
                        for m2 in range(2):
                            dti = 2 * h + m2
                            bank = env.psB[4 + dti // 4]
                            col = (dti % 4) * 128
                            st_ = first[4 + dti // 4]
                            first[4 + dti // 4] = False
                            sp_ = (last_of[4 + dti // 4] == (ci, h, jls, m2))
                            P.op("tensor", lambda e, bank=bank, col=col, jt=jt,
                                 h=h, m2=m2, jls=jls, st_=st_, sp_=sp_: e.matmul(
                                     bank[:, col:col + 128],
                                     env.SB[:, jt * 1024 + h * 256 + m2 * 128:
                                            jt * 1024 + h * 256 + m2 * 128 + 128],
                                     bandT[:, jls * 128:(jls + 1) * 128],
                                     start=st_, stop=sp_),
                                 reads=[env.SB, env.SF], writes=[bank])
            for b2 in range(2):
                P.op("vector", lambda e, b2=b2, iw=iw: e.tensor_copy(
                    out=SFbf[:, 3024 + iw * 1024 + b2 * 512:
                             3024 + iw * 1024 + b2 * 512 + 512],
                    in_=env.psB[4 + b2][:]), reads=[env.psB[4 + b2]],
                    writes=[env.SF])
        for k in range(ND):
            ws = env.wslot[k % 2]
            P.dma("gpsimd", ws[:], d["aa_Wout"][k * 128:(k + 1) * 128, :],
                  writes=[ws], name="wold")
            for iw in range(4):
                for h2 in range(2):
                    po = env.psB[iw * 2 + h2]
                    P.op("tensor", lambda e, po=po, k=k, iw=iw, h2=h2, ws=ws:
                         e.matmul(po[:],
                                  SFbf[:, 3024 + iw * 1024 + k * 128:
                                       3024 + iw * 1024 + (k + 1) * 128],
                                  ws[:, h2 * 512:(h2 + 1) * 512],
                                  start=(k == 0), stop=(k == ND - 1)),
                         reads=[env.SF, ws], writes=[po])
        for iw in range(4):
            it = itc * 4 + iw
            for h2 in range(2):
                po = env.psB[iw * 2 + h2]
                P.op("vector", lambda e, po=po, h2=h2: e.tensor_copy(
                    out=env.scF1[:, h2 * 512:(h2 + 1) * 512], in_=po[:]),
                    reads=[po], writes=[env.scF1])
            xi = env.x[it]
            P.op("gpsimd", lambda e, xi=xi: e.tensor_tensor(
                out=xi[:], in0=xi[:], in1=env.scF1[:], op=ALU.add),
                reads=[xi, env.scF1], writes=[xi])



MTC = 256
MNC = T_SEQ // MTC


def spec_l0attna():
    def f(key, fn):
        return lambda i, key=key, fn=fn: np.ascontiguousarray(fn(np.asarray(i[key])))

    def tile16(a):
        return a.reshape(16, 128).T

    return {
        "ma_inpT": ((1024, 4096), F32, f("L0_attn_a_h_in_proj_w", lambda a: a.T)),
        "ma_convw": ((128, 64), F32, f("L0_attn_a_h_conv_w", lambda a:
            a.reshape(16, 128, 4).transpose(1, 0, 2).reshape(128, 64))),
        "ma_convb": ((128, 16), F32, f("L0_attn_a_h_conv_b", tile16)),
        "ma_dtb": ((128, 16), F32, f("L0_attn_a_h_dt_proj_b", tile16)),
        "ma_D": ((128, 16), F32, f("L0_attn_a_h_D_param", tile16)),
        "ma_xpwT": ((2048, 96), F32, f("L0_attn_a_h_x_proj_w", lambda a: a.T)),
        "ma_dtwT": ((64, 2048), F32, f("L0_attn_a_h_dt_proj_w", lambda a: a.T)),
        "ma_AlogT": ((128, 256), F32, f("L0_attn_a_h_A_log", lambda a:
            a.reshape(256, 8, 16).transpose(1, 2, 0).reshape(128, 256))),
        "ma_opwT": ((2048, 1024), F32, f("L0_attn_a_h_out_proj_w", lambda a: a.T)),
    }


def build_l0attna(P, env, stage=99):
    layernorm_to_xnT(P, env)
    d = env.dram
    SFbf = env.SF[:].bitcast(BF16)
    S1bF = env.SA1b[:].bitcast(F32)
    A_sb = S1bF[:, 0:256]
    hcar = S1bF[:, 256:512]
    halo = env.SA1b[:, 1024:1072]
    selD = env.SA0[:, 4096:6144]
    selR = env.SA0[:, 6144:8192]
    if env.once("masmall"):
        env.ma_cw = P.sb("s_ma_cw", [128, 64], F32)
        env.ma_cb = P.sb("s_ma_cb", [128, 16], F32)
        env.ma_dtb = P.sb("s_ma_dtb", [128, 16], F32)
        env.ma_Dp = P.sb("s_ma_Dp", [128, 16], F32)
        env.ma_selB = P.sb("s_ma_selB", [32, 256], BF16)
    P.dma("sync", env.ma_cw[:], d["ma_convw"], writes=[env.ma_cw])
    P.dma("sync", env.ma_cb[:], d["ma_convb"], writes=[env.ma_cb])
    P.dma("sync", env.ma_dtb[:], d["ma_dtb"], writes=[env.ma_dtb])
    P.dma("sync", env.ma_Dp[:], d["ma_D"], writes=[env.ma_Dp])
    P.dma("sync", A_sb, d["ma_AlogT"], writes=[env.SA1b])
    P.op("scalar", lambda e: e.activation(out=A_sb, in_=A_sb, func=AF.Exp),
         reads=[env.SA1b], writes=[env.SA1b])
    P.op("vector", lambda e: e.tensor_scalar(
        out=A_sb, in0=A_sb, scalar1=-1.0, scalar2=None, op0=ALU.mult),
        reads=[env.SA1b], writes=[env.SA1b])
    P.op("gpsimd", lambda e: e.memset(hcar, 0.0), writes=[env.SA1b])
    P.op("gpsimd", lambda e: e.memset(halo, 0.0), writes=[env.SA1b])
    P.op("gpsimd", lambda e: e.memset(env.ma_selB[:], 0.0), writes=[env.ma_selB])
    for which in range(2):
        P.op("gpsimd", lambda e, which=which: e.affine_select(
            out=env.ma_selB[:, which * 128:(which + 1) * 128].rearrange(
                "s (dlo n) -> s dlo n", n=16),
            in_=env.ma_selB[:, which * 128:(which + 1) * 128].rearrange(
                "s (dlo n) -> s dlo n", n=16),
            compare_op=ALU.not_equal, fill=1.0, base=-16 * which,
            pattern=[[0, 8], [-1, 16]], channel_multiplier=1),
            reads=[env.ma_selB], writes=[env.ma_selB])
    for r8 in range(16):
        sl = selD[:, r8 * 128:(r8 + 1) * 128]
        P.op("gpsimd", lambda e, sl=sl: e.memset(sl, 0.0), writes=[env.SA0])
        P.op("gpsimd", lambda e, sl=sl, r8=r8: e.affine_select(
            out=sl.rearrange("s (dlo n) -> s dlo n", n=16),
            in_=sl.rearrange("s (dlo n) -> s dlo n", n=16),
            compare_op=ALU.not_equal, fill=1.0, base=-8 * r8,
            pattern=[[-1, 8], [0, 16]], channel_multiplier=1),
            reads=[env.SA0], writes=[env.SA0])
        sr = selR[:, r8 * 128:(r8 + 1) * 128]
        P.op("gpsimd", lambda e, sr=sr: e.memset(sr, 1.0), writes=[env.SA0])
        P.op("gpsimd", lambda e, sr=sr, r8=r8: e.affine_select(
            out=sr, in_=sr, compare_op=ALU.is_ge, fill=0.0, base=16 * 8 * r8,
            pattern=[[-16, 128]], channel_multiplier=1),
            reads=[env.SA0], writes=[env.SA0])
        P.op("gpsimd", lambda e, sr=sr, r8=r8: e.affine_select(
            out=sr, in_=sr, compare_op=ALU.is_ge, fill=0.0,
            base=15 - 16 * 8 * r8, pattern=[[16, 128]], channel_multiplier=-1),
            reads=[env.SA0], writes=[env.SA0])
    if stage < 1:
        return

    xin = [env.SA0[:, dt * MTC:(dt + 1) * MTC] for dt in range(16)]
    zt = [env.SA1a[:, dt * MTC:(dt + 1) * MTC] for dt in range(16)]
    ut = [env.SB[:, dt * MTC:(dt + 1) * MTC] for dt in range(16)]
    yt = [env.SB[:, 4096 + dt * MTC:4096 + (dt + 1) * MTC] for dt in range(16)]
    dl = [SFbf[:, dt * MTC:(dt + 1) * MTC] for dt in range(16)]
    xdbl = env.wslot[1][0:96, 0:MTC]
    brep = env.wslot[1][:, 256:512]
    crep = env.wslot[0][:, 512:768]
    dtw_t = [env.wslot[1][0:64, 512 + j * 128:512 + (j + 1) * 128] for j in range(2)]
    xpw_t = [env.wslot[0][:, 768 + j * 128:768 + (j + 1) * 128] for j in range(2)]
    du = env.wslot[0][:, 0:256]
    slz = env.wslot[0][:, 256:512]
    uacc = env.scF1[:, 0:256]
    a_f = env.scF1[:, 256:512]
    b_f = env.scF1[:, 512:768]
    h_f = env.scF2[:, 0:256]
    hC = env.scF2[:].bitcast(BF16)[:, 512:768]
    S2b = env.scF2

    for c in range(MNC):
        t0 = c * MTC
        for et in range(32):
            ws = env.wslot[et % 2]
            P.dma("gpsimd", ws[:].rearrange("p (k m) -> p k m", k=8),
                  d["ma_inpT"][:, et * 128:(et + 1) * 128].rearrange(
                      "(k p) m -> p k m", p=128), writes=[ws], name="mainp")
            ph = env.psB[et % 2]
            for k in range(ND):
                P.op("tensor", lambda e, ph=ph, ws=ws, k=k, t0=t0: e.matmul(
                    ph[:, 0:MTC], ws[:, k * 128:(k + 1) * 128],
                    xnt(env, k, t0, t0 + MTC),
                    start=(k == 0), stop=(k == ND - 1)),
                    reads=[ws, env.xnT], writes=[ph])
            dst = xin[et] if et < 16 else zt[et - 16]
            dstb = env.SA0 if et < 16 else env.SA1a
            P.op("scalar", lambda e, ph=ph, dst=dst: e.activation(
                out=dst, in_=ph[:, 0:MTC], func=AF.Copy),
                reads=[ph], writes=[dstb])
        if stage < 2:
            continue
        uA = env.scF1[:, 0:256]
        uB = env.scF1[:, 256:512]
        for dt in range(16):
            P.op("vector", lambda e, dt=dt, uA=uA: e.tensor_scalar(
                out=uA, in0=xin[dt], scalar1=env.ma_cw[:, dt * 4 + 3:dt * 4 + 4],
                scalar2=None, op0=ALU.mult),
                reads=[env.SA0, env.ma_cw], writes=[env.scF1])
            cur, nxt = uA, uB
            for j in (2, 1, 0):
                s = 3 - j
                P.op("vector", lambda e, dt=dt, j=j, s=s, cur=cur, nxt=nxt:
                     e.scalar_tensor_tensor(
                         out=nxt[:, s:MTC], in0=xin[dt][:, 0:MTC - s],
                         scalar=env.ma_cw[:, dt * 4 + j:dt * 4 + j + 1],
                         in1=cur[:, s:MTC], op0=ALU.mult, op1=ALU.add),
                     reads=[env.SA0, env.ma_cw, env.scF1], writes=[env.scF1])
                P.op("vector", lambda e, dt=dt, j=j, s=s, cur=cur, nxt=nxt:
                     e.scalar_tensor_tensor(
                         out=nxt[:, 0:s], in0=halo[:, dt * 3 + 3 - s:dt * 3 + 3],
                         scalar=env.ma_cw[:, dt * 4 + j:dt * 4 + j + 1],
                         in1=cur[:, 0:s], op0=ALU.mult, op1=ALU.add),
                     reads=[env.SA1b, env.ma_cw, env.scF1], writes=[env.scF1])
                cur, nxt = nxt, cur
            P.op("scalar", lambda e, dt=dt, cur=cur: e.activation(
                out=ut[dt], in_=cur, func=AF.Silu,
                bias=env.ma_cb[:, dt:dt + 1]),
                reads=[env.scF1, env.ma_cb], writes=[env.SB])
        for dt in range(16):
            P.op("gpsimd", lambda e, dt=dt: e.tensor_copy(
                out=halo[:, dt * 3:dt * 3 + 3], in_=xin[dt][:, MTC - 3:MTC]),
                reads=[env.SA0], writes=[env.SA1b])
        if stage < 3:
            continue
        px = env.psB[2]
        for dt in range(16):
            xp = xpw_t[dt % 2]
            P.dma("gpsimd", xp[:, 0:96], d["ma_xpwT"][dt * 128:(dt + 1) * 128, :],
                  writes=[env.wslot[0]], name="maxpw")
            P.op("tensor", lambda e, px=px, xp=xp, dt=dt: e.matmul(
                px[0:96, 0:MTC], xp[:, 0:96], ut[dt],
                start=(dt == 0), stop=(dt == 15)),
                reads=[env.wslot[0], env.SB], writes=[px])
        P.op("scalar", lambda e, px=px: e.activation(
            out=env.wslot[1][0:64, 0:MTC], in_=px[0:64, 0:MTC], func=AF.Copy),
            reads=[px], writes=[env.wslot[1]])
        BC_sb = env.wslot[1][0:32, 768:1024]
        P.op("scalar", lambda e, px=px, BC_sb=BC_sb: e.activation(
            out=BC_sb, in_=px[64:96, 0:MTC], func=AF.Copy),
            reads=[px], writes=[env.wslot[1]])
        for (rep, which) in ((brep, 0), (crep, 1)):
            pr = env.psB[3]
            P.op("tensor", lambda e, pr=pr, which=which, BC_sb=BC_sb: e.matmul(
                pr[:, 0:MTC], env.ma_selB[:, which * 128:(which + 1) * 128],
                BC_sb, start=True, stop=True),
                reads=[env.ma_selB, env.wslot[1]], writes=[pr])
            P.op("vector", lambda e, pr=pr, rep=rep: e.tensor_copy(
                out=rep, in_=pr[:, 0:MTC]), reads=[pr],
                writes=[env.wslot[1] if rep is brep else env.wslot[0]])
        for dt in range(16):
            dwt = dtw_t[dt % 2]
            P.dma("gpsimd", dwt, d["ma_dtwT"][:, dt * 128:(dt + 1) * 128],
                  writes=[env.wslot[1]], name="madtw")
            pd = env.psB[3]
            P.op("tensor", lambda e, pd=pd, dwt=dwt: e.matmul(
                pd[:, 0:MTC], dwt, env.wslot[1][0:64, 0:MTC],
                start=True, stop=True),
                reads=[env.wslot[1]], writes=[pd])
            P.op("scalar", lambda e, pd=pd, dt=dt: e.activation(
                out=S2b[:, 256:512], in_=pd[:, 0:MTC], func=AF.Abs,
                bias=env.ma_dtb[:, dt:dt + 1]),
                reads=[pd, env.ma_dtb], writes=[env.scF2])
            P.op("scalar", lambda e: e.activation(
                out=S2b[:, 512:768], in_=S2b[:, 256:512], func=AF.Exp, scale=-1.0),
                reads=[env.scF2], writes=[env.scF2])
            P.op("scalar", lambda e: e.activation(
                out=S2b[:, 256:512], in_=S2b[:, 512:768], func=AF.Ln,
                bias=env.cst[:, 1:2]), reads=[env.scF2, env.cst], writes=[env.scF2])
            P.op("vector", lambda e, pd=pd, dt=dt: e.tensor_scalar(
                out=S2b[:, 512:768], in0=pd[:, 0:MTC],
                scalar1=env.ma_dtb[:, dt:dt + 1], scalar2=0.0,
                op0=ALU.add, op1=ALU.max),
                reads=[pd, env.ma_dtb], writes=[env.scF2])
            P.op("vector", lambda e, dt=dt: e.tensor_tensor(
                out=dl[dt], in0=S2b[:, 256:512], in1=S2b[:, 512:768], op=ALU.add),
                reads=[env.scF2], writes=[env.SF])
        if stage < 4:
            continue
        for dt in range(16):
            P.op("vector", lambda e, dt=dt: e.tensor_tensor(
                out=du, in0=dl[dt], in1=ut[dt], op=ALU.mult),
                reads=[env.SF, env.SB], writes=[env.wslot[0]])
            yps = env.psB[2]
            for gg in range(16):
                g = dt * 16 + gg
                pr1 = env.psB[4 + 2 * (gg % 2)]
                pr2 = env.psB[5 + 2 * (gg % 2)]
                sl = selD[:, (g % 16) * 128:(g % 16 + 1) * 128]
                P.op("tensor", lambda e, pr1=pr1, sl=sl, dt=dt: e.matmul(
                    pr1[:, 0:MTC], sl, dl[dt], start=True, stop=True),
                    reads=[env.SA0, env.SF], writes=[pr1])
                P.op("tensor", lambda e, pr2=pr2, sl=sl: e.matmul(
                    pr2[:, 0:MTC], sl, du, start=True, stop=True),
                    reads=[env.SA0, env.wslot[0]], writes=[pr2])
                P.op("scalar", lambda e, pr1=pr1, g=g: e.activation(
                    out=a_f, in_=pr1[:, 0:MTC], func=AF.Exp,
                    scale=A_sb[:, g:g + 1]),
                    reads=[pr1, env.SA1b], writes=[env.scF1])
                P.op("vector", lambda e, pr2=pr2: e.tensor_tensor(
                    out=b_f, in0=pr2[:, 0:MTC], in1=brep, op=ALU.mult),
                    reads=[pr2, env.wslot[1]], writes=[env.scF1])
                P.op("vector", lambda e, g=g: e.tensor_tensor_scan(
                    out=h_f, data0=a_f, data1=b_f, initial=hcar[:, g:g + 1],
                    op0=ALU.mult, op1=ALU.add),
                    reads=[env.scF1, env.SA1b], writes=[env.scF2])
                P.op("vector", lambda e, g=g: e.tensor_copy(
                    out=hcar[:, g:g + 1], in_=h_f[:, MTC - 1:MTC]),
                    reads=[env.scF2], writes=[env.SA1b])
                P.op("gpsimd", lambda e: e.tensor_tensor(
                    out=hC, in0=h_f, in1=crep, op=ALU.mult),
                    reads=[env.scF2, env.wslot[0]], writes=[env.scF2])
                P.op("tensor", lambda e, yps=yps, gg=gg: e.matmul(
                    yps[:, 0:MTC], selR[:, gg * 128:(gg + 1) * 128], hC,
                    start=(gg == 0), stop=(gg == 15)),
                    reads=[env.SA0, env.scF2], writes=[yps])
            P.op("scalar", lambda e, dt=dt: e.activation(
                out=slz, in_=zt[dt], func=AF.Silu),
                reads=[env.SA1a], writes=[env.wslot[0]])
            P.op("vector", lambda e, yps=yps, dt=dt: e.scalar_tensor_tensor(
                out=a_f, in0=ut[dt], scalar=env.ma_Dp[:, dt:dt + 1], in1=yps[:, 0:MTC],
                op0=ALU.mult, op1=ALU.add),
                reads=[env.SB, env.ma_Dp, yps], writes=[env.scF1])
            P.op("vector", lambda e, dt=dt: e.tensor_tensor(
                out=yt[dt], in0=a_f, in1=slz, op=ALU.mult),
                reads=[env.scF1, env.wslot[0]], writes=[env.SB])
        if stage < 5:
            continue
        for k in range(16):
            ws = env.wslot[k % 2]
            P.dma("gpsimd", ws[:], d["ma_opwT"][k * 128:(k + 1) * 128, :],
                  writes=[ws], name="maopw")
            for tt2 in range(2):
                for h2 in range(2):
                    po = env.psB[tt2 * 2 + h2]
                    P.op("tensor", lambda e, po=po, ws=ws, k=k, tt2=tt2, h2=h2:
                         e.matmul(po[:], yt[k][:, tt2 * 128:(tt2 + 1) * 128],
                                  ws[:, h2 * 512:(h2 + 1) * 512],
                                  start=(k == 0), stop=(k == 15)),
                         reads=[env.SB, ws], writes=[po])
        for tt2 in range(2):
            it = c * 2 + tt2
            for h2 in range(2):
                po = env.psB[tt2 * 2 + h2]
                P.op("vector", lambda e, po=po, h2=h2: e.tensor_copy(
                    out=env.scF1[:, h2 * 512:(h2 + 1) * 512], in_=po[:]),
                    reads=[po], writes=[env.scF1])
            xi = env.x[it]
            P.op("gpsimd", lambda e, xi=xi: e.tensor_tensor(
                out=xi[:], in0=xi[:], in1=env.scF1[:], op=ALU.add),
                reads=[xi, env.scF1], writes=[xi])



def spec_none():
    return {}


def build_lnonly(P, env):
    layernorm_to_xnT(P, env)
    for i in range(NT):
        tm = env.xn_tm[0]
        pstp = env.ps_tp[0]
        pstp_bf = pstp[:].bitcast(BF16)
        for k in range(ND):
            src_ap = env.xnT[:, k * T_SEQ + i * 128:k * T_SEQ + (i + 1) * 128]
            P.op("tensor", lambda e, src_ap=src_ap, pstp_bf=pstp_bf, k=k: e.transpose(
                pstp_bf[:, k * 128:(k + 1) * 128], src_ap, env.ident[:]),
                reads=[env.xnT, env.ident], writes=[pstp])
        xi = env.x[i]
        P.op("vector", lambda e, xi=xi, pstp_bf=pstp_bf: e.tensor_copy(
            out=xi[:], in_=pstp_bf), reads=[pstp], writes=[xi])


SUBLAYERS = {
    0: ("l0attna", spec_l0attna, build_l0attna),
    21: ("maa_s1", spec_l0attna, lambda P, env: build_l0attna(P, env, 1)),
    22: ("maa_s2", spec_l0attna, lambda P, env: build_l0attna(P, env, 2)),
    23: ("maa_s3", spec_l0attna, lambda P, env: build_l0attna(P, env, 3)),
    24: ("maa_s4", spec_l0attna, lambda P, env: build_l0attna(P, env, 4)),
    1: ("l0ffn_a", spec_l0ffn, build_l0ffn),
    3: ("l0ffn_b", spec_l0ffn, build_l0ffn),
    2: ("l0attnb", spec_l0attnb, build_l0attnb),
    31: ("ab_s1", spec_l0attnb, lambda P, env: build_l0attnb(P, env, 1)),
    34: ("ab_dbg", spec_l0attnb, lambda P, env: (build_l0attnb(P, env, 2), [
        P.op("vector", lambda e, i=i: e.tensor_copy(
            out=env.x[i][:], in_=env.SB[:, i * 2048:(i + 1) * 2048].bitcast(F32)),
            reads=[env.SB], writes=[env.x[i]]) for i in range(8)])[0]),
    32: ("ab_s2", spec_l0attnb, lambda P, env: build_l0attnb(P, env, 2)),
    41: ("l0f_s0", spec_l0ffn, lambda P, env: build_l0ffn(P, env, 0)),
    42: ("l0f_s1", spec_l0ffn, lambda P, env: build_l0ffn(P, env, 1)),
    5: ("l1ffn_a", spec_l1ffn, build_l1ffn),
    4: ("l1attna", spec_l1attna, build_l1attna),
    51: ("aa_s1", spec_l1attna, lambda P, env: build_l1attna(P, env, 1)),
    6: ("l1attnb", spec_l1attnb, build_l1attnb),
    61: ("bb_s1", spec_l1attnb, lambda P, env: build_l1attnb(P, env, 1)),
    62: ("bb_dbg", spec_l1attnb, lambda P, env: (build_l1attnb(P, env, 2.5), [
        P.op("vector", lambda e, i=i: e.tensor_copy(
            out=env.x[i][:], in_=env.SB[:, 8192 + i * 2048:8192 + (i + 1) * 2048]
            .bitcast(F32)), reads=[env.SB], writes=[env.x[i]]) for i in range(4)])[0]),
    7: ("l1ffn_b", spec_l1ffn, build_l1ffn),
    98: ("pass", spec_none, lambda P, env: None),
    50: ("l1f_s0", spec_l1ffn, lambda P, env: build_l1ffn(P, env, 0)),
    51: ("l1f_s1", spec_l1ffn, lambda P, env: build_l1ffn(P, env, 1)),
    52: ("l1f_s2", spec_l1ffn, lambda P, env: build_l1ffn(P, env, 2)),
    53: ("l1f_s3", spec_l1ffn, lambda P, env: build_l1ffn(P, env, 3)),
    54: ("l1f_s4", spec_l1ffn, lambda P, env: build_l1ffn(P, env, 4)),
    55: ("l1f_s5", spec_l1ffn, lambda P, env: build_l1ffn(P, env, 5)),
    99: ("lnonly", spec_none, build_lnonly),
}

ALL_IDX = [0, 1, 2, 3, 4, 5, 6, 7]
LAUNCH_PLAN = [[0, 1, 2, 3, 4, 5, 6, 7]]


def build_program(idxs):
    nc = bass.Bass("TRN2", target_bir_lowering=False)
    x_in = nc.dram_tensor("x_in", [T_SEQ, D_MODEL], F32, kind="ExternalInput").ap()
    x_out = nc.dram_tensor("x_out", [T_SEQ, D_MODEL], F32, kind="ExternalOutput").ap()
    specs = {}
    for i in idxs:
        specs.update(SUBLAYERS[i][1]())
    dram = {}
    for name, (shape, dtype, _fn) in specs.items():
        dram[name] = nc.dram_tensor(name, list(shape), dtype, kind="ExternalInput").ap()
    P = Prog(nc)
    env = Env()
    env.dram = dram
    env.aa_scr = nc.dram_tensor("aa_scr", [272, 512], BF16, kind="Internal").ap()
    env.aa_scrbuf = Buf("aa_scr")
    with P._ctx:
        alloc_common(P, env)
        build_identity(P, env)
        load_x(P, env, x_in)
        for i in idxs:
            SUBLAYERS[i][2](P, env)
        store_x(P, env, x_out)
        with nc.Block() as block:
            P.finalize(block)
    return nc, specs


def kernel(**inputs):
    x = np.ascontiguousarray(np.asarray(inputs["L0_attn_a_hidden"], dtype=np.float32))
    for idxs in LAUNCH_PLAN:
        nc, specs = build_program(idxs)
        wmap = {name: np.ascontiguousarray(np.asarray(fn(inputs), dtype=np.float32))
                for name, (_s, _d, fn) in specs.items()}
        in_maps = [dict(wmap, x_in=x[b]) for b in range(N_CORES)]
        res = run_bass_kernel_spmd(nc, in_maps, core_ids=list(range(N_CORES)))
        x = np.stack([np.asarray(res.results[b]["x_out"]) for b in range(N_CORES)])
    return x


# version 05
# speedup vs baseline: 1.1238x; 1.1220x over previous
import math
import numpy as np
import ml_dtypes
from contextlib import ExitStack

import concourse.bass as bass
import concourse.mybir as mybir
from concourse.bass_utils import run_bass_kernel_spmd

F32 = mybir.dt.float32
BF16 = mybir.dt.bfloat16
AF = mybir.ActivationFunctionType
ALU = mybir.AluOpType
AX = mybir.AxisListType

ENGINES = ("tensor", "vector", "scalar", "gpsimd", "sync")
DMA_QUEUES = ("sync", "scalar", "gpsimd")

T_SEQ = 2048
D_MODEL = 1024
NT = T_SEQ // 128
ND = D_MODEL // 128
N_CORES = 8



class Buf:
    __slots__ = ("name", "writers", "readers")

    def __init__(self, name):
        self.name = name
        self.writers = []
        self.readers = []


class Op:
    __slots__ = ("fn", "deps", "inc", "seq", "name", "is_dma", "dma_queue_idx")

    def __init__(self, fn, deps, seq, name, is_dma=False):
        self.fn = fn
        self.deps = deps
        self.inc = False
        self.seq = seq
        self.name = name
        self.is_dma = is_dma
        self.dma_queue_idx = -1


class Prog:
    def __init__(self, nc):
        self.nc = nc
        self.streams = {e: [] for e in ENGINES}
        self.dep_targets = set()
        self.dma_idx = {q: 0 for q in DMA_QUEUES}
        self._ctx = ExitStack()
        self.bufmap = {}
        self._handles = []

    def sb(self, name, shape, dtype):
        h = self._ctx.enter_context(self.nc.sbuf_tensor(name, list(shape), dtype))
        self.bufmap[id(h)] = Buf(name)
        self._handles.append(h)
        return h

    def ps(self, name, shape, dtype):
        h = self._ctx.enter_context(self.nc.psum_tensor(name, list(shape), dtype))
        self.bufmap[id(h)] = Buf(name)
        self._handles.append(h)
        return h

    def _b(self, t):
        if isinstance(t, Buf):
            return t
        return self.bufmap[id(t)]

    def _mkdeps(self, eng, seq, rbufs, wbufs, is_dma=False):
        deps = []
        for b in rbufs:
            deps.extend(b.writers)
        for b in wbufs:
            deps.extend(b.writers)
            deps.extend(b.readers)
        dd = {}
        for d in deps:
            if d[0] == "e" and d[1] == eng and not is_dma:
                if eng == "tensor":
                    continue
                if d[2] >= seq:
                    continue
            key = (d[0], d[1])
            if key not in dd or dd[key][2] < d[2]:
                dd[key] = d
        return list(dd.values())

    def op(self, eng, fn, reads=(), writes=(), name=""):
        seq = len(self.streams[eng])
        rbufs = [self._b(t) for t in reads]
        wbufs = [self._b(t) for t in writes]
        deps = self._mkdeps(eng, seq, rbufs, wbufs)
        o = Op(fn, deps, seq, name)
        self.streams[eng].append(o)
        me = ("e", eng, seq)
        for b in rbufs:
            b.readers.append(me)
        for b in wbufs:
            b.writers = [me]
            b.readers = []
        for d in deps:
            if d[0] == "e":
                self.dep_targets.add((d[1], d[2]))
        return o

    def split_buf(self, parent, names):
        pw = self._b(parent)
        out = []
        for n in names:
            b = Buf(n)
            b.writers = list(pw.writers)
            b.readers = list(pw.readers)
            out.append(b)
        return out

    def merge_buf(self, parent, bufs):
        pw = self._b(parent)
        ws, rs = list(pw.writers), list(pw.readers)
        for b in bufs:
            ws.extend(b.writers)
            rs.extend(b.readers)
        pw.writers = ws
        pw.readers = rs

    def dma(self, queue, out, in_, reads=(), writes=(), name="dma", **kw):
        seq = len(self.streams[queue])
        rbufs = [self._b(t) for t in reads]
        wbufs = [self._b(t) for t in writes]
        deps = self._mkdeps(queue, seq, rbufs, wbufs, is_dma=True)
        idx = self.dma_idx[queue]
        self.dma_idx[queue] = idx + 1

        def fn(eng, out=out, in_=in_, kw=kw):
            return eng.dma_start(out=out, in_=in_, **kw)

        o = Op(fn, deps, seq, name, is_dma=True)
        o.dma_queue_idx = idx
        self.streams[queue].append(o)
        me = ("d", queue, idx)
        for b in rbufs:
            b.readers.append(me)
        for b in wbufs:
            b.writers = [me]
            b.readers = []
        for d in deps:
            if d[0] == "e":
                self.dep_targets.add((d[1], d[2]))
        return o

    def finalize(self, block):
        for (eng, seq) in self.dep_targets:
            self.streams[eng][seq].inc = True
        prefix = {}
        for eng in ENGINES:
            c = 0
            pc = []
            for o in self.streams[eng]:
                if o.inc and not o.is_dma:
                    c += 1
                pc.append(c)
            prefix[eng] = pc
        self.sem = {}
        self.dsem = {}
        for eng in ENGINES:
            if any((o.inc and not o.is_dma) for o in self.streams[eng]):
                self.sem[eng] = self._ctx.enter_context(
                    self.nc.semaphore("sem_" + eng))
        for q in DMA_QUEUES:
            if any(o.is_dma for o in self.streams[q]):
                self.dsem[q] = self._ctx.enter_context(
                    self.nc.semaphore("dsem_" + q))

        def emit(proxy, eng_name):
            waited = {}
            for o in self.streams[eng_name]:
                for d in o.deps:
                    if d[0] == "e":
                        sem = self.sem[d[1]]
                        val = prefix[d[1]][d[2]]
                    else:
                        sem = self.dsem[d[1]]
                        val = 16 * (d[2] + 1)
                    key = id(sem)
                    if waited.get(key, -1) >= val:
                        continue
                    waited[key] = val
                    proxy.wait_ge(sem, val)
                if o.fn is None:
                    continue
                inst = o.fn(proxy)
                if o.is_dma:
                    inst.then_inc(self.dsem[eng_name], 16)
                elif o.inc:
                    inst.then_inc(self.sem[eng_name], 1)

        for eng_name in ENGINES:
            getattr(block, eng_name)(lambda proxy, e=eng_name: emit(proxy, e))


class Env:
    def __init__(self):
        self.pool = {}
        self.flags = set()

    def get(self, P, name, shape, dtype):
        if name not in self.pool:
            self.pool[name] = P.sb(name, list(shape), dtype)
        return self.pool[name]

    def once(self, flag):
        if flag in self.flags:
            return False
        self.flags.add(flag)
        return True



def build_identity(P, env):
    env.ident = P.sb("ident", [128, 128], BF16)
    P.op("gpsimd", lambda e: e.memset(env.ident[:], 0.0), writes=[env.ident])
    P.op("gpsimd", lambda e: e.affine_select(
        out=env.ident[:], in_=env.ident[:], compare_op=ALU.not_equal, fill=1.0,
        base=0, pattern=[[-1, 128]], channel_multiplier=1),
        reads=[env.ident], writes=[env.ident])


def alloc_common(P, env):
    env.xnT = P.sb("xnT", [128, ND * T_SEQ], BF16)
    env.stats = [P.sb(f"st{i}", [128, 8], F32) for i in range(NT)]
    env.xn_tm = [P.sb(f"xntm{i}", [128, D_MODEL], BF16) for i in range(2)]
    env.psB = [P.ps(f"psB{i}", [128, 512], F32) for i in range(8)]
    env.psA = env.psB[:6]
    env.ps_tp = [env.psB[6], env.psB[7]]
    env.SA0a = P.sb("SA0a", [128, 4096], BF16)
    env.SA0b = P.sb("SA0b", [128, 4096], BF16)
    env.SA1a = P.sb("SA1a", [128, 4096], BF16)
    env.SA1b = P.sb("SA1b", [128, 4096], BF16)
    env.SB = P.sb("SB", [128, 16384], BF16)
    env.SF = P.sb("SF", [128, 4096], F32)
    env.gbB = P.sb("gbB", [128, 2048], BF16)
    env.scF1 = P.sb("scF1", [128, 1024], F32)
    env.scF2 = P.sb("scF2", [128, 1024], F32)
    env.ones_f = P.sb("ones_f", [1, 128], F32)
    P.op("gpsimd", lambda e: e.memset(env.ones_f[:], 1.0), writes=[env.ones_f])
    env.wslot = [P.sb(f"wslot{j}", [128, 1024], BF16) for j in range(2)]
    env.lam_sb = P.sb("lam_sb", [1, 16], F32)
    env.lamrep = P.sb("lamrep", [128, 8], F32)
    env.row_sc = P.sb("row_sc", [1, 1024], F32)
    env.srow = env.row_sc[0:1, 0:512]
    P.bufmap[id(env.srow)] = P._b(env.row_sc)
    env.cst = P.sb("cst", [128, 4], F32)
    P.op("gpsimd", lambda e: e.memset(env.cst[:, 0:1], 1e-5), writes=[env.cst])
    P.op("gpsimd", lambda e: e.memset(env.cst[:, 1:2], 1.0), writes=[env.cst])
    P.op("gpsimd", lambda e: e.memset(env.cst[:, 2:3], 1e-6), writes=[env.cst])
    env.identf = P.sb("identf", [128, 128], F32)
    P.op("gpsimd", lambda e: e.memset(env.identf[:], 0.0), writes=[env.identf])
    P.op("gpsimd", lambda e: e.affine_select(
        out=env.identf[:], in_=env.identf[:], compare_op=ALU.not_equal, fill=1.0,
        base=0, pattern=[[-1, 128]], channel_multiplier=1),
        reads=[env.identf], writes=[env.identf])
    env.selH = P.sb("selH", [128, 2], BF16)
    P.op("gpsimd", lambda e: e.memset(env.selH[:], 1.0), writes=[env.selH])
    P.op("gpsimd", lambda e: e.affine_select(
        out=env.selH[:], in_=env.selH[:], compare_op=ALU.is_ge, fill=0.0,
        base=0, pattern=[[-64, 2]], channel_multiplier=1),
        reads=[env.selH], writes=[env.selH])
    P.op("gpsimd", lambda e: e.affine_select(
        out=env.selH[:], in_=env.selH[:], compare_op=ALU.is_ge, fill=0.0,
        base=63, pattern=[[64, 2]], channel_multiplier=-1),
        reads=[env.selH], writes=[env.selH])
    env.rts = P.sb("rts", [128, 64], F32)


def load_x(P, env, x_dram):
    env.x = [P.sb(f"x{i}", [128, D_MODEL], F32) for i in range(NT)]
    for i in range(NT):
        P.dma("sync", env.x[i][:], x_dram[i * 128:(i + 1) * 128, :],
              writes=[env.x[i]], name=f"ldx{i}")


def store_x(P, env, x_out_dram):
    for i in range(NT):
        P.dma("sync", x_out_dram[i * 128:(i + 1) * 128, :], env.x[i][:],
              reads=[env.x[i]], name=f"stx{i}")
    o = Op(None, [("d", "sync", P.dma_idx["sync"] - 1)],
           len(P.streams["sync"]), "final")
    P.streams["sync"].append(o)


def layernorm_to_xnT(P, env):
    for i in range(NT):
        st = env.stats[i]
        xi = env.x[i]
        P.op("vector", lambda e, xi=xi, st=st: e.reduce_sum(
            st[:, 0:1], xi[:], axis=AX.X), reads=[xi], writes=[st])
        P.op("vector", lambda e, st=st: e.tensor_scalar(
            out=st[:, 1:2], in0=st[:, 0:1], scalar1=-1.0 / D_MODEL, scalar2=None,
            op0=ALU.mult), reads=[st], writes=[st])
        sqs = env.xn_tm[i % 2]
        P.op("scalar", lambda e, xi=xi, st=st, sqs=sqs: e.activation(
            out=sqs[:], in_=xi[:], func=AF.Square, bias=st[:, 1:2],
            accum_out=st[:, 2:3]), reads=[xi, st], writes=[sqs, st])
        P.op("scalar", lambda e, st=st: e.activation(
            out=st[:, 3:4], in_=st[:, 2:3], func=AF.Ln, scale=1.0 / D_MODEL,
            bias=env.cst[:, 0:1]), reads=[st, env.cst], writes=[st])
        P.op("scalar", lambda e, st=st: e.activation(
            out=st[:, 4:5], in_=st[:, 3:4], func=AF.Exp, scale=-0.5),
            reads=[st], writes=[st])
        P.op("vector", lambda e, st=st: e.tensor_tensor(
            out=st[:, 5:6], in0=st[:, 1:2], in1=st[:, 4:5], op=ALU.mult),
            reads=[st], writes=[st])
        tm = env.xn_tm[i % 2]
        P.op("vector", lambda e, xi=xi, st=st, tm=tm: e.tensor_scalar(
            out=tm[:], in0=xi[:], scalar1=st[:, 4:5], scalar2=st[:, 5:6],
            op0=ALU.mult, op1=ALU.add), reads=[xi, st], writes=[tm])
        pstp = env.ps_tp[i % 2]
        pstp_bf = pstp[:].bitcast(BF16)
        for k in range(ND):
            P.op("tensor", lambda e, tm=tm, pstp_bf=pstp_bf, k=k: e.transpose(
                pstp_bf[:, k * 128:(k + 1) * 128], tm[:, k * 128:(k + 1) * 128],
                env.ident[:]), reads=[tm, env.ident], writes=[pstp])
        dst = env.xnT[:].rearrange("p (k t) -> p k t", k=ND)[:, :, i * 128:(i + 1) * 128]
        src_ap = pstp_bf.rearrange("p (k t) -> p k t", k=ND)
        P.op("vector", lambda e, dst=dst, src_ap=src_ap: e.tensor_copy(
            out=dst, in_=src_ap), reads=[pstp], writes=[env.xnT])


def xnt(env, k, t0, t1):
    return env.xnT[:, k * T_SEQ + t0:k * T_SEQ + t1]


def residual_add_tile(P, env, i, cols, src_ap, reads):
    xi = env.x[i]
    P.op("gpsimd", lambda e, xi=xi, src_ap=src_ap, cols=cols: e.tensor_tensor(
        out=xi[:, cols], in0=xi[:, cols], in1=src_ap, op=ALU.add),
        reads=[xi] + list(reads), writes=[xi])



def spec_l1ffn():
    def t(i, key):
        return np.ascontiguousarray(np.asarray(i[key]).T)

    def mu(i):
        return np.ascontiguousarray(np.concatenate([
            np.asarray(i["L1_ffn_mu_r"]).reshape(8, 128).T,
            np.asarray(i["L1_ffn_mu_k"]).reshape(8, 128).T], axis=1))

    return {
        "f1_WrT": ((1024, 1024), F32, lambda i: t(i, "L1_ffn_W_r")),
        "f1_WkT": ((1024, 1024), F32, lambda i: t(i, "L1_ffn_W_k")),
        "f1_WvT": ((1024, 1024), F32, lambda i: t(i, "L1_ffn_W_v")),
        "f1_mu": ((128, 16), F32, mu),
    }


def build_l1ffn(P, env, stage=99):
    layernorm_to_xnT(P, env)
    d = env.dram
    mu = env.get(P, "f1mu_sb", (128, 16), F32)
    if env.once("f1mu"):
        P.dma("sync", mu[:], d["f1_mu"], writes=[mu], name="f1mu")
    WrT = env.SB[:, 0:8192]
    WkT = env.SB[:, 8192:16384]
    WvT = env.SF[:].bitcast(BF16)
    for (t, tb_, w, off) in ((WrT, env.SB, "f1_WrT", 0), (WkT, env.SB, "f1_WkT", 0),
                             (WvT, env.SF, "f1_WvT", 0)):
        for k in range(ND):
            P.dma("gpsimd", t[:, k * 1024:(k + 1) * 1024],
                  d[w][k * 128:(k + 1) * 128, :], writes=[tb_], name=w + str(k))
    xr = [env.SA0a[:, k * 512:(k + 1) * 512] for k in range(ND)]
    xk = [env.SA0b[:, k * 512:(k + 1) * 512] for k in range(ND)]
    ka = [env.SA1a[:, k * 512:(k + 1) * 512] for k in range(ND)]
    rsb = [env.SA1b[:, j * 1024:(j + 1) * 1024] for j in range(4)]
    tmp = env.scF2[:].bitcast(BF16)[:, 0:512]
    krf = env.scF1[:, 0:512]
    osc = env.scF1[:, 512:1024]

    if stage < 1:
        return
    for c in range(4):
        t0 = c * 512
        for k in range(ND):
            for (dst, mcol) in ((xr[k], k), (xk[k], 8 + k)):
                lo = 1 if t0 == 0 else 0
                dstb = env.SA0a if mcol < 8 else env.SA0b
                if t0 == 0:
                    P.op("vector", lambda e, dst=dst, k=k, mcol=mcol: e.tensor_scalar(
                        out=dst[:, 0:1], in0=xnt(env, k, 0, 1),
                        scalar1=mu[:, mcol:mcol + 1], scalar2=None, op0=ALU.mult),
                        reads=[env.xnT, mu], writes=[dstb])
                xs = xnt(env, k, t0 + lo, t0 + 512)
                xp = xnt(env, k, t0 + lo - 1, t0 + 511)
                P.op("vector", lambda e, xs=xs, xp=xp, lo=lo: e.tensor_tensor(
                    out=tmp[:, lo:512], in0=xs, in1=xp, op=ALU.subtract),
                    reads=[env.xnT], writes=[env.scF2])
                P.op("vector", lambda e, dst=dst, xp=xp, lo=lo, mcol=mcol:
                     e.scalar_tensor_tensor(
                         out=dst[:, lo:512], in0=tmp[:, lo:512],
                         scalar=mu[:, mcol:mcol + 1], in1=xp,
                         op0=ALU.mult, op1=ALU.add),
                     reads=[env.scF2, env.xnT, mu], writes=[dstb])
        if stage < 2:
            continue
        for f in range(ND):
            pk = env.psA[2 + (f % 2)]
            for k in range(ND):
                lhsT = WkT[:, k * 1024 + f * 128:k * 1024 + (f + 1) * 128]
                P.op("tensor", lambda e, pk=pk, lhsT=lhsT, xkk=xk[k], k=k: e.matmul(
                    pk[:], lhsT, xkk, start=(k == 0), stop=(k == ND - 1)),
                    reads=[env.SB, env.SA0b], writes=[pk])
            P.op("vector", lambda e, pk=pk: e.tensor_scalar(
                out=krf, in0=pk[:], scalar1=0.0, scalar2=None, op0=ALU.max),
                reads=[pk], writes=[env.scF1])
            P.op("scalar", lambda e, f=f: e.activation(
                out=ka[f], in_=krf, func=AF.Square),
                reads=[env.scF1], writes=[env.SA1a])
        if stage < 3:
            continue
        for tt in range(4):
            for h in range(2):
                pr = env.psA[h]
                for k in range(ND):
                    lhsT = xr[k][:, tt * 128:(tt + 1) * 128]
                    rhs = WrT[:, k * 1024 + h * 512:k * 1024 + h * 512 + 512]
                    P.op("tensor", lambda e, pr=pr, lhsT=lhsT, rhs=rhs, k=k: e.matmul(
                        pr[:], lhsT, rhs, start=(k == 0), stop=(k == ND - 1)),
                        reads=[env.SB, env.SA0a], writes=[pr])
                P.op("scalar", lambda e, pr=pr, tt=tt, h=h: e.activation(
                    out=rsb[tt][:, h * 512:(h + 1) * 512], in_=pr[:], func=AF.Sigmoid),
                    reads=[pr], writes=[env.SA1b])
        if stage < 4:
            continue
        for tt in range(4):
            i_tile = c * 4 + tt
            for h in range(2):
                po = env.psA[4 + h]
                for f in range(ND):
                    lhsT = ka[f][:, tt * 128:(tt + 1) * 128]
                    rhs = WvT[:, f * 1024 + h * 512:f * 1024 + h * 512 + 512]
                    P.op("tensor", lambda e, po=po, lhsT=lhsT, rhs=rhs, f=f: e.matmul(
                        po[:], lhsT, rhs, start=(f == 0), stop=(f == ND - 1)),
                        reads=[env.SF, env.SA1a], writes=[po])
                if stage < 5:
                    P.op("vector", lambda e, po=po: e.tensor_copy(
                        out=osc, in_=po[:]), reads=[po], writes=[env.scF1])
                else:
                    P.op("vector", lambda e, po=po, tt=tt, h=h: e.tensor_tensor(
                        out=osc, in0=po[:], in1=rsb[tt][:, h * 512:(h + 1) * 512],
                        op=ALU.mult), reads=[po, env.SA1b], writes=[env.scF1])
                if stage >= 6:
                    residual_add_tile(P, env, i_tile, slice(h * 512, (h + 1) * 512),
                                      osc, [env.scF1])




def spec_l0ffn():
    def f(key, fn):
        return lambda i, key=key, fn=fn: fn(np.asarray(i[key]))

    return {
        "fr_WrT": ((1024, 8), F32, f("L0_ffn_Wr", lambda a: np.ascontiguousarray(a.T))),
        "fr_br": ((1, 8), F32, f("L0_ffn_br", lambda a: a.reshape(1, 8))),
        "f0_W1T": ((8192, 4096), F32, f("L0_ffn_W1", lambda a: np.ascontiguousarray(
            a.transpose(0, 2, 1).reshape(8192, 4096)))),
        "f0_b1": ((128, 256), F32, f("L0_ffn_b1", lambda a: np.ascontiguousarray(
            a.reshape(8, 32, 128).transpose(2, 0, 1).reshape(128, 256)))),
        "f0_W2T": ((32768, 1024), F32, f("L0_ffn_W2", lambda a: np.ascontiguousarray(
            a.transpose(0, 2, 1).reshape(32768, 1024)))),
        "f0_b2": ((1, 8192), F32, f("L0_ffn_b2", lambda a: a.reshape(1, 8192))),
        "f0_gb": ((1, 2048), F32, lambda i: np.concatenate(
            [np.asarray(i["L0_ffn_gamma"]), np.asarray(i["L0_ffn_beta"])]).reshape(1, 2048)),
    }


def build_l0ffn(P, env, stage=99):
    layernorm_to_xnT(P, env)
    d = env.dram
    if env.once("f0small"):
        env.wr_sb = P.sb("wr_sb", [128, 64], BF16)
        for k in range(ND):
            P.dma("gpsimd", env.wr_sb[:, 8 * k:8 * (k + 1)],
                  d["fr_WrT"][k * 128:(k + 1) * 128, :], writes=[env.wr_sb])
        env.br_row = P.sb("br_row", [1, 8], F32)
        P.dma("sync", env.br_row[:], d["fr_br"], writes=[env.br_row])
        env.b1_sb = P.sb("b1_sb", [128, 256], F32)
        P.dma("sync", env.b1_sb[:], d["f0_b1"], writes=[env.b1_sb])

        pass
        env.brb = P.sb("brb", [128, 8], F32)
        env.coef = P.sb("coef", [128, 128], F32)
        pb = env.psB[0]
        P.op("tensor", lambda e, pb=pb: e.matmul(
            pb[:, 0:8], env.ones_f[0:1, :], env.br_row[0:1, :],
            start=True, stop=True), reads=[env.ones_f, env.br_row], writes=[pb])
        P.op("vector", lambda e, pb=pb: e.tensor_copy(out=env.brb[:], in_=pb[:, 0:8]),
             reads=[pb], writes=[env.brb])
        for gb2 in range(2):
            P.dma("sync", env.row_sc[0:1, :],
                  d["f0_gb"][0:1, gb2 * 1024:(gb2 + 1) * 1024],
                  writes=[env.row_sc], name="gbld")
            for j in range(2):
                pb = env.psB[j % 2]
                P.op("tensor", lambda e, pb=pb, j=j: e.matmul(
                    pb[:], env.ones_f[0:1, :], env.row_sc[0:1, j * 512:(j + 1) * 512],
                    start=True, stop=True), reads=[env.ones_f, env.row_sc],
                    writes=[pb])
                P.op("vector", lambda e, pb=pb, j=j, gb2=gb2: e.tensor_copy(
                    out=env.gbB[:, gb2 * 1024 + j * 512:gb2 * 1024 + (j + 1) * 512],
                    in_=pb[:]), reads=[pb], writes=[env.gbB])

    rts = env.rts
    for i in range(NT):
        pr = env.psB[i % 2]
        for k in range(ND):
            P.op("tensor", lambda e, pr=pr, k=k, i=i: e.matmul(
                pr[:, 0:8], xnt(env, k, i * 128, (i + 1) * 128),
                env.wr_sb[:, 8 * k:8 * (k + 1)], start=(k == 0), stop=(k == ND - 1)),
                reads=[env.xnT, env.wr_sb], writes=[pr])
        lg, mx = rts[:, 0:8], rts[:, 8:16]
        e1, e2, t8, e2b = rts[:, 16:24], rts[:, 24:32], rts[:, 40:48], rts[:, 48:56]
        c1, c2 = rts[:, 0:8], rts[:, 8:16]
        P.op("vector", lambda e, pr=pr, lg=lg: e.tensor_tensor(
            out=lg, in0=pr[:, 0:8], in1=env.brb[:], op=ALU.add),
            reads=[pr, env.brb], writes=[rts])
        P.op("vector", lambda e, lg=lg, mx=mx: e.max(out=mx, in_=lg),
             reads=[rts], writes=[rts])
        P.op("vector", lambda e, mx=mx: e.tensor_tensor(
            out=rts[:, 32:33], in0=mx[:, 0:1], in1=mx[:, 1:2], op=ALU.subtract),
            reads=[rts], writes=[rts])
        P.op("scalar", lambda e: e.activation(
            out=rts[:, 33:34], in_=rts[:, 32:33], func=AF.Sigmoid),
            reads=[rts], writes=[rts])
        P.op("vector", lambda e: e.tensor_scalar(
            out=rts[:, 34:35], in0=rts[:, 33:34], scalar1=-1.0, scalar2=1.0,
            op0=ALU.mult, op1=ALU.add), reads=[rts], writes=[rts])
        P.op("vector", lambda e, lg=lg, e1=e1, mx=mx: e.tensor_scalar(
            out=e1, in0=lg, scalar1=mx[:, 0:1], scalar2=None, op0=ALU.is_equal),
            reads=[rts], writes=[rts])
        P.op("vector", lambda e, lg=lg, e2=e2, mx=mx: e.tensor_scalar(
            out=e2, in0=lg, scalar1=mx[:, 1:2], scalar2=None, op0=ALU.is_equal),
            reads=[rts], writes=[rts])
        P.op("vector", lambda e, t8=t8, e1=e1, e2=e2: e.tensor_tensor(
            out=t8, in0=e2, in1=e1, op=ALU.mult), reads=[rts], writes=[rts])
        P.op("vector", lambda e, e2b=e2b, e2=e2, t8=t8: e.tensor_tensor(
            out=e2b, in0=e2, in1=t8, op=ALU.subtract), reads=[rts], writes=[rts])
        P.op("vector", lambda e, c1=c1, e1=e1: e.tensor_scalar(
            out=c1, in0=e1, scalar1=rts[:, 33:34], scalar2=None, op0=ALU.mult),
            reads=[rts], writes=[rts])
        P.op("vector", lambda e, c2=c2, e2b=e2b: e.tensor_scalar(
            out=c2, in0=e2b, scalar1=rts[:, 34:35], scalar2=None, op0=ALU.mult),
            reads=[rts], writes=[rts])
        P.op("vector", lambda e, c1=c1, c2=c2, i=i: e.tensor_tensor(
            out=env.coef[:, 8 * i:8 * (i + 1)], in0=c1, in1=c2, op=ALU.add),
            reads=[rts], writes=[env.coef])
    if stage < 1:
        return

    w1v = d["f0_W1T"].rearrange("(k p) f -> p k f", p=128)
    for tb in range(4):
        for e_x in range(8):
            P.dma("sync", env.row_sc[0:1, 0:1024],
                  d["f0_b2"][0:1, e_x * 1024:(e_x + 1) * 1024],
                  writes=[env.row_sc], name="b2ld")
            for fc in range(8):
                half = env.SA0a if fc % 2 == 0 else env.SA0b
                src_ap = w1v[:, e_x * 8:(e_x + 1) * 8, fc * 512:(fc + 1) * 512]
                dst_ap = half.rearrange("p (k f) -> p k f", k=8)
                P.dma("gpsimd", dst_ap, src_ap, writes=[half], name="w1ld")
                for ft in range(4):
                    fg = fc * 4 + ft
                    ph = env.psB[fg % 2]
                    for k in range(ND):
                        lhsT = half[:, k * 512 + ft * 128:k * 512 + (ft + 1) * 128]
                        half = half
                        P.op("tensor", lambda e, ph=ph, lhsT=lhsT, k=k, tb=tb: e.matmul(
                            ph[:], lhsT, xnt(env, k, tb * 512, (tb + 1) * 512),
                            start=(k == 0), stop=(k == ND - 1)),
                            reads=[half, env.xnT], writes=[ph])
                    P.op("scalar", lambda e, ph=ph, fg=fg, e_x=e_x: e.activation(
                        out=env.SB[:, fg * 512:(fg + 1) * 512], in_=ph[:], func=AF.Relu,
                        bias=env.b1_sb[:, e_x * 32 + fg:e_x * 32 + fg + 1]),
                        reads=[ph, env.b1_sb], writes=[env.SB])
            w2v = d["f0_W2T"]
            for fgp in range(8):
                buf = env.SA1a if fgp % 2 == 0 else env.SA1b
                src_ap = w2v[e_x * 4096 + fgp * 512:e_x * 4096 + (fgp + 1) * 512, :]
                dst_ap = buf[:].rearrange("p (j f) -> p j f", j=4)
                P.dma("gpsimd", dst_ap,
                      src_ap.rearrange("(j p) f -> p j f", p=128),
                      writes=[buf], name="w2ld")
                for j in range(4):
                    fg = fgp * 4 + j
                    for tt in range(4):
                        for h in range(2):
                            po = env.psB[tt * 2 + h]
                            lhsT = env.SB[:, fg * 512 + tt * 128:fg * 512 + (tt + 1) * 128]
                            rhs = buf[:, j * 1024 + h * 512:j * 1024 + h * 512 + 512]
                            P.op("tensor", lambda e, po=po, lhsT=lhsT, rhs=rhs, fg=fg:
                                 e.matmul(po[:], lhsT, rhs, start=(fg == 0), stop=False),
                                 reads=[env.SB, buf], writes=[po])
            for tt in range(4):
                i_tile = tb * 4 + tt
                for h in range(2):
                    po = env.psB[tt * 2 + h]
                    P.op("tensor", lambda e, po=po, h=h: e.matmul(
                        po[:], env.ones_f[0:1, :],
                        env.row_sc[0:1, h * 512:h * 512 + 512],
                        start=False, stop=True),
                        reads=[env.ones_f, env.row_sc], writes=[po])
                    cc = env.coef[:, 8 * i_tile + e_x:8 * i_tile + e_x + 1]
                    if e_x == 0:
                        P.op("scalar", lambda e, po=po, cc=cc, tt=tt, h=h: e.activation(
                            out=env.SF[:, tt * 1024 + h * 512:tt * 1024 + (h + 1) * 512
                                       if False else tt * 1024 + h * 512 + 512],
                            in_=po[:], func=AF.Copy, scale=cc),
                            reads=[po, env.coef], writes=[env.SF])
                    else:
                        P.op("scalar", lambda e, po=po, cc=cc: e.activation(
                            out=env.scF1[:, 0:512], in_=po[:], func=AF.Copy, scale=cc),
                            reads=[po, env.coef], writes=[env.scF1])
                        P.op("gpsimd", lambda e, tt=tt, h=h: e.tensor_tensor(
                            out=env.SF[:, tt * 1024 + h * 512:tt * 1024 + h * 512 + 512],
                            in0=env.SF[:, tt * 1024 + h * 512:tt * 1024 + h * 512 + 512],
                            in1=env.scF1[:, 0:512], op=ALU.add),
                            reads=[env.SF, env.scF1], writes=[env.SF])
        if stage < 2:
            continue
        for tt in range(4):
            i = tb * 4 + tt
            st = env.stats[i]
            xi = env.x[i]
            P.op("vector", lambda e, xi=xi, st=st: e.tensor_scalar(
                out=env.scF1[:], in0=xi[:], scalar1=st[:, 4:5], scalar2=st[:, 5:6],
                op0=ALU.mult, op1=ALU.add), reads=[xi, st], writes=[env.scF1])
            P.op("vector", lambda e, tt=tt: e.tensor_tensor(
                out=env.scF2[:], in0=env.scF1[:],
                in1=env.SF[:, tt * 1024:(tt + 1) * 1024], op=ALU.add),
                reads=[env.scF1, env.SF], writes=[env.scF2])
            P.op("vector", lambda e, st=st: e.reduce_sum(
                st[:, 0:1], env.scF2[:], axis=AX.X), reads=[env.scF2, st], writes=[st])
            P.op("vector", lambda e, st=st: e.tensor_scalar(
                out=st[:, 1:2], in0=st[:, 0:1], scalar1=-1.0 / D_MODEL, scalar2=None,
                op0=ALU.mult), reads=[st], writes=[st])
            P.op("scalar", lambda e, st=st: e.activation(
                out=env.xn_tm[0][:], in_=env.scF2[:], func=AF.Square, bias=st[:, 1:2],
                accum_out=st[:, 2:3]), reads=[env.scF2, st],
                writes=[env.xn_tm[0], st])
            P.op("scalar", lambda e, st=st: e.activation(
                out=st[:, 3:4], in_=st[:, 2:3], func=AF.Ln, scale=1.0 / D_MODEL,
                bias=env.cst[:, 0:1]), reads=[st, env.cst], writes=[st])
            P.op("scalar", lambda e, st=st: e.activation(
                out=st[:, 4:5], in_=st[:, 3:4], func=AF.Exp, scale=-0.5),
                reads=[st], writes=[st])
            P.op("vector", lambda e, st=st: e.tensor_tensor(
                out=st[:, 5:6], in0=st[:, 1:2], in1=st[:, 4:5], op=ALU.mult),
                reads=[st], writes=[st])
            P.op("vector", lambda e, st=st: e.tensor_scalar(
                out=env.scF1[:], in0=env.scF2[:], scalar1=st[:, 4:5], scalar2=st[:, 5:6],
                op0=ALU.mult, op1=ALU.add), reads=[env.scF2, st], writes=[env.scF1])
            P.op("vector", lambda e: e.tensor_tensor(
                out=env.scF2[:], in0=env.scF1[:], in1=env.gbB[:, 0:1024], op=ALU.mult),
                reads=[env.scF1, env.gbB], writes=[env.scF2])
            P.op("vector", lambda e: e.tensor_tensor(
                out=env.scF1[:], in0=env.scF2[:], in1=env.gbB[:, 1024:2048], op=ALU.add),
                reads=[env.scF2, env.gbB], writes=[env.scF1])
            P.op("gpsimd", lambda e, xi=xi: e.tensor_tensor(
                out=xi[:], in0=xi[:], in1=env.scF1[:], op=ALU.add),
                reads=[xi, env.scF1], writes=[xi])



def spec_l0attnb():
    def raw(key):
        return lambda i, key=key: np.asarray(i[key])

    return {
        "ab_Wq": ((1024, 1024), F32, raw("L0_attn_b_Wq")),
        "ab_Wk": ((1024, 1024), F32, raw("L0_attn_b_Wk")),
        "ab_Wv": ((1024, 1024), F32, raw("L0_attn_b_Wv")),
        "ab_Wqp": ((1024, 1024), F32, raw("L0_attn_b_Wqp")),
        "ab_Wkp": ((1024, 1024), F32, raw("L0_attn_b_Wkp")),
        "ab_Wo": ((1024, 1024), F32, raw("L0_attn_b_Wo")),
        "ab_lam": ((1, 16), F32, lambda i: np.ascontiguousarray(
            np.asarray(i["L0_attn_b_lambda_raw"]).reshape(1, 16))),
    }


def _proj_fm(P, env, wdram_col, wslot, evac):
    P.dma("gpsimd", wslot[:].rearrange("p (k m) -> p k m", k=8),
          wdram_col.rearrange("(k p) m -> p k m", p=128), writes=[wslot], name="wld")
    for tc in range(4):
        ph = env.psB[4 + tc % 2]
        for k in range(ND):
            lhsT = wslot[:, k * 128:(k + 1) * 128]
            P.op("tensor", lambda e, ph=ph, lhsT=lhsT, k=k, tc=tc: e.matmul(
                ph[:], lhsT, xnt(env, k, tc * 512, (tc + 1) * 512),
                start=(k == 0), stop=(k == ND - 1)),
                reads=[wslot, env.xnT], writes=[ph])
        evac(tc, ph)


def build_l0attnb(P, env, stage=99):
    layernorm_to_xnT(P, env)
    d = env.dram
    if env.once("absmall"):
        pass
        env.selHT_f = P.sb("selHT_f", [2, 128], F32)
        P.op("gpsimd", lambda e: e.memset(env.selHT_f[:], 1.0), writes=[env.selHT_f])
        P.op("gpsimd", lambda e: e.affine_select(
            out=env.selHT_f[:], in_=env.selHT_f[:], compare_op=ALU.is_ge, fill=0.0,
            base=0, pattern=[[1, 128]], channel_multiplier=-64),
            reads=[env.selHT_f], writes=[env.selHT_f])
        P.op("gpsimd", lambda e: e.affine_select(
            out=env.selHT_f[:], in_=env.selHT_f[:], compare_op=ALU.is_ge, fill=0.0,
            base=63, pattern=[[-1, 128]], channel_multiplier=64),
            reads=[env.selHT_f], writes=[env.selHT_f])
    P.dma("sync", env.lam_sb[:], d["ab_lam"], writes=[env.lam_sb], name="lamld")
    P.op("scalar", lambda e: e.activation(
        out=env.lam_sb[:], in_=env.lam_sb[:], func=AF.Tanh),
        reads=[env.lam_sb], writes=[env.lam_sb])
    for g in range(8):
        pb = env.psB[0]
        for hl in range(2):
            P.op("tensor", lambda e, pb=pb, g=g, hl=hl: e.matmul(
                pb[64 * hl:64 * (hl + 1), 0:1], env.ones_f[0:1, 0:64],
                env.lam_sb[0:1, 2 * g + hl:2 * g + hl + 1],
                start=(hl == 0), stop=(hl == 1)),
                reads=[env.ones_f, env.lam_sb], writes=[pb])
        P.op("vector", lambda e, pb=pb, g=g: e.tensor_scalar(
            out=env.lamrep[:, g:g + 1], in0=pb[:, 0:1], scalar1=0.5 / 8.0,
            scalar2=None, op0=ALU.mult), reads=[pb], writes=[env.lamrep])

    SFbf = env.SF[:].bitcast(BF16)
    qT = env.SA0a[:, 0:2048]
    qcT = env.SA0a[:, 2048:4096]
    qsT = env.SA0b[:, 0:2048]
    kT = env.SA0b[:, 2048:4096]
    kcT = env.SA1a[:, 0:2048]
    ksT = env.SA1a[:, 2048:4096]
    vtm = env.SA1b[:, 0:2048]
    spq = SFbf[:, 0:2048]
    sq2 = SFbf[:, 2048:4096]
    ampq = SFbf[:, 4096:6144]
    ampk = SFbf[:, 6144:8192]
    atnA = SFbf[:, 0:1024]
    atnB = SFbf[:, 1024:2048]
    asc = env.rts

    for g in range(8):
        def ev_sc(dst, dstbuf, scale):
            def f(tc, ph):
                P.op("vector", lambda e, ph=ph, tc=tc: e.tensor_scalar(
                    out=dst[:, tc * 512:(tc + 1) * 512], in0=ph[:],
                    scalar1=scale, scalar2=None, op0=ALU.mult),
                    reads=[ph], writes=[dstbuf])
            return f

        _proj_fm(P, env, d["ab_Wq"][:, g * 128:(g + 1) * 128], env.wslot[0],
                 ev_sc(qT, env.SA0a, 0.125))
        _proj_fm(P, env, d["ab_Wk"][:, g * 128:(g + 1) * 128], env.wslot[1],
                 ev_sc(kT, env.SA0b, 1.0))
        P.dma("gpsimd", env.wslot[0][:].rearrange("p (k m) -> p k m", k=8),
              d["ab_Wv"][:, g * 128:(g + 1) * 128].rearrange("(k p) m -> p k m", p=128),
              writes=[env.wslot[0]], name="wvld")
        for tt in range(NT):
            pv = env.psB[4 + tt % 2]
            for k in range(ND):
                P.op("tensor", lambda e, pv=pv, k=k, tt=tt: e.matmul(
                    pv[:, 0:128], xnt(env, k, tt * 128, (tt + 1) * 128),
                    env.wslot[0][:, k * 128:(k + 1) * 128],
                    start=(k == 0), stop=(k == ND - 1)),
                    reads=[env.wslot[0], env.xnT], writes=[pv])
            P.op("scalar", lambda e, pv=pv, tt=tt: e.activation(
                out=vtm[:, tt * 128:(tt + 1) * 128], in_=pv[:, 0:128], func=AF.Copy),
                reads=[pv], writes=[env.SA1b])
        for (srcT, srcbuf, ampdst, sc8) in ((qT, env.SA0a, ampq, 8.0),
                                            (kT, env.SA0b, ampk, 1.0)):
            for hh in range(2):
                cl = slice(hh * 1024, (hh + 1) * 1024)
                P.op("scalar", lambda e, srcT=srcT, cl=cl, sc8=sc8: e.activation(
                    out=env.scF1[:], in_=srcT[:, cl], func=AF.Abs, scale=sc8),
                    reads=[srcbuf], writes=[env.scF1])
                P.op("scalar", lambda e: e.activation(
                    out=env.scF2[:], in_=env.scF1[:], func=AF.Exp, scale=-1.0),
                    reads=[env.scF1], writes=[env.scF2])
                P.op("scalar", lambda e: e.activation(
                    out=env.scF1[:], in_=env.scF2[:], func=AF.Ln,
                    bias=env.cst[:, 1:2]), reads=[env.scF2, env.cst],
                    writes=[env.scF1])
                P.op("vector", lambda e, srcT=srcT, cl=cl, sc8=sc8: e.tensor_scalar(
                    out=env.scF2[:], in0=srcT[:, cl], scalar1=sc8, scalar2=0.0,
                    op0=ALU.mult, op1=ALU.max), reads=[srcbuf], writes=[env.scF2])
                P.op("vector", lambda e, cl=cl: e.tensor_tensor(
                    out=spq[:, cl], in0=env.scF1[:], in1=env.scF2[:], op=ALU.add),
                    reads=[env.scF1, env.scF2], writes=[env.SF])
            P.op("scalar", lambda e: e.activation(
                out=sq2[:], in_=spq[:], func=AF.Square), reads=[env.SF],
                writes=[env.SF])
            for tc in range(4):
                pa = env.psB[tc % 2]
                P.op("tensor", lambda e, pa=pa, tc=tc: e.matmul(
                    pa[0:2, :], env.selH[:], sq2[:, tc * 512:(tc + 1) * 512],
                    start=True, stop=True), reads=[env.selH, env.SF], writes=[pa])
                P.op("scalar", lambda e, pa=pa: e.activation(
                    out=env.scF1[0:2, 0:512], in_=pa[0:2, :], func=AF.Ln,
                    scale=1.0 / 64.0, bias=env.cst[0:2, 2:3]),
                    reads=[pa, env.cst], writes=[env.scF1])
                P.op("scalar", lambda e: e.activation(
                    out=env.scF1[0:2, 512:1024], in_=env.scF1[0:2, 0:512],
                    func=AF.Exp, scale=-0.5), reads=[env.scF1], writes=[env.scF1])
                pr = env.psB[2 + tc % 2]
                P.op("tensor", lambda e, pr=pr: e.matmul(
                    pr[:], env.selHT_f[:], env.scF1[0:2, 512:1024],
                    start=True, stop=True),
                    reads=[env.selHT_f, env.scF1], writes=[pr])
                P.op("vector", lambda e, pr=pr, tc=tc, ampdst=ampdst: e.tensor_tensor(
                    out=ampdst[:, tc * 512:(tc + 1) * 512],
                    in0=spq[:, tc * 512:(tc + 1) * 512], in1=pr[:], op=ALU.mult),
                    reads=[env.SF, pr], writes=[env.SF])
        for (wnm, srcamp, cdst, sdst, dstbuf, uselam) in (
                ("ab_Wqp", ampq, qcT, qsT, None, True),
                ("ab_Wkp", ampk, kcT, ksT, env.SA1a, False)):
            def ev_ph(tc, ph, srcamp=srcamp, cdst=cdst, sdst=sdst,
                      dstbuf=dstbuf, uselam=uselam, g=g):
                cbuf = env.SA0a if uselam else env.SA1a
                sbuf_ = env.SA0b if uselam else env.SA1a
                cs = slice(tc * 512, (tc + 1) * 512)
                P.op("scalar", lambda e, ph=ph: e.activation(
                    out=env.scF1[:, 0:512], in_=ph[:], func=AF.Tanh),
                    reads=[ph], writes=[env.scF1])
                P.op("scalar", lambda e: e.activation(
                    out=env.scF2[:, 0:512], in_=env.scF1[:, 0:512], func=AF.Sin,
                    scale=math.pi), reads=[env.scF1], writes=[env.scF2])
                P.op("scalar", lambda e: e.activation(
                    out=env.scF1[:, 512:1024], in_=env.scF1[:, 0:512], func=AF.Sin,
                    scale=math.pi / 2), reads=[env.scF1], writes=[env.scF1])
                P.op("scalar", lambda e: e.activation(
                    out=env.scF2[:, 512:1024], in_=env.scF1[:, 512:1024],
                    func=AF.Square), reads=[env.scF1], writes=[env.scF2])
                P.op("vector", lambda e: e.tensor_scalar(
                    out=env.scF1[:, 512:1024], in0=env.scF2[:, 512:1024],
                    scalar1=-2.0, scalar2=1.0, op0=ALU.mult, op1=ALU.add),
                    reads=[env.scF2], writes=[env.scF1])
                if uselam:
                    P.op("vector", lambda e, cs=cs, g=g: e.scalar_tensor_tensor(
                        out=cdst[:, cs], in0=env.scF1[:, 512:1024],
                        scalar=env.lamrep[:, g:g + 1], in1=srcamp[:, cs],
                        op0=ALU.mult, op1=ALU.mult),
                        reads=[env.scF1, env.lamrep, env.SF], writes=[cbuf])
                    P.op("vector", lambda e, cs=cs, g=g: e.scalar_tensor_tensor(
                        out=sdst[:, cs], in0=env.scF2[:, 0:512],
                        scalar=env.lamrep[:, g:g + 1], in1=srcamp[:, cs],
                        op0=ALU.mult, op1=ALU.mult),
                        reads=[env.scF2, env.lamrep, env.SF], writes=[sbuf_])
                else:
                    P.op("vector", lambda e, cs=cs: e.tensor_tensor(
                        out=cdst[:, cs], in0=env.scF1[:, 512:1024],
                        in1=srcamp[:, cs], op=ALU.mult),
                        reads=[env.scF1, env.SF], writes=[cbuf])
                    P.op("vector", lambda e, cs=cs: e.tensor_tensor(
                        out=sdst[:, cs], in0=env.scF2[:, 0:512],
                        in1=srcamp[:, cs], op=ALU.mult),
                        reads=[env.scF2, env.SF], writes=[sbuf_])
            _proj_fm(P, env, d[wnm][:, g * 128:(g + 1) * 128],
                     env.wslot[0] if uselam else env.wslot[1], ev_ph)
        if stage < 2:
            continue
        parts = ((qT, kT), (qcT, kcT), (qsT, ksT))
        for ic in range(4):
            for hl in range(2):
                rows = slice(64 * hl, 64 * (hl + 1))
                for iw in range(4):
                    ii = ic * 4 + iw
                    jw = (ii + 1) * 128
                    nb = (jw + 511) // 512
                    for jc in range(nb):
                        n = min(512, jw - jc * 512)
                        pA = env.psB[jc]
                        for pi, (qa, kb) in enumerate(parts):
                            P.op("tensor", lambda e, pA=pA, qa=qa, kb=kb, pi=pi,
                                 ii=ii, jc=jc, n=n, rows=rows: e.matmul(
                                     pA[:, 0:n], qa[rows, ii * 128:(ii + 1) * 128],
                                     kb[rows, jc * 512:jc * 512 + n],
                                     start=(pi == 0), stop=(pi == 2)),
                                 reads=[env.SA0a, env.SA0b, env.SA1a], writes=[pA])
                        P.op("vector", lambda e, pA=pA, jc=jc, n=n: e.reduce_max(
                            asc[:, jc:jc + 1], pA[:, 0:n], axis=AX.X),
                            reads=[pA], writes=[asc])
                    for jc in range(1, nb):
                        P.op("vector", lambda e, jc=jc: e.tensor_tensor(
                            out=asc[:, 0:1], in0=asc[:, 0:1], in1=asc[:, jc:jc + 1],
                            op=ALU.max), reads=[asc], writes=[asc], name="mxcomb")
                    P.op("vector", lambda e: e.tensor_scalar(
                        out=asc[:, 5:6], in0=asc[:, 0:1], scalar1=-1.0, scalar2=None,
                        op0=ALU.mult), reads=[asc], writes=[asc])
                    for jc in range(nb):
                        n = min(512, jw - jc * 512)
                        pA = env.psB[jc]
                        full = n if jc < nb - 1 else n - 128
                        if full > 0:
                            P.op("scalar", lambda e, pA=pA, full=full, jc=jc:
                                 e.activation(
                                     out=atnA[:, 0:full], in_=pA[:, 0:full],
                                     func=AF.Exp, bias=asc[:, 5:6],
                                     accum_out=asc[:, 8 + jc:9 + jc]),
                                 reads=[pA, asc], writes=[env.SF, asc])
                        else:
                            P.op("gpsimd", lambda e, jc=jc: e.memset(
                                asc[:, 8 + jc:9 + jc], 0.0), writes=[asc])
                    pA = env.psB[nb - 1]
                    n = jw - (nb - 1) * 512
                    P.op("scalar", lambda e, pA=pA, n=n: e.activation(
                        out=atnB[:, 0:128], in_=pA[:, n - 128:n], func=AF.Exp,
                        bias=asc[:, 5:6]), reads=[pA, asc], writes=[env.SF])
                    P.op("gpsimd", lambda e: e.affine_select(
                        out=atnB[:, 0:128], in_=atnB[:, 0:128],
                        compare_op=ALU.is_ge, fill=0.0, base=0,
                        pattern=[[-1, 128]], channel_multiplier=1),
                        reads=[env.SF], writes=[env.SF])
                    P.op("vector", lambda e: e.reduce_sum(
                        asc[:, 12:13], atnB[:, 0:128], axis=AX.X),
                        reads=[env.SF], writes=[asc])
                    for jc in range(nb):
                        P.op("vector", lambda e, jc=jc: e.tensor_tensor(
                            out=asc[:, 12:13], in0=asc[:, 12:13],
                            in1=asc[:, 8 + jc:9 + jc], op=ALU.add),
                            reads=[asc], writes=[asc], name="sumcomb")
                    P.op("scalar", lambda e: e.activation(
                        out=asc[:, 13:14], in_=asc[:, 12:13], func=AF.Ln),
                        reads=[asc], writes=[asc])
                    P.op("vector", lambda e: e.tensor_tensor(
                        out=asc[:, 14:15], in0=asc[:, 13:14], in1=asc[:, 0:1],
                        op=ALU.add), reads=[asc], writes=[asc])
                    P.op("vector", lambda e, iw=iw: e.tensor_scalar(
                        out=asc[:, 16 + iw:17 + iw], in0=asc[:, 14:15],
                        scalar1=-1.0, scalar2=None, op0=ALU.mult),
                        reads=[asc], writes=[asc])
                ps6 = env.psB[6]
                for iw in range(4):
                    P.op("tensor", lambda e, ps6=ps6, iw=iw: e.transpose(
                        ps6[0:1, iw * 128:(iw + 1) * 128],
                        asc[:, 16 + iw:17 + iw], env.identf[:]),
                        reads=[asc, env.identf], writes=[ps6])
                P.op("vector", lambda e, ps6=ps6: e.tensor_copy(
                    out=env.srow[0:1, :], in_=ps6[0:1, 0:512]),
                    reads=[ps6], writes=[env.srow])
                pav = env.psB[4 + hl]
                njt = 4 * ic + 4
                for jt in range(njt):
                    di = max(0, (jt - 4 * ic)) * 128
                    i0 = ic * 512 + di
                    nn = 512 - di
                    pB = env.psB[jt % 4]
                    for pi, (qa, kb) in enumerate(parts):
                        P.op("tensor", lambda e, pB=pB, qa=qa, kb=kb, pi=pi, jt=jt,
                             i0=i0, nn=nn, rows=rows: e.matmul(
                                 pB[:, 0:nn], kb[rows, jt * 128:(jt + 1) * 128],
                                 qa[rows, i0:i0 + nn], start=(pi == 0), stop=False),
                             reads=[env.SA0a, env.SA0b, env.SA1a], writes=[pB])
                    P.op("tensor", lambda e, pB=pB, i0=i0, nn=nn, ic=ic: e.matmul(
                        pB[:, 0:nn], env.ones_f[0:1, :],
                        env.srow[0:1, i0 - ic * 512:i0 - ic * 512 + nn],
                        start=False, stop=True),
                        reads=[env.ones_f, env.srow], writes=[pB])
                    at = atnA if jt % 2 == 0 else atnB
                    P.op("scalar", lambda e, pB=pB, at=at, nn=nn: e.activation(
                        out=at[:, 0:nn], in_=pB[:, 0:nn], func=AF.Exp),
                        reads=[pB], writes=[env.SF])
                    if jt >= 4 * ic:
                        P.op("gpsimd", lambda e, at=at: e.affine_select(
                            out=at[:, 0:128], in_=at[:, 0:128],
                            compare_op=ALU.is_ge, fill=0.0, base=0,
                            pattern=[[1, 128]], channel_multiplier=-1),
                            reads=[env.SF], writes=[env.SF])
                    P.op("tensor", lambda e, pav=pav, at=at, jt=jt, di=di, nn=nn,
                         hl=hl: e.matmul(
                             pav[64 * hl:64 * hl + 64, di:di + nn],
                             vtm[:, jt * 128 + 64 * hl:jt * 128 + 64 * hl + 64],
                             at[:, 0:nn], start=(jt == 0 and di == 0),
                             stop=(jt == njt - 1)),
                         reads=[env.SA1b, env.SF], writes=[pav])
                if hl == 1:
                    for hx in range(2):
                        pvx = env.psB[4 + hx]
                        P.op("scalar", lambda e, pvx=pvx, ic=ic, g=g, hx=hx:
                             e.activation(
                                 out=env.SB[64 * hx:64 * (hx + 1),
                                            g * 2048 + ic * 512:g * 2048 + (ic + 1) * 512],
                                 in_=pvx[64 * hx:64 * (hx + 1), :], func=AF.Copy),
                             reads=[pvx], writes=[env.SB])
    if stage < 3:
        return
    for itc in range(4):
        for j in range(ND):
            ws = env.wslot[j % 2]
            P.dma("gpsimd", ws[:], d["ab_Wo"][j * 128:(j + 1) * 128, :],
                  writes=[ws], name="wold")
            for iw in range(4):
                it = itc * 4 + iw
                for h in range(2):
                    po = env.psB[iw * 2 + h]
                    P.op("tensor", lambda e, po=po, j=j, it=it, h=h, ws=ws: e.matmul(
                        po[:], env.SB[:, j * 2048 + it * 128:j * 2048 + (it + 1) * 128],
                        ws[:, h * 512:(h + 1) * 512],
                        start=(j == 0), stop=(j == ND - 1)),
                        reads=[env.SB, ws], writes=[po])
        for iw in range(4):
            it = itc * 4 + iw
            for h in range(2):
                po = env.psB[iw * 2 + h]
                P.op("vector", lambda e, po=po, h=h: e.tensor_copy(
                    out=env.scF1[:, h * 512:(h + 1) * 512], in_=po[:]),
                    reads=[po], writes=[env.scF1])
            xi = env.x[it]
            P.op("gpsimd", lambda e, xi=xi: e.tensor_tensor(
                out=xi[:], in0=xi[:], in1=env.scF1[:], op=ALU.add),
                reads=[xi, env.scF1], writes=[xi])




def spec_l1attnb():
    def t(key):
        return lambda i, key=key: np.ascontiguousarray(np.asarray(i[key]).T)

    return {
        "bb_W1qT": ((1024, 64), F32, t("L1_attn_b_W1_q")),
        "bb_W2qT": ((1024, 64), F32, t("L1_attn_b_W2_q")),
        "bb_W1kT": ((1024, 64), F32, t("L1_attn_b_W1_k")),
        "bb_W2kT": ((1024, 64), F32, t("L1_attn_b_W2_k")),
        "bb_WvT": ((1024, 1024), F32, t("L1_attn_b_W_v")),
        "bb_WoT": ((1024, 1024), F32, t("L1_attn_b_W_out")),
        "bb_bv": ((1, 1024), F32, lambda i: np.ascontiguousarray(
            np.asarray(i["L1_attn_b_b_v"]).reshape(1, 1024))),
        "bb_bout": ((1, 1024), F32, lambda i: np.ascontiguousarray(
            np.asarray(i["L1_attn_b_b_out"]).reshape(1, 1024))),
    }


_PAIRS = [(0, 1), (0, 2), (0, 3), (1, 2), (1, 3), (2, 3)]
_JMAP = [(5, 1.0), (4, -1.0), (3, 1.0), (2, 1.0), (1, -1.0), (0, 1.0)]


def build_l1attnb(P, env, stage=99):
    layernorm_to_xnT(P, env)
    d = env.dram
    SFbf = env.SF[:].bitcast(BF16)

    stg = [(env.SA1a[0:8, 0:2048], env.SA1a[0:8, 2048:4096], env.SA1a),
           (env.SA1b[0:8, 0:2048], env.SA1b[0:8, 2048:4096], env.SA1b)]
    for (sl, wa, wb) in ((env.wslot[0], "bb_W1qT", "bb_W2qT"),
                         (env.wslot[1], "bb_W1kT", "bb_W2kT")):
        for (off, w) in ((0, wa), (512, wb)):
            P.dma("gpsimd", sl[:, off:off + 512].rearrange("p (k m) -> p k m", k=8),
                  d[w][:, :].rearrange("(k p) m -> p k m", p=128), writes=[sl],
                  name="bbw")
    LqT = env.SA0a[:, 0:2048]
    JkT = env.SA0a[:, 2048:4096]
    for tt in range(NT):
        for (pi, (sl, off)) in enumerate(((env.wslot[0], 0), (env.wslot[0], 512),
                                          (env.wslot[1], 0), (env.wslot[1], 512))):
            pq = env.psB[pi]
            for k in range(ND):
                P.op("tensor", lambda e, pq=pq, sl=sl, off=off, k=k, tt=tt: e.matmul(
                    pq[:, 0:64], xnt(env, k, tt * 128, (tt + 1) * 128),
                    sl[:, off + k * 64:off + (k + 1) * 64],
                    start=(k == 0), stop=(k == ND - 1)),
                    reads=[sl, env.xnT], writes=[pq])
        for pi in range(4):
            P.op("vector", lambda e, pi=pi: e.tensor_copy(
                out=env.scF2[:, 512 + pi * 64:512 + (pi + 1) * 64],
                in_=env.psB[pi][:, 0:64]), reads=[env.psB[pi]], writes=[env.scF2])
        for (pa, pb_, dst0, side) in ((env.scF2[:, 512:576], env.scF2[:, 576:640], 0, "q"),
                                      (env.scF2[:, 640:704], env.scF2[:, 704:768], 128, "k")):
            for c, (i, j) in enumerate(_PAIRS):
                if side == "q":
                    dc, sign = c, 1.0
                else:
                    dc = next(m for m, (srcc, _s) in enumerate(_JMAP) if srcc == c)
                    sign = _JMAP[dc][1]
                m1 = env.scF2[:, 256:272]
                m2 = env.scF2[:, 272:288]
                P.op("vector", lambda e, pa=pa, pb_=pb_, i=i, j=j, m1=m1:
                     e.tensor_tensor(out=m1, in0=pa[:, i:64:4], in1=pb_[:, j:64:4],
                                     op=ALU.mult), reads=[env.scF2], writes=[env.scF2])
                P.op("vector", lambda e, pa=pa, pb_=pb_, i=i, j=j, m2=m2:
                     e.tensor_tensor(out=m2, in0=pa[:, j:64:4], in1=pb_[:, i:64:4],
                                     op=ALU.mult), reads=[env.scF2], writes=[env.scF2])
                a_, b_ = (m1, m2) if sign > 0 else (m2, m1)
                P.op("vector", lambda e, a_=a_, b_=b_, dst0=dst0, dc=dc:
                     e.tensor_tensor(out=env.scF1[:, dst0 + dc:dst0 + 128:8],
                                     in0=a_, in1=b_, op=ALU.subtract),
                     reads=[env.scF2], writes=[env.scF1])
            P.op("scalar", lambda e, dst0=dst0: e.activation(
                out=env.scF2[:, 0:128], in_=env.scF1[:, dst0:dst0 + 128],
                func=AF.Square), reads=[env.scF1], writes=[env.scF2])
            sv = env.scF2[:, 0:128].rearrange("p (h c) -> p h c", h=16)[:, :, 0:6]
            P.op("vector", lambda e, sv=sv: e.reduce_sum(
                env.rts[:, 0:16], sv, axis=AX.X), reads=[env.scF2], writes=[env.rts])
            P.op("vector", lambda e: e.tensor_scalar(
                out=env.rts[:, 16:32], in0=env.rts[:, 0:16], scalar1=1e-24,
                scalar2=None, op0=ALU.max), reads=[env.rts], writes=[env.rts])
            P.op("scalar", lambda e: e.activation(
                out=env.rts[:, 0:16], in_=env.rts[:, 16:32], func=AF.Ln),
                reads=[env.rts], writes=[env.rts])
            P.op("scalar", lambda e: e.activation(
                out=env.rts[:, 16:32], in_=env.rts[:, 0:16], func=AF.Exp,
                scale=-0.5), reads=[env.rts], writes=[env.rts])
            if side == "q":
                P.op("vector", lambda e: e.tensor_scalar(
                    out=env.rts[:, 16:32], in0=env.rts[:, 16:32],
                    scalar1=6.0 ** -0.5, scalar2=None, op0=ALU.mult),
                    reads=[env.rts], writes=[env.rts])
            for c in range(6):
                P.op("vector", lambda e, dst0=dst0, c=c: e.tensor_tensor(
                    out=SFbf[:, dst0 + c:dst0 + 128:8],
                    in0=env.scF1[:, dst0 + c:dst0 + 128:8], in1=env.rts[:, 16:32],
                    op=ALU.mult), reads=[env.scF1, env.rts], writes=[env.SF])
            for pc in (6, 7):
                P.op("vector", lambda e, dst0=dst0, pc=pc: e.memset(
                    SFbf[:, dst0 + pc:dst0 + 128:8], 0.0), writes=[env.SF])
        for (src0, dstT) in ((0, LqT), (128, JkT)):
            ptp = env.ps_tp[tt % 2]
            ptpb = ptp[:].bitcast(BF16)
            P.op("tensor", lambda e, src0=src0, ptpb=ptpb: e.transpose(
                ptpb[:, 0:128], SFbf[:, src0:src0 + 128], env.ident[:]),
                reads=[env.SF, env.ident], writes=[ptp])
            P.op("vector", lambda e, dstT=dstT, tt=tt, ptpb=ptpb: e.tensor_copy(
                out=dstT[:, tt * 128:(tt + 1) * 128], in_=ptpb[:, 0:128]),
                reads=[ptp], writes=[env.SA0a])
    if stage < 2:
        return
    atnA = SFbf[:, 4096:5120]
    atnB = SFbf[:, 5120:6144]
    asc = env.rts
    for half in range(2):
        P.dma("sync", env.row_sc[0:1, 0:1024], d["bb_bv"], writes=[env.row_sc])
        P.dma("gpsimd", env.SA1a[:].rearrange("p (k m) -> p k m", k=8),
              d["bb_WvT"][:, half * 512:(half + 1) * 512].rearrange(
                  "(k p) m -> p k m", p=128), writes=[env.SA1a], name="wvld")
        for tt in range(NT):
            pv = env.psB[4 + tt % 2]
            for k in range(ND):
                P.op("tensor", lambda e, pv=pv, k=k, tt=tt: e.matmul(
                    pv[:], xnt(env, k, tt * 128, (tt + 1) * 128),
                    env.SA1a[:, k * 512:(k + 1) * 512],
                    start=(k == 0), stop=False), reads=[env.SA1a, env.xnT],
                    writes=[pv])
            P.op("tensor", lambda e, pv=pv, half=half: e.matmul(
                pv[:], env.ones_f[0:1, :],
                env.row_sc[0:1, half * 512:half * 512 + 512],
                start=False, stop=True), reads=[env.ones_f, env.row_sc], writes=[pv])
            P.op("scalar", lambda e, pv=pv, tt=tt: e.activation(
                out=env.SB[:, tt * 512:(tt + 1) * 512], in_=pv[:], func=AF.Copy),
                reads=[pv], writes=[env.SB])
        for gp in range(4):
            for hl0 in range(2):
                habs0 = 8 * half + 2 * gp + hl0
                P.dma("sync", stg[hl0][0], LqT[8 * habs0:8 * habs0 + 8, :],
                      reads=[env.SA0a], writes=[stg[hl0][2]], name="stgq")
                P.dma("sync", stg[hl0][1], JkT[8 * habs0:8 * habs0 + 8, :],
                      reads=[env.SA0a], writes=[stg[hl0][2]], name="stgk")
            for ic in range(4):
                for hl in range(2):
                    habs = 8 * half + 2 * gp + hl
                    stgQ, stgK, stgB = stg[hl]
                    vcol = (habs % 8) * 64
                    for iw in range(4):
                        ii = ic * 4 + iw
                        jw = (ii + 1) * 128
                        nb = (jw + 511) // 512
                        for jc in range(nb):
                            n = min(512, jw - jc * 512)
                            pA = env.psB[jc]
                            P.op("tensor", lambda e, pA=pA, ii=ii, jc=jc, n=n,
                                 stgQ=stgQ, stgK=stgK:
                                 e.matmul(pA[:, 0:n],
                                          stgQ[:, ii * 128:(ii + 1) * 128],
                                          stgK[:, jc * 512:jc * 512 + n],
                                          start=True, stop=True),
                                 reads=[stgB], writes=[pA])
                            P.op("vector", lambda e, pA=pA, jc=jc, n=n: e.reduce_max(
                                asc[:, jc:jc + 1], pA[:, 0:n], axis=AX.X),
                                reads=[pA], writes=[asc])
                        for jc in range(1, nb):
                            P.op("vector", lambda e, jc=jc: e.tensor_tensor(
                                out=asc[:, 0:1], in0=asc[:, 0:1],
                                in1=asc[:, jc:jc + 1], op=ALU.max),
                                reads=[asc], writes=[asc])
                        P.op("vector", lambda e: e.tensor_scalar(
                            out=asc[:, 5:6], in0=asc[:, 0:1], scalar1=-1.0,
                            scalar2=None, op0=ALU.mult), reads=[asc], writes=[asc])
                        for jc in range(nb):
                            n = min(512, jw - jc * 512)
                            pA = env.psB[jc]
                            full = n if jc < nb - 1 else n - 128
                            if full > 0:
                                P.op("scalar", lambda e, pA=pA, full=full, jc=jc:
                                     e.activation(
                                         out=atnA[:, 0:full], in_=pA[:, 0:full],
                                         func=AF.Exp, bias=asc[:, 5:6],
                                         accum_out=asc[:, 8 + jc:9 + jc]),
                                     reads=[pA, asc], writes=[env.SF, asc])
                            else:
                                P.op("gpsimd", lambda e, jc=jc: e.memset(
                                    asc[:, 8 + jc:9 + jc], 0.0), writes=[asc])
                        pA = env.psB[nb - 1]
                        n = jw - (nb - 1) * 512
                        P.op("scalar", lambda e, pA=pA, n=n: e.activation(
                            out=atnB[:, 0:128], in_=pA[:, n - 128:n], func=AF.Exp,
                            bias=asc[:, 5:6]), reads=[pA, asc], writes=[env.SF])
                        P.op("gpsimd", lambda e: e.affine_select(
                            out=atnB[:, 0:128], in_=atnB[:, 0:128],
                            compare_op=ALU.is_ge, fill=0.0, base=0,
                            pattern=[[-1, 128]], channel_multiplier=1),
                            reads=[env.SF], writes=[env.SF])
                        P.op("vector", lambda e: e.reduce_sum(
                            asc[:, 12:13], atnB[:, 0:128], axis=AX.X),
                            reads=[env.SF], writes=[asc])
                        for jc in range(nb):
                            P.op("vector", lambda e, jc=jc: e.tensor_tensor(
                                out=asc[:, 12:13], in0=asc[:, 12:13],
                                in1=asc[:, 8 + jc:9 + jc], op=ALU.add),
                                reads=[asc], writes=[asc])
                        P.op("scalar", lambda e: e.activation(
                            out=asc[:, 13:14], in_=asc[:, 12:13], func=AF.Ln),
                            reads=[asc], writes=[asc])
                        P.op("vector", lambda e: e.tensor_tensor(
                            out=asc[:, 14:15], in0=asc[:, 13:14], in1=asc[:, 0:1],
                            op=ALU.add), reads=[asc], writes=[asc])
                        P.op("vector", lambda e, iw=iw: e.tensor_scalar(
                            out=asc[:, 16 + iw:17 + iw], in0=asc[:, 14:15],
                            scalar1=-1.0, scalar2=None, op0=ALU.mult),
                            reads=[asc], writes=[asc])
                    ps6 = env.psB[6]
                    for iw in range(4):
                        P.op("tensor", lambda e, ps6=ps6, iw=iw: e.transpose(
                            ps6[0:1, iw * 128:(iw + 1) * 128],
                            asc[:, 16 + iw:17 + iw], env.identf[:]),
                            reads=[asc, env.identf], writes=[ps6])
                    P.op("vector", lambda e, ps6=ps6: e.tensor_copy(
                        out=env.srow[0:1, :], in_=ps6[0:1, 0:512]),
                        reads=[ps6], writes=[env.srow])
                    pav = env.psB[5 - hl]
                    njt = 4 * ic + 4
                    for jt in range(njt):
                        di = max(0, (jt - 4 * ic)) * 128
                        i0 = ic * 512 + di
                        nn = 512 - di
                        pB = env.psB[jt % 4]
                        P.op("tensor", lambda e, pB=pB, jt=jt, i0=i0, nn=nn,
                             stgQ=stgQ, stgK=stgK:
                             e.matmul(pB[:, 0:nn],
                                      stgK[:, jt * 128:(jt + 1) * 128],
                                      stgQ[:, i0:i0 + nn],
                                      start=True, stop=False),
                             reads=[stgB], writes=[pB])
                        P.op("tensor", lambda e, pB=pB, i0=i0, nn=nn, ic=ic:
                             e.matmul(pB[:, 0:nn], env.ones_f[0:1, :],
                                      env.srow[0:1, i0 - ic * 512:i0 - ic * 512 + nn],
                                      start=False, stop=True),
                             reads=[env.ones_f, env.srow], writes=[pB])
                        at = atnA if jt % 2 == 0 else atnB
                        P.op("scalar", lambda e, pB=pB, at=at, nn=nn: e.activation(
                            out=at[:, 0:nn], in_=pB[:, 0:nn], func=AF.Exp),
                            reads=[pB], writes=[env.SF])
                        if jt >= 4 * ic:
                            P.op("gpsimd", lambda e, at=at: e.affine_select(
                                out=at[:, 0:128], in_=at[:, 0:128],
                                compare_op=ALU.is_ge, fill=0.0, base=0,
                                pattern=[[1, 128]], channel_multiplier=-1),
                                reads=[env.SF], writes=[env.SF])
                        P.op("tensor", lambda e, pav=pav, at=at, jt=jt, di=di,
                             nn=nn, hl=hl, vcol=vcol: e.matmul(
                                 pav[64 * hl:64 * hl + 64, di:di + nn],
                                 env.SB[:, jt * 512 + vcol:jt * 512 + vcol + 64],
                                 at[:, 0:nn], start=(jt == 0 and di == 0),
                                 stop=(jt == njt - 1)),
                             reads=[env.SB, env.SF], writes=[pav])
                    if hl == 1:
                        for hx in range(2):
                            pvx = env.psB[5 - hx]
                            P.op("scalar", lambda e, pvx=pvx, ic=ic, gp=gp, hx=hx:
                                 e.activation(
                                     out=env.SB[64 * hx:64 * (hx + 1),
                                                8192 + gp * 2048 + ic * 512:
                                                8192 + gp * 2048 + (ic + 1) * 512],
                                     in_=pvx[64 * hx:64 * (hx + 1), :],
                                     func=AF.Copy),
                                 reads=[pvx], writes=[env.SB])
        if stage < 3:
            continue
        P.dma("sync", env.row_sc[0:1, 0:1024], d["bb_bout"], writes=[env.row_sc],
              name="boutld")
        for itc in range(4):
            for gp in range(4):
                ws = env.wslot[gp % 2]
                P.dma("gpsimd", ws[:],
                      d["bb_WoT"][half * 512 + gp * 128:half * 512 + (gp + 1) * 128, :],
                      writes=[ws], name="wood")
                for iw in range(4):
                    it = itc * 4 + iw
                    for h2 in range(2):
                        po = env.psB[iw * 2 + h2]
                        P.op("tensor", lambda e, po=po, gp=gp, it=it, h2=h2, ws=ws,
                             half=half: e.matmul(
                                 po[:],
                                 env.SB[:, 8192 + gp * 2048 + it * 128:
                                        8192 + gp * 2048 + (it + 1) * 128],
                                 ws[:, h2 * 512:(h2 + 1) * 512],
                                 start=(gp == 0), stop=(half == 1 and gp == 3)),
                             reads=[env.SB, ws], writes=[po])
            for iw in range(4):
                it = itc * 4 + iw
                for h2 in range(2):
                    po = env.psB[iw * 2 + h2]
                    if half == 0:
                        P.op("tensor", lambda e, po=po, h2=h2: e.matmul(
                            po[:], env.ones_f[0:1, :],
                            env.row_sc[0:1, h2 * 512:h2 * 512 + 512],
                            start=False, stop=True),
                            reads=[env.ones_f, env.row_sc], writes=[po])
                    P.op("vector", lambda e, po=po, h2=h2: e.tensor_copy(
                        out=env.scF1[:, h2 * 512:(h2 + 1) * 512], in_=po[:]),
                        reads=[po], writes=[env.scF1])
                xi = env.x[it]
                P.op("gpsimd", lambda e, xi=xi: e.tensor_tensor(
                    out=xi[:], in0=xi[:], in1=env.scF1[:], op=ALU.add),
                    reads=[xi, env.scF1], writes=[xi])



def spec_l1attna():
    def raw(key):
        return lambda i, key=key: np.ascontiguousarray(np.asarray(i[key]))

    return {
        "aa_Wqv": ((1024, 2048), F32, raw("L1_attn_a_W_qv")),
        "aa_W20": ((1024, 84), F32, raw("L1_attn_a_W2_0")),
        "aa_W21": ((1024, 164), F32, raw("L1_attn_a_W2_1")),
        "aa_Wout": ((1024, 1024), F32, raw("L1_attn_a_W_out")),
        "aa_sw": ((1, 2), F32, lambda i: np.ascontiguousarray(
            np.asarray(i["L1_attn_a_scale_weights"]).reshape(1, 2))),
    }


def build_l1attna(P, env, stage=99):
    layernorm_to_xnT(P, env)
    d = env.dram
    SFbf = env.SF[:].bitcast(BF16)
    scr = env.aa_scr
    scrbuf = env.aa_scrbuf
    P.dma("sync", env.lam_sb[0:1, 0:2], d["aa_sw"], writes=[env.lam_sb])
    P.op("vector", lambda e: e.tensor_tensor(
        out=env.lam_sb[0:1, 2:3], in0=env.lam_sb[0:1, 0:1],
        in1=env.lam_sb[0:1, 1:2], op=ALU.subtract),
        reads=[env.lam_sb], writes=[env.lam_sb])
    P.op("scalar", lambda e: e.activation(
        out=env.lam_sb[0:1, 3:4], in_=env.lam_sb[0:1, 2:3], func=AF.Sigmoid),
        reads=[env.lam_sb], writes=[env.lam_sb])
    P.op("vector", lambda e: e.tensor_scalar(
        out=env.lam_sb[0:1, 4:5], in0=env.lam_sb[0:1, 3:4], scalar1=-1.0,
        scalar2=1.0, op0=ALU.mult, op1=ALU.add),
        reads=[env.lam_sb], writes=[env.lam_sb])
    pb = env.psB[0]
    for j in range(2):
        P.op("tensor", lambda e, pb=pb, j=j: e.matmul(
            pb[:, j:j + 1], env.ones_f[0:1, :], env.lam_sb[0:1, 3 + j:4 + j],
            start=(j == 0), stop=(j == 1)), reads=[env.ones_f, env.lam_sb],
            writes=[pb])
    P.op("vector", lambda e, pb=pb: e.tensor_copy(
        out=env.lamrep[:, 6:8], in_=pb[:, 0:2]), reads=[pb], writes=[env.lamrep])
    P.op("gpsimd", lambda e: e.memset(SFbf[:, 0:512], 0.0), writes=[env.SF])
    for zz in range(2):
        P.dma("scalar", scr[zz * 136:zz * 136 + 128, :], SFbf[:, 0:512],
              reads=[env.SF], writes=[scrbuf], name="scrz0")
        P.dma("scalar", scr[zz * 136 + 128:zz * 136 + 136, :], SFbf[0:8, 0:512],
              reads=[env.SF], writes=[scrbuf], name="scrz1")

    for hh in range(2):
        buf = env.SA1a if hh == 0 else env.SA1b
        P.dma("gpsimd", buf[:].rearrange("p (k m) -> p k m", k=4),
              d["aa_Wqv"][hh * 512:(hh + 1) * 512, 1024:2048].rearrange(
                  "(k p) m -> p k m", p=128), writes=[buf], name="wvld")
    for tt in range(NT):
        for h2 in range(2):
            pv = env.psB[4 + h2]
            for k in range(ND):
                buf = env.SA1a if k < 4 else env.SA1b
                P.op("tensor", lambda e, pv=pv, k=k, tt=tt, h2=h2, buf=buf: e.matmul(
                    pv[:], xnt(env, k, tt * 128, (tt + 1) * 128),
                    buf[:, (k % 4) * 1024 + h2 * 512:(k % 4) * 1024 + h2 * 512 + 512],
                    start=(k == 0), stop=(k == ND - 1)),
                    reads=[buf, env.xnT], writes=[pv])
            P.op("scalar", lambda e, pv=pv, tt=tt, h2=h2: e.activation(
                out=env.SB[:, tt * 1024 + h2 * 512:tt * 1024 + h2 * 512 + 512],
                in_=pv[:], func=AF.Copy), reads=[pv], writes=[env.SB])
    qr = ([env.SA0a[:, m * 2048:(m + 1) * 2048] for m in range(2)]
          + [env.SA0b[:, m * 2048:(m + 1) * 2048] for m in range(2)]
          + [env.SA1a[:, m * 2048:(m + 1) * 2048] for m in range(2)]
          + [env.SA1b[:, m * 2048:(m + 1) * 2048] for m in range(2)])
    qrb = [env.SA0a] * 2 + [env.SA0b] * 2 + [env.SA1a] * 2 + [env.SA1b] * 2
    for m in range(8):
        ws = env.wslot[m % 2]
        P.dma("gpsimd", ws[:].rearrange("p (k f) -> p k f", k=8),
              d["aa_Wqv"][:, m * 128:(m + 1) * 128].rearrange(
                  "(k p) f -> p k f", p=128), writes=[ws], name="wqld")
        for tc in range(4):
            ph = env.psB[tc % 2]
            for k in range(ND):
                P.op("tensor", lambda e, ph=ph, ws=ws, k=k, tc=tc: e.matmul(
                    ph[:], ws[:, k * 128:(k + 1) * 128],
                    xnt(env, k, tc * 512, (tc + 1) * 512),
                    start=(k == 0), stop=(k == ND - 1)),
                    reads=[ws, env.xnT], writes=[ph])
            P.op("scalar", lambda e, ph=ph, m=m, tc=tc: e.activation(
                out=qr[m][:, tc * 512:(tc + 1) * 512], in_=ph[:], func=AF.Relu),
                reads=[ph], writes=[qrb[m]])
    if stage < 2:
        return
    W2sb = {0: SFbf[:, 1040:1712], 1: SFbf[:, 1712:3024]}
    for (ci, wnm, wc) in ((0, "aa_W20", 84), (1, "aa_W21", 164)):
        for m in range(8):
            P.dma("gpsimd", W2sb[ci][:, m * wc:(m + 1) * wc],
                  d[wnm][m * 128:(m + 1) * 128, :], writes=[env.SF], name="w2ld")
    attnw = SFbf[:, 0:656]
    band = SFbf[:, 656:1040]
    bandT = SFbf[:, 7120:7504]
    CSZ = ((0, 21, 10), (1, 41, 20))
    for itc in range(4):
        for iw in range(4):
            tt = itc * 4 + iw
            first = {4: True, 5: True}
            last_of = {}
            for (ci, c, half) in CSZ:
                for h in range(4):
                    for jls in range(3):
                        jt = tt - 1 + jls
                        if 0 <= jt < NT:
                            for m2 in range(2):
                                dti = 2 * h + m2
                                last_of[4 + dti // 4] = (ci, h, jls, m2)
            for (ci, c, half) in CSZ:
                pl = env.psB[2 + ci]
                wc = 84 if ci == 0 else 164
                for m in range(8):
                    P.op("tensor", lambda e, pl=pl, m=m, tt=tt, ci=ci, wc=wc:
                         e.matmul(pl[:, 0:wc],
                                  qr[m][:, tt * 128:(tt + 1) * 128],
                                  W2sb[ci][:, m * wc:(m + 1) * wc],
                                  start=(m == 0), stop=(m == ND - 1)),
                         reads=[qrb[m], env.SF], writes=[pl])
                P.op("vector", lambda e, pl=pl, wc=wc: e.tensor_copy(
                    out=env.scF1[:, 0:wc], in_=pl[:, 0:wc]), reads=[pl],
                    writes=[env.scF1])
                for h in range(4):
                    cs = slice(h * c, (h + 1) * c)
                    P.op("vector", lambda e, cs=cs: e.reduce_max(
                        env.rts[:, 0:1], env.scF1[:, cs], axis=AX.X),
                        reads=[env.scF1], writes=[env.rts])
                    P.op("vector", lambda e: e.tensor_scalar(
                        out=env.rts[:, 1:2], in0=env.rts[:, 0:1], scalar1=-1.0,
                        scalar2=None, op0=ALU.mult), reads=[env.rts],
                        writes=[env.rts])
                    P.op("scalar", lambda e, cs=cs, c=c: e.activation(
                        out=env.scF2[:, 0:c], in_=env.scF1[:, cs], func=AF.Exp,
                        bias=env.rts[:, 1:2], accum_out=env.rts[:, 2:3]),
                        reads=[env.scF1, env.rts], writes=[env.scF2, env.rts])
                    P.op("vector", lambda e: e.reciprocal(
                        out=env.rts[:, 3:4], in_=env.rts[:, 2:3]),
                        reads=[env.rts], writes=[env.rts])
                    P.op("vector", lambda e, cs=cs, c=c, ci=ci: e.tensor_scalar(
                        out=attnw[:, cs], in0=env.scF2[:, 0:c],
                        scalar1=env.rts[:, 3:4], scalar2=env.lamrep[:, 6 + ci:7 + ci],
                        op0=ALU.mult, op1=ALU.mult),
                        reads=[env.scF2, env.rts, env.lamrep], writes=[env.SF])
                for h in range(4):
                    P.dma("scalar", bass.AP(
                        tensor=scr.tensor, offset=(ci * 136 + 1) * 512,
                        ap=[[512, 128], [1, c]]),
                        attnw[:, h * c:(h + 1) * c],
                        reads=[env.SF], writes=[scrbuf], name="bandout")
                    P.dma("scalar", band,
                          bass.AP(tensor=scr.tensor,
                                  offset=ci * 136 * 512 + 384 + half,
                                  ap=[[511, 128], [1, 384]]),
                          reads=[scrbuf], writes=[env.SF], name="bandin")
                    for jls in range(3):
                        ptp = env.psB[2 + jls % 2]
                        ptpb = ptp[:].bitcast(BF16)
                        P.op("tensor", lambda e, ptpb=ptpb, jls=jls: e.transpose(
                            ptpb[:, 0:128], band[:, jls * 128:(jls + 1) * 128],
                            env.ident[:]), reads=[env.SF, env.ident],
                            writes=[env.psB[2 + jls % 2]])
                        P.op("vector", lambda e, ptpb=ptpb, jls=jls: e.tensor_copy(
                            out=bandT[:, jls * 128:(jls + 1) * 128],
                            in_=ptpb[:, 0:128]),
                            reads=[env.psB[2 + jls % 2]], writes=[env.SF])
                        jt = tt - 1 + jls
                        if not (0 <= jt < NT):
                            continue
                        for m2 in range(2):
                            dti = 2 * h + m2
                            bank = env.psB[4 + dti // 4]
                            col = (dti % 4) * 128
                            st_ = first[4 + dti // 4]
                            first[4 + dti // 4] = False
                            sp_ = (last_of[4 + dti // 4] == (ci, h, jls, m2))
                            P.op("tensor", lambda e, bank=bank, col=col, jt=jt,
                                 h=h, m2=m2, jls=jls, st_=st_, sp_=sp_: e.matmul(
                                     bank[:, col:col + 128],
                                     env.SB[:, jt * 1024 + h * 256 + m2 * 128:
                                            jt * 1024 + h * 256 + m2 * 128 + 128],
                                     bandT[:, jls * 128:(jls + 1) * 128],
                                     start=st_, stop=sp_),
                                 reads=[env.SB, env.SF], writes=[bank])
            for b2 in range(2):
                P.op("vector", lambda e, b2=b2, iw=iw: e.tensor_copy(
                    out=SFbf[:, 3024 + iw * 1024 + b2 * 512:
                             3024 + iw * 1024 + b2 * 512 + 512],
                    in_=env.psB[4 + b2][:]), reads=[env.psB[4 + b2]],
                    writes=[env.SF])
        for k in range(ND):
            ws = env.wslot[k % 2]
            P.dma("gpsimd", ws[:], d["aa_Wout"][k * 128:(k + 1) * 128, :],
                  writes=[ws], name="wold")
            for iw in range(4):
                for h2 in range(2):
                    po = env.psB[iw * 2 + h2]
                    P.op("tensor", lambda e, po=po, k=k, iw=iw, h2=h2, ws=ws:
                         e.matmul(po[:],
                                  SFbf[:, 3024 + iw * 1024 + k * 128:
                                       3024 + iw * 1024 + (k + 1) * 128],
                                  ws[:, h2 * 512:(h2 + 1) * 512],
                                  start=(k == 0), stop=(k == ND - 1)),
                         reads=[env.SF, ws], writes=[po])
        for iw in range(4):
            it = itc * 4 + iw
            for h2 in range(2):
                po = env.psB[iw * 2 + h2]
                P.op("vector", lambda e, po=po, h2=h2: e.tensor_copy(
                    out=env.scF1[:, h2 * 512:(h2 + 1) * 512], in_=po[:]),
                    reads=[po], writes=[env.scF1])
            xi = env.x[it]
            P.op("gpsimd", lambda e, xi=xi: e.tensor_tensor(
                out=xi[:], in0=xi[:], in1=env.scF1[:], op=ALU.add),
                reads=[xi, env.scF1], writes=[xi])



MTC = 256
MNC = T_SEQ // MTC


def spec_l0attna():
    def f(key, fn):
        return lambda i, key=key, fn=fn: np.ascontiguousarray(fn(np.asarray(i[key])))

    def tile16(a):
        return a.reshape(16, 128).T

    return {
        "ma_inpT": ((1024, 4096), F32, f("L0_attn_a_h_in_proj_w", lambda a: a.T)),
        "ma_convw": ((128, 64), F32, f("L0_attn_a_h_conv_w", lambda a:
            a.reshape(16, 128, 4).transpose(1, 0, 2).reshape(128, 64))),
        "ma_convb": ((128, 16), F32, f("L0_attn_a_h_conv_b", tile16)),
        "ma_dtb": ((128, 16), F32, f("L0_attn_a_h_dt_proj_b", tile16)),
        "ma_D": ((128, 16), F32, f("L0_attn_a_h_D_param", tile16)),
        "ma_xpwT": ((2048, 96), F32, f("L0_attn_a_h_x_proj_w", lambda a: a.T)),
        "ma_dtwT": ((64, 2048), F32, f("L0_attn_a_h_dt_proj_w", lambda a: a.T)),
        "ma_AlogT": ((128, 256), F32, f("L0_attn_a_h_A_log", lambda a:
            a.reshape(256, 8, 16).transpose(1, 2, 0).reshape(128, 256))),
        "ma_opwT": ((2048, 1024), F32, f("L0_attn_a_h_out_proj_w", lambda a: a.T)),
    }


def build_l0attna(P, env, stage=99):
    layernorm_to_xnT(P, env)
    d = env.dram
    SFbf = env.SF[:].bitcast(BF16)
    S1bF = env.SA1b[:].bitcast(F32)
    A_sb = S1bF[:, 0:256]
    hcar = S1bF[:, 256:512]
    halo = env.SA1b[:, 1024:1072]
    selD = env.SA0b[:, 0:2048]
    selR = env.SA0b[:, 2048:4096]
    if env.once("masmall"):
        env.ma_cw = P.sb("s_ma_cw", [128, 64], F32)
        env.ma_cb = P.sb("s_ma_cb", [128, 16], F32)
        env.ma_dtb = P.sb("s_ma_dtb", [128, 16], F32)
        env.ma_Dp = P.sb("s_ma_Dp", [128, 16], F32)
        env.ma_selB = P.sb("s_ma_selB", [32, 256], BF16)
    P.dma("sync", env.ma_cw[:], d["ma_convw"], writes=[env.ma_cw])
    P.dma("sync", env.ma_cb[:], d["ma_convb"], writes=[env.ma_cb])
    P.dma("sync", env.ma_dtb[:], d["ma_dtb"], writes=[env.ma_dtb])
    P.dma("sync", env.ma_Dp[:], d["ma_D"], writes=[env.ma_Dp])
    P.dma("sync", A_sb, d["ma_AlogT"], writes=[env.SA1b])
    P.op("scalar", lambda e: e.activation(out=A_sb, in_=A_sb, func=AF.Exp),
         reads=[env.SA1b], writes=[env.SA1b])
    P.op("vector", lambda e: e.tensor_scalar(
        out=A_sb, in0=A_sb, scalar1=-1.0, scalar2=None, op0=ALU.mult),
        reads=[env.SA1b], writes=[env.SA1b])
    P.op("gpsimd", lambda e: e.memset(hcar, 0.0), writes=[env.SA1b])
    P.op("gpsimd", lambda e: e.memset(halo, 0.0), writes=[env.SA1b])
    P.op("gpsimd", lambda e: e.memset(env.ma_selB[:], 0.0), writes=[env.ma_selB])
    for which in range(2):
        P.op("gpsimd", lambda e, which=which: e.affine_select(
            out=env.ma_selB[:, which * 128:(which + 1) * 128].rearrange(
                "s (dlo n) -> s dlo n", n=16),
            in_=env.ma_selB[:, which * 128:(which + 1) * 128].rearrange(
                "s (dlo n) -> s dlo n", n=16),
            compare_op=ALU.not_equal, fill=1.0, base=-16 * which,
            pattern=[[0, 8], [-1, 16]], channel_multiplier=1),
            reads=[env.ma_selB], writes=[env.ma_selB])
    for r8 in range(16):
        sl = selD[:, r8 * 128:(r8 + 1) * 128]
        P.op("gpsimd", lambda e, sl=sl: e.memset(sl, 0.0), writes=[env.SA0b])
        P.op("gpsimd", lambda e, sl=sl, r8=r8: e.affine_select(
            out=sl.rearrange("s (dlo n) -> s dlo n", n=16),
            in_=sl.rearrange("s (dlo n) -> s dlo n", n=16),
            compare_op=ALU.not_equal, fill=1.0, base=-8 * r8,
            pattern=[[-1, 8], [0, 16]], channel_multiplier=1),
            reads=[env.SA0b], writes=[env.SA0b])
        sr = selR[:, r8 * 128:(r8 + 1) * 128]
        P.op("gpsimd", lambda e, sr=sr: e.memset(sr, 1.0), writes=[env.SA0b])
        P.op("gpsimd", lambda e, sr=sr, r8=r8: e.affine_select(
            out=sr, in_=sr, compare_op=ALU.is_ge, fill=0.0, base=16 * 8 * r8,
            pattern=[[-16, 128]], channel_multiplier=1),
            reads=[env.SA0b], writes=[env.SA0b])
        P.op("gpsimd", lambda e, sr=sr, r8=r8: e.affine_select(
            out=sr, in_=sr, compare_op=ALU.is_ge, fill=0.0,
            base=15 - 16 * 8 * r8, pattern=[[16, 128]], channel_multiplier=-1),
            reads=[env.SA0b], writes=[env.SA0b])
    if stage < 1:
        return

    xin = [env.SA0a[:, dt * MTC:(dt + 1) * MTC] for dt in range(16)]
    zt = [env.SA1a[:, dt * MTC:(dt + 1) * MTC] for dt in range(16)]
    ut = [env.SB[:, dt * MTC:(dt + 1) * MTC] for dt in range(16)]
    yt = [env.SB[:, 4096 + dt * MTC:4096 + (dt + 1) * MTC] for dt in range(16)]
    dl = [SFbf[:, dt * MTC:(dt + 1) * MTC] for dt in range(16)]
    xdbl = env.wslot[1][0:96, 0:MTC]
    brep = env.wslot[1][:, 256:512]
    crep = env.wslot[0][:, 512:768]
    dtw_t = [env.wslot[1][0:64, 512 + j * 128:512 + (j + 1) * 128] for j in range(2)]
    xpw_t = [env.wslot[0][:, 768 + j * 128:768 + (j + 1) * 128] for j in range(2)]
    du = env.wslot[0][:, 0:256]
    slz = env.wslot[0][:, 256:512]
    uacc = env.scF1[:, 0:256]
    a_f = env.scF1[:, 256:512]
    b_f = env.scF1[:, 512:768]
    h_f = env.scF2[:, 0:256]
    hC = env.scF2[:].bitcast(BF16)[:, 512:768]
    S2b = env.scF2

    for c in range(MNC):
        t0 = c * MTC
        for et in range(32):
            ws = env.wslot[et % 2]
            P.dma("gpsimd", ws[:].rearrange("p (k m) -> p k m", k=8),
                  d["ma_inpT"][:, et * 128:(et + 1) * 128].rearrange(
                      "(k p) m -> p k m", p=128), writes=[ws], name="mainp")
            ph = env.psB[et % 2]
            for k in range(ND):
                P.op("tensor", lambda e, ph=ph, ws=ws, k=k, t0=t0: e.matmul(
                    ph[:, 0:MTC], ws[:, k * 128:(k + 1) * 128],
                    xnt(env, k, t0, t0 + MTC),
                    start=(k == 0), stop=(k == ND - 1)),
                    reads=[ws, env.xnT], writes=[ph])
            dst = xin[et] if et < 16 else zt[et - 16]
            dstb = env.SA0a if et < 16 else env.SA1a
            P.op("scalar", lambda e, ph=ph, dst=dst: e.activation(
                out=dst, in_=ph[:, 0:MTC], func=AF.Copy),
                reads=[ph], writes=[dstb])
        if stage < 2:
            continue
        uA = env.scF1[:, 0:256]
        uB = env.scF1[:, 256:512]
        for dt in range(16):
            P.op("vector", lambda e, dt=dt, uA=uA: e.tensor_scalar(
                out=uA, in0=xin[dt], scalar1=env.ma_cw[:, dt * 4 + 3:dt * 4 + 4],
                scalar2=None, op0=ALU.mult),
                reads=[env.SA0a, env.ma_cw], writes=[env.scF1])
            cur, nxt = uA, uB
            for j in (2, 1, 0):
                s = 3 - j
                P.op("vector", lambda e, dt=dt, j=j, s=s, cur=cur, nxt=nxt:
                     e.scalar_tensor_tensor(
                         out=nxt[:, s:MTC], in0=xin[dt][:, 0:MTC - s],
                         scalar=env.ma_cw[:, dt * 4 + j:dt * 4 + j + 1],
                         in1=cur[:, s:MTC], op0=ALU.mult, op1=ALU.add),
                     reads=[env.SA0a, env.ma_cw, env.scF1], writes=[env.scF1])
                P.op("vector", lambda e, dt=dt, j=j, s=s, cur=cur, nxt=nxt:
                     e.scalar_tensor_tensor(
                         out=nxt[:, 0:s], in0=halo[:, dt * 3 + 3 - s:dt * 3 + 3],
                         scalar=env.ma_cw[:, dt * 4 + j:dt * 4 + j + 1],
                         in1=cur[:, 0:s], op0=ALU.mult, op1=ALU.add),
                     reads=[env.SA1b, env.ma_cw, env.scF1], writes=[env.scF1])
                cur, nxt = nxt, cur
            P.op("scalar", lambda e, dt=dt, cur=cur: e.activation(
                out=ut[dt], in_=cur, func=AF.Silu,
                bias=env.ma_cb[:, dt:dt + 1]),
                reads=[env.scF1, env.ma_cb], writes=[env.SB])
        for dt in range(16):
            P.op("gpsimd", lambda e, dt=dt: e.tensor_copy(
                out=halo[:, dt * 3:dt * 3 + 3], in_=xin[dt][:, MTC - 3:MTC]),
                reads=[env.SA0a], writes=[env.SA1b])
        if stage < 3:
            continue
        px = env.psB[2]
        for dt in range(16):
            xp = xpw_t[dt % 2]
            P.dma("gpsimd", xp[:, 0:96], d["ma_xpwT"][dt * 128:(dt + 1) * 128, :],
                  writes=[env.wslot[0]], name="maxpw")
            P.op("tensor", lambda e, px=px, xp=xp, dt=dt: e.matmul(
                px[0:96, 0:MTC], xp[:, 0:96], ut[dt],
                start=(dt == 0), stop=(dt == 15)),
                reads=[env.wslot[0], env.SB], writes=[px])
        P.op("scalar", lambda e, px=px: e.activation(
            out=env.wslot[1][0:64, 0:MTC], in_=px[0:64, 0:MTC], func=AF.Copy),
            reads=[px], writes=[env.wslot[1]])
        BC_sb = env.wslot[1][0:32, 768:1024]
        P.op("scalar", lambda e, px=px, BC_sb=BC_sb: e.activation(
            out=BC_sb, in_=px[64:96, 0:MTC], func=AF.Copy),
            reads=[px], writes=[env.wslot[1]])
        for (rep, which) in ((brep, 0), (crep, 1)):
            pr = env.psB[3]
            P.op("tensor", lambda e, pr=pr, which=which, BC_sb=BC_sb: e.matmul(
                pr[:, 0:MTC], env.ma_selB[:, which * 128:(which + 1) * 128],
                BC_sb, start=True, stop=True),
                reads=[env.ma_selB, env.wslot[1]], writes=[pr])
            P.op("vector", lambda e, pr=pr, rep=rep: e.tensor_copy(
                out=rep, in_=pr[:, 0:MTC]), reads=[pr],
                writes=[env.wslot[1] if rep is brep else env.wslot[0]])
        for dt in range(16):
            dwt = dtw_t[dt % 2]
            P.dma("gpsimd", dwt, d["ma_dtwT"][:, dt * 128:(dt + 1) * 128],
                  writes=[env.wslot[1]], name="madtw")
            pd = env.psB[3]
            P.op("tensor", lambda e, pd=pd, dwt=dwt: e.matmul(
                pd[:, 0:MTC], dwt, env.wslot[1][0:64, 0:MTC],
                start=True, stop=True),
                reads=[env.wslot[1]], writes=[pd])
            P.op("scalar", lambda e, pd=pd, dt=dt: e.activation(
                out=S2b[:, 256:512], in_=pd[:, 0:MTC], func=AF.Abs,
                bias=env.ma_dtb[:, dt:dt + 1]),
                reads=[pd, env.ma_dtb], writes=[env.scF2])
            P.op("scalar", lambda e: e.activation(
                out=S2b[:, 512:768], in_=S2b[:, 256:512], func=AF.Exp, scale=-1.0),
                reads=[env.scF2], writes=[env.scF2])
            P.op("scalar", lambda e: e.activation(
                out=S2b[:, 256:512], in_=S2b[:, 512:768], func=AF.Ln,
                bias=env.cst[:, 1:2]), reads=[env.scF2, env.cst], writes=[env.scF2])
            P.op("vector", lambda e, pd=pd, dt=dt: e.tensor_scalar(
                out=S2b[:, 512:768], in0=pd[:, 0:MTC],
                scalar1=env.ma_dtb[:, dt:dt + 1], scalar2=0.0,
                op0=ALU.add, op1=ALU.max),
                reads=[pd, env.ma_dtb], writes=[env.scF2])
            P.op("vector", lambda e, dt=dt: e.tensor_tensor(
                out=dl[dt], in0=S2b[:, 256:512], in1=S2b[:, 512:768], op=ALU.add),
                reads=[env.scF2], writes=[env.SF])
        if stage < 4:
            continue
        for dt in range(16):
            P.op("vector", lambda e, dt=dt: e.tensor_tensor(
                out=du, in0=dl[dt], in1=ut[dt], op=ALU.mult),
                reads=[env.SF, env.SB], writes=[env.wslot[0]])
            yps = env.psB[2]
            for gg in range(16):
                g = dt * 16 + gg
                pr1 = env.psB[4 + 2 * (gg % 2)]
                pr2 = env.psB[5 + 2 * (gg % 2)]
                sl = selD[:, (g % 16) * 128:(g % 16 + 1) * 128]
                P.op("tensor", lambda e, pr1=pr1, sl=sl, dt=dt: e.matmul(
                    pr1[:, 0:MTC], sl, dl[dt], start=True, stop=True),
                    reads=[env.SA0b, env.SF], writes=[pr1])
                P.op("tensor", lambda e, pr2=pr2, sl=sl: e.matmul(
                    pr2[:, 0:MTC], sl, du, start=True, stop=True),
                    reads=[env.SA0b, env.wslot[0]], writes=[pr2])
                P.op("scalar", lambda e, pr1=pr1, g=g: e.activation(
                    out=a_f, in_=pr1[:, 0:MTC], func=AF.Exp,
                    scale=A_sb[:, g:g + 1]),
                    reads=[pr1, env.SA1b], writes=[env.scF1])
                P.op("vector", lambda e, pr2=pr2: e.tensor_tensor(
                    out=b_f, in0=pr2[:, 0:MTC], in1=brep, op=ALU.mult),
                    reads=[pr2, env.wslot[1]], writes=[env.scF1])
                P.op("vector", lambda e, g=g: e.tensor_tensor_scan(
                    out=h_f, data0=a_f, data1=b_f, initial=hcar[:, g:g + 1],
                    op0=ALU.mult, op1=ALU.add),
                    reads=[env.scF1, env.SA1b], writes=[env.scF2])
                P.op("vector", lambda e, g=g: e.tensor_copy(
                    out=hcar[:, g:g + 1], in_=h_f[:, MTC - 1:MTC]),
                    reads=[env.scF2], writes=[env.SA1b])
                P.op("gpsimd", lambda e: e.tensor_tensor(
                    out=hC, in0=h_f, in1=crep, op=ALU.mult),
                    reads=[env.scF2, env.wslot[0]], writes=[env.scF2])
                P.op("tensor", lambda e, yps=yps, gg=gg: e.matmul(
                    yps[:, 0:MTC], selR[:, gg * 128:(gg + 1) * 128], hC,
                    start=(gg == 0), stop=(gg == 15)),
                    reads=[env.SA0b, env.scF2], writes=[yps])
            P.op("scalar", lambda e, dt=dt: e.activation(
                out=slz, in_=zt[dt], func=AF.Silu),
                reads=[env.SA1a], writes=[env.wslot[0]])
            P.op("vector", lambda e, yps=yps, dt=dt: e.scalar_tensor_tensor(
                out=a_f, in0=ut[dt], scalar=env.ma_Dp[:, dt:dt + 1], in1=yps[:, 0:MTC],
                op0=ALU.mult, op1=ALU.add),
                reads=[env.SB, env.ma_Dp, yps], writes=[env.scF1])
            P.op("vector", lambda e, dt=dt: e.tensor_tensor(
                out=yt[dt], in0=a_f, in1=slz, op=ALU.mult),
                reads=[env.scF1, env.wslot[0]], writes=[env.SB])
        if stage < 5:
            continue
        for k in range(16):
            ws = env.wslot[k % 2]
            P.dma("gpsimd", ws[:], d["ma_opwT"][k * 128:(k + 1) * 128, :],
                  writes=[ws], name="maopw")
            for tt2 in range(2):
                for h2 in range(2):
                    po = env.psB[tt2 * 2 + h2]
                    P.op("tensor", lambda e, po=po, ws=ws, k=k, tt2=tt2, h2=h2:
                         e.matmul(po[:], yt[k][:, tt2 * 128:(tt2 + 1) * 128],
                                  ws[:, h2 * 512:(h2 + 1) * 512],
                                  start=(k == 0), stop=(k == 15)),
                         reads=[env.SB, ws], writes=[po])
        for tt2 in range(2):
            it = c * 2 + tt2
            for h2 in range(2):
                po = env.psB[tt2 * 2 + h2]
                P.op("vector", lambda e, po=po, h2=h2: e.tensor_copy(
                    out=env.scF1[:, h2 * 512:(h2 + 1) * 512], in_=po[:]),
                    reads=[po], writes=[env.scF1])
            xi = env.x[it]
            P.op("gpsimd", lambda e, xi=xi: e.tensor_tensor(
                out=xi[:], in0=xi[:], in1=env.scF1[:], op=ALU.add),
                reads=[xi, env.scF1], writes=[xi])



def spec_none():
    return {}


def build_lnonly(P, env):
    layernorm_to_xnT(P, env)
    for i in range(NT):
        tm = env.xn_tm[0]
        pstp = env.ps_tp[0]
        pstp_bf = pstp[:].bitcast(BF16)
        for k in range(ND):
            src_ap = env.xnT[:, k * T_SEQ + i * 128:k * T_SEQ + (i + 1) * 128]
            P.op("tensor", lambda e, src_ap=src_ap, pstp_bf=pstp_bf, k=k: e.transpose(
                pstp_bf[:, k * 128:(k + 1) * 128], src_ap, env.ident[:]),
                reads=[env.xnT, env.ident], writes=[pstp])
        xi = env.x[i]
        P.op("vector", lambda e, xi=xi, pstp_bf=pstp_bf: e.tensor_copy(
            out=xi[:], in_=pstp_bf), reads=[pstp], writes=[xi])


SUBLAYERS = {
    0: ("l0attna", spec_l0attna, build_l0attna),
    21: ("maa_s1", spec_l0attna, lambda P, env: build_l0attna(P, env, 1)),
    22: ("maa_s2", spec_l0attna, lambda P, env: build_l0attna(P, env, 2)),
    23: ("maa_s3", spec_l0attna, lambda P, env: build_l0attna(P, env, 3)),
    24: ("maa_s4", spec_l0attna, lambda P, env: build_l0attna(P, env, 4)),
    1: ("l0ffn_a", spec_l0ffn, build_l0ffn),
    3: ("l0ffn_b", spec_l0ffn, build_l0ffn),
    2: ("l0attnb", spec_l0attnb, build_l0attnb),
    31: ("ab_s1", spec_l0attnb, lambda P, env: build_l0attnb(P, env, 1)),
    34: ("ab_dbg", spec_l0attnb, lambda P, env: (build_l0attnb(P, env, 2), [
        P.op("vector", lambda e, i=i: e.tensor_copy(
            out=env.x[i][:], in_=env.SB[:, i * 2048:(i + 1) * 2048].bitcast(F32)),
            reads=[env.SB], writes=[env.x[i]]) for i in range(8)])[0]),
    32: ("ab_s2", spec_l0attnb, lambda P, env: build_l0attnb(P, env, 2)),
    41: ("l0f_s0", spec_l0ffn, lambda P, env: build_l0ffn(P, env, 0)),
    42: ("l0f_s1", spec_l0ffn, lambda P, env: build_l0ffn(P, env, 1)),
    5: ("l1ffn_a", spec_l1ffn, build_l1ffn),
    4: ("l1attna", spec_l1attna, build_l1attna),
    51: ("aa_s1", spec_l1attna, lambda P, env: build_l1attna(P, env, 1)),
    6: ("l1attnb", spec_l1attnb, build_l1attnb),
    61: ("bb_s1", spec_l1attnb, lambda P, env: build_l1attnb(P, env, 1)),
    62: ("bb_dbg", spec_l1attnb, lambda P, env: (build_l1attnb(P, env, 2.5), [
        P.op("vector", lambda e, i=i: e.tensor_copy(
            out=env.x[i][:], in_=env.SB[:, 8192 + i * 2048:8192 + (i + 1) * 2048]
            .bitcast(F32)), reads=[env.SB], writes=[env.x[i]]) for i in range(4)])[0]),
    7: ("l1ffn_b", spec_l1ffn, build_l1ffn),
    98: ("pass", spec_none, lambda P, env: None),
    50: ("l1f_s0", spec_l1ffn, lambda P, env: build_l1ffn(P, env, 0)),
    51: ("l1f_s1", spec_l1ffn, lambda P, env: build_l1ffn(P, env, 1)),
    52: ("l1f_s2", spec_l1ffn, lambda P, env: build_l1ffn(P, env, 2)),
    53: ("l1f_s3", spec_l1ffn, lambda P, env: build_l1ffn(P, env, 3)),
    54: ("l1f_s4", spec_l1ffn, lambda P, env: build_l1ffn(P, env, 4)),
    55: ("l1f_s5", spec_l1ffn, lambda P, env: build_l1ffn(P, env, 5)),
    99: ("lnonly", spec_none, build_lnonly),
}

ALL_IDX = [0, 1, 2, 3, 4, 5, 6, 7]
LAUNCH_PLAN = [[0, 1, 2, 3, 4, 5, 6, 7]]


def build_program(idxs):
    nc = bass.Bass("TRN2", target_bir_lowering=False)
    x_in = nc.dram_tensor("x_in", [T_SEQ, D_MODEL], F32, kind="ExternalInput").ap()
    x_out = nc.dram_tensor("x_out", [T_SEQ, D_MODEL], F32, kind="ExternalOutput").ap()
    specs = {}
    for i in idxs:
        specs.update(SUBLAYERS[i][1]())
    dram = {}
    for name, (shape, dtype, _fn) in specs.items():
        dram[name] = nc.dram_tensor(name, list(shape), dtype, kind="ExternalInput").ap()
    P = Prog(nc)
    env = Env()
    env.dram = dram
    env.aa_scr = nc.dram_tensor("aa_scr", [272, 512], BF16, kind="Internal").ap()
    env.aa_scrbuf = Buf("aa_scr")
    with P._ctx:
        alloc_common(P, env)
        build_identity(P, env)
        load_x(P, env, x_in)
        for i in idxs:
            SUBLAYERS[i][2](P, env)
        store_x(P, env, x_out)
        with nc.Block() as block:
            P.finalize(block)
    return nc, specs


def kernel(**inputs):
    x = np.ascontiguousarray(np.asarray(inputs["L0_attn_a_hidden"], dtype=np.float32))
    for idxs in LAUNCH_PLAN:
        nc, specs = build_program(idxs)
        wmap = {name: np.ascontiguousarray(np.asarray(fn(inputs), dtype=np.float32))
                for name, (_s, _d, fn) in specs.items()}
        in_maps = [dict(wmap, x_in=x[b]) for b in range(N_CORES)]
        res = run_bass_kernel_spmd(nc, in_maps, core_ids=list(range(N_CORES)))
        x = np.stack([np.asarray(res.results[b]["x_out"]) for b in range(N_CORES)])
    return x


# version 08
# speedup vs baseline: 1.1660x; 1.0375x over previous
import math
import numpy as np
import ml_dtypes
from contextlib import ExitStack

import concourse.bass as bass
import concourse.mybir as mybir
from concourse.bass_utils import run_bass_kernel_spmd

F32 = mybir.dt.float32
BF16 = mybir.dt.bfloat16
AF = mybir.ActivationFunctionType
ALU = mybir.AluOpType
AX = mybir.AxisListType

ENGINES = ("tensor", "vector", "scalar", "gpsimd", "sync")
DMA_QUEUES = ("sync", "scalar", "gpsimd")

T_SEQ = 2048
D_MODEL = 1024
NT = T_SEQ // 128
ND = D_MODEL // 128
N_CORES = 8



class Buf:
    __slots__ = ("name", "writers", "readers")

    def __init__(self, name):
        self.name = name
        self.writers = []
        self.readers = []


class Op:
    __slots__ = ("fn", "deps", "inc", "seq", "name", "is_dma", "dma_queue_idx")

    def __init__(self, fn, deps, seq, name, is_dma=False):
        self.fn = fn
        self.deps = deps
        self.inc = False
        self.seq = seq
        self.name = name
        self.is_dma = is_dma
        self.dma_queue_idx = -1


class Prog:
    def __init__(self, nc):
        self.nc = nc
        self.streams = {e: [] for e in ENGINES}
        self.dep_targets = set()
        self.dma_idx = {q: 0 for q in DMA_QUEUES}
        self._ctx = ExitStack()
        self.bufmap = {}
        self._handles = []

    def sb(self, name, shape, dtype):
        h = self._ctx.enter_context(self.nc.sbuf_tensor(name, list(shape), dtype))
        self.bufmap[id(h)] = Buf(name)
        self._handles.append(h)
        return h

    def ps(self, name, shape, dtype):
        h = self._ctx.enter_context(self.nc.psum_tensor(name, list(shape), dtype))
        self.bufmap[id(h)] = Buf(name)
        self._handles.append(h)
        return h

    def _b(self, t):
        if isinstance(t, Buf):
            return t
        return self.bufmap[id(t)]

    def _mkdeps(self, eng, seq, rbufs, wbufs, is_dma=False):
        deps = []
        for b in rbufs:
            deps.extend(b.writers)
        for b in wbufs:
            deps.extend(b.writers)
            deps.extend(b.readers)
        dd = {}
        for d in deps:
            if d[0] == "e" and d[1] == eng and not is_dma:
                if eng == "tensor":
                    continue
                if d[2] >= seq:
                    continue
            key = (d[0], d[1])
            if key not in dd or dd[key][2] < d[2]:
                dd[key] = d
        return list(dd.values())

    def op(self, eng, fn, reads=(), writes=(), name=""):
        seq = len(self.streams[eng])
        rbufs = [self._b(t) for t in reads]
        wbufs = [self._b(t) for t in writes]
        deps = self._mkdeps(eng, seq, rbufs, wbufs)
        o = Op(fn, deps, seq, name)
        self.streams[eng].append(o)
        me = ("e", eng, seq)
        for b in rbufs:
            b.readers.append(me)
        for b in wbufs:
            b.writers = [me]
            b.readers = []
        for d in deps:
            if d[0] == "e":
                self.dep_targets.add((d[1], d[2]))
        return o

    def split_buf(self, parent, names):
        pw = self._b(parent)
        out = []
        for n in names:
            b = Buf(n)
            b.writers = list(pw.writers)
            b.readers = list(pw.readers)
            out.append(b)
        return out

    def merge_buf(self, parent, bufs):
        pw = self._b(parent)
        ws, rs = list(pw.writers), list(pw.readers)
        for b in bufs:
            ws.extend(b.writers)
            rs.extend(b.readers)
        pw.writers = ws
        pw.readers = rs

    def dma(self, queue, out, in_, reads=(), writes=(), name="dma", **kw):
        seq = len(self.streams[queue])
        rbufs = [self._b(t) for t in reads]
        wbufs = [self._b(t) for t in writes]
        deps = self._mkdeps(queue, seq, rbufs, wbufs, is_dma=True)
        idx = self.dma_idx[queue]
        self.dma_idx[queue] = idx + 1

        def fn(eng, out=out, in_=in_, kw=kw):
            return eng.dma_start(out=out, in_=in_, **kw)

        o = Op(fn, deps, seq, name, is_dma=True)
        o.dma_queue_idx = idx
        self.streams[queue].append(o)
        me = ("d", queue, idx)
        for b in rbufs:
            b.readers.append(me)
        for b in wbufs:
            b.writers = [me]
            b.readers = []
        for d in deps:
            if d[0] == "e":
                self.dep_targets.add((d[1], d[2]))
        return o

    def finalize(self, block):
        for (eng, seq) in self.dep_targets:
            self.streams[eng][seq].inc = True
        prefix = {}
        for eng in ENGINES:
            c = 0
            pc = []
            for o in self.streams[eng]:
                if o.inc and not o.is_dma:
                    c += 1
                pc.append(c)
            prefix[eng] = pc
        self.sem = {}
        self.dsem = {}
        for eng in ENGINES:
            if any((o.inc and not o.is_dma) for o in self.streams[eng]):
                self.sem[eng] = self._ctx.enter_context(
                    self.nc.semaphore("sem_" + eng))
        for q in DMA_QUEUES:
            if any(o.is_dma for o in self.streams[q]):
                self.dsem[q] = self._ctx.enter_context(
                    self.nc.semaphore("dsem_" + q))

        def emit(proxy, eng_name):
            waited = {}
            for o in self.streams[eng_name]:
                for d in o.deps:
                    if d[0] == "e":
                        sem = self.sem[d[1]]
                        val = prefix[d[1]][d[2]]
                    else:
                        sem = self.dsem[d[1]]
                        val = 16 * (d[2] + 1)
                    key = id(sem)
                    if waited.get(key, -1) >= val:
                        continue
                    waited[key] = val
                    proxy.wait_ge(sem, val)
                if o.fn is None:
                    continue
                inst = o.fn(proxy)
                if o.is_dma:
                    inst.then_inc(self.dsem[eng_name], 16)
                elif o.inc:
                    inst.then_inc(self.sem[eng_name], 1)

        for eng_name in ENGINES:
            getattr(block, eng_name)(lambda proxy, e=eng_name: emit(proxy, e))


class Env:
    def __init__(self):
        self.pool = {}
        self.flags = set()

    def get(self, P, name, shape, dtype):
        if name not in self.pool:
            self.pool[name] = P.sb(name, list(shape), dtype)
        return self.pool[name]

    def once(self, flag):
        if flag in self.flags:
            return False
        self.flags.add(flag)
        return True



def build_identity(P, env):
    env.ident = P.sb("ident", [128, 128], BF16)
    P.op("gpsimd", lambda e: e.memset(env.ident[:], 0.0), writes=[env.ident])
    P.op("gpsimd", lambda e: e.affine_select(
        out=env.ident[:], in_=env.ident[:], compare_op=ALU.not_equal, fill=1.0,
        base=0, pattern=[[-1, 128]], channel_multiplier=1),
        reads=[env.ident], writes=[env.ident])


def alloc_common(P, env):
    env.xnT = P.sb("xnT", [128, ND * T_SEQ], BF16)
    env.stats = [P.sb(f"st{i}", [128, 8], F32) for i in range(NT)]
    env.xn_tm = [P.sb(f"xntm{i}", [128, D_MODEL], BF16) for i in range(2)]
    env.psB = [P.ps(f"psB{i}", [128, 512], F32) for i in range(8)]
    env.psA = env.psB[:6]
    env.ps_tp = [env.psB[6], env.psB[7]]
    env.SA0a = P.sb("SA0a", [128, 4096], BF16)
    env.SA0b = P.sb("SA0b", [128, 4096], BF16)
    env.SA1a = P.sb("SA1a", [128, 4096], BF16)
    env.SA1b = P.sb("SA1b", [128, 4096], BF16)
    env.SB = P.sb("SB", [128, 16384], BF16)
    env.SF = P.sb("SF", [128, 4096], F32)
    env.gbB = P.sb("gbB", [128, 2048], BF16)
    env.scF1 = P.sb("scF1", [128, 1024], F32)
    env.scF2 = P.sb("scF2", [128, 1024], F32)
    env.ones_f = P.sb("ones_f", [1, 128], F32)
    P.op("gpsimd", lambda e: e.memset(env.ones_f[:], 1.0), writes=[env.ones_f])
    env.wslot = [P.sb(f"wslot{j}", [128, 1024], BF16) for j in range(2)]
    env.lam_sb = P.sb("lam_sb", [1, 16], F32)
    env.lamrep = P.sb("lamrep", [128, 8], F32)
    env.row_sc = P.sb("row_sc", [1, 1024], F32)
    env.srow = env.row_sc[0:1, 0:512]
    P.bufmap[id(env.srow)] = P._b(env.row_sc)
    env.cst = P.sb("cst", [128, 4], F32)
    P.op("gpsimd", lambda e: e.memset(env.cst[:, 0:1], 1e-5), writes=[env.cst])
    P.op("gpsimd", lambda e: e.memset(env.cst[:, 1:2], 1.0), writes=[env.cst])
    P.op("gpsimd", lambda e: e.memset(env.cst[:, 2:3], 1e-6), writes=[env.cst])
    env.identf = P.sb("identf", [128, 128], F32)
    P.op("gpsimd", lambda e: e.memset(env.identf[:], 0.0), writes=[env.identf])
    P.op("gpsimd", lambda e: e.affine_select(
        out=env.identf[:], in_=env.identf[:], compare_op=ALU.not_equal, fill=1.0,
        base=0, pattern=[[-1, 128]], channel_multiplier=1),
        reads=[env.identf], writes=[env.identf])
    env.selH = P.sb("selH", [128, 2], BF16)
    P.op("gpsimd", lambda e: e.memset(env.selH[:], 1.0), writes=[env.selH])
    P.op("gpsimd", lambda e: e.affine_select(
        out=env.selH[:], in_=env.selH[:], compare_op=ALU.is_ge, fill=0.0,
        base=0, pattern=[[-64, 2]], channel_multiplier=1),
        reads=[env.selH], writes=[env.selH])
    P.op("gpsimd", lambda e: e.affine_select(
        out=env.selH[:], in_=env.selH[:], compare_op=ALU.is_ge, fill=0.0,
        base=63, pattern=[[64, 2]], channel_multiplier=-1),
        reads=[env.selH], writes=[env.selH])
    env.rts = P.sb("rts", [128, 64], F32)


def load_x(P, env, x_dram):
    env.x = [P.sb(f"x{i}", [128, D_MODEL], F32) for i in range(NT)]
    for i in range(NT):
        P.dma("sync", env.x[i][:], x_dram[i * 128:(i + 1) * 128, :],
              writes=[env.x[i]], name=f"ldx{i}")


def store_x(P, env, x_out_dram):
    for i in range(NT):
        P.dma("sync", x_out_dram[i * 128:(i + 1) * 128, :], env.x[i][:],
              reads=[env.x[i]], name=f"stx{i}")
    o = Op(None, [("d", "sync", P.dma_idx["sync"] - 1)],
           len(P.streams["sync"]), "final")
    P.streams["sync"].append(o)


def layernorm_to_xnT(P, env):
    for i in range(NT):
        st = env.stats[i]
        xi = env.x[i]
        P.op("vector", lambda e, xi=xi, st=st: e.reduce_sum(
            st[:, 0:1], xi[:], axis=AX.X), reads=[xi], writes=[st])
        P.op("vector", lambda e, st=st: e.tensor_scalar(
            out=st[:, 1:2], in0=st[:, 0:1], scalar1=-1.0 / D_MODEL, scalar2=None,
            op0=ALU.mult), reads=[st], writes=[st])
        sqs = env.xn_tm[i % 2]
        P.op("scalar", lambda e, xi=xi, st=st, sqs=sqs: e.activation(
            out=sqs[:], in_=xi[:], func=AF.Square, bias=st[:, 1:2],
            accum_out=st[:, 2:3]), reads=[xi, st], writes=[sqs, st])
        P.op("scalar", lambda e, st=st: e.activation(
            out=st[:, 3:4], in_=st[:, 2:3], func=AF.Ln, scale=1.0 / D_MODEL,
            bias=env.cst[:, 0:1]), reads=[st, env.cst], writes=[st])
        P.op("scalar", lambda e, st=st: e.activation(
            out=st[:, 4:5], in_=st[:, 3:4], func=AF.Exp, scale=-0.5),
            reads=[st], writes=[st])
        P.op("vector", lambda e, st=st: e.tensor_tensor(
            out=st[:, 5:6], in0=st[:, 1:2], in1=st[:, 4:5], op=ALU.mult),
            reads=[st], writes=[st])
        tm = env.xn_tm[i % 2]
        P.op("vector", lambda e, xi=xi, st=st, tm=tm: e.tensor_scalar(
            out=tm[:], in0=xi[:], scalar1=st[:, 4:5], scalar2=st[:, 5:6],
            op0=ALU.mult, op1=ALU.add), reads=[xi, st], writes=[tm])
        pstp = env.ps_tp[i % 2]
        pstp_bf = pstp[:].bitcast(BF16)
        for k in range(ND):
            P.op("tensor", lambda e, tm=tm, pstp_bf=pstp_bf, k=k: e.transpose(
                pstp_bf[:, k * 128:(k + 1) * 128], tm[:, k * 128:(k + 1) * 128],
                env.ident[:]), reads=[tm, env.ident], writes=[pstp])
        dst = env.xnT[:].rearrange("p (k t) -> p k t", k=ND)[:, :, i * 128:(i + 1) * 128]
        src_ap = pstp_bf.rearrange("p (k t) -> p k t", k=ND)
        P.op("vector", lambda e, dst=dst, src_ap=src_ap: e.tensor_copy(
            out=dst, in_=src_ap), reads=[pstp], writes=[env.xnT])


def xnt(env, k, t0, t1):
    return env.xnT[:, k * T_SEQ + t0:k * T_SEQ + t1]


def residual_add_tile(P, env, i, cols, src_ap, reads):
    xi = env.x[i]
    P.op("gpsimd", lambda e, xi=xi, src_ap=src_ap, cols=cols: e.tensor_tensor(
        out=xi[:, cols], in0=xi[:, cols], in1=src_ap, op=ALU.add),
        reads=[xi] + list(reads), writes=[xi])



def spec_l1ffn():
    def t(i, key):
        return np.ascontiguousarray(np.asarray(i[key]).T)

    def mu(i):
        return np.ascontiguousarray(np.concatenate([
            np.asarray(i["L1_ffn_mu_r"]).reshape(8, 128).T,
            np.asarray(i["L1_ffn_mu_k"]).reshape(8, 128).T], axis=1))

    return {
        "f1_WrT": ((1024, 1024), F32, lambda i: t(i, "L1_ffn_W_r")),
        "f1_WkT": ((1024, 1024), F32, lambda i: t(i, "L1_ffn_W_k")),
        "f1_WvT": ((1024, 1024), F32, lambda i: t(i, "L1_ffn_W_v")),
        "f1_mu": ((128, 16), F32, mu),
    }


def build_l1ffn(P, env, stage=99):
    layernorm_to_xnT(P, env)
    d = env.dram
    mu = env.get(P, "f1mu_sb", (128, 16), F32)
    if env.once("f1mu"):
        P.dma("sync", mu[:], d["f1_mu"], writes=[mu], name="f1mu")
    WrT = env.SB[:, 0:8192]
    WkT = env.SB[:, 8192:16384]
    WvT = env.SF[:].bitcast(BF16)
    for (t, tb_, w, off) in ((WrT, env.SB, "f1_WrT", 0), (WkT, env.SB, "f1_WkT", 0),
                             (WvT, env.SF, "f1_WvT", 0)):
        for k in range(ND):
            P.dma("gpsimd", t[:, k * 1024:(k + 1) * 1024],
                  d[w][k * 128:(k + 1) * 128, :], writes=[tb_], name=w + str(k))
    xr = [env.SA0a[:, k * 512:(k + 1) * 512] for k in range(ND)]
    xk = [env.SA0b[:, k * 512:(k + 1) * 512] for k in range(ND)]
    ka = [env.SA1a[:, k * 512:(k + 1) * 512] for k in range(ND)]
    rsb = [env.SA1b[:, j * 1024:(j + 1) * 1024] for j in range(4)]
    tmp = env.scF2[:].bitcast(BF16)[:, 0:512]
    krf = env.scF1[:, 0:512]
    osc = env.scF1[:, 512:1024]

    if stage < 1:
        return
    for c in range(4):
        t0 = c * 512
        for k in range(ND):
            for (dst, mcol) in ((xr[k], k), (xk[k], 8 + k)):
                lo = 1 if t0 == 0 else 0
                dstb = env.SA0a if mcol < 8 else env.SA0b
                if t0 == 0:
                    P.op("vector", lambda e, dst=dst, k=k, mcol=mcol: e.tensor_scalar(
                        out=dst[:, 0:1], in0=xnt(env, k, 0, 1),
                        scalar1=mu[:, mcol:mcol + 1], scalar2=None, op0=ALU.mult),
                        reads=[env.xnT, mu], writes=[dstb])
                xs = xnt(env, k, t0 + lo, t0 + 512)
                xp = xnt(env, k, t0 + lo - 1, t0 + 511)
                P.op("vector", lambda e, xs=xs, xp=xp, lo=lo: e.tensor_tensor(
                    out=tmp[:, lo:512], in0=xs, in1=xp, op=ALU.subtract),
                    reads=[env.xnT], writes=[env.scF2])
                P.op("vector", lambda e, dst=dst, xp=xp, lo=lo, mcol=mcol:
                     e.scalar_tensor_tensor(
                         out=dst[:, lo:512], in0=tmp[:, lo:512],
                         scalar=mu[:, mcol:mcol + 1], in1=xp,
                         op0=ALU.mult, op1=ALU.add),
                     reads=[env.scF2, env.xnT, mu], writes=[dstb])
        if stage < 2:
            continue
        for f in range(ND):
            pk = env.psA[2 + (f % 2)]
            for k in range(ND):
                lhsT = WkT[:, k * 1024 + f * 128:k * 1024 + (f + 1) * 128]
                P.op("tensor", lambda e, pk=pk, lhsT=lhsT, xkk=xk[k], k=k: e.matmul(
                    pk[:], lhsT, xkk, start=(k == 0), stop=(k == ND - 1)),
                    reads=[env.SB, env.SA0b], writes=[pk])
            P.op("vector", lambda e, pk=pk: e.tensor_scalar(
                out=krf, in0=pk[:], scalar1=0.0, scalar2=None, op0=ALU.max),
                reads=[pk], writes=[env.scF1])
            P.op("scalar", lambda e, f=f: e.activation(
                out=ka[f], in_=krf, func=AF.Square),
                reads=[env.scF1], writes=[env.SA1a])
        if stage < 3:
            continue
        for tt in range(4):
            for h in range(2):
                pr = env.psA[h]
                for k in range(ND):
                    lhsT = xr[k][:, tt * 128:(tt + 1) * 128]
                    rhs = WrT[:, k * 1024 + h * 512:k * 1024 + h * 512 + 512]
                    P.op("tensor", lambda e, pr=pr, lhsT=lhsT, rhs=rhs, k=k: e.matmul(
                        pr[:], lhsT, rhs, start=(k == 0), stop=(k == ND - 1)),
                        reads=[env.SB, env.SA0a], writes=[pr])
                P.op("scalar", lambda e, pr=pr, tt=tt, h=h: e.activation(
                    out=rsb[tt][:, h * 512:(h + 1) * 512], in_=pr[:], func=AF.Sigmoid),
                    reads=[pr], writes=[env.SA1b])
        if stage < 4:
            continue
        for tt in range(4):
            i_tile = c * 4 + tt
            for h in range(2):
                po = env.psA[4 + h]
                for f in range(ND):
                    lhsT = ka[f][:, tt * 128:(tt + 1) * 128]
                    rhs = WvT[:, f * 1024 + h * 512:f * 1024 + h * 512 + 512]
                    P.op("tensor", lambda e, po=po, lhsT=lhsT, rhs=rhs, f=f: e.matmul(
                        po[:], lhsT, rhs, start=(f == 0), stop=(f == ND - 1)),
                        reads=[env.SF, env.SA1a], writes=[po])
                if stage < 5:
                    P.op("vector", lambda e, po=po: e.tensor_copy(
                        out=osc, in_=po[:]), reads=[po], writes=[env.scF1])
                else:
                    P.op("vector", lambda e, po=po, tt=tt, h=h: e.tensor_tensor(
                        out=osc, in0=po[:], in1=rsb[tt][:, h * 512:(h + 1) * 512],
                        op=ALU.mult), reads=[po, env.SA1b], writes=[env.scF1])
                if stage >= 6:
                    residual_add_tile(P, env, i_tile, slice(h * 512, (h + 1) * 512),
                                      osc, [env.scF1])




def spec_l0ffn():
    def f(key, fn):
        return lambda i, key=key, fn=fn: fn(np.asarray(i[key]))

    return {
        "fr_WrT": ((1024, 8), F32, f("L0_ffn_Wr", lambda a: np.ascontiguousarray(a.T))),
        "fr_br": ((1, 8), F32, f("L0_ffn_br", lambda a: a.reshape(1, 8))),
        "f0_W1T": ((8192, 4096), F32, f("L0_ffn_W1", lambda a: np.ascontiguousarray(
            a.transpose(0, 2, 1).reshape(8192, 4096)))),
        "f0_b1": ((128, 256), F32, f("L0_ffn_b1", lambda a: np.ascontiguousarray(
            a.reshape(8, 32, 128).transpose(2, 0, 1).reshape(128, 256)))),
        "f0_W2T": ((32768, 1024), F32, f("L0_ffn_W2", lambda a: np.ascontiguousarray(
            a.transpose(0, 2, 1).reshape(32768, 1024)))),
        "f0_b2": ((1, 8192), F32, f("L0_ffn_b2", lambda a: a.reshape(1, 8192))),
        "f0_gb": ((1, 2048), F32, lambda i: np.concatenate(
            [np.asarray(i["L0_ffn_gamma"]), np.asarray(i["L0_ffn_beta"])]).reshape(1, 2048)),
    }


def build_l0ffn(P, env, stage=99):
    layernorm_to_xnT(P, env)
    d = env.dram
    if env.once("f0small"):
        env.wr_sb = P.sb("wr_sb", [128, 64], BF16)
        for k in range(ND):
            P.dma("gpsimd", env.wr_sb[:, 8 * k:8 * (k + 1)],
                  d["fr_WrT"][k * 128:(k + 1) * 128, :], writes=[env.wr_sb])
        env.br_row = P.sb("br_row", [1, 8], F32)
        P.dma("sync", env.br_row[:], d["fr_br"], writes=[env.br_row])
        env.b1_sb = P.sb("b1_sb", [128, 256], F32)
        P.dma("sync", env.b1_sb[:], d["f0_b1"], writes=[env.b1_sb])

        pass
        env.brb = P.sb("brb", [128, 8], F32)
        env.coef = P.sb("coef", [128, 128], F32)
        pb = env.psB[0]
        P.op("tensor", lambda e, pb=pb: e.matmul(
            pb[:, 0:8], env.ones_f[0:1, :], env.br_row[0:1, :],
            start=True, stop=True), reads=[env.ones_f, env.br_row], writes=[pb])
        P.op("vector", lambda e, pb=pb: e.tensor_copy(out=env.brb[:], in_=pb[:, 0:8]),
             reads=[pb], writes=[env.brb])
        for gb2 in range(2):
            P.dma("sync", env.row_sc[0:1, :],
                  d["f0_gb"][0:1, gb2 * 1024:(gb2 + 1) * 1024],
                  writes=[env.row_sc], name="gbld")
            for j in range(2):
                pb = env.psB[j % 2]
                P.op("tensor", lambda e, pb=pb, j=j: e.matmul(
                    pb[:], env.ones_f[0:1, :], env.row_sc[0:1, j * 512:(j + 1) * 512],
                    start=True, stop=True), reads=[env.ones_f, env.row_sc],
                    writes=[pb])
                P.op("vector", lambda e, pb=pb, j=j, gb2=gb2: e.tensor_copy(
                    out=env.gbB[:, gb2 * 1024 + j * 512:gb2 * 1024 + (j + 1) * 512],
                    in_=pb[:]), reads=[pb], writes=[env.gbB])

    rts = env.rts
    for i in range(NT):
        pr = env.psB[i % 2]
        for k in range(ND):
            P.op("tensor", lambda e, pr=pr, k=k, i=i: e.matmul(
                pr[:, 0:8], xnt(env, k, i * 128, (i + 1) * 128),
                env.wr_sb[:, 8 * k:8 * (k + 1)], start=(k == 0), stop=(k == ND - 1)),
                reads=[env.xnT, env.wr_sb], writes=[pr])
        lg, mx = rts[:, 0:8], rts[:, 8:16]
        e1, e2, t8, e2b = rts[:, 16:24], rts[:, 24:32], rts[:, 40:48], rts[:, 48:56]
        c1, c2 = rts[:, 0:8], rts[:, 8:16]
        P.op("vector", lambda e, pr=pr, lg=lg: e.tensor_tensor(
            out=lg, in0=pr[:, 0:8], in1=env.brb[:], op=ALU.add),
            reads=[pr, env.brb], writes=[rts])
        P.op("vector", lambda e, lg=lg, mx=mx: e.max(out=mx, in_=lg),
             reads=[rts], writes=[rts])
        P.op("vector", lambda e, mx=mx: e.tensor_tensor(
            out=rts[:, 32:33], in0=mx[:, 0:1], in1=mx[:, 1:2], op=ALU.subtract),
            reads=[rts], writes=[rts])
        P.op("scalar", lambda e: e.activation(
            out=rts[:, 33:34], in_=rts[:, 32:33], func=AF.Sigmoid),
            reads=[rts], writes=[rts])
        P.op("vector", lambda e: e.tensor_scalar(
            out=rts[:, 34:35], in0=rts[:, 33:34], scalar1=-1.0, scalar2=1.0,
            op0=ALU.mult, op1=ALU.add), reads=[rts], writes=[rts])
        P.op("vector", lambda e, lg=lg, e1=e1, mx=mx: e.tensor_scalar(
            out=e1, in0=lg, scalar1=mx[:, 0:1], scalar2=None, op0=ALU.is_equal),
            reads=[rts], writes=[rts])
        P.op("vector", lambda e, lg=lg, e2=e2, mx=mx: e.tensor_scalar(
            out=e2, in0=lg, scalar1=mx[:, 1:2], scalar2=None, op0=ALU.is_equal),
            reads=[rts], writes=[rts])
        P.op("vector", lambda e, t8=t8, e1=e1, e2=e2: e.tensor_tensor(
            out=t8, in0=e2, in1=e1, op=ALU.mult), reads=[rts], writes=[rts])
        P.op("vector", lambda e, e2b=e2b, e2=e2, t8=t8: e.tensor_tensor(
            out=e2b, in0=e2, in1=t8, op=ALU.subtract), reads=[rts], writes=[rts])
        P.op("vector", lambda e, c1=c1, e1=e1: e.tensor_scalar(
            out=c1, in0=e1, scalar1=rts[:, 33:34], scalar2=None, op0=ALU.mult),
            reads=[rts], writes=[rts])
        P.op("vector", lambda e, c2=c2, e2b=e2b: e.tensor_scalar(
            out=c2, in0=e2b, scalar1=rts[:, 34:35], scalar2=None, op0=ALU.mult),
            reads=[rts], writes=[rts])
        P.op("vector", lambda e, c1=c1, c2=c2, i=i: e.tensor_tensor(
            out=env.coef[:, 8 * i:8 * (i + 1)], in0=c1, in1=c2, op=ALU.add),
            reads=[rts], writes=[env.coef])
    if stage < 1:
        return

    w1v = d["f0_W1T"].rearrange("(k p) f -> p k f", p=128)
    for tb in range(4):
        for e_x in range(8):
            P.dma("sync", env.row_sc[0:1, 0:1024],
                  d["f0_b2"][0:1, e_x * 1024:(e_x + 1) * 1024],
                  writes=[env.row_sc], name="b2ld")
            for fc in range(8):
                half = env.SA0a if fc % 2 == 0 else env.SA0b
                src_ap = w1v[:, e_x * 8:(e_x + 1) * 8, fc * 512:(fc + 1) * 512]
                dst_ap = half.rearrange("p (k f) -> p k f", k=8)
                P.dma("gpsimd", dst_ap, src_ap, writes=[half], name="w1ld")
                for ft in range(4):
                    fg = fc * 4 + ft
                    ph = env.psB[fg % 2]
                    for k in range(ND):
                        lhsT = half[:, k * 512 + ft * 128:k * 512 + (ft + 1) * 128]
                        half = half
                        P.op("tensor", lambda e, ph=ph, lhsT=lhsT, k=k, tb=tb: e.matmul(
                            ph[:], lhsT, xnt(env, k, tb * 512, (tb + 1) * 512),
                            start=(k == 0), stop=(k == ND - 1)),
                            reads=[half, env.xnT], writes=[ph])
                    P.op("scalar", lambda e, ph=ph, fg=fg, e_x=e_x: e.activation(
                        out=env.SB[:, fg * 512:(fg + 1) * 512], in_=ph[:], func=AF.Relu,
                        bias=env.b1_sb[:, e_x * 32 + fg:e_x * 32 + fg + 1]),
                        reads=[ph, env.b1_sb], writes=[env.SB])
            w2v = d["f0_W2T"]
            for fgp in range(8):
                buf = env.SA1a if fgp % 2 == 0 else env.SA1b
                src_ap = w2v[e_x * 4096 + fgp * 512:e_x * 4096 + (fgp + 1) * 512, :]
                dst_ap = buf[:].rearrange("p (j f) -> p j f", j=4)
                P.dma("gpsimd", dst_ap,
                      src_ap.rearrange("(j p) f -> p j f", p=128),
                      writes=[buf], name="w2ld")
                for j in range(4):
                    fg = fgp * 4 + j
                    for tt in range(4):
                        for h in range(2):
                            po = env.psB[tt * 2 + h]
                            lhsT = env.SB[:, fg * 512 + tt * 128:fg * 512 + (tt + 1) * 128]
                            rhs = buf[:, j * 1024 + h * 512:j * 1024 + h * 512 + 512]
                            P.op("tensor", lambda e, po=po, lhsT=lhsT, rhs=rhs, fg=fg:
                                 e.matmul(po[:], lhsT, rhs, start=(fg == 0), stop=False),
                                 reads=[env.SB, buf], writes=[po])
            for tt in range(4):
                i_tile = tb * 4 + tt
                for h in range(2):
                    po = env.psB[tt * 2 + h]
                    P.op("tensor", lambda e, po=po, h=h: e.matmul(
                        po[:], env.ones_f[0:1, :],
                        env.row_sc[0:1, h * 512:h * 512 + 512],
                        start=False, stop=True),
                        reads=[env.ones_f, env.row_sc], writes=[po])
                    cc = env.coef[:, 8 * i_tile + e_x:8 * i_tile + e_x + 1]
                    if e_x == 0:
                        P.op("scalar", lambda e, po=po, cc=cc, tt=tt, h=h: e.activation(
                            out=env.SF[:, tt * 1024 + h * 512:tt * 1024 + (h + 1) * 512
                                       if False else tt * 1024 + h * 512 + 512],
                            in_=po[:], func=AF.Copy, scale=cc),
                            reads=[po, env.coef], writes=[env.SF])
                    else:
                        P.op("scalar", lambda e, po=po, cc=cc: e.activation(
                            out=env.scF1[:, 0:512], in_=po[:], func=AF.Copy, scale=cc),
                            reads=[po, env.coef], writes=[env.scF1])
                        P.op("gpsimd", lambda e, tt=tt, h=h: e.tensor_tensor(
                            out=env.SF[:, tt * 1024 + h * 512:tt * 1024 + h * 512 + 512],
                            in0=env.SF[:, tt * 1024 + h * 512:tt * 1024 + h * 512 + 512],
                            in1=env.scF1[:, 0:512], op=ALU.add),
                            reads=[env.SF, env.scF1], writes=[env.SF])
        if stage < 2:
            continue
        for tt in range(4):
            i = tb * 4 + tt
            st = env.stats[i]
            xi = env.x[i]
            P.op("vector", lambda e, xi=xi, st=st: e.tensor_scalar(
                out=env.scF1[:], in0=xi[:], scalar1=st[:, 4:5], scalar2=st[:, 5:6],
                op0=ALU.mult, op1=ALU.add), reads=[xi, st], writes=[env.scF1])
            P.op("vector", lambda e, tt=tt: e.tensor_tensor(
                out=env.scF2[:], in0=env.scF1[:],
                in1=env.SF[:, tt * 1024:(tt + 1) * 1024], op=ALU.add),
                reads=[env.scF1, env.SF], writes=[env.scF2])
            P.op("vector", lambda e, st=st: e.reduce_sum(
                st[:, 0:1], env.scF2[:], axis=AX.X), reads=[env.scF2, st], writes=[st])
            P.op("vector", lambda e, st=st: e.tensor_scalar(
                out=st[:, 1:2], in0=st[:, 0:1], scalar1=-1.0 / D_MODEL, scalar2=None,
                op0=ALU.mult), reads=[st], writes=[st])
            P.op("scalar", lambda e, st=st: e.activation(
                out=env.xn_tm[0][:], in_=env.scF2[:], func=AF.Square, bias=st[:, 1:2],
                accum_out=st[:, 2:3]), reads=[env.scF2, st],
                writes=[env.xn_tm[0], st])
            P.op("scalar", lambda e, st=st: e.activation(
                out=st[:, 3:4], in_=st[:, 2:3], func=AF.Ln, scale=1.0 / D_MODEL,
                bias=env.cst[:, 0:1]), reads=[st, env.cst], writes=[st])
            P.op("scalar", lambda e, st=st: e.activation(
                out=st[:, 4:5], in_=st[:, 3:4], func=AF.Exp, scale=-0.5),
                reads=[st], writes=[st])
            P.op("vector", lambda e, st=st: e.tensor_tensor(
                out=st[:, 5:6], in0=st[:, 1:2], in1=st[:, 4:5], op=ALU.mult),
                reads=[st], writes=[st])
            P.op("vector", lambda e, st=st: e.tensor_scalar(
                out=env.scF1[:], in0=env.scF2[:], scalar1=st[:, 4:5], scalar2=st[:, 5:6],
                op0=ALU.mult, op1=ALU.add), reads=[env.scF2, st], writes=[env.scF1])
            P.op("vector", lambda e: e.tensor_tensor(
                out=env.scF2[:], in0=env.scF1[:], in1=env.gbB[:, 0:1024], op=ALU.mult),
                reads=[env.scF1, env.gbB], writes=[env.scF2])
            P.op("vector", lambda e: e.tensor_tensor(
                out=env.scF1[:], in0=env.scF2[:], in1=env.gbB[:, 1024:2048], op=ALU.add),
                reads=[env.scF2, env.gbB], writes=[env.scF1])
            P.op("gpsimd", lambda e, xi=xi: e.tensor_tensor(
                out=xi[:], in0=xi[:], in1=env.scF1[:], op=ALU.add),
                reads=[xi, env.scF1], writes=[xi])



def spec_l0attnb():
    def raw(key):
        return lambda i, key=key: np.asarray(i[key])

    return {
        "ab_Wq": ((1024, 1024), F32, raw("L0_attn_b_Wq")),
        "ab_Wk": ((1024, 1024), F32, raw("L0_attn_b_Wk")),
        "ab_Wv": ((1024, 1024), F32, raw("L0_attn_b_Wv")),
        "ab_Wqp": ((1024, 1024), F32, raw("L0_attn_b_Wqp")),
        "ab_Wkp": ((1024, 1024), F32, raw("L0_attn_b_Wkp")),
        "ab_Wo": ((1024, 1024), F32, raw("L0_attn_b_Wo")),
        "ab_lam": ((1, 16), F32, lambda i: np.ascontiguousarray(
            np.asarray(i["L0_attn_b_lambda_raw"]).reshape(1, 16))),
    }


def _proj_fm(P, env, wdram_col, wslot, evac):
    P.dma("gpsimd", wslot[:].rearrange("p (k m) -> p k m", k=8),
          wdram_col.rearrange("(k p) m -> p k m", p=128), writes=[wslot], name="wld")
    for tc in range(4):
        ph = env.psB[4 + tc % 2]
        for k in range(ND):
            lhsT = wslot[:, k * 128:(k + 1) * 128]
            P.op("tensor", lambda e, ph=ph, lhsT=lhsT, k=k, tc=tc: e.matmul(
                ph[:], lhsT, xnt(env, k, tc * 512, (tc + 1) * 512),
                start=(k == 0), stop=(k == ND - 1)),
                reads=[wslot, env.xnT], writes=[ph])
        evac(tc, ph)


def build_l0attnb(P, env, stage=99):
    layernorm_to_xnT(P, env)
    d = env.dram
    if env.once("absmall"):
        pass
        env.selHT_f = P.sb("selHT_f", [2, 128], F32)
        P.op("gpsimd", lambda e: e.memset(env.selHT_f[:], 1.0), writes=[env.selHT_f])
        P.op("gpsimd", lambda e: e.affine_select(
            out=env.selHT_f[:], in_=env.selHT_f[:], compare_op=ALU.is_ge, fill=0.0,
            base=0, pattern=[[1, 128]], channel_multiplier=-64),
            reads=[env.selHT_f], writes=[env.selHT_f])
        P.op("gpsimd", lambda e: e.affine_select(
            out=env.selHT_f[:], in_=env.selHT_f[:], compare_op=ALU.is_ge, fill=0.0,
            base=63, pattern=[[-1, 128]], channel_multiplier=64),
            reads=[env.selHT_f], writes=[env.selHT_f])
    P.dma("sync", env.lam_sb[:], d["ab_lam"], writes=[env.lam_sb], name="lamld")
    P.op("scalar", lambda e: e.activation(
        out=env.lam_sb[:], in_=env.lam_sb[:], func=AF.Tanh),
        reads=[env.lam_sb], writes=[env.lam_sb])
    for g in range(8):
        pb = env.psB[0]
        for hl in range(2):
            P.op("tensor", lambda e, pb=pb, g=g, hl=hl: e.matmul(
                pb[64 * hl:64 * (hl + 1), 0:1], env.ones_f[0:1, 0:64],
                env.lam_sb[0:1, 2 * g + hl:2 * g + hl + 1],
                start=(hl == 0), stop=(hl == 1)),
                reads=[env.ones_f, env.lam_sb], writes=[pb])
        P.op("vector", lambda e, pb=pb, g=g: e.tensor_scalar(
            out=env.lamrep[:, g:g + 1], in0=pb[:, 0:1], scalar1=0.5 / 8.0,
            scalar2=None, op0=ALU.mult), reads=[pb], writes=[env.lamrep])

    SFbf = env.SF[:].bitcast(BF16)
    qT = env.SA0a[:, 0:2048]
    qcT = env.SA0a[:, 2048:4096]
    qsT = env.SA0b[:, 0:2048]
    kT = env.SA0b[:, 2048:4096]
    kcT = env.SA1a[:, 0:2048]
    ksT = env.SA1a[:, 2048:4096]
    vtm = env.SA1b[:, 0:2048]
    spq = SFbf[:, 0:2048]
    sq2 = SFbf[:, 2048:4096]
    ampq = SFbf[:, 4096:6144]
    ampk = SFbf[:, 6144:8192]
    atnA = SFbf[:, 0:1024]
    atnB = SFbf[:, 1024:2048]
    asc = env.rts

    for g in range(8):
        def ev_sc(dst, dstbuf, scale):
            def f(tc, ph):
                P.op("vector", lambda e, ph=ph, tc=tc: e.tensor_scalar(
                    out=dst[:, tc * 512:(tc + 1) * 512], in0=ph[:],
                    scalar1=scale, scalar2=None, op0=ALU.mult),
                    reads=[ph], writes=[dstbuf])
            return f

        _proj_fm(P, env, d["ab_Wq"][:, g * 128:(g + 1) * 128], env.wslot[0],
                 ev_sc(qT, env.SA0a, 0.125))
        _proj_fm(P, env, d["ab_Wk"][:, g * 128:(g + 1) * 128], env.wslot[1],
                 ev_sc(kT, env.SA0b, 1.0))
        P.dma("gpsimd", env.wslot[0][:].rearrange("p (k m) -> p k m", k=8),
              d["ab_Wv"][:, g * 128:(g + 1) * 128].rearrange("(k p) m -> p k m", p=128),
              writes=[env.wslot[0]], name="wvld")
        for tt in range(NT):
            pv = env.psB[4 + tt % 2]
            for k in range(ND):
                P.op("tensor", lambda e, pv=pv, k=k, tt=tt: e.matmul(
                    pv[:, 0:128], xnt(env, k, tt * 128, (tt + 1) * 128),
                    env.wslot[0][:, k * 128:(k + 1) * 128],
                    start=(k == 0), stop=(k == ND - 1)),
                    reads=[env.wslot[0], env.xnT], writes=[pv])
            P.op("scalar", lambda e, pv=pv, tt=tt: e.activation(
                out=vtm[:, tt * 128:(tt + 1) * 128], in_=pv[:, 0:128], func=AF.Copy),
                reads=[pv], writes=[env.SA1b])
        for (srcT, srcbuf, ampdst, sc8) in ((qT, env.SA0a, ampq, 8.0),
                                            (kT, env.SA0b, ampk, 1.0)):
            for hh in range(2):
                cl = slice(hh * 1024, (hh + 1) * 1024)
                P.op("scalar", lambda e, srcT=srcT, cl=cl, sc8=sc8: e.activation(
                    out=env.scF1[:], in_=srcT[:, cl], func=AF.Abs, scale=sc8),
                    reads=[srcbuf], writes=[env.scF1])
                P.op("scalar", lambda e: e.activation(
                    out=env.scF2[:], in_=env.scF1[:], func=AF.Exp, scale=-1.0),
                    reads=[env.scF1], writes=[env.scF2])
                P.op("scalar", lambda e: e.activation(
                    out=env.scF1[:], in_=env.scF2[:], func=AF.Ln,
                    bias=env.cst[:, 1:2]), reads=[env.scF2, env.cst],
                    writes=[env.scF1])
                P.op("vector", lambda e, srcT=srcT, cl=cl, sc8=sc8: e.tensor_scalar(
                    out=env.scF2[:], in0=srcT[:, cl], scalar1=sc8, scalar2=0.0,
                    op0=ALU.mult, op1=ALU.max), reads=[srcbuf], writes=[env.scF2])
                P.op("vector", lambda e, cl=cl: e.tensor_tensor(
                    out=spq[:, cl], in0=env.scF1[:], in1=env.scF2[:], op=ALU.add),
                    reads=[env.scF1, env.scF2], writes=[env.SF])
            P.op("scalar", lambda e: e.activation(
                out=sq2[:], in_=spq[:], func=AF.Square), reads=[env.SF],
                writes=[env.SF])
            for tc in range(4):
                pa = env.psB[tc % 2]
                P.op("tensor", lambda e, pa=pa, tc=tc: e.matmul(
                    pa[0:2, :], env.selH[:], sq2[:, tc * 512:(tc + 1) * 512],
                    start=True, stop=True), reads=[env.selH, env.SF], writes=[pa])
                P.op("scalar", lambda e, pa=pa: e.activation(
                    out=env.scF1[0:2, 0:512], in_=pa[0:2, :], func=AF.Ln,
                    scale=1.0 / 64.0, bias=env.cst[0:2, 2:3]),
                    reads=[pa, env.cst], writes=[env.scF1])
                P.op("scalar", lambda e: e.activation(
                    out=env.scF1[0:2, 512:1024], in_=env.scF1[0:2, 0:512],
                    func=AF.Exp, scale=-0.5), reads=[env.scF1], writes=[env.scF1])
                pr = env.psB[2 + tc % 2]
                P.op("tensor", lambda e, pr=pr: e.matmul(
                    pr[:], env.selHT_f[:], env.scF1[0:2, 512:1024],
                    start=True, stop=True),
                    reads=[env.selHT_f, env.scF1], writes=[pr])
                P.op("vector", lambda e, pr=pr, tc=tc, ampdst=ampdst: e.tensor_tensor(
                    out=ampdst[:, tc * 512:(tc + 1) * 512],
                    in0=spq[:, tc * 512:(tc + 1) * 512], in1=pr[:], op=ALU.mult),
                    reads=[env.SF, pr], writes=[env.SF])
        for (wnm, srcamp, cdst, sdst, dstbuf, uselam) in (
                ("ab_Wqp", ampq, qcT, qsT, None, True),
                ("ab_Wkp", ampk, kcT, ksT, env.SA1a, False)):
            def ev_ph(tc, ph, srcamp=srcamp, cdst=cdst, sdst=sdst,
                      dstbuf=dstbuf, uselam=uselam, g=g):
                cbuf = env.SA0a if uselam else env.SA1a
                sbuf_ = env.SA0b if uselam else env.SA1a
                cs = slice(tc * 512, (tc + 1) * 512)
                P.op("scalar", lambda e, ph=ph: e.activation(
                    out=env.scF1[:, 0:512], in_=ph[:], func=AF.Tanh),
                    reads=[ph], writes=[env.scF1])
                P.op("scalar", lambda e: e.activation(
                    out=env.scF2[:, 0:512], in_=env.scF1[:, 0:512], func=AF.Sin,
                    scale=math.pi), reads=[env.scF1], writes=[env.scF2])
                P.op("scalar", lambda e: e.activation(
                    out=env.scF1[:, 512:1024], in_=env.scF1[:, 0:512], func=AF.Sin,
                    scale=math.pi / 2), reads=[env.scF1], writes=[env.scF1])
                P.op("scalar", lambda e: e.activation(
                    out=env.scF2[:, 512:1024], in_=env.scF1[:, 512:1024],
                    func=AF.Square), reads=[env.scF1], writes=[env.scF2])
                P.op("vector", lambda e: e.tensor_scalar(
                    out=env.scF1[:, 512:1024], in0=env.scF2[:, 512:1024],
                    scalar1=-2.0, scalar2=1.0, op0=ALU.mult, op1=ALU.add),
                    reads=[env.scF2], writes=[env.scF1])
                if uselam:
                    P.op("vector", lambda e, cs=cs, g=g: e.scalar_tensor_tensor(
                        out=cdst[:, cs], in0=env.scF1[:, 512:1024],
                        scalar=env.lamrep[:, g:g + 1], in1=srcamp[:, cs],
                        op0=ALU.mult, op1=ALU.mult),
                        reads=[env.scF1, env.lamrep, env.SF], writes=[cbuf])
                    P.op("vector", lambda e, cs=cs, g=g: e.scalar_tensor_tensor(
                        out=sdst[:, cs], in0=env.scF2[:, 0:512],
                        scalar=env.lamrep[:, g:g + 1], in1=srcamp[:, cs],
                        op0=ALU.mult, op1=ALU.mult),
                        reads=[env.scF2, env.lamrep, env.SF], writes=[sbuf_])
                else:
                    P.op("vector", lambda e, cs=cs: e.tensor_tensor(
                        out=cdst[:, cs], in0=env.scF1[:, 512:1024],
                        in1=srcamp[:, cs], op=ALU.mult),
                        reads=[env.scF1, env.SF], writes=[cbuf])
                    P.op("vector", lambda e, cs=cs: e.tensor_tensor(
                        out=sdst[:, cs], in0=env.scF2[:, 0:512],
                        in1=srcamp[:, cs], op=ALU.mult),
                        reads=[env.scF2, env.SF], writes=[sbuf_])
            _proj_fm(P, env, d[wnm][:, g * 128:(g + 1) * 128],
                     env.wslot[0] if uselam else env.wslot[1], ev_ph)
        if stage < 2:
            continue
        parts = ((qT, kT), (qcT, kcT), (qsT, ksT))
        for ic in range(4):
            for hl in range(2):
                rows = slice(64 * hl, 64 * (hl + 1))
                for iw in range(4):
                    ii = ic * 4 + iw
                    jw = (ii + 1) * 128
                    nb = (jw + 511) // 512
                    for jc in range(nb):
                        n = min(512, jw - jc * 512)
                        pA = env.psB[jc]
                        for pi, (qa, kb) in enumerate(parts):
                            P.op("tensor", lambda e, pA=pA, qa=qa, kb=kb, pi=pi,
                                 ii=ii, jc=jc, n=n, rows=rows: e.matmul(
                                     pA[:, 0:n], qa[rows, ii * 128:(ii + 1) * 128],
                                     kb[rows, jc * 512:jc * 512 + n],
                                     start=(pi == 0), stop=(pi == 2)),
                                 reads=[env.SA0a, env.SA0b, env.SA1a], writes=[pA])
                        P.op("vector", lambda e, pA=pA, jc=jc, n=n: e.reduce_max(
                            asc[:, jc:jc + 1], pA[:, 0:n], axis=AX.X),
                            reads=[pA], writes=[asc])
                    for jc in range(1, nb):
                        P.op("vector", lambda e, jc=jc: e.tensor_tensor(
                            out=asc[:, 0:1], in0=asc[:, 0:1], in1=asc[:, jc:jc + 1],
                            op=ALU.max), reads=[asc], writes=[asc], name="mxcomb")
                    P.op("vector", lambda e: e.tensor_scalar(
                        out=asc[:, 5:6], in0=asc[:, 0:1], scalar1=-1.0, scalar2=None,
                        op0=ALU.mult), reads=[asc], writes=[asc])
                    for jc in range(nb):
                        n = min(512, jw - jc * 512)
                        pA = env.psB[jc]
                        full = n if jc < nb - 1 else n - 128
                        if full > 0:
                            P.op("scalar", lambda e, pA=pA, full=full, jc=jc:
                                 e.activation(
                                     out=atnA[:, 0:full], in_=pA[:, 0:full],
                                     func=AF.Exp, bias=asc[:, 5:6],
                                     accum_out=asc[:, 8 + jc:9 + jc]),
                                 reads=[pA, asc], writes=[env.SF, asc])
                        else:
                            P.op("gpsimd", lambda e, jc=jc: e.memset(
                                asc[:, 8 + jc:9 + jc], 0.0), writes=[asc])
                    pA = env.psB[nb - 1]
                    n = jw - (nb - 1) * 512
                    P.op("scalar", lambda e, pA=pA, n=n: e.activation(
                        out=atnB[:, 0:128], in_=pA[:, n - 128:n], func=AF.Exp,
                        bias=asc[:, 5:6]), reads=[pA, asc], writes=[env.SF])
                    P.op("gpsimd", lambda e: e.affine_select(
                        out=atnB[:, 0:128], in_=atnB[:, 0:128],
                        compare_op=ALU.is_ge, fill=0.0, base=0,
                        pattern=[[-1, 128]], channel_multiplier=1),
                        reads=[env.SF], writes=[env.SF])
                    P.op("vector", lambda e: e.reduce_sum(
                        asc[:, 12:13], atnB[:, 0:128], axis=AX.X),
                        reads=[env.SF], writes=[asc])
                    for jc in range(nb):
                        P.op("vector", lambda e, jc=jc: e.tensor_tensor(
                            out=asc[:, 12:13], in0=asc[:, 12:13],
                            in1=asc[:, 8 + jc:9 + jc], op=ALU.add),
                            reads=[asc], writes=[asc], name="sumcomb")
                    P.op("scalar", lambda e: e.activation(
                        out=asc[:, 13:14], in_=asc[:, 12:13], func=AF.Ln),
                        reads=[asc], writes=[asc])
                    P.op("vector", lambda e: e.tensor_tensor(
                        out=asc[:, 14:15], in0=asc[:, 13:14], in1=asc[:, 0:1],
                        op=ALU.add), reads=[asc], writes=[asc])
                    P.op("vector", lambda e, iw=iw: e.tensor_scalar(
                        out=asc[:, 16 + iw:17 + iw], in0=asc[:, 14:15],
                        scalar1=-1.0, scalar2=None, op0=ALU.mult),
                        reads=[asc], writes=[asc])
                ps6 = env.psB[6]
                for iw in range(4):
                    P.op("tensor", lambda e, ps6=ps6, iw=iw: e.transpose(
                        ps6[0:1, iw * 128:(iw + 1) * 128],
                        asc[:, 16 + iw:17 + iw], env.identf[:]),
                        reads=[asc, env.identf], writes=[ps6])
                P.op("vector", lambda e, ps6=ps6: e.tensor_copy(
                    out=env.srow[0:1, :], in_=ps6[0:1, 0:512]),
                    reads=[ps6], writes=[env.srow])
                pav = env.psB[4 + hl]
                njt = 4 * ic + 4
                for jt in range(njt):
                    di = max(0, (jt - 4 * ic)) * 128
                    i0 = ic * 512 + di
                    nn = 512 - di
                    pB = env.psB[jt % 4]
                    for pi, (qa, kb) in enumerate(parts):
                        P.op("tensor", lambda e, pB=pB, qa=qa, kb=kb, pi=pi, jt=jt,
                             i0=i0, nn=nn, rows=rows: e.matmul(
                                 pB[:, 0:nn], kb[rows, jt * 128:(jt + 1) * 128],
                                 qa[rows, i0:i0 + nn], start=(pi == 0), stop=False),
                             reads=[env.SA0a, env.SA0b, env.SA1a], writes=[pB])
                    P.op("tensor", lambda e, pB=pB, i0=i0, nn=nn, ic=ic: e.matmul(
                        pB[:, 0:nn], env.ones_f[0:1, :],
                        env.srow[0:1, i0 - ic * 512:i0 - ic * 512 + nn],
                        start=False, stop=True),
                        reads=[env.ones_f, env.srow], writes=[pB])
                    at = atnA if jt % 2 == 0 else atnB
                    P.op("scalar", lambda e, pB=pB, at=at, nn=nn: e.activation(
                        out=at[:, 0:nn], in_=pB[:, 0:nn], func=AF.Exp),
                        reads=[pB], writes=[env.SF])
                    if jt >= 4 * ic:
                        P.op("gpsimd", lambda e, at=at: e.affine_select(
                            out=at[:, 0:128], in_=at[:, 0:128],
                            compare_op=ALU.is_ge, fill=0.0, base=0,
                            pattern=[[1, 128]], channel_multiplier=-1),
                            reads=[env.SF], writes=[env.SF])
                    P.op("tensor", lambda e, pav=pav, at=at, jt=jt, di=di, nn=nn,
                         hl=hl: e.matmul(
                             pav[64 * hl:64 * hl + 64, di:di + nn],
                             vtm[:, jt * 128 + 64 * hl:jt * 128 + 64 * hl + 64],
                             at[:, 0:nn], start=(jt == 0 and di == 0),
                             stop=(jt == njt - 1)),
                         reads=[env.SA1b, env.SF], writes=[pav])
                if hl == 1:
                    for hx in range(2):
                        pvx = env.psB[4 + hx]
                        P.op("scalar", lambda e, pvx=pvx, ic=ic, g=g, hx=hx:
                             e.activation(
                                 out=env.SB[64 * hx:64 * (hx + 1),
                                            g * 2048 + ic * 512:g * 2048 + (ic + 1) * 512],
                                 in_=pvx[64 * hx:64 * (hx + 1), :], func=AF.Copy),
                             reads=[pvx], writes=[env.SB])
    if stage < 3:
        return
    for itc in range(4):
        for j in range(ND):
            ws = env.wslot[j % 2]
            P.dma("gpsimd", ws[:], d["ab_Wo"][j * 128:(j + 1) * 128, :],
                  writes=[ws], name="wold")
            for iw in range(4):
                it = itc * 4 + iw
                for h in range(2):
                    po = env.psB[iw * 2 + h]
                    P.op("tensor", lambda e, po=po, j=j, it=it, h=h, ws=ws: e.matmul(
                        po[:], env.SB[:, j * 2048 + it * 128:j * 2048 + (it + 1) * 128],
                        ws[:, h * 512:(h + 1) * 512],
                        start=(j == 0), stop=(j == ND - 1)),
                        reads=[env.SB, ws], writes=[po])
        for iw in range(4):
            it = itc * 4 + iw
            for h in range(2):
                po = env.psB[iw * 2 + h]
                P.op("vector", lambda e, po=po, h=h: e.tensor_copy(
                    out=env.scF1[:, h * 512:(h + 1) * 512], in_=po[:]),
                    reads=[po], writes=[env.scF1])
            xi = env.x[it]
            P.op("gpsimd", lambda e, xi=xi: e.tensor_tensor(
                out=xi[:], in0=xi[:], in1=env.scF1[:], op=ALU.add),
                reads=[xi, env.scF1], writes=[xi])




def spec_l1attnb():
    def t(key):
        return lambda i, key=key: np.ascontiguousarray(np.asarray(i[key]).T)

    return {
        "bb_W1qT": ((1024, 64), F32, t("L1_attn_b_W1_q")),
        "bb_W2qT": ((1024, 64), F32, t("L1_attn_b_W2_q")),
        "bb_W1kT": ((1024, 64), F32, t("L1_attn_b_W1_k")),
        "bb_W2kT": ((1024, 64), F32, t("L1_attn_b_W2_k")),
        "bb_WvT": ((1024, 1024), F32, t("L1_attn_b_W_v")),
        "bb_WoT": ((1024, 1024), F32, t("L1_attn_b_W_out")),
        "bb_bv": ((1, 1024), F32, lambda i: np.ascontiguousarray(
            np.asarray(i["L1_attn_b_b_v"]).reshape(1, 1024))),
        "bb_bout": ((1, 1024), F32, lambda i: np.ascontiguousarray(
            np.asarray(i["L1_attn_b_b_out"]).reshape(1, 1024))),
    }


_PAIRS = [(0, 1), (0, 2), (0, 3), (1, 2), (1, 3), (2, 3)]
_JMAP = [(5, 1.0), (4, -1.0), (3, 1.0), (2, 1.0), (1, -1.0), (0, 1.0)]


def build_l1attnb(P, env, stage=99):
    layernorm_to_xnT(P, env)
    d = env.dram
    SFbf = env.SF[:].bitcast(BF16)

    stg = [(env.SA1a[0:8, 0:2048], env.SA1a[0:8, 2048:4096], env.SA1a),
           (env.SA1b[0:8, 0:2048], env.SA1b[0:8, 2048:4096], env.SA1b)]
    for (sl, wa, wb) in ((env.wslot[0], "bb_W1qT", "bb_W2qT"),
                         (env.wslot[1], "bb_W1kT", "bb_W2kT")):
        for (off, w) in ((0, wa), (512, wb)):
            P.dma("gpsimd", sl[:, off:off + 512].rearrange("p (k m) -> p k m", k=8),
                  d[w][:, :].rearrange("(k p) m -> p k m", p=128), writes=[sl],
                  name="bbw")
    LqT = env.SA0a[:, 0:2048]
    JkT = env.SA0a[:, 2048:4096]
    for tt in range(NT):
        for (pi, (sl, off)) in enumerate(((env.wslot[0], 0), (env.wslot[0], 512),
                                          (env.wslot[1], 0), (env.wslot[1], 512))):
            pq = env.psB[pi]
            for k in range(ND):
                P.op("tensor", lambda e, pq=pq, sl=sl, off=off, k=k, tt=tt: e.matmul(
                    pq[:, 0:64], xnt(env, k, tt * 128, (tt + 1) * 128),
                    sl[:, off + k * 64:off + (k + 1) * 64],
                    start=(k == 0), stop=(k == ND - 1)),
                    reads=[sl, env.xnT], writes=[pq])
        for pi in range(4):
            P.op("vector", lambda e, pi=pi: e.tensor_copy(
                out=env.scF2[:, 512 + pi * 64:512 + (pi + 1) * 64],
                in_=env.psB[pi][:, 0:64]), reads=[env.psB[pi]], writes=[env.scF2])
        for (pa, pb_, dst0, side) in ((env.scF2[:, 512:576], env.scF2[:, 576:640], 0, "q"),
                                      (env.scF2[:, 640:704], env.scF2[:, 704:768], 128, "k")):
            for c, (i, j) in enumerate(_PAIRS):
                if side == "q":
                    dc, sign = c, 1.0
                else:
                    dc = next(m for m, (srcc, _s) in enumerate(_JMAP) if srcc == c)
                    sign = _JMAP[dc][1]
                m1 = env.scF2[:, 256:272]
                m2 = env.scF2[:, 272:288]
                P.op("vector", lambda e, pa=pa, pb_=pb_, i=i, j=j, m1=m1:
                     e.tensor_tensor(out=m1, in0=pa[:, i:64:4], in1=pb_[:, j:64:4],
                                     op=ALU.mult), reads=[env.scF2], writes=[env.scF2])
                P.op("vector", lambda e, pa=pa, pb_=pb_, i=i, j=j, m2=m2:
                     e.tensor_tensor(out=m2, in0=pa[:, j:64:4], in1=pb_[:, i:64:4],
                                     op=ALU.mult), reads=[env.scF2], writes=[env.scF2])
                a_, b_ = (m1, m2) if sign > 0 else (m2, m1)
                P.op("vector", lambda e, a_=a_, b_=b_, dst0=dst0, dc=dc:
                     e.tensor_tensor(out=env.scF1[:, dst0 + dc:dst0 + 128:8],
                                     in0=a_, in1=b_, op=ALU.subtract),
                     reads=[env.scF2], writes=[env.scF1])
            P.op("scalar", lambda e, dst0=dst0: e.activation(
                out=env.scF2[:, 0:128], in_=env.scF1[:, dst0:dst0 + 128],
                func=AF.Square), reads=[env.scF1], writes=[env.scF2])
            sv = env.scF2[:, 0:128].rearrange("p (h c) -> p h c", h=16)[:, :, 0:6]
            P.op("vector", lambda e, sv=sv: e.reduce_sum(
                env.rts[:, 0:16], sv, axis=AX.X), reads=[env.scF2], writes=[env.rts])
            P.op("vector", lambda e: e.tensor_scalar(
                out=env.rts[:, 16:32], in0=env.rts[:, 0:16], scalar1=1e-24,
                scalar2=None, op0=ALU.max), reads=[env.rts], writes=[env.rts])
            P.op("scalar", lambda e: e.activation(
                out=env.rts[:, 0:16], in_=env.rts[:, 16:32], func=AF.Ln),
                reads=[env.rts], writes=[env.rts])
            P.op("scalar", lambda e: e.activation(
                out=env.rts[:, 16:32], in_=env.rts[:, 0:16], func=AF.Exp,
                scale=-0.5), reads=[env.rts], writes=[env.rts])
            if side == "q":
                P.op("vector", lambda e: e.tensor_scalar(
                    out=env.rts[:, 16:32], in0=env.rts[:, 16:32],
                    scalar1=6.0 ** -0.5, scalar2=None, op0=ALU.mult),
                    reads=[env.rts], writes=[env.rts])
            for c in range(6):
                P.op("vector", lambda e, dst0=dst0, c=c: e.tensor_tensor(
                    out=SFbf[:, dst0 + c:dst0 + 128:8],
                    in0=env.scF1[:, dst0 + c:dst0 + 128:8], in1=env.rts[:, 16:32],
                    op=ALU.mult), reads=[env.scF1, env.rts], writes=[env.SF])
            for pc in (6, 7):
                P.op("vector", lambda e, dst0=dst0, pc=pc: e.memset(
                    SFbf[:, dst0 + pc:dst0 + 128:8], 0.0), writes=[env.SF])
        for (src0, dstT) in ((0, LqT), (128, JkT)):
            ptp = env.ps_tp[tt % 2]
            ptpb = ptp[:].bitcast(BF16)
            P.op("tensor", lambda e, src0=src0, ptpb=ptpb: e.transpose(
                ptpb[:, 0:128], SFbf[:, src0:src0 + 128], env.ident[:]),
                reads=[env.SF, env.ident], writes=[ptp])
            P.op("vector", lambda e, dstT=dstT, tt=tt, ptpb=ptpb: e.tensor_copy(
                out=dstT[:, tt * 128:(tt + 1) * 128], in_=ptpb[:, 0:128]),
                reads=[ptp], writes=[env.SA0a])
    if stage < 2:
        return
    atnA = SFbf[:, 4096:5120]
    atnB = SFbf[:, 5120:6144]
    asc = env.rts
    for half in range(2):
        P.dma("sync", env.row_sc[0:1, 0:1024], d["bb_bv"], writes=[env.row_sc])
        P.dma("gpsimd", env.SA1a[:].rearrange("p (k m) -> p k m", k=8),
              d["bb_WvT"][:, half * 512:(half + 1) * 512].rearrange(
                  "(k p) m -> p k m", p=128), writes=[env.SA1a], name="wvld")
        for tt in range(NT):
            pv = env.psB[4 + tt % 2]
            for k in range(ND):
                P.op("tensor", lambda e, pv=pv, k=k, tt=tt: e.matmul(
                    pv[:], xnt(env, k, tt * 128, (tt + 1) * 128),
                    env.SA1a[:, k * 512:(k + 1) * 512],
                    start=(k == 0), stop=False), reads=[env.SA1a, env.xnT],
                    writes=[pv])
            P.op("tensor", lambda e, pv=pv, half=half: e.matmul(
                pv[:], env.ones_f[0:1, :],
                env.row_sc[0:1, half * 512:half * 512 + 512],
                start=False, stop=True), reads=[env.ones_f, env.row_sc], writes=[pv])
            P.op("scalar", lambda e, pv=pv, tt=tt: e.activation(
                out=env.SB[:, tt * 512:(tt + 1) * 512], in_=pv[:], func=AF.Copy),
                reads=[pv], writes=[env.SB])
        for gp in range(4):
            for hl0 in range(2):
                habs0 = 8 * half + 2 * gp + hl0
                P.dma("sync", stg[hl0][0], LqT[8 * habs0:8 * habs0 + 8, :],
                      reads=[env.SA0a], writes=[stg[hl0][2]], name="stgq")
                P.dma("sync", stg[hl0][1], JkT[8 * habs0:8 * habs0 + 8, :],
                      reads=[env.SA0a], writes=[stg[hl0][2]], name="stgk")
            for ic in range(4):
                for hl in range(2):
                    habs = 8 * half + 2 * gp + hl
                    stgQ, stgK, stgB = stg[hl]
                    vcol = (habs % 8) * 64
                    for iw in range(4):
                        ii = ic * 4 + iw
                        jw = (ii + 1) * 128
                        nb = (jw + 511) // 512
                        for jc in range(nb):
                            n = min(512, jw - jc * 512)
                            pA = env.psB[jc]
                            P.op("tensor", lambda e, pA=pA, ii=ii, jc=jc, n=n,
                                 stgQ=stgQ, stgK=stgK:
                                 e.matmul(pA[:, 0:n],
                                          stgQ[:, ii * 128:(ii + 1) * 128],
                                          stgK[:, jc * 512:jc * 512 + n],
                                          start=True, stop=True),
                                 reads=[stgB], writes=[pA])
                            P.op("vector", lambda e, pA=pA, jc=jc, n=n: e.reduce_max(
                                asc[:, jc:jc + 1], pA[:, 0:n], axis=AX.X),
                                reads=[pA], writes=[asc])
                        for jc in range(1, nb):
                            P.op("vector", lambda e, jc=jc: e.tensor_tensor(
                                out=asc[:, 0:1], in0=asc[:, 0:1],
                                in1=asc[:, jc:jc + 1], op=ALU.max),
                                reads=[asc], writes=[asc])
                        P.op("vector", lambda e: e.tensor_scalar(
                            out=asc[:, 5:6], in0=asc[:, 0:1], scalar1=-1.0,
                            scalar2=None, op0=ALU.mult), reads=[asc], writes=[asc])
                        for jc in range(nb):
                            n = min(512, jw - jc * 512)
                            pA = env.psB[jc]
                            full = n if jc < nb - 1 else n - 128
                            if full > 0:
                                P.op("scalar", lambda e, pA=pA, full=full, jc=jc:
                                     e.activation(
                                         out=atnA[:, 0:full], in_=pA[:, 0:full],
                                         func=AF.Exp, bias=asc[:, 5:6],
                                         accum_out=asc[:, 8 + jc:9 + jc]),
                                     reads=[pA, asc], writes=[env.SF, asc])
                            else:
                                P.op("gpsimd", lambda e, jc=jc: e.memset(
                                    asc[:, 8 + jc:9 + jc], 0.0), writes=[asc])
                        pA = env.psB[nb - 1]
                        n = jw - (nb - 1) * 512
                        P.op("scalar", lambda e, pA=pA, n=n: e.activation(
                            out=atnB[:, 0:128], in_=pA[:, n - 128:n], func=AF.Exp,
                            bias=asc[:, 5:6]), reads=[pA, asc], writes=[env.SF])
                        P.op("gpsimd", lambda e: e.affine_select(
                            out=atnB[:, 0:128], in_=atnB[:, 0:128],
                            compare_op=ALU.is_ge, fill=0.0, base=0,
                            pattern=[[-1, 128]], channel_multiplier=1),
                            reads=[env.SF], writes=[env.SF])
                        P.op("vector", lambda e: e.reduce_sum(
                            asc[:, 12:13], atnB[:, 0:128], axis=AX.X),
                            reads=[env.SF], writes=[asc])
                        for jc in range(nb):
                            P.op("vector", lambda e, jc=jc: e.tensor_tensor(
                                out=asc[:, 12:13], in0=asc[:, 12:13],
                                in1=asc[:, 8 + jc:9 + jc], op=ALU.add),
                                reads=[asc], writes=[asc])
                        P.op("scalar", lambda e: e.activation(
                            out=asc[:, 13:14], in_=asc[:, 12:13], func=AF.Ln),
                            reads=[asc], writes=[asc])
                        P.op("vector", lambda e: e.tensor_tensor(
                            out=asc[:, 14:15], in0=asc[:, 13:14], in1=asc[:, 0:1],
                            op=ALU.add), reads=[asc], writes=[asc])
                        P.op("vector", lambda e, iw=iw: e.tensor_scalar(
                            out=asc[:, 16 + iw:17 + iw], in0=asc[:, 14:15],
                            scalar1=-1.0, scalar2=None, op0=ALU.mult),
                            reads=[asc], writes=[asc])
                    ps6 = env.psB[6]
                    for iw in range(4):
                        P.op("tensor", lambda e, ps6=ps6, iw=iw: e.transpose(
                            ps6[0:1, iw * 128:(iw + 1) * 128],
                            asc[:, 16 + iw:17 + iw], env.identf[:]),
                            reads=[asc, env.identf], writes=[ps6])
                    P.op("vector", lambda e, ps6=ps6: e.tensor_copy(
                        out=env.srow[0:1, :], in_=ps6[0:1, 0:512]),
                        reads=[ps6], writes=[env.srow])
                    pav = env.psB[5 - hl]
                    njt = 4 * ic + 4
                    for jt in range(njt):
                        di = max(0, (jt - 4 * ic)) * 128
                        i0 = ic * 512 + di
                        nn = 512 - di
                        pB = env.psB[jt % 4]
                        P.op("tensor", lambda e, pB=pB, jt=jt, i0=i0, nn=nn,
                             stgQ=stgQ, stgK=stgK:
                             e.matmul(pB[:, 0:nn],
                                      stgK[:, jt * 128:(jt + 1) * 128],
                                      stgQ[:, i0:i0 + nn],
                                      start=True, stop=False),
                             reads=[stgB], writes=[pB])
                        P.op("tensor", lambda e, pB=pB, i0=i0, nn=nn, ic=ic:
                             e.matmul(pB[:, 0:nn], env.ones_f[0:1, :],
                                      env.srow[0:1, i0 - ic * 512:i0 - ic * 512 + nn],
                                      start=False, stop=True),
                             reads=[env.ones_f, env.srow], writes=[pB])
                        at = atnA if jt % 2 == 0 else atnB
                        P.op("scalar", lambda e, pB=pB, at=at, nn=nn: e.activation(
                            out=at[:, 0:nn], in_=pB[:, 0:nn], func=AF.Exp),
                            reads=[pB], writes=[env.SF])
                        if jt >= 4 * ic:
                            P.op("gpsimd", lambda e, at=at: e.affine_select(
                                out=at[:, 0:128], in_=at[:, 0:128],
                                compare_op=ALU.is_ge, fill=0.0, base=0,
                                pattern=[[1, 128]], channel_multiplier=-1),
                                reads=[env.SF], writes=[env.SF])
                        P.op("tensor", lambda e, pav=pav, at=at, jt=jt, di=di,
                             nn=nn, hl=hl, vcol=vcol: e.matmul(
                                 pav[64 * hl:64 * hl + 64, di:di + nn],
                                 env.SB[:, jt * 512 + vcol:jt * 512 + vcol + 64],
                                 at[:, 0:nn], start=(jt == 0 and di == 0),
                                 stop=(jt == njt - 1)),
                             reads=[env.SB, env.SF], writes=[pav])
                    if hl == 1:
                        for hx in range(2):
                            pvx = env.psB[5 - hx]
                            P.op("scalar", lambda e, pvx=pvx, ic=ic, gp=gp, hx=hx:
                                 e.activation(
                                     out=env.SB[64 * hx:64 * (hx + 1),
                                                8192 + gp * 2048 + ic * 512:
                                                8192 + gp * 2048 + (ic + 1) * 512],
                                     in_=pvx[64 * hx:64 * (hx + 1), :],
                                     func=AF.Copy),
                                 reads=[pvx], writes=[env.SB])
        if stage < 3:
            continue
        P.dma("sync", env.row_sc[0:1, 0:1024], d["bb_bout"], writes=[env.row_sc],
              name="boutld")
        for itc in range(4):
            for gp in range(4):
                ws = env.wslot[gp % 2]
                P.dma("gpsimd", ws[:],
                      d["bb_WoT"][half * 512 + gp * 128:half * 512 + (gp + 1) * 128, :],
                      writes=[ws], name="wood")
                for iw in range(4):
                    it = itc * 4 + iw
                    for h2 in range(2):
                        po = env.psB[iw * 2 + h2]
                        P.op("tensor", lambda e, po=po, gp=gp, it=it, h2=h2, ws=ws,
                             half=half: e.matmul(
                                 po[:],
                                 env.SB[:, 8192 + gp * 2048 + it * 128:
                                        8192 + gp * 2048 + (it + 1) * 128],
                                 ws[:, h2 * 512:(h2 + 1) * 512],
                                 start=(gp == 0), stop=(half == 1 and gp == 3)),
                             reads=[env.SB, ws], writes=[po])
            for iw in range(4):
                it = itc * 4 + iw
                for h2 in range(2):
                    po = env.psB[iw * 2 + h2]
                    if half == 0:
                        P.op("tensor", lambda e, po=po, h2=h2: e.matmul(
                            po[:], env.ones_f[0:1, :],
                            env.row_sc[0:1, h2 * 512:h2 * 512 + 512],
                            start=False, stop=True),
                            reads=[env.ones_f, env.row_sc], writes=[po])
                    P.op("vector", lambda e, po=po, h2=h2: e.tensor_copy(
                        out=env.scF1[:, h2 * 512:(h2 + 1) * 512], in_=po[:]),
                        reads=[po], writes=[env.scF1])
                xi = env.x[it]
                P.op("gpsimd", lambda e, xi=xi: e.tensor_tensor(
                    out=xi[:], in0=xi[:], in1=env.scF1[:], op=ALU.add),
                    reads=[xi, env.scF1], writes=[xi])



def spec_l1attna():
    def raw(key):
        return lambda i, key=key: np.ascontiguousarray(np.asarray(i[key]))

    return {
        "aa_Wqv": ((1024, 2048), F32, raw("L1_attn_a_W_qv")),
        "aa_W20": ((1024, 84), F32, raw("L1_attn_a_W2_0")),
        "aa_W21": ((1024, 164), F32, raw("L1_attn_a_W2_1")),
        "aa_Wout": ((1024, 1024), F32, raw("L1_attn_a_W_out")),
        "aa_sw": ((1, 2), F32, lambda i: np.ascontiguousarray(
            np.asarray(i["L1_attn_a_scale_weights"]).reshape(1, 2))),
    }


def build_l1attna(P, env, stage=99):
    layernorm_to_xnT(P, env)
    d = env.dram
    SFbf = env.SF[:].bitcast(BF16)
    scr = env.aa_scr
    scrbuf = env.aa_scrbuf
    P.dma("sync", env.lam_sb[0:1, 0:2], d["aa_sw"], writes=[env.lam_sb])
    P.op("vector", lambda e: e.tensor_tensor(
        out=env.lam_sb[0:1, 2:3], in0=env.lam_sb[0:1, 0:1],
        in1=env.lam_sb[0:1, 1:2], op=ALU.subtract),
        reads=[env.lam_sb], writes=[env.lam_sb])
    P.op("scalar", lambda e: e.activation(
        out=env.lam_sb[0:1, 3:4], in_=env.lam_sb[0:1, 2:3], func=AF.Sigmoid),
        reads=[env.lam_sb], writes=[env.lam_sb])
    P.op("vector", lambda e: e.tensor_scalar(
        out=env.lam_sb[0:1, 4:5], in0=env.lam_sb[0:1, 3:4], scalar1=-1.0,
        scalar2=1.0, op0=ALU.mult, op1=ALU.add),
        reads=[env.lam_sb], writes=[env.lam_sb])
    pb = env.psB[0]
    for j in range(2):
        P.op("tensor", lambda e, pb=pb, j=j: e.matmul(
            pb[:, j:j + 1], env.ones_f[0:1, :], env.lam_sb[0:1, 3 + j:4 + j],
            start=(j == 0), stop=(j == 1)), reads=[env.ones_f, env.lam_sb],
            writes=[pb])
    P.op("vector", lambda e, pb=pb: e.tensor_copy(
        out=env.lamrep[:, 6:8], in_=pb[:, 0:2]), reads=[pb], writes=[env.lamrep])
    P.op("gpsimd", lambda e: e.memset(SFbf[:, 0:512], 0.0), writes=[env.SF])
    for zz in range(2):
        P.dma("scalar", scr[zz * 136:zz * 136 + 128, :], SFbf[:, 0:512],
              reads=[env.SF], writes=[scrbuf], name="scrz0")
        P.dma("scalar", scr[zz * 136 + 128:zz * 136 + 136, :], SFbf[0:8, 0:512],
              reads=[env.SF], writes=[scrbuf], name="scrz1")

    for hh in range(2):
        buf = env.SA1a if hh == 0 else env.SA1b
        P.dma("gpsimd", buf[:].rearrange("p (k m) -> p k m", k=4),
              d["aa_Wqv"][hh * 512:(hh + 1) * 512, 1024:2048].rearrange(
                  "(k p) m -> p k m", p=128), writes=[buf], name="wvld")
    for tt in range(NT):
        for h2 in range(2):
            pv = env.psB[4 + h2]
            for k in range(ND):
                buf = env.SA1a if k < 4 else env.SA1b
                P.op("tensor", lambda e, pv=pv, k=k, tt=tt, h2=h2, buf=buf: e.matmul(
                    pv[:], xnt(env, k, tt * 128, (tt + 1) * 128),
                    buf[:, (k % 4) * 1024 + h2 * 512:(k % 4) * 1024 + h2 * 512 + 512],
                    start=(k == 0), stop=(k == ND - 1)),
                    reads=[buf, env.xnT], writes=[pv])
            P.op("scalar", lambda e, pv=pv, tt=tt, h2=h2: e.activation(
                out=env.SB[:, tt * 1024 + h2 * 512:tt * 1024 + h2 * 512 + 512],
                in_=pv[:], func=AF.Copy), reads=[pv], writes=[env.SB])
    qr = ([env.SA0a[:, m * 2048:(m + 1) * 2048] for m in range(2)]
          + [env.SA0b[:, m * 2048:(m + 1) * 2048] for m in range(2)]
          + [env.SA1a[:, m * 2048:(m + 1) * 2048] for m in range(2)]
          + [env.SA1b[:, m * 2048:(m + 1) * 2048] for m in range(2)])
    qrb = [env.SA0a] * 2 + [env.SA0b] * 2 + [env.SA1a] * 2 + [env.SA1b] * 2
    for m in range(8):
        ws = env.wslot[m % 2]
        P.dma("gpsimd", ws[:].rearrange("p (k f) -> p k f", k=8),
              d["aa_Wqv"][:, m * 128:(m + 1) * 128].rearrange(
                  "(k p) f -> p k f", p=128), writes=[ws], name="wqld")
        for tc in range(4):
            ph = env.psB[tc % 2]
            for k in range(ND):
                P.op("tensor", lambda e, ph=ph, ws=ws, k=k, tc=tc: e.matmul(
                    ph[:], ws[:, k * 128:(k + 1) * 128],
                    xnt(env, k, tc * 512, (tc + 1) * 512),
                    start=(k == 0), stop=(k == ND - 1)),
                    reads=[ws, env.xnT], writes=[ph])
            P.op("scalar", lambda e, ph=ph, m=m, tc=tc: e.activation(
                out=qr[m][:, tc * 512:(tc + 1) * 512], in_=ph[:], func=AF.Relu),
                reads=[ph], writes=[qrb[m]])
    if stage < 2:
        return
    W2sb = {0: SFbf[:, 1040:1712], 1: SFbf[:, 1712:3024]}
    for (ci, wnm, wc) in ((0, "aa_W20", 84), (1, "aa_W21", 164)):
        for m in range(8):
            P.dma("gpsimd", W2sb[ci][:, m * wc:(m + 1) * wc],
                  d[wnm][m * 128:(m + 1) * 128, :], writes=[env.SF], name="w2ld")
    attnw = SFbf[:, 0:656]
    band = SFbf[:, 656:1040]
    bandT = SFbf[:, 7120:7504]
    CSZ = ((0, 21, 10), (1, 41, 20))
    for itc in range(4):
        for iw in range(4):
            tt = itc * 4 + iw
            first = {4: True, 5: True}
            last_of = {}
            for (ci, c, half) in CSZ:
                for h in range(4):
                    for jls in range(3):
                        jt = tt - 1 + jls
                        if 0 <= jt < NT:
                            for m2 in range(2):
                                dti = 2 * h + m2
                                last_of[4 + dti // 4] = (ci, h, jls, m2)
            for (ci, c, half) in CSZ:
                pl = env.psB[2 + ci]
                wc = 84 if ci == 0 else 164
                for m in range(8):
                    P.op("tensor", lambda e, pl=pl, m=m, tt=tt, ci=ci, wc=wc:
                         e.matmul(pl[:, 0:wc],
                                  qr[m][:, tt * 128:(tt + 1) * 128],
                                  W2sb[ci][:, m * wc:(m + 1) * wc],
                                  start=(m == 0), stop=(m == ND - 1)),
                         reads=[qrb[m], env.SF], writes=[pl])
                P.op("vector", lambda e, pl=pl, wc=wc: e.tensor_copy(
                    out=env.scF1[:, 0:wc], in_=pl[:, 0:wc]), reads=[pl],
                    writes=[env.scF1])
                for h in range(4):
                    cs = slice(h * c, (h + 1) * c)
                    P.op("vector", lambda e, cs=cs: e.reduce_max(
                        env.rts[:, 0:1], env.scF1[:, cs], axis=AX.X),
                        reads=[env.scF1], writes=[env.rts])
                    P.op("vector", lambda e: e.tensor_scalar(
                        out=env.rts[:, 1:2], in0=env.rts[:, 0:1], scalar1=-1.0,
                        scalar2=None, op0=ALU.mult), reads=[env.rts],
                        writes=[env.rts])
                    P.op("scalar", lambda e, cs=cs, c=c: e.activation(
                        out=env.scF2[:, 0:c], in_=env.scF1[:, cs], func=AF.Exp,
                        bias=env.rts[:, 1:2], accum_out=env.rts[:, 2:3]),
                        reads=[env.scF1, env.rts], writes=[env.scF2, env.rts])
                    P.op("vector", lambda e: e.reciprocal(
                        out=env.rts[:, 3:4], in_=env.rts[:, 2:3]),
                        reads=[env.rts], writes=[env.rts])
                    P.op("vector", lambda e, cs=cs, c=c, ci=ci: e.tensor_scalar(
                        out=attnw[:, cs], in0=env.scF2[:, 0:c],
                        scalar1=env.rts[:, 3:4], scalar2=env.lamrep[:, 6 + ci:7 + ci],
                        op0=ALU.mult, op1=ALU.mult),
                        reads=[env.scF2, env.rts, env.lamrep], writes=[env.SF])
                for h in range(4):
                    P.dma("scalar", bass.AP(
                        tensor=scr.tensor, offset=(ci * 136 + 1) * 512,
                        ap=[[512, 128], [1, c]]),
                        attnw[:, h * c:(h + 1) * c],
                        reads=[env.SF], writes=[scrbuf], name="bandout")
                    P.dma("scalar", band,
                          bass.AP(tensor=scr.tensor,
                                  offset=ci * 136 * 512 + 384 + half,
                                  ap=[[511, 128], [1, 384]]),
                          reads=[scrbuf], writes=[env.SF], name="bandin")
                    for jls in range(3):
                        ptp = env.psB[2 + jls % 2]
                        ptpb = ptp[:].bitcast(BF16)
                        P.op("tensor", lambda e, ptpb=ptpb, jls=jls: e.transpose(
                            ptpb[:, 0:128], band[:, jls * 128:(jls + 1) * 128],
                            env.ident[:]), reads=[env.SF, env.ident],
                            writes=[env.psB[2 + jls % 2]])
                        P.op("vector", lambda e, ptpb=ptpb, jls=jls: e.tensor_copy(
                            out=bandT[:, jls * 128:(jls + 1) * 128],
                            in_=ptpb[:, 0:128]),
                            reads=[env.psB[2 + jls % 2]], writes=[env.SF])
                        jt = tt - 1 + jls
                        if not (0 <= jt < NT):
                            continue
                        for m2 in range(2):
                            dti = 2 * h + m2
                            bank = env.psB[4 + dti // 4]
                            col = (dti % 4) * 128
                            st_ = first[4 + dti // 4]
                            first[4 + dti // 4] = False
                            sp_ = (last_of[4 + dti // 4] == (ci, h, jls, m2))
                            P.op("tensor", lambda e, bank=bank, col=col, jt=jt,
                                 h=h, m2=m2, jls=jls, st_=st_, sp_=sp_: e.matmul(
                                     bank[:, col:col + 128],
                                     env.SB[:, jt * 1024 + h * 256 + m2 * 128:
                                            jt * 1024 + h * 256 + m2 * 128 + 128],
                                     bandT[:, jls * 128:(jls + 1) * 128],
                                     start=st_, stop=sp_),
                                 reads=[env.SB, env.SF], writes=[bank])
            for b2 in range(2):
                P.op("vector", lambda e, b2=b2, iw=iw: e.tensor_copy(
                    out=SFbf[:, 3024 + iw * 1024 + b2 * 512:
                             3024 + iw * 1024 + b2 * 512 + 512],
                    in_=env.psB[4 + b2][:]), reads=[env.psB[4 + b2]],
                    writes=[env.SF])
        for k in range(ND):
            ws = env.wslot[k % 2]
            P.dma("gpsimd", ws[:], d["aa_Wout"][k * 128:(k + 1) * 128, :],
                  writes=[ws], name="wold")
            for iw in range(4):
                for h2 in range(2):
                    po = env.psB[iw * 2 + h2]
                    P.op("tensor", lambda e, po=po, k=k, iw=iw, h2=h2, ws=ws:
                         e.matmul(po[:],
                                  SFbf[:, 3024 + iw * 1024 + k * 128:
                                       3024 + iw * 1024 + (k + 1) * 128],
                                  ws[:, h2 * 512:(h2 + 1) * 512],
                                  start=(k == 0), stop=(k == ND - 1)),
                         reads=[env.SF, ws], writes=[po])
        for iw in range(4):
            it = itc * 4 + iw
            for h2 in range(2):
                po = env.psB[iw * 2 + h2]
                P.op("vector", lambda e, po=po, h2=h2: e.tensor_copy(
                    out=env.scF1[:, h2 * 512:(h2 + 1) * 512], in_=po[:]),
                    reads=[po], writes=[env.scF1])
            xi = env.x[it]
            P.op("gpsimd", lambda e, xi=xi: e.tensor_tensor(
                out=xi[:], in0=xi[:], in1=env.scF1[:], op=ALU.add),
                reads=[xi, env.scF1], writes=[xi])



MTC = 256
MNC = T_SEQ // MTC


def spec_l0attna():
    def f(key, fn):
        return lambda i, key=key, fn=fn: np.ascontiguousarray(fn(np.asarray(i[key])))

    def tile16(a):
        return a.reshape(16, 128).T

    return {
        "ma_inpT": ((1024, 4096), F32, f("L0_attn_a_h_in_proj_w", lambda a: a.T)),
        "ma_convw": ((128, 64), F32, f("L0_attn_a_h_conv_w", lambda a:
            a.reshape(16, 128, 4).transpose(1, 0, 2).reshape(128, 64))),
        "ma_convb": ((128, 16), F32, f("L0_attn_a_h_conv_b", tile16)),
        "ma_dtb": ((128, 16), F32, f("L0_attn_a_h_dt_proj_b", tile16)),
        "ma_D": ((128, 16), F32, f("L0_attn_a_h_D_param", tile16)),
        "ma_xpwT": ((2048, 96), F32, f("L0_attn_a_h_x_proj_w", lambda a: a.T)),
        "ma_dtwT": ((64, 2048), F32, f("L0_attn_a_h_dt_proj_w", lambda a: a.T)),
        "ma_AlogT": ((128, 256), F32, f("L0_attn_a_h_A_log", lambda a:
            a.reshape(256, 8, 16).transpose(1, 2, 0).reshape(128, 256))),
        "ma_opwT": ((2048, 1024), F32, f("L0_attn_a_h_out_proj_w", lambda a: a.T)),
    }


def build_l0attna(P, env, stage=99):
    layernorm_to_xnT(P, env)
    d = env.dram
    SFbf = env.SF[:].bitcast(BF16)
    halo = env.SA1b[:, 1024:1072]
    selD = env.SA0b[:, 0:2048]
    selR = env.SA0b[:, 2048:4096]
    NSL = 3
    if env.once("masmall"):
        env.ma_cw = P.sb("s_ma_cw", [128, 64], F32)
        env.ma_cb = P.sb("s_ma_cb", [128, 16], F32)
        env.ma_dtb = P.sb("s_ma_dtb", [128, 16], F32)
        env.ma_Dp = P.sb("s_ma_Dp", [128, 16], F32)
        env.ma_selB = P.sb("s_ma_selB", [32, 256], BF16)
        SBfv = env.SB[:].bitcast(F32)

        def carve(name, ap):
            P.bufmap[id(ap)] = Buf(name)
            return ap
        env.ma_sa = [carve(f"ma_sa{s}", SBfv[:, 4096 + s * 256:4096 + (s + 1) * 256])
                     for s in range(NSL)]
        env.ma_sbb = [carve(f"ma_sbb{s}", SBfv[:, 4864 + s * 256:4864 + (s + 1) * 256])
                      for s in range(NSL)]
        env.ma_sh = [carve(f"ma_sh{s}", SBfv[:, 5632 + s * 256:5632 + (s + 1) * 256])
                     for s in range(NSL)]
        env.ma_gf = [carve(f"ma_gf{j}", SBfv[:, 6400 + j * 256:6400 + (j + 1) * 256])
                     for j in range(2)]
        env.ma_shc = [carve(f"ma_shc{s}",
                            env.SB[:, 13824 + s * 256:13824 + (s + 1) * 256])
                      for s in range(NSL)]
        env.ma_slz = [carve(f"ma_slz{j}",
                            env.SB[:, 14592 + j * 256:14592 + (j + 1) * 256])
                      for j in range(2)]
        env.ma_du = [carve(f"ma_du{j}",
                           env.SB[:, 15104 + j * 256:15104 + (j + 1) * 256])
                     for j in range(2)]
        env.ma_brep = carve("ma_brep", env.SB[:, 15616:15872])
        env.ma_crep = carve("ma_crep", env.SB[:, 15872:16128])
        env.ma_A = carve("ma_A", env.SF[:, 2048:2304])
        env.ma_hcar = carve("ma_hcar", env.SF[:, 2304:2560])
    A_sb = env.ma_A
    hcar = env.ma_hcar
    P.dma("sync", env.ma_cw[:], d["ma_convw"], writes=[env.ma_cw])
    P.dma("sync", env.ma_cb[:], d["ma_convb"], writes=[env.ma_cb])
    P.dma("sync", env.ma_dtb[:], d["ma_dtb"], writes=[env.ma_dtb])
    P.dma("sync", env.ma_Dp[:], d["ma_D"], writes=[env.ma_Dp])
    P.dma("sync", A_sb, d["ma_AlogT"], writes=[env.ma_A])
    P.op("scalar", lambda e: e.activation(out=A_sb, in_=A_sb, func=AF.Exp),
         reads=[env.ma_A], writes=[env.ma_A])
    P.op("vector", lambda e: e.tensor_scalar(
        out=A_sb, in0=A_sb, scalar1=-1.0, scalar2=None, op0=ALU.mult),
        reads=[env.ma_A], writes=[env.ma_A])
    P.op("gpsimd", lambda e: e.memset(hcar, 0.0), writes=[env.ma_hcar])
    P.op("gpsimd", lambda e: e.memset(halo, 0.0), writes=[env.SA1b])
    P.op("gpsimd", lambda e: e.memset(env.ma_selB[:], 0.0), writes=[env.ma_selB])
    for which in range(2):
        P.op("gpsimd", lambda e, which=which: e.affine_select(
            out=env.ma_selB[:, which * 128:(which + 1) * 128].rearrange(
                "s (dlo n) -> s dlo n", n=16),
            in_=env.ma_selB[:, which * 128:(which + 1) * 128].rearrange(
                "s (dlo n) -> s dlo n", n=16),
            compare_op=ALU.not_equal, fill=1.0, base=-16 * which,
            pattern=[[0, 8], [-1, 16]], channel_multiplier=1),
            reads=[env.ma_selB], writes=[env.ma_selB])
    for r8 in range(16):
        sl = selD[:, r8 * 128:(r8 + 1) * 128]
        P.op("gpsimd", lambda e, sl=sl: e.memset(sl, 0.0), writes=[env.SA0b])
        P.op("gpsimd", lambda e, sl=sl, r8=r8: e.affine_select(
            out=sl.rearrange("s (dlo n) -> s dlo n", n=16),
            in_=sl.rearrange("s (dlo n) -> s dlo n", n=16),
            compare_op=ALU.not_equal, fill=1.0, base=-8 * r8,
            pattern=[[-1, 8], [0, 16]], channel_multiplier=1),
            reads=[env.SA0b], writes=[env.SA0b])
        sr = selR[:, r8 * 128:(r8 + 1) * 128]
        P.op("gpsimd", lambda e, sr=sr: e.memset(sr, 1.0), writes=[env.SA0b])
        P.op("gpsimd", lambda e, sr=sr, r8=r8: e.affine_select(
            out=sr, in_=sr, compare_op=ALU.is_ge, fill=0.0, base=16 * 8 * r8,
            pattern=[[-16, 128]], channel_multiplier=1),
            reads=[env.SA0b], writes=[env.SA0b])
        P.op("gpsimd", lambda e, sr=sr, r8=r8: e.affine_select(
            out=sr, in_=sr, compare_op=ALU.is_ge, fill=0.0,
            base=15 - 16 * 8 * r8, pattern=[[16, 128]], channel_multiplier=-1),
            reads=[env.SA0b], writes=[env.SA0b])
    if stage < 1:
        return

    xin = [env.SA0a[:, dt * MTC:(dt + 1) * MTC] for dt in range(16)]
    zt = [env.SA1a[:, dt * MTC:(dt + 1) * MTC] for dt in range(16)]
    ut = [env.SB[:, dt * MTC:(dt + 1) * MTC] for dt in range(16)]
    yt = [env.SB[:, 4096 + dt * MTC:4096 + (dt + 1) * MTC] for dt in range(16)]
    dl = [SFbf[:, dt * MTC:(dt + 1) * MTC] for dt in range(16)]
    xdbl = env.wslot[1][0:96, 0:MTC]
    brep = env.ma_brep
    crep = env.ma_crep
    dtw_t = [env.wslot[1][0:64, 512 + j * 128:512 + (j + 1) * 128] for j in range(2)]
    xpw_t = [env.wslot[0][:, 768 + j * 128:768 + (j + 1) * 128] for j in range(2)]
    S2b = env.scF2

    for c in range(MNC):
        t0 = c * MTC
        for et in range(32):
            ws = env.wslot[et % 2]
            P.dma("gpsimd", ws[:].rearrange("p (k m) -> p k m", k=8),
                  d["ma_inpT"][:, et * 128:(et + 1) * 128].rearrange(
                      "(k p) m -> p k m", p=128), writes=[ws], name="mainp")
            ph = env.psB[et % 2]
            for k in range(ND):
                P.op("tensor", lambda e, ph=ph, ws=ws, k=k, t0=t0: e.matmul(
                    ph[:, 0:MTC], ws[:, k * 128:(k + 1) * 128],
                    xnt(env, k, t0, t0 + MTC),
                    start=(k == 0), stop=(k == ND - 1)),
                    reads=[ws, env.xnT], writes=[ph])
            dst = xin[et] if et < 16 else zt[et - 16]
            dstb = env.SA0a if et < 16 else env.SA1a
            P.op("scalar", lambda e, ph=ph, dst=dst: e.activation(
                out=dst, in_=ph[:, 0:MTC], func=AF.Copy),
                reads=[ph], writes=[dstb])
        if stage < 2:
            continue
        uA = env.scF1[:, 0:256]
        uB = env.scF1[:, 256:512]
        for dt in range(16):
            P.op("vector", lambda e, dt=dt, uA=uA: e.tensor_scalar(
                out=uA, in0=xin[dt], scalar1=env.ma_cw[:, dt * 4 + 3:dt * 4 + 4],
                scalar2=None, op0=ALU.mult),
                reads=[env.SA0a, env.ma_cw], writes=[env.scF1])
            cur, nxt = uA, uB
            for j in (2, 1, 0):
                s = 3 - j
                P.op("vector", lambda e, dt=dt, j=j, s=s, cur=cur, nxt=nxt:
                     e.scalar_tensor_tensor(
                         out=nxt[:, s:MTC], in0=xin[dt][:, 0:MTC - s],
                         scalar=env.ma_cw[:, dt * 4 + j:dt * 4 + j + 1],
                         in1=cur[:, s:MTC], op0=ALU.mult, op1=ALU.add),
                     reads=[env.SA0a, env.ma_cw, env.scF1], writes=[env.scF1])
                P.op("vector", lambda e, dt=dt, j=j, s=s, cur=cur, nxt=nxt:
                     e.scalar_tensor_tensor(
                         out=nxt[:, 0:s], in0=halo[:, dt * 3 + 3 - s:dt * 3 + 3],
                         scalar=env.ma_cw[:, dt * 4 + j:dt * 4 + j + 1],
                         in1=cur[:, 0:s], op0=ALU.mult, op1=ALU.add),
                     reads=[env.SA1b, env.ma_cw, env.scF1], writes=[env.scF1])
                cur, nxt = nxt, cur
            P.op("scalar", lambda e, dt=dt, cur=cur: e.activation(
                out=ut[dt], in_=cur, func=AF.Silu,
                bias=env.ma_cb[:, dt:dt + 1]),
                reads=[env.scF1, env.ma_cb], writes=[env.SB])
        for dt in range(16):
            P.op("gpsimd", lambda e, dt=dt: e.tensor_copy(
                out=halo[:, dt * 3:dt * 3 + 3], in_=xin[dt][:, MTC - 3:MTC]),
                reads=[env.SA0a], writes=[env.SA1b])
        if stage < 3:
            continue
        px = env.psB[2]
        for dt in range(16):
            xp = xpw_t[dt % 2]
            P.dma("gpsimd", xp[:, 0:96], d["ma_xpwT"][dt * 128:(dt + 1) * 128, :],
                  writes=[env.wslot[0]], name="maxpw")
            P.op("tensor", lambda e, px=px, xp=xp, dt=dt: e.matmul(
                px[0:96, 0:MTC], xp[:, 0:96], ut[dt],
                start=(dt == 0), stop=(dt == 15)),
                reads=[env.wslot[0], env.SB], writes=[px])
        P.op("scalar", lambda e, px=px: e.activation(
            out=env.wslot[1][0:64, 0:MTC], in_=px[0:64, 0:MTC], func=AF.Copy),
            reads=[px], writes=[env.wslot[1]])
        BC_sb = env.wslot[1][0:32, 768:1024]
        P.op("scalar", lambda e, px=px, BC_sb=BC_sb: e.activation(
            out=BC_sb, in_=px[64:96, 0:MTC], func=AF.Copy),
            reads=[px], writes=[env.wslot[1]])
        for (rep, repbuf, which) in ((brep, env.ma_brep, 0), (crep, env.ma_crep, 1)):
            pr = env.psB[3]
            P.op("tensor", lambda e, pr=pr, which=which, BC_sb=BC_sb: e.matmul(
                pr[:, 0:MTC], env.ma_selB[:, which * 128:(which + 1) * 128],
                BC_sb, start=True, stop=True),
                reads=[env.ma_selB, env.wslot[1]], writes=[pr])
            P.op("vector", lambda e, pr=pr, rep=rep: e.tensor_copy(
                out=rep, in_=pr[:, 0:MTC]), reads=[pr], writes=[repbuf])
        for dt in range(16):
            dwt = dtw_t[dt % 2]
            P.dma("gpsimd", dwt, d["ma_dtwT"][:, dt * 128:(dt + 1) * 128],
                  writes=[env.wslot[1]], name="madtw")
            pd = env.psB[3]
            P.op("tensor", lambda e, pd=pd, dwt=dwt: e.matmul(
                pd[:, 0:MTC], dwt, env.wslot[1][0:64, 0:MTC],
                start=True, stop=True),
                reads=[env.wslot[1]], writes=[pd])
            P.op("scalar", lambda e, pd=pd, dt=dt: e.activation(
                out=S2b[:, 256:512], in_=pd[:, 0:MTC], func=AF.Abs,
                bias=env.ma_dtb[:, dt:dt + 1]),
                reads=[pd, env.ma_dtb], writes=[env.scF2])
            P.op("scalar", lambda e: e.activation(
                out=S2b[:, 512:768], in_=S2b[:, 256:512], func=AF.Exp, scale=-1.0),
                reads=[env.scF2], writes=[env.scF2])
            P.op("scalar", lambda e: e.activation(
                out=S2b[:, 256:512], in_=S2b[:, 512:768], func=AF.Ln,
                bias=env.cst[:, 1:2]), reads=[env.scF2, env.cst], writes=[env.scF2])
            P.op("vector", lambda e, pd=pd, dt=dt: e.tensor_scalar(
                out=S2b[:, 512:768], in0=pd[:, 0:MTC],
                scalar1=env.ma_dtb[:, dt:dt + 1], scalar2=0.0,
                op0=ALU.add, op1=ALU.max),
                reads=[pd, env.ma_dtb], writes=[env.scF2])
            P.op("vector", lambda e, dt=dt: e.tensor_tensor(
                out=dl[dt], in0=S2b[:, 256:512], in1=S2b[:, 512:768], op=ALU.add),
                reads=[env.scF2], writes=[env.SF])
        if stage < 4:
            continue
        for dt in range(16):
            du = env.ma_du[dt % 2]
            P.op("vector", lambda e, dt=dt, du=du: e.tensor_tensor(
                out=du[:], in0=dl[dt], in1=ut[dt], op=ALU.mult),
                reads=[env.SF, env.SB], writes=[du])
            yps = env.psB[2]
            for gg in range(16):
                g = dt * 16 + gg
                s = g % NSL
                a_f, b_f = env.ma_sa[s], env.ma_sbb[s]
                h_f, hC = env.ma_sh[s], env.ma_shc[s]
                pr1 = env.psB[4 + 2 * (gg % 2)]
                pr2 = env.psB[5 + 2 * (gg % 2)]
                sl = selD[:, (g % 16) * 128:(g % 16 + 1) * 128]
                P.op("tensor", lambda e, pr1=pr1, sl=sl, dt=dt: e.matmul(
                    pr1[:, 0:MTC], sl, dl[dt], start=True, stop=True),
                    reads=[env.SA0b, env.SF], writes=[pr1])
                P.op("tensor", lambda e, pr2=pr2, sl=sl, du=du: e.matmul(
                    pr2[:, 0:MTC], sl, du[:], start=True, stop=True),
                    reads=[env.SA0b, du], writes=[pr2])
                P.op("scalar", lambda e, pr1=pr1, g=g, a_f=a_f: e.activation(
                    out=a_f[:], in_=pr1[:, 0:MTC], func=AF.Exp,
                    scale=A_sb[:, g:g + 1]),
                    reads=[pr1, env.ma_A], writes=[a_f])
                P.op("vector", lambda e, pr2=pr2, b_f=b_f: e.tensor_tensor(
                    out=b_f[:], in0=pr2[:, 0:MTC], in1=brep, op=ALU.mult),
                    reads=[pr2, env.ma_brep], writes=[b_f])
                P.op("vector", lambda e, g=g, a_f=a_f, b_f=b_f, h_f=h_f:
                     e.tensor_tensor_scan(
                         out=h_f[:], data0=a_f[:], data1=b_f[:],
                         initial=hcar[:, g:g + 1], op0=ALU.mult, op1=ALU.add),
                     reads=[a_f, b_f, env.ma_hcar], writes=[h_f])
                if c < MNC - 1:
                    P.op("vector", lambda e, g=g, h_f=h_f: e.tensor_copy(
                        out=hcar[:, g:g + 1], in_=h_f[:, MTC - 1:MTC]),
                        reads=[h_f], writes=[env.ma_hcar])
                P.op("gpsimd", lambda e, h_f=h_f, hC=hC: e.tensor_tensor(
                    out=hC[:], in0=h_f[:], in1=crep, op=ALU.mult),
                    reads=[h_f, env.ma_crep], writes=[hC])
                P.op("tensor", lambda e, yps=yps, gg=gg, hC=hC: e.matmul(
                    yps[:, 0:MTC], selR[:, gg * 128:(gg + 1) * 128], hC[:],
                    start=(gg == 0), stop=(gg == 15)),
                    reads=[env.SA0b, hC], writes=[yps])
            slz = env.ma_slz[dt % 2]
            g_f = env.ma_gf[dt % 2]
            P.op("scalar", lambda e, dt=dt, slz=slz: e.activation(
                out=slz[:], in_=zt[dt], func=AF.Silu),
                reads=[env.SA1a], writes=[slz])
            P.op("vector", lambda e, yps=yps, dt=dt, g_f=g_f: e.scalar_tensor_tensor(
                out=g_f[:], in0=ut[dt], scalar=env.ma_Dp[:, dt:dt + 1],
                in1=yps[:, 0:MTC], op0=ALU.mult, op1=ALU.add),
                reads=[env.SB, env.ma_Dp, yps], writes=[g_f])
            P.op("vector", lambda e, dt=dt, g_f=g_f, slz=slz: e.tensor_tensor(
                out=yt[dt], in0=g_f[:], in1=slz[:], op=ALU.mult),
                reads=[g_f, slz], writes=[env.SB])
        if stage < 5:
            continue
        for k in range(16):
            ws = env.wslot[k % 2]
            P.dma("gpsimd", ws[:], d["ma_opwT"][k * 128:(k + 1) * 128, :],
                  writes=[ws], name="maopw")
            for tt2 in range(2):
                for h2 in range(2):
                    po = env.psB[tt2 * 2 + h2]
                    P.op("tensor", lambda e, po=po, ws=ws, k=k, tt2=tt2, h2=h2:
                         e.matmul(po[:], yt[k][:, tt2 * 128:(tt2 + 1) * 128],
                                  ws[:, h2 * 512:(h2 + 1) * 512],
                                  start=(k == 0), stop=(k == 15)),
                         reads=[env.SB, ws], writes=[po])
        for tt2 in range(2):
            it = c * 2 + tt2
            for h2 in range(2):
                po = env.psB[tt2 * 2 + h2]
                P.op("vector", lambda e, po=po, h2=h2: e.tensor_copy(
                    out=env.scF1[:, h2 * 512:(h2 + 1) * 512], in_=po[:]),
                    reads=[po], writes=[env.scF1])
            xi = env.x[it]
            P.op("gpsimd", lambda e, xi=xi: e.tensor_tensor(
                out=xi[:], in0=xi[:], in1=env.scF1[:], op=ALU.add),
                reads=[xi, env.scF1], writes=[xi])



def spec_none():
    return {}


def build_lnonly(P, env):
    layernorm_to_xnT(P, env)
    for i in range(NT):
        tm = env.xn_tm[0]
        pstp = env.ps_tp[0]
        pstp_bf = pstp[:].bitcast(BF16)
        for k in range(ND):
            src_ap = env.xnT[:, k * T_SEQ + i * 128:k * T_SEQ + (i + 1) * 128]
            P.op("tensor", lambda e, src_ap=src_ap, pstp_bf=pstp_bf, k=k: e.transpose(
                pstp_bf[:, k * 128:(k + 1) * 128], src_ap, env.ident[:]),
                reads=[env.xnT, env.ident], writes=[pstp])
        xi = env.x[i]
        P.op("vector", lambda e, xi=xi, pstp_bf=pstp_bf: e.tensor_copy(
            out=xi[:], in_=pstp_bf), reads=[pstp], writes=[xi])


SUBLAYERS = {
    0: ("l0attna", spec_l0attna, build_l0attna),
    21: ("maa_s1", spec_l0attna, lambda P, env: build_l0attna(P, env, 1)),
    22: ("maa_s2", spec_l0attna, lambda P, env: build_l0attna(P, env, 2)),
    23: ("maa_s3", spec_l0attna, lambda P, env: build_l0attna(P, env, 3)),
    24: ("maa_s4", spec_l0attna, lambda P, env: build_l0attna(P, env, 4)),
    1: ("l0ffn_a", spec_l0ffn, build_l0ffn),
    3: ("l0ffn_b", spec_l0ffn, build_l0ffn),
    2: ("l0attnb", spec_l0attnb, build_l0attnb),
    31: ("ab_s1", spec_l0attnb, lambda P, env: build_l0attnb(P, env, 1)),
    34: ("ab_dbg", spec_l0attnb, lambda P, env: (build_l0attnb(P, env, 2), [
        P.op("vector", lambda e, i=i: e.tensor_copy(
            out=env.x[i][:], in_=env.SB[:, i * 2048:(i + 1) * 2048].bitcast(F32)),
            reads=[env.SB], writes=[env.x[i]]) for i in range(8)])[0]),
    32: ("ab_s2", spec_l0attnb, lambda P, env: build_l0attnb(P, env, 2)),
    41: ("l0f_s0", spec_l0ffn, lambda P, env: build_l0ffn(P, env, 0)),
    42: ("l0f_s1", spec_l0ffn, lambda P, env: build_l0ffn(P, env, 1)),
    5: ("l1ffn_a", spec_l1ffn, build_l1ffn),
    4: ("l1attna", spec_l1attna, build_l1attna),
    51: ("aa_s1", spec_l1attna, lambda P, env: build_l1attna(P, env, 1)),
    6: ("l1attnb", spec_l1attnb, build_l1attnb),
    61: ("bb_s1", spec_l1attnb, lambda P, env: build_l1attnb(P, env, 1)),
    62: ("bb_dbg", spec_l1attnb, lambda P, env: (build_l1attnb(P, env, 2.5), [
        P.op("vector", lambda e, i=i: e.tensor_copy(
            out=env.x[i][:], in_=env.SB[:, 8192 + i * 2048:8192 + (i + 1) * 2048]
            .bitcast(F32)), reads=[env.SB], writes=[env.x[i]]) for i in range(4)])[0]),
    7: ("l1ffn_b", spec_l1ffn, build_l1ffn),
    98: ("pass", spec_none, lambda P, env: None),
    50: ("l1f_s0", spec_l1ffn, lambda P, env: build_l1ffn(P, env, 0)),
    51: ("l1f_s1", spec_l1ffn, lambda P, env: build_l1ffn(P, env, 1)),
    52: ("l1f_s2", spec_l1ffn, lambda P, env: build_l1ffn(P, env, 2)),
    53: ("l1f_s3", spec_l1ffn, lambda P, env: build_l1ffn(P, env, 3)),
    54: ("l1f_s4", spec_l1ffn, lambda P, env: build_l1ffn(P, env, 4)),
    55: ("l1f_s5", spec_l1ffn, lambda P, env: build_l1ffn(P, env, 5)),
    99: ("lnonly", spec_none, build_lnonly),
}

ALL_IDX = [0, 1, 2, 3, 4, 5, 6, 7]
LAUNCH_PLAN = [[0, 1, 2, 3, 4, 5, 6, 7]]


def build_program(idxs):
    nc = bass.Bass("TRN2", target_bir_lowering=False)
    x_in = nc.dram_tensor("x_in", [T_SEQ, D_MODEL], F32, kind="ExternalInput").ap()
    x_out = nc.dram_tensor("x_out", [T_SEQ, D_MODEL], F32, kind="ExternalOutput").ap()
    specs = {}
    for i in idxs:
        specs.update(SUBLAYERS[i][1]())
    dram = {}
    for name, (shape, dtype, _fn) in specs.items():
        dram[name] = nc.dram_tensor(name, list(shape), dtype, kind="ExternalInput").ap()
    P = Prog(nc)
    env = Env()
    env.dram = dram
    env.aa_scr = nc.dram_tensor("aa_scr", [272, 512], BF16, kind="Internal").ap()
    env.aa_scrbuf = Buf("aa_scr")
    with P._ctx:
        alloc_common(P, env)
        build_identity(P, env)
        load_x(P, env, x_in)
        for i in idxs:
            SUBLAYERS[i][2](P, env)
        store_x(P, env, x_out)
        with nc.Block() as block:
            P.finalize(block)
    return nc, specs


def kernel(**inputs):
    x = np.ascontiguousarray(np.asarray(inputs["L0_attn_a_hidden"], dtype=np.float32))
    for idxs in LAUNCH_PLAN:
        nc, specs = build_program(idxs)
        wmap = {name: np.ascontiguousarray(np.asarray(fn(inputs), dtype=np.float32))
                for name, (_s, _d, fn) in specs.items()}
        in_maps = [dict(wmap, x_in=x[b]) for b in range(N_CORES)]
        res = run_bass_kernel_spmd(nc, in_maps, core_ids=list(range(N_CORES)))
        x = np.stack([np.asarray(res.results[b]["x_out"]) for b in range(N_CORES)])
    return x
